# version 02
# baseline (speedup 1.0000x reference)
import numpy as np
import ml_dtypes
import concourse.bass as bass
import concourse.mybir as mybir
from concourse.bass_utils import run_bass_kernel_spmd
from contextlib import ExitStack

F32 = mybir.dt.float32
BF16 = mybir.dt.bfloat16
AF = mybir.ActivationFunctionType
ALU = mybir.AluOpType
AX = mybir.AxisListType
BFNP = ml_dtypes.bfloat16

NCORE = 8
S = 2048
DM = 2048
EPS = 1e-5

H0, DH, BS0 = 32, 64, 256
NB0 = S // BS0
_l0_base = 1.0 / 2.0 ** (8.0 / H0)
_l0_factor = 1.0 - 0.0 / (32 - 1 + 1e-5) + 1e-5
SLOPES = [float((_l0_base ** (h + 1)) * _l0_factor) for h in range(H0)]

E0, HS, HR = 8, 8192, 1024
H1, SC1 = 32, 0.125
E1, R1, LSCALE = 8, 8, 2.0

_prog_cache = {}


def cdiv(a, b):
    return -(-a // b)


class Pipe:

    def __init__(self, st, nc, name, step=1):
        self.h = st.enter_context(nc.semaphore(name))
        self.step = step
        self.emitted = 0

    def inc(self, instr, n=1):
        instr.then_inc(self.h, self.step * n)
        self.emitted += self.step * n
        return instr

    def wait(self, eng, units):
        if units > 0:
            eng.wait_ge(self.h, self.step * units)


class Chain:

    def __init__(self, st, nc, name):
        self.h = st.enter_context(nc.semaphore(name))
        self.n = 0
        self.emitted = 0

    def seal(self, instr):
        instr.then_inc(self.h, 1)
        self.n += 1
        self.emitted = self.n
        return instr

    def gate(self, eng):
        if self.n > 0:
            eng.wait_ge(self.h, self.n)


class SPipe:

    def __init__(self, st, nc, name, nslots, step=16):
        self.p = [Pipe(st, nc, f"{name}{i}", step) for i in range(nslots)]
        self.n = nslots

    def inc(self, instr, k, n=1):
        self.p[k % self.n].inc(instr, n)
        return instr

    def wait(self, eng, k, uses):
        self.p[k % self.n].wait(eng, uses)

    def wait_all(self, eng, total_units_per_slot):
        for p in self.p:
            p.wait(eng, total_units_per_slot)


class SSched:

    def __init__(self, pipes):
        self.pipes = pipes
        self.counts = {k: 0 for k in pipes}
        self.steps = []
        self.res = {}

    def add(self, eng, fn, reads=(), writes=(), extra=()):
        if eng == "gp":
            writes = tuple(writes) + ("__gp_tok",)
        deps = {}
        for r in reads:
            ent = self.res.get(r)
            if ent and ent["w"]:
                e2, v = ent["w"]
                deps[e2] = max(deps.get(e2, 0), v)
        for w_ in writes:
            ent = self.res.get(w_)
            if ent:
                if ent["w"]:
                    e2, v = ent["w"]
                    deps[e2] = max(deps.get(e2, 0), v)
                for (e2, v) in ent["r"]:
                    deps[e2] = max(deps.get(e2, 0), v)
        self.counts[eng] += 1
        tag = (eng, self.counts[eng])
        for r in reads:
            self.res.setdefault(r, {"w": None, "r": []})["r"].append(tag)
        for w_ in writes:
            self.res[w_] = {"w": tag, "r": []}
        self.steps.append((eng, fn, deps, tag, list(extra)))
        return tag

    def emit(self, eng_name, engine):
        for (eng, fn, deps, tag, extra) in self.steps:
            if eng != eng_name:
                continue
            for e2, v in deps.items():
                if e2 == eng_name and v >= tag[1]:
                    continue
                self.pipes[e2].wait(engine, v)
            for (pipe, units) in extra:
                pipe.wait(engine, units)
            self.pipes[eng].inc(fn(engine))


class Bar:

    def __init__(self, st, nc):
        self.clr = Pipe(st, nc, "bar_clr")
        self.eng = {k: Pipe(st, nc, f"bar_{k}") for k in ("pe", "dve", "act", "gp", "sp")}
        self.counts = {k: 0 for k in self.eng}


class PhaseCtx:
    def __init__(self, nc, st, bar, pidx):
        self.nc, self.st, self.bar, self.pidx = nc, st, bar, pidx
        self.sems = []

    def pipe(self, name, step=1):
        p = Pipe(self.st, self.nc, f"{name}_{self.pidx}", step)
        self.sems.append(p)
        return p

    def spipe(self, name, nslots, step=16):
        sp = SPipe(self.st, self.nc, f"{name}_{self.pidx}", nslots, step)
        self.sems.extend(sp.p)
        return sp

    def chain(self, name):
        c = Chain(self.st, self.nc, f"{name}_{self.pidx}")
        self.sems.append(c)
        return c

    def wrap(self, blk):
        self.raw_blk = blk
        return blk


class PhBlock:
    _MAP = {"gpsimd": "gp", "vector": "dve", "scalar": "act",
            "tensor": "pe", "sync": "sp"}

    def __init__(self, blk, bar, pidx):
        self._blk, self._bar, self._pidx = blk, bar, pidx

    def _deco(self, attr):
        eng_key = self._MAP[attr]
        bar, pidx = self._bar, self._pidx

        def decorator(f):
            def body(eng):
                bar.clr.wait(eng, pidx)
                f(eng)
                bar.eng[eng_key].inc(eng.nop())
            bar.counts[eng_key] += 1
            getattr(self._blk, attr)(body)
        return decorator

    @property
    def gpsimd(self):
        return self._deco("gpsimd")

    @property
    def vector(self):
        return self._deco("vector")

    @property
    def scalar(self):
        return self._deco("scalar")

    @property
    def tensor(self):
        return self._deco("tensor")

    @property
    def sync(self):
        return self._deco("sync")



def _sbt(nc, ph, name, *a, **kw):
    return nc.sbuf_tensor(f"p{ph.pidx}_{name}", *a, **kw)


def _pst(nc, ph, name, *a, **kw):
    return nc.psum_tensor(f"p{ph.pidx}_{name}", *a, **kw)


def _phase_boundary(nc, bar, ph):
    bb = ph.raw_blk

    @bb.gpsimd
    def _(g):
        for k, cnt in bar.counts.items():
            if cnt:
                bar.eng[k].wait(g, cnt)
        for p in ph.sems:
            if p.emitted:
                g.wait_ge(p.h, p.emitted)
        for p in ph.sems:
            g.sem_clear(p.h)
        bar.clr.inc(g.nop())


def _consts():
    ident_f32 = np.eye(128, dtype=np.float32)
    ident_bf = np.eye(128, dtype=BFNP)
    ntri = np.where(np.arange(128)[:, None] >= np.arange(128)[None, :], -1.0, 0.0)
    out = dict(
        ident_f32=ident_f32,
        ident_bf=ident_bf,
        ntri_bf=ntri.astype(BFNP),
        ones_row_bf=np.ones((1, 128), dtype=BFNP),
        negcol_bf=np.full((128, 1), -1.0, dtype=BFNP),
        onescol64_bf=np.ones((64, 1), dtype=BFNP),
        onesf_row=np.ones((1, 128), dtype=np.float32),
        onesf_col=np.ones((128, 1), dtype=np.float32),
    )
    rrep = np.zeros((8, 64), dtype=BFNP)
    for e in range(8):
        rrep[e, e * 8:(e + 1) * 8] = 1.0
    out["rrep"] = rrep
    return out


_CONSTS = _consts()


def _decl_consts(nc, names):
    aps = {}
    for n in names:
        arr = _CONSTS[n]
        dt = BF16 if arr.dtype == BFNP else F32
        aps[n] = nc.dram_tensor(n, list(arr.shape), dt, kind="ExternalInput").ap()
    return aps


def _const_inputs(names):
    return {n: _CONSTS[n] for n in names}


def _emit_p1(nc, T, ph):
    x, x0_fm, xn0_tm, C = T["x_tm"], T["x0_fm"], T["xn0_tm"], T
    NT = S // 128

    if True:
        e = ph.st.enter_context
        xt = [e(_sbt(nc, ph, f"xt{i}", [128, DM], F32)) for i in range(2)]
        xnt = e(_sbt(nc, ph, "xnt", [128, DM], BF16))
        sq = e(_sbt(nc, ph, "sq", [128, DM], F32))
        idf = e(_sbt(nc, ph, "idf", [128, 128], F32))
        ssum = e(_sbt(nc, ph, "ssum", [128, 1], F32))
        ssq = e(_sbt(nc, ph, "ssq", [128, 1], F32))
        mu = e(_sbt(nc, ph, "mu", [128, 1], F32))
        var = e(_sbt(nc, ph, "var", [128, 1], F32))
        var2 = e(_sbt(nc, ph, "var2", [128, 1], F32))
        lnv = e(_sbt(nc, ph, "lnv", [128, 1], F32))
        rstd = e(_sbt(nc, ph, "rstd", [128, 1], F32))
        bcol = e(_sbt(nc, ph, "bcol", [128, 1], F32))
        epscol = e(_sbt(nc, ph, "epscol", [128, 1], F32))
        evf = [e(_sbt(nc, ph, f"evf{i}", [128, 128], F32)) for i in range(2)]
        psf = [e(_pst(nc, ph, f"psf{i}", [128, 128], F32)) for i in range(2)]

        ld = ph.spipe("ld", 2)
        lc = ph.pipe("lc", 16)
        st_n = ph.pipe("st_n", 16)
        st_f = ph.spipe("st_f", 2)
        s_red = ph.pipe("s_red")
        s_sq = ph.pipe("s_sq")
        s_var = ph.pipe("s_var")
        s_rstd = ph.pipe("s_rstd")
        s_app = ph.pipe("s_app")
        s_tp = ph.pipe("s_tp")
        s_ev = ph.pipe("s_ev")
        cv = ph.chain("cv")
        ca = ph.chain("ca")
        blk = ph.wrap(e(nc.Block()))

        @blk.gpsimd
        def _(g):
            g.memset(epscol[:, :], EPS)
            lc.inc(g.dma_start(idf[:, :], C["ident_f32"]))
            for it in range(NT + 2):
                if it >= 2:
                    u = it - 2
                    s_app.wait(g, u + 1)
                    st_n.inc(g.dma_start(xn0_tm[u * 128:(u + 1) * 128, :], xnt[:, :]))
                    for dc in range(16):
                        k = u * 16 + dc
                        s_ev.wait(g, k + 1)
                        st_f.inc(g.dma_start(
                            x0_fm[dc * 128:(dc + 1) * 128, u * 128:(u + 1) * 128],
                            evf[dc % 2][:, :]), k)
                if it < NT:
                    tc = it
                    if tc >= 2:
                        s_red.wait(g, tc - 1)
                        s_app.wait(g, tc - 1)
                        s_tp.wait(g, 16 * (tc - 1))
                    ld.inc(g.dma_start(xt[tc % 2][:, :], x[tc * 128:(tc + 1) * 128, :]), tc)
            st_f.wait_all(g, NT * 16 // 2)
            st_n.wait(g, NT)

        @blk.vector
        def _(v):
            for tc in range(NT):
                ld.wait(v, tc, tc // 2 + 1)
                s_red.inc(v.reduce_sum(ssum[:, :], xt[tc % 2][:, :], axis=AX.X))
                s_red.wait(v, tc + 1)
                s_sq.wait(v, tc + 1)
                s_rstd.wait(v, tc)
                cv.seal(v.tensor_scalar_mul(mu[:, :], ssum[:, :], 1.0 / DM))
                cv.seal(v.tensor_scalar_mul(var2[:, :], ssq[:, :], 1.0 / DM))
                cv.gate(v)
                cv.seal(v.scalar_tensor_tensor(bcol[:, :], mu[:, :], -1.0, mu[:, :], ALU.mult, ALU.mult))
                cv.gate(v)
                s_var.inc(v.tensor_tensor(var[:, :], var2[:, :], bcol[:, :], ALU.add))
                s_rstd.wait(v, tc + 1)
                cv.seal(v.scalar_tensor_tensor(bcol[:, :], mu[:, :], -1.0, rstd[:, :], ALU.mult, ALU.mult))
                cv.gate(v)
                if tc >= 1:
                    st_n.wait(v, tc)
                s_app.inc(v.tensor_scalar(xnt[:, :], xt[tc % 2][:, :], rstd[:, :], bcol[:, :], ALU.mult, ALU.add))
                for dc in range(16):
                    k = tc * 16 + dc
                    s_tp.wait(v, k + 1)
                    if k >= 2:
                        st_f.wait(v, k, k // 2)
                    s_ev.inc(v.tensor_copy(evf[dc % 2][:, :], psf[dc % 2][:, :]))

        @blk.scalar
        def _(sc):
            for tc in range(NT):
                ld.wait(sc, tc, tc // 2 + 1)
                s_var.wait(sc, tc)
                s_sq.inc(sc.activation(sq[:, :], xt[tc % 2][:, :], AF.Square,
                                       accum_out=ssq[:, :]))
                s_var.wait(sc, tc + 1)
                s_app.wait(sc, tc)
                ca.seal(sc.activation(lnv[:, :], var[:, :], AF.Ln, bias=epscol[:, :]))
                ca.gate(sc)
                s_rstd.inc(sc.activation(rstd[:, :], lnv[:, :], AF.Exp, scale=-0.5))

        @blk.tensor
        def _(t):
            lc.wait(t, 1)
            for tc in range(NT):
                ld.wait(t, tc, tc // 2 + 1)
                for dc in range(16):
                    k = tc * 16 + dc
                    if k >= 2:
                        s_ev.wait(t, k - 1)
                    s_tp.inc(t.transpose(psf[dc % 2][:, :],
                                         xt[tc % 2][:, dc * 128:(dc + 1) * 128],
                                         idf[:, :]))



def _emit_p2(nc, T, ph):
    qkv_w, router_w, ebias = T["qkv_w"], T["router_w"], T["ebias"]
    xn0_tm, qkv_fm, combT, C = T["xn0_tm"], T["qkv_fm"], T["combT"], T
    qkv_bfw = nc.dram_tensor("qkv_bfw", [3 * H0 * DH, DM], BF16, kind="Internal").ap()
    router_bfw = nc.dram_tensor("router_bfw", [E0, DM], BF16, kind="Internal").ap()
    NOB = (3 * H0 * DH) // 512
    NTC = S // 128
    NTT = S // 512

    if True:
        e = ph.st.enter_context
        xn = [e(_sbt(nc, ph, f"xn{i}", [128, S], BF16)) for i in range(16)]
        wbuf = [e(_sbt(nc, ph, f"wbuf{i}", [128, 16 * 512], BF16, is_transpose_dest=True)) for i in range(2)]
        evq = [e(_sbt(nc, ph, f"evq{i}", [128, 512], BF16)) for i in range(4)]
        rsb = e(_sbt(nc, ph, "rsb", [E0, DM], BF16))
        routerT = e(_sbt(nc, ph, "routerT", [128, 128], BF16))
        idb = e(_sbt(nc, ph, "idb", [128, 128], BF16))
        onesf = e(_sbt(nc, ph, "onesf", [1, 128], F32))
        eb = e(_sbt(nc, ph, "eb", [1, E0], F32))
        bias_bc = e(_sbt(nc, ph, "bias_bc", [128, E0], F32))
        sc_tm = [e(_sbt(nc, ph, f"sc_tm{i}", [128, E0], F32)) for i in range(2)]
        comb_tm = [e(_sbt(nc, ph, f"comb_tm{i}", [128, E0], BF16)) for i in range(2)]
        combT_sb = e(_sbt(nc, ph, "combT_sb", [E0, S], BF16))
        sel = e(_sbt(nc, ph, "sel", [128, E0], F32))
        m1 = e(_sbt(nc, ph, "m1", [128, 1], F32))
        m2 = e(_sbt(nc, ph, "m2", [128, 1], F32))
        is1 = e(_sbt(nc, ph, "is1", [128, E0], F32))
        sel2 = e(_sbt(nc, ph, "sel2", [128, E0], F32))
        ind = e(_sbt(nc, ph, "ind", [128, E0], F32))
        wsc = e(_sbt(nc, ph, "wsc", [128, E0], F32))
        ssum = e(_sbt(nc, ph, "ssumr", [128, 1], F32))
        rcp = e(_sbt(nc, ph, "rcp", [128, 1], F32))

        ps_rtp = e(_pst(nc, ph, "ps_rtp", [128, E0], BF16))
        ps_ctp = e(_pst(nc, ph, "ps_ctp", [E0, 128], BF16))
        ps_sc = [e(_pst(nc, ph, f"ps_sc{i}", [128, E0], F32)) for i in range(2)]
        ps_q = [e(_pst(nc, ph, f"ps_q{i}", [128, 512], F32)) for i in range(4)]

        rc = ph.pipe("rc", 16)
        qc = ph.pipe("qc", 16)
        gl = ph.pipe("gl", 16)
        sxn = ph.pipe("sxn", 16)
        sw = ph.spipe("sw", 2)
        stc = ph.pipe("stc", 16)
        r_pe = ph.pipe("r_pe")
        r_dv = ph.pipe("r_dv")
        r_ac = ph.pipe("r_ac")
        s_mm = ph.pipe("s_mm")
        evp = ph.pipe("evp")
        stq = ph.spipe("stq", 4)
        cdv = ph.chain("cdv")
        blk = ph.wrap(e(nc.Block()))

        @blk.gpsimd
        def _(g):
            rc.inc(g.dma_start(router_bfw, router_w))
            for c in range(6):
                qc.inc(g.dma_start(qkv_bfw[c * 1024:(c + 1) * 1024, :],
                                   qkv_w[c * 1024:(c + 1) * 1024, :]))
            rc.wait(g, 1)
            gl.inc(g.dma_start(rsb[:, :], router_bfw))
            gl.inc(g.dma_start(eb[:, :], ebias.rearrange("(b a) -> b a", b=1)))
            gl.inc(g.dma_start(idb[:, :], C["ident_bf"]))
            gl.inc(g.dma_start(onesf[:, :], C["onesf_row"]))
            r_dv.wait(g, 49)
            stc.inc(g.dma_start(combT, combT_sb[:, :]))
            for ob in range(NOB):
                for tt in range(NTT):
                    for oc4 in range(4):
                        gidx = ob * 16 + tt * 4 + oc4
                        evp.wait(g, gidx + 1)
                        stq.inc(g.dma_start(
                            qkv_fm[ob * 512 + oc4 * 128: ob * 512 + (oc4 + 1) * 128,
                                   tt * 512:(tt + 1) * 512],
                            evq[gidx % 4][:, :]), gidx)
            stq.wait_all(g, NOB * 16 // 4)
            stc.wait(g, 1)

        @blk.sync
        def _(s):
            for ic in range(16):
                sxn.inc(s.dma_start_transpose(xn[ic][:, :], xn0_tm[:, ic * 128:(ic + 1) * 128]))
            qc.wait(s, 6)
            for ob in range(NOB):
                if ob >= 2:
                    s_mm.wait(s, 16 * (ob - 1))
                for ic in range(16):
                    sw.inc(s.dma_start_transpose(
                        wbuf[ob % 2][:, ic * 512:(ic + 1) * 512],
                        qkv_bfw[ob * 512:(ob + 1) * 512, ic * 128:(ic + 1) * 128]), ob)

        @blk.tensor
        def _(t):
            gl.wait(t, 4)
            for ic in range(16):
                if ic >= 1:
                    r_dv.wait(t, ic)
                r_pe.inc(t.transpose(ps_rtp[:, :], rsb[:, ic * 128:(ic + 1) * 128],
                                     idb[0:E0, 0:E0]))
            r_dv.wait(t, 16)
            r_pe.inc(t.matmul(ps_sc[0][:, :], onesf[:, :], eb[:, :], start=True, stop=True))
            sxn.wait(t, 16)
            for tc in range(NTC):
                r_dv.wait(t, 17)
                if tc >= 2:
                    r_ac.wait(t, tc - 1)
                for ic in range(16):
                    r_pe.inc(t.matmul(ps_sc[tc % 2][:, :],
                                      xn[ic][:, tc * 128:(tc + 1) * 128],
                                      routerT[:, ic * 8:(ic + 1) * 8],
                                      start=(ic == 0), stop=(ic == 15)), n=(1 if ic == 15 else 0))
                if tc >= 2:
                    j = tc - 2
                    r_dv.wait(t, 18 + 2 * j)
                    r_pe.inc(t.transpose(ps_ctp[:, :], comb_tm[j % 2][:, :], idb[:, :]))
            for j in (14, 15):
                r_dv.wait(t, 18 + 2 * j)
                r_pe.inc(t.transpose(ps_ctp[:, :], comb_tm[j % 2][:, :], idb[:, :]))
            for ob in range(NOB):
                sw.wait(t, ob, 16 * (ob // 2 + 1))
                for tt in range(NTT):
                    for oc4 in range(4):
                        gidx = ob * 16 + tt * 4 + oc4
                        if gidx >= 4:
                            evp.wait(t, gidx - 3)
                        for ic in range(16):
                            s_mm.inc(t.matmul(ps_q[oc4][:, :],
                                              wbuf[ob % 2][:, ic * 512 + oc4 * 128: ic * 512 + (oc4 + 1) * 128],
                                              xn[ic][:, tt * 512:(tt + 1) * 512],
                                              start=(ic == 0), stop=(ic == 15)),
                                     n=(1 if ic == 15 else 0))

        @blk.vector
        def _(v):
            for ic in range(16):
                r_pe.wait(v, ic + 1)
                r_dv.inc(v.tensor_copy(routerT[:, ic * 8:(ic + 1) * 8], ps_rtp[:, :]))
            r_pe.wait(v, 17)
            r_dv.inc(v.tensor_copy(bias_bc[:, :], ps_sc[0][:, :]))
            def pe_ctp_unit(j):
                return 21 + 2 * j if j <= 13 else (48 if j == 14 else 49)
            for tc in range(NTC):
                if tc >= 1:
                    r_pe.wait(v, pe_ctp_unit(tc - 1))
                    r_dv.inc(v.tensor_copy(combT_sb[:, (tc - 1) * 128:tc * 128], ps_ctp[:, :]))
                r_ac.wait(v, tc + 1)
                if tc >= 2:
                    r_pe.wait(v, pe_ctp_unit(tc - 2))
                cdv.seal(v.tensor_tensor(sel[:, :], sc_tm[tc % 2][:, :], bias_bc[:, :], ALU.add))
                cdv.gate(v)
                cdv.seal(v.reduce_max(m1[:, :], sel[:, :], axis=AX.X))
                cdv.gate(v)
                cdv.seal(v.tensor_single_scalar(is1[:, :], sel[:, :], m1[:, 0:1], ALU.is_ge))
                cdv.gate(v)
                cdv.seal(v.scalar_tensor_tensor(sel2[:, :], is1[:, :], -1e9, sel[:, :], ALU.mult, ALU.add))
                cdv.gate(v)
                cdv.seal(v.reduce_max(m2[:, :], sel2[:, :], axis=AX.X))
                cdv.gate(v)
                cdv.seal(v.tensor_single_scalar(ind[:, :], sel[:, :], m2[:, 0:1], ALU.is_ge))
                cdv.gate(v)
                cdv.seal(v.tensor_tensor(wsc[:, :], ind[:, :], sc_tm[tc % 2][:, :], ALU.mult))
                cdv.gate(v)
                cdv.seal(v.reduce_sum(ssum[:, :], wsc[:, :], axis=AX.X))
                cdv.gate(v)
                cdv.seal(v.tensor_single_scalar(ssum[:, :], ssum[:, :], 1e-9, ALU.add))
                cdv.gate(v)
                cdv.seal(v.reciprocal(rcp[:, :], ssum[:, :]))
                cdv.gate(v)
                r_dv.inc(v.tensor_single_scalar(comb_tm[tc % 2][:, :], wsc[:, :], rcp[:, 0:1], ALU.mult))
            r_pe.wait(v, pe_ctp_unit(15))
            r_dv.inc(v.tensor_copy(combT_sb[:, 15 * 128:16 * 128], ps_ctp[:, :]))

        @blk.scalar
        def _(sc):
            for tc in range(NTC):
                r_pe.wait(sc, 18 + tc if tc < 2 else 16 + 2 * tc)
                if tc >= 2:
                    r_dv.wait(sc, 18 + 2 * (tc - 2))
                r_ac.inc(sc.activation(sc_tm[tc % 2][:, :], ps_sc[tc % 2][:, :], AF.Sigmoid))
            for ob in range(NOB):
                for tt in range(NTT):
                    for oc4 in range(4):
                        gidx = ob * 16 + tt * 4 + oc4
                        s_mm.wait(sc, gidx + 1)
                        if gidx >= 4:
                            stq.wait(sc, gidx, gidx // 4)
                        evp.inc(sc.activation(evq[gidx % 4][:, :], ps_q[oc4][:, :], AF.Silu))



def _emit_p3(nc, T, ph):
    qkv_fm, norm_w, attnn_fm, C = T["qkv_fm"], T["norm_w"], T["attnn_fm"], T
    NM = H0 * NB0

    if True:
        e = ph.st.enter_context
        qh = [e(_sbt(nc, ph, f"qh{i}", [64, S], BF16)) for i in range(2)]
        kh = [e(_sbt(nc, ph, f"kh{i}", [64, S], BF16)) for i in range(2)]
        vh = [e(_sbt(nc, ph, f"vh{i}", [64, S], BF16)) for i in range(2)]
        attn = [e(_sbt(nc, ph, f"attn{i}", [128, S], BF16)) for i in range(16)]
        D = [e(_sbt(nc, ph, f"D{i}", [128, BS0], F32)) for i in range(2)]
        m01 = [e(_sbt(nc, ph, f"m01{i}", [128, BS0], F32)) for i in range(2)]
        Dc = [e(_sbt(nc, ph, f"Dc{i}", [128, BS0], F32)) for i in range(2)]
        eexp = [e(_sbt(nc, ph, f"eexp{i}", [128, BS0], F32)) for i in range(2)]
        ddT = [e(_sbt(nc, ph, f"ddT{i}", [128, BS0], BF16)) for i in range(2)]
        ior = e(_sbt(nc, ph, "ior", [1, BS0], F32))
        ibc = e(_sbt(nc, ph, "ibc", [64, BS0], F32))
        iocol = [e(_sbt(nc, ph, f"iocol{i}", [128, 1], F32)) for i in range(2)]
        qd = [e(_sbt(nc, ph, f"qd{i}", [64, BS0], BF16)) for i in range(2)]
        kdc = [e(_sbt(nc, ph, f"kdc{i}", [128, 2], F32)) for i in range(2)]
        awT = [e(_sbt(nc, ph, f"awT{i}", [128, BS0], BF16)) for i in range(2)]
        kdkT = [e(_sbt(nc, ph, f"kdkT{i}", [128, 64], BF16)) for i in range(2)]
        vbT = [e(_sbt(nc, ph, f"vbT{i}", [128, 64], BF16)) for i in range(2)]
        qdq = e(_sbt(nc, ph, "qdq", [64, BS0], BF16))
        S_f = [e(_sbt(nc, ph, f"S_f{i}", [64, 64], F32)) for i in range(2)]
        S_b = e(_sbt(nc, ph, "S_b", [64, 64], BF16))
        sq_b = e(_sbt(nc, ph, "sq_b", [64, BS0], BF16))
        ssb = [e(_sbt(nc, ph, f"ssb{i}", [1, S], F32)) for i in range(2)]
        lnr = e(_sbt(nc, ph, "lnr", [1, S], F32))
        rstd_r = e(_sbt(nc, ph, "rstd_r", [1, S], F32))
        ebias1 = e(_sbt(nc, ph, "ebias1", [1, 1], F32))
        ebias2 = e(_sbt(nc, ph, "ebias2", [1, 1], F32))
        nwrow = e(_sbt(nc, ph, "nwrow", [1, DM], F32))
        nw_cols = e(_sbt(nc, ph, "nw_cols", [128, 16], F32))
        o64 = e(_sbt(nc, ph, "o64", [64, 1], BF16))
        onesf = e(_sbt(nc, ph, "onesf", [1, 128], F32))
        idf = e(_sbt(nc, ph, "idf", [128, 128], F32))
        idb3 = e(_sbt(nc, ph, "idb3", [128, 128], BF16))
        anout = [e(_sbt(nc, ph, f"anout{i}", [128, 512], BF16)) for i in range(4)]

        ps_sc = [e(_pst(nc, ph, f"ps_sc{i}", [128, 512], F32)) for i in range(2)]
        ps_y = e(_pst(nc, ph, "ps_y", [64, BS0], F32))
        ps_S = e(_pst(nc, ph, "ps_S", [64, 64], F32))
        ps_tp = [e(_pst(nc, ph, f"ps_tp{i}", [128, 64], BF16)) for i in range(2)]
        ps_ss = e(_pst(nc, ph, "ps_ss", [1, BS0], F32))
        ps_nw = e(_pst(nc, ph, "ps_nw", [128, 16], F32))

        g_init = ph.pipe("g_init", 16)
        g_eng = ph.pipe("g_eng")
        qld = ph.spipe("qld", 2)
        d_pre = ph.pipe("d_pre")
        p_pre = ph.pipe("p_pre")
        a_set = ph.pipe("a_set")
        d_set = ph.pipe("d_set")
        p_sc = ph.pipe("p_sc")
        d_aw = ph.pipe("d_aw")
        p_tp = ph.pipe("p_tp")
        d_tp = ph.pipe("d_tp")
        d_qd = ph.pipe("d_qd")
        p_y = ph.pipe("p_y")
        p_S = ph.pipe("p_S")
        d_S = ph.pipe("d_S")
        a_y = ph.pipe("a_y")
        a_sq = ph.pipe("a_sq")
        p_ss = ph.pipe("p_ss")
        d_ss = ph.pipe("d_ss")
        a_r = ph.pipe("a_r")
        n_pe = ph.pipe("n_pe")
        n_dv = ph.pipe("n_dv")
        n_st = ph.spipe("n_st", 4)
        cdv = ph.chain("cdv3")
        cac = ph.chain("cac3")
        blk = ph.wrap(e(nc.Block()))

        @blk.gpsimd
        def _(g):
            g_eng.inc(g.iota(D[0][:, :], [[1, BS0]], base=0, channel_multiplier=-1,
                             allow_small_or_imprecise_dtypes=True))
            g_eng.inc(g.iota(D[1][:, :], [[1, BS0]], base=-128, channel_multiplier=-1,
                             allow_small_or_imprecise_dtypes=True))
            g_eng.inc(g.iota(ior[:, :], [[1, BS0]], base=1, channel_multiplier=0,
                             allow_small_or_imprecise_dtypes=True))
            g_eng.inc(g.iota(iocol[0][:, :], [[0, 1]], base=1 - BS0, channel_multiplier=1,
                             allow_small_or_imprecise_dtypes=True))
            g_eng.inc(g.iota(iocol[1][:, :], [[0, 1]], base=1 - BS0 + 128, channel_multiplier=1,
                             allow_small_or_imprecise_dtypes=True))
            g_eng.inc(g.memset(ebias1[:, :], float(DM) * EPS))
            g_eng.inc(g.memset(ebias2[:, :], 0.5 * float(np.log(DM))))
            g_eng.inc(g.memset(o64[:, :], 1.0))
            g_init.inc(g.dma_start(nwrow[:, :], norm_w.rearrange("(b a) -> b a", b=1)))
            g_init.inc(g.dma_start(idf[:, :], C["ident_f32"]))
            g_init.inc(g.dma_start(onesf[:, :], C["onesf_row"]))
            g_init.inc(g.dma_start(idb3[:, :], C["ident_bf"]))
            for h in range(H0):
                if h >= 2:
                    p_ss.wait(g, NB0 * (h - 1))
                    d_qd.wait(g, NB0 * (h - 1))
                base = h * 3 * DH
                qld.inc(g.dma_start(qh[h % 2][:, :], qkv_fm[base:base + DH, :]), h)
                qld.inc(g.dma_start(kh[h % 2][:, :], qkv_fm[base + DH:base + 2 * DH, :]), h)
                qld.inc(g.dma_start(vh[h % 2][:, :], qkv_fm[base + 2 * DH:base + 3 * DH, :]), h)
            for tt in range(4):
                for fc in range(16):
                    k = tt * 16 + fc
                    n_dv.wait(g, k + 2)
                    n_st.inc(g.dma_start(
                        attnn_fm[fc * 128:(fc + 1) * 128, tt * 512:(tt + 1) * 512],
                        anout[k % 4][:, :]), k)
            n_st.wait_all(g, 16)

        @blk.tensor
        def _(t):
            g_init.wait(t, 4)
            g_eng.wait(t, 8)
            p_pre.inc(t.matmul(ps_sc[0][0:64, 0:BS0], onesf[:, 0:64], ior[:, :],
                               start=True, stop=True))
            for h in range(H0):
                qld.wait(t, h, 3 * (h // 2 + 1))
                for n in range(NB0):
                    m = h * NB0 + n
                    d_aw.wait(t, 2 * m)
                    if m == 0:
                        d_pre.wait(t, 1)
                    for c in range(2):
                        p_sc.inc(t.matmul(
                            ps_sc[c][:, 0:BS0],
                            kh[h % 2][:, n * BS0 + c * 128: n * BS0 + (c + 1) * 128],
                            qh[h % 2][:, n * BS0:(n + 1) * BS0],
                            start=True, stop=True), n=(1 if c == 1 else 0))
                    for j in range(4):
                        tix = 4 * m + j
                        c = j % 2
                        src_t = kh[h % 2] if j < 2 else vh[h % 2]
                        if tix >= 2:
                            d_tp.wait(t, tix - 1)
                        p_tp.inc(t.transpose(
                            ps_tp[j % 2][:, :],
                            src_t[:, n * BS0 + c * 128: n * BS0 + (c + 1) * 128],
                            idb3[0:DH, 0:DH]))
                    d_aw.wait(t, 2 * m + 2)
                    d_tp.wait(t, 4 * m + 4)
                    d_qd.wait(t, m + 1)
                    d_S.wait(t, 9 * h + 1 + n)
                    a_sq.wait(t, m)
                    t.matmul(ps_y[:, :], vbT[0][:, :], awT[0][:, :], start=True, stop=False)
                    t.matmul(ps_y[:, :], vbT[1][:, :], awT[1][:, :], start=False, stop=False)
                    p_y.inc(t.matmul(ps_y[:, :], S_b[:, :], qdq[:, :], start=False, stop=True))
                    t.matmul(ps_S[:, :], kdkT[0][:, :], vbT[0][:, :], start=True, stop=False)
                    p_S.inc(t.matmul(ps_S[:, :], kdkT[1][:, :], vbT[1][:, :], start=False, stop=True))
                    a_sq.wait(t, m + 1)
                    d_ss.wait(t, m)
                    p_ss.inc(t.matmul(ps_ss[:, :], o64[:, :], sq_b[:, :], start=True, stop=True))
            for c in range(16):
                n_pe.inc(t.transpose(ps_nw[:, c:c + 1], nwrow[:, c * 128:(c + 1) * 128],
                                     idf[0:1, 0:1]))
            a_r.wait(t, 1)
            d_aw.wait(t, 2 * NM)
            for tt in range(4):
                if tt >= 2:
                    n_dv.wait(t, 1 + (tt - 2) * 16 + 16)
                n_pe.inc(t.matmul(ps_sc[tt % 2][:, :], onesf[:, :],
                                  rstd_r[:, tt * 512:(tt + 1) * 512], start=True, stop=True))

        @blk.vector
        def _(v):
            g_init.wait(v, 4)
            g_eng.wait(v, 8)
            for c in range(2):
                v.tensor_single_scalar(m01[c][:, :], D[c][:, :], 0.0, ALU.is_ge)
                v.tensor_single_scalar(Dc[c][:, :], D[c][:, :], 0.0, ALU.max)
            cdv.seal(v.memset(ssb[0][:, :], 0.0))
            p_pre.wait(v, 1)
            d_pre.inc(v.tensor_copy(ibc[:, :], ps_sc[0][0:64, 0:BS0]))
            for h in range(H0):
                a_set.wait(v, 5 * (h + 1))
                d_set.inc(v.tensor_tensor(ddT[0][:, :], eexp[0][:, :], m01[0][:, :], ALU.mult))
                d_set.inc(v.tensor_tensor(ddT[1][:, :], eexp[1][:, :], m01[1][:, :], ALU.mult))
                p_S.wait(v, NB0 * h)
                p_y.wait(v, NB0 * h)
                d_S.wait(v, 9 * h)
                v.memset(S_f[0][:, :], 0.0)
                d_S.inc(v.memset(S_b[:, :], 0.0))
                d_set.wait(v, 2 * (h + 1))
                for n in range(NB0):
                    m = h * NB0 + n
                    p_sc.wait(v, m + 1)
                    for c in range(2):
                        d_aw.inc(v.tensor_tensor(awT[c][:, :], ps_sc[c][:, 0:BS0],
                                                 ddT[c][:, :], ALU.mult))
                    for j in range(4):
                        tix = 4 * m + j
                        c = j % 2
                        p_tp.wait(v, tix + 1)
                        if j < 2:
                            d_tp.inc(v.tensor_single_scalar(kdkT[c][:, :], ps_tp[j % 2][:, :],
                                                            kdc[h % 2][:, c:c + 1], ALU.mult))
                        else:
                            d_tp.inc(v.tensor_copy(vbT[c][:, :], ps_tp[j % 2][:, :]))
                    p_y.wait(v, m)
                    d_qd.inc(v.tensor_tensor(qdq[:, :], qh[h % 2][:, n * BS0:(n + 1) * BS0],
                                             qd[h % 2][:, :], ALU.mult))
                    p_ss.wait(v, m + 1)
                    if m == 0:
                        cdv.gate(v)
                    d_ss.inc(v.tensor_tensor(ssb[(h + 1) % 2][:, n * BS0:(n + 1) * BS0],
                                             ssb[h % 2][:, n * BS0:(n + 1) * BS0],
                                             ps_ss[:, :], ALU.add))
                    p_S.wait(v, m + 1)
                    p_y.wait(v, m + 1)
                    bd = float(np.exp(-SLOPES[h] * BS0))
                    cdv.seal(v.scalar_tensor_tensor(S_f[(n + 1) % 2][:, :], S_f[n % 2][:, :],
                                                    bd, ps_S[:, :], ALU.mult, ALU.add))
                    cdv.gate(v)
                    d_S.inc(v.tensor_copy(S_b[:, :], S_f[(n + 1) % 2][:, :]))
            n_pe.wait(v, 16)
            n_dv.inc(v.tensor_copy(nw_cols[:, :], ps_nw[:, :]))
            n_dv.wait(v, 1)
            for tt in range(4):
                n_pe.wait(v, 16 + tt + 1)
                for fc in range(16):
                    k = tt * 16 + fc
                    if k >= 4:
                        n_st.wait(v, k, k // 4)
                    n_dv.inc(v.scalar_tensor_tensor(anout[k % 4][:, :],
                                                    attn[fc][:, tt * 512:(tt + 1) * 512],
                                                    nw_cols[:, fc:fc + 1],
                                                    ps_sc[tt % 2][:, :],
                                                    ALU.mult, ALU.mult))

        @blk.scalar
        def _(sc):
            g_init.wait(sc, 4)
            g_eng.wait(sc, 8)
            d_pre.wait(sc, 1)
            for h in range(H0):
                slope = SLOPES[h]
                if h >= 1:
                    d_set.wait(sc, 2 * h)
                if h >= 2:
                    d_qd.wait(sc, NB0 * (h - 1))
                    d_tp.wait(sc, 4 * NB0 * (h - 1))
                for c in range(2):
                    a_set.inc(sc.activation(eexp[c][:, :], Dc[c][:, :], AF.Exp, scale=-slope))
                a_set.inc(sc.activation(qd[h % 2][:, :], ibc[:, :], AF.Exp, scale=-slope))
                for c in range(2):
                    a_set.inc(sc.activation(kdc[h % 2][:, c:c + 1], iocol[c][:, :], AF.Exp, scale=slope))
                for n in range(NB0):
                    m = h * NB0 + n
                    p_y.wait(sc, m + 1)
                    fc, off = h // 2, (h % 2) * 64
                    a_y.inc(sc.copy(attn[fc][off:off + 64, n * BS0:(n + 1) * BS0], ps_y[:, :]))
                    p_ss.wait(sc, m)
                    a_sq.inc(sc.activation(sq_b[:, :], ps_y[:, :], AF.Square))
            d_ss.wait(sc, NM)
            cac.seal(sc.activation(lnr[:, :], ssb[0][:, :], AF.Ln, bias=ebias1[:, :]))
            cac.gate(sc)
            a_r.inc(sc.activation(rstd_r[:, :], lnr[:, :], AF.Exp, scale=-0.5,
                                  bias=ebias2[:, :]))



def _emit_p4(nc, T, ph):
    gate_w, out_w = T["gate_w"], T["out_w"]
    xn0_tm, attnn_fm, s0_fm = T["xn0_tm"], T["attnn_fm"], T["s0_fm"]
    gate_bfw = nc.dram_tensor("gate_bfw", [DM, DM], BF16, kind="Internal").ap()
    outw_bfw = nc.dram_tensor("outw_bfw", [DM, DM], BF16, kind="Internal").ap()

    if True:
        e = ph.st.enter_context
        xn = [e(_sbt(nc, ph, f"xn{i}", [128, S], BF16)) for i in range(16)]
        asl = [e(_sbt(nc, ph, f"asl{i}", [128, 512], BF16)) for i in range(16)]
        zsl = [e(_sbt(nc, ph, f"zsl{i}", [128, 512], BF16)) for i in range(16)]
        wA = [e(_sbt(nc, ph, f"wA{i}", [128, 16 * 512], BF16, is_transpose_dest=True)) for i in range(2)]
        wB = [e(_sbt(nc, ph, f"wB{i}", [128, 16 * 512], BF16, is_transpose_dest=True)) for i in range(2)]
        gsig = [e(_sbt(nc, ph, f"gsig{i}", [128, 512], F32)) for i in range(4)]
        sout = [e(_sbt(nc, ph, f"sout{i}", [128, 512], BF16)) for i in range(4)]
        psA = [e(_pst(nc, ph, f"psA{i}", [128, 512], F32)) for i in range(4)]
        psB = [e(_pst(nc, ph, f"psB{i}", [128, 512], F32)) for i in range(4)]

        gc = ph.pipe("gc", 16)
        ocst = ph.pipe("ocst", 16)
        sxn = ph.pipe("sxn", 16)
        ald = ph.pipe("ald", 16)
        swA = ph.spipe("swA", 2)
        swB = ph.spipe("swB", 2)
        a_mm = ph.pipe("a_mm")
        a_ev = ph.pipe("a_ev")
        d_z = ph.pipe("d_z")
        b_mm = ph.pipe("b_mm")
        b_ev = ph.pipe("b_ev")
        stp = ph.spipe("stp", 4)
        blk = ph.wrap(e(nc.Block()))

        @blk.gpsimd
        def _(g):
            for c in range(4):
                gc.inc(g.dma_start(gate_bfw[c * 512:(c + 1) * 512, :],
                                   gate_w[c * 512:(c + 1) * 512, :]))
            for c in range(4):
                ocst.inc(g.dma_start(outw_bfw[c * 512:(c + 1) * 512, :],
                                     out_w[c * 512:(c + 1) * 512, :]))
            for tt in range(4):
                d_z.wait(g, 16 * tt)
                for ic in range(16):
                    ald.inc(g.dma_start(asl[ic][:, :],
                                        attnn_fm[ic * 128:(ic + 1) * 128,
                                                 tt * 512:(tt + 1) * 512]))
                for gb in range(16):
                    k = tt * 16 + gb
                    b_ev.wait(g, k + 1)
                    ob, oc4 = gb // 4, gb % 4
                    stp.inc(g.dma_start(
                        s0_fm[ob * 512 + oc4 * 128: ob * 512 + (oc4 + 1) * 128,
                              tt * 512:(tt + 1) * 512],
                        sout[k % 4][:, :]), k)
            stp.wait_all(g, 16)

        @blk.sync
        def _(s):
            for ic in range(16):
                sxn.inc(s.dma_start_transpose(xn[ic][:, :], xn0_tm[:, ic * 128:(ic + 1) * 128]))
            gc.wait(s, 4)
            ocst.wait(s, 4)
            for tt in range(4):
                for ob in range(4):
                    ga = tt * 4 + ob
                    if ga >= 2:
                        a_mm.wait(s, 4 * (ga - 1))
                    for ic in range(16):
                        swA.inc(s.dma_start_transpose(
                            wA[ga % 2][:, ic * 512:(ic + 1) * 512],
                            gate_bfw[ob * 512:(ob + 1) * 512, ic * 128:(ic + 1) * 128]), ga)
                for ob in range(4):
                    gb = tt * 4 + ob
                    if gb >= 2:
                        b_mm.wait(s, 4 * (gb - 1))
                    for ic in range(16):
                        swB.inc(s.dma_start_transpose(
                            wB[gb % 2][:, ic * 512:(ic + 1) * 512],
                            outw_bfw[ob * 512:(ob + 1) * 512, ic * 128:(ic + 1) * 128]), gb)

        @blk.tensor
        def _(t):
            sxn.wait(t, 16)
            for tt in range(4):
                for ob in range(4):
                    ga = tt * 4 + ob
                    swA.wait(t, ga, 16 * (ga // 2 + 1))
                    for oc4 in range(4):
                        gidx = ga * 4 + oc4
                        if gidx >= 4:
                            a_ev.wait(t, gidx - 3)
                        for ic in range(16):
                            a_mm.inc(t.matmul(psA[oc4][:, :],
                                              wA[ga % 2][:, ic * 512 + oc4 * 128: ic * 512 + (oc4 + 1) * 128],
                                              xn[ic][:, tt * 512:(tt + 1) * 512],
                                              start=(ic == 0), stop=(ic == 15)),
                                     n=(1 if ic == 15 else 0))
                d_z.wait(t, 16 * (tt + 1))
                for ob in range(4):
                    gb = tt * 4 + ob
                    swB.wait(t, gb, 16 * (gb // 2 + 1))
                    for oc4 in range(4):
                        gidx = gb * 4 + oc4
                        if gidx >= 4:
                            b_ev.wait(t, gidx - 3)
                        for ic in range(16):
                            b_mm.inc(t.matmul(psB[oc4][:, :],
                                              wB[gb % 2][:, ic * 512 + oc4 * 128: ic * 512 + (oc4 + 1) * 128],
                                              zsl[ic][:, :],
                                              start=(ic == 0), stop=(ic == 15)),
                                     n=(1 if ic == 15 else 0))

        @blk.scalar
        def _(sc):
            for tt in range(4):
                for gidx in range(tt * 16, (tt + 1) * 16):
                    a_mm.wait(sc, gidx + 1)
                    if gidx >= 4:
                        d_z.wait(sc, gidx - 3)
                    a_ev.inc(sc.activation(gsig[gidx % 4][:, :], psA[gidx % 4 % 4][:, :], AF.Sigmoid))
                for gidx in range(tt * 16, (tt + 1) * 16):
                    b_mm.wait(sc, gidx + 1)
                    if gidx >= 4:
                        stp.wait(sc, gidx, gidx // 4)
                    b_ev.inc(sc.copy(sout[gidx % 4][:, :], psB[gidx % 4][:, :]))

        @blk.vector
        def _(v):
            for tt in range(4):
                ald.wait(v, 16 * (tt + 1))
                if tt >= 1:
                    b_mm.wait(v, 16 * tt)
                for gidx in range(tt * 16, (tt + 1) * 16):
                    a_ev.wait(v, gidx + 1)
                    ocg = (gidx % 16) // 4 * 4 + gidx % 4
                    ocg = gidx % 16
                    d_z.inc(v.tensor_tensor(zsl[ocg][:, :], gsig[gidx % 4][:, :],
                                            asl[ocg][:, :], ALU.mult))



def _p6_sched():
    sched = []
    for h in range(H1):
        for it in range(2):
            jbs = list(range((it * 1024 + 1024) // 128))[::-1]
            for j, jb in enumerate(jbs):
                sched.append(dict(h=h, it=it, jb=jb, first=(j == 0),
                                  last=(jb == 0), jbmax=jbs[0]))
    pe = act = dve = pool = 0
    for s in sched:
        s["pe0"] = pe
        s["z_end"] = pe + 2
        nsm = 2 if s["first"] else 4
        s["s_end"] = s["z_end"] + nsm
        s["cs_end"] = s["s_end"] + 2
        s["o_end"] = s["cs_end"] + 2
        pe = s["o_end"]
        s["act0"] = act
        s["u_end"] = act + 1
        s["sp_end"] = s["u_end"] + 1
        s["es_end"] = s["sp_end"] + 2
        act = s["es_end"]
        if s["last"]:
            act += 2
        s["act_end"] = act
        s["dve0"] = dve
        s["za_end"] = dve + 2
        s["attn_end"] = s["za_end"] + 1
        dve = s["attn_end"]
        if not s["last"]:
            dve += 2
        s["cv_end"] = dve
        s["dve_end"] = dve
        s["pool0"] = pool
        if not s["last"]:
            pool += 2
        s["cr_end"] = pool
    return sched


def _emit_p6(nc, T, ph):
    wq, wk, wv, wo = T["wq"], T["wk"], T["wv"], T["wo"]
    xn1_fm, s1_fm, C = T["xn1_fm"], T["s1_fm"], T
    bfw = [nc.dram_tensor(n, [DM, DM], BF16, kind="Internal").ap()
           for n in ("wq_bf", "wk_bf", "wv_bf", "wo_bf")]
    qkvf = [nc.dram_tensor(n, [DM, S], BF16, kind="Internal").ap()
            for n in ("q_fm", "k_fm", "v_fm")]
    sched = _p6_sched()

    if True:
        e = ph.st.enter_context
        xsl = [[e(_sbt(nc, ph, f"xsl{j}_{i}", [128, 512], BF16)) for i in range(16)]
               for j in range(2)]
        ofm = [e(_sbt(nc, ph, f"ofm{i}", [128, S], BF16)) for i in range(16)]
        wbuf = [e(_sbt(nc, ph, f"wbuf{i}", [128, 16 * 512], BF16)) for i in range(2)]
        evb = [e(_sbt(nc, ph, f"evb{i}", [128, 512], BF16)) for i in range(4)]
        qh = [e(_sbt(nc, ph, f"qh{i}", [64, S], BF16)) for i in range(2)]
        kh = [e(_sbt(nc, ph, f"kh{i}", [64, S], BF16)) for i in range(2)]
        vtb = [e(_sbt(nc, ph, f"vtb{i}", [128, 16 * 64], BF16, is_transpose_dest=True)) for i in range(2)]
        za = e(_sbt(nc, ph, "za", [128, 1024], F32))
        ub = e(_sbt(nc, ph, "ub", [128, 1024], F32))
        esb = e(_sbt(nc, ph, "esb", [128, 1024], F32))
        spb = e(_sbt(nc, ph, "spb", [128, 1024], BF16))
        atb = e(_sbt(nc, ph, "atb", [128, 1024], BF16))
        cbr = e(_sbt(nc, ph, "cbr", [1, 1024], BF16))
        colev = e(_sbt(nc, ph, "colev", [1, 1024], F32))
        carf = [e(_sbt(nc, ph, f"carf{i}", [1, 1024], F32)) for i in range(2)]
        mskm = [e(_sbt(nc, ph, f"mskm{i}", [128, 1024], BF16)) for i in range(8)]
        ntri = e(_sbt(nc, ph, "ntri", [128, 128], BF16))
        orow = e(_sbt(nc, ph, "orow", [1, 128], BF16))
        ncol = e(_sbt(nc, ph, "ncol", [128, 1], BF16))
        pp = [e(_pst(nc, ph, f"pp{i}", [128, 512], F32)) for i in range(8)]

        cst = ph.pipe("cst", 16)
        gld = ph.pipe("gld", 16)
        g_eng = ph.pipe("g_eng")
        xlp = ph.spipe("xlp", 2)
        swp = ph.spipe("swp", 2)
        p_q = ph.pipe("p_q")
        a_qv = ph.pipe("a_qv")
        st_qv = ph.spipe("st_qv", 4)
        qldp = ph.spipe("qldp", 2)
        p_at = ph.pipe("p_at")
        a_at = ph.pipe("a_at")
        d_at = ph.pipe("d_at")
        d_msk = ph.pipe("d_msk")
        d_cv = ph.pipe("d_cv")
        g_cr = ph.pipe("g_cr")
        blk = ph.wrap(e(nc.Block()))

        NG3 = 3 * 4 * 16
        last_of_head = {}
        for i, s_ in enumerate(sched):
            last_of_head[s_["h"]] = s_

        @blk.gpsimd
        def _(g):
            for wi in range(4):
                cst.inc(g.dma_start(bfw[wi], [wq, wk, wv, wo][wi]))
            gld.inc(g.dma_start(ntri[:, :], C["ntri_bf"]))
            gld.inc(g.dma_start(orow[:, :], C["ones_row_bf"]))
            gld.inc(g.dma_start(ncol[:, :], C["negcol_bf"]))
            g_eng.inc(g.iota(za[:, :], [[1, 1024]], base=0, channel_multiplier=-1,
                             allow_small_or_imprecise_dtypes=True))
            for tt in range(4):
                if tt >= 2:
                    p_q.wait(g, 48 * (tt - 1))
                for ic in range(16):
                    xlp.inc(g.dma_start(
                        xsl[tt % 2][ic][:, :],
                        xn1_fm[ic * 128:(ic + 1) * 128, tt * 512:(tt + 1) * 512]), tt)
                for gi in range(tt * 48, (tt + 1) * 48):
                    rem = gi % 48
                    wi, ob, oc4 = rem // 16, (rem % 16) // 4, rem % 4
                    a_qv.wait(g, gi + 1)
                    st_qv.inc(g.dma_start(
                        qkvf[wi][ob * 512 + oc4 * 128: ob * 512 + (oc4 + 1) * 128,
                                 tt * 512:(tt + 1) * 512],
                        evb[gi % 4][:, :]), gi)
            lidx = 0
            for i, s_ in enumerate(sched):
                if s_["last"]:
                    lidx = 0
                    continue
                d_at.wait(g, s_["cv_end"])
                g_cr.wait(g, s_["pool0"])
                if lidx == 0:
                    g_cr.inc(g.tensor_copy(carf[0][:, :], colev[:, :]))
                else:
                    g_cr.inc(g.tensor_tensor(carf[lidx % 2][:, :], carf[(lidx + 1) % 2][:, :],
                                             colev[:, :], ALU.add))
                g_cr.wait(g, s_["pool0"] + 1)
                g_cr.inc(g.tensor_copy(cbr[:, :], carf[lidx % 2][:, :]))
                lidx += 1
            for gi in range(64):
                ob, tt, oc4 = gi // 16, (gi % 16) // 4, gi % 4
                a_qv.wait(g, NG3 + gi + 1)
                st_qv.inc(g.dma_start(
                    s1_fm[ob * 512 + oc4 * 128: ob * 512 + (oc4 + 1) * 128,
                          tt * 512:(tt + 1) * 512],
                    evb[(NG3 + gi) % 4][:, :]), NG3 + gi)
            st_qv.wait_all(g, (NG3 + 64) // 4)

        @blk.sync
        def _(s):
            cst.wait(s, 4)
            for w in range(48):
                tt, gw = w // 12, w % 12
                wi, ob = gw // 4, gw % 4
                if w >= 2:
                    p_q.wait(s, 4 * (w - 1))
                for ic in range(16):
                    swp.inc(s.dma_start(
                        wbuf[w % 2][:, ic * 512:(ic + 1) * 512],
                        bfw[wi][ic * 128:(ic + 1) * 128, ob * 512:(ob + 1) * 512]), w)
            st_qv.wait_all(s, NG3 // 4)
            for h in range(H1):
                if h >= 2:
                    p_at.wait(s, last_of_head[h - 1]["o_end"])
                base = h * 64
                qldp.inc(s.dma_start(qh[h % 2][:, :], qkvf[0][base:base + 64, :]), h)
                qldp.inc(s.dma_start(kh[h % 2][:, :], qkvf[1][base:base + 64, :]), h)
                for jc in range(16):
                    qldp.inc(s.dma_start_transpose(
                        vtb[h % 2][:, jc * 64:(jc + 1) * 64],
                        qkvf[2][base:base + 64, jc * 128:(jc + 1) * 128]), h)
            for ob in range(4):
                w = 48 + ob
                if ob < 2:
                    p_q.wait(s, 4 * (w - 1))
                else:
                    p_q.wait(s, NG3 + 16 * (ob - 1))
                for ic in range(16):
                    swp.inc(s.dma_start(
                        wbuf[w % 2][:, ic * 512:(ic + 1) * 512],
                        bfw[3][ic * 128:(ic + 1) * 128, ob * 512:(ob + 1) * 512]), w)

        @blk.tensor
        def _(t):
            for gi in range(NG3):
                tt, rem = gi // 48, gi % 48
                wi, ob, oc4 = rem // 16, (rem % 16) // 4, rem % 4
                w = tt * 12 + wi * 4 + ob
                swp.wait(t, w, 16 * (w // 2 + 1))
                xlp.wait(t, tt, 16 * (tt // 2 + 1))
                if gi >= 4:
                    a_qv.wait(t, gi - 3)
                for ic in range(16):
                    p_q.inc(t.matmul(pp[oc4][:, :],
                                     wbuf[w % 2][:, ic * 512 + oc4 * 128: ic * 512 + (oc4 + 1) * 128],
                                     xsl[tt % 2][ic][:, :],
                                     start=(ic == 0), stop=(ic == 15)),
                            n=(1 if ic == 15 else 0))
            gld.wait(t, 3)
            d_msk.wait(t, 8)
            a_qv.wait(t, NG3)
            for i, s_ in enumerate(sched):
                h, it, jb = s_["h"], s_["it"], s_["jb"]
                prev = sched[i - 1] if i > 0 else None
                it0 = it * 1024
                qldp.wait(t, h, 18 * (h // 2 + 1))
                if prev is not None:
                    d_at.wait(t, prev["za_end"])
                for half in range(2):
                    p_at.inc(t.matmul(pp[half][:, :],
                                      kh[h % 2][:, jb * 128:(jb + 1) * 128],
                                      qh[h % 2][:, it0 + half * 512: it0 + (half + 1) * 512],
                                      start=True, stop=True))
                a_at.wait(t, s_["sp_end"])
                if prev is not None:
                    a_at.wait(t, prev["es_end"])
                for half in range(2):
                    p_at.inc(t.matmul(pp[2 + half][:, :], ntri[:, :],
                                      spb[:, half * 512:(half + 1) * 512],
                                      start=True, stop=s_["first"]))
                if not s_["first"]:
                    g_cr.wait(t, prev["cr_end"])
                    for half in range(2):
                        p_at.inc(t.matmul(pp[2 + half][:, :], orow[:, :],
                                          cbr[:, half * 512:(half + 1) * 512],
                                          start=False, stop=True))
                if prev is not None:
                    d_at.wait(t, prev["cv_end"])
                for half in range(2):
                    p_at.inc(t.matmul(pp[6 + half][0:1, :], ncol[:, :],
                                      spb[:, half * 512:(half + 1) * 512],
                                      start=True, stop=True))
                d_at.wait(t, s_["attn_end"])
                if s_["first"] and prev is not None:
                    a_at.wait(t, prev["act_end"])
                for half in range(2):
                    p_at.inc(t.matmul(pp[4 + half][0:64, :],
                                      vtb[h % 2][:, jb * 64:(jb + 1) * 64],
                                      atb[:, half * 512:(half + 1) * 512],
                                      start=s_["first"], stop=s_["last"]))
            d_at.wait(t, sched[-1]["dve_end"])
            a_at.wait(t, sched[-1]["act_end"])
            for gi in range(64):
                ob, tt, oc4 = gi // 16, (gi % 16) // 4, gi % 4
                w = 48 + ob
                swp.wait(t, w, 16 * (w // 2 + 1))
                a_qv.wait(t, NG3 + gi - 3)
                for ic in range(16):
                    p_q.inc(t.matmul(pp[oc4][:, :],
                                     wbuf[w % 2][:, ic * 512 + oc4 * 128: ic * 512 + (oc4 + 1) * 128],
                                     ofm[ic][:, tt * 512:(tt + 1) * 512],
                                     start=(ic == 0), stop=(ic == 15)),
                            n=(1 if ic == 15 else 0))

        @blk.vector
        def _(v):
            g_eng.wait(v, 1)
            for r8 in range(8):
                d_msk.inc(v.tensor_scalar(mskm[r8][:, :], za[:, :], 128.0 * r8 + 0.5, -10000.0,
                                          ALU.is_le, ALU.mult))
            d_msk.wait(v, 8)
            for i, s_ in enumerate(sched):
                h, it, jb = s_["h"], s_["it"], s_["jb"]
                prev = sched[i - 1] if i > 0 else None
                rel = jb * 128 - it * 1024
                p_at.wait(v, s_["z_end"])
                if prev is not None:
                    a_at.wait(v, prev["u_end"])
                for half in range(2):
                    if rel >= 0:
                        d_at.inc(v.scalar_tensor_tensor(
                            za[:, half * 512:(half + 1) * 512],
                            pp[half][:, :], 80.0,
                            mskm[rel // 128][:, half * 512:(half + 1) * 512],
                            ALU.min, ALU.add))
                    else:
                        d_at.inc(v.tensor_single_scalar(
                            za[:, half * 512:(half + 1) * 512],
                            pp[half][:, :], 80.0, ALU.min))
                a_at.wait(v, s_["es_end"])
                if prev is not None:
                    p_at.wait(v, prev["o_end"])
                d_at.inc(v.tensor_tensor(atb[:, :], ub[:, :], esb[:, :], ALU.mult))
                if not s_["last"]:
                    p_at.wait(v, s_["cs_end"])
                    g_cr.wait(v, s_["pool0"])
                    d_at.inc(v.tensor_copy(colev[:, 0:512], pp[6][0:1, :]))
                    d_at.inc(v.tensor_copy(colev[:, 512:1024], pp[7][0:1, :]))

        @blk.scalar
        def _(sc):
            for gi in range(NG3):
                wi = (gi % 48) // 16
                p_q.wait(sc, gi + 1)
                if gi >= 4:
                    st_qv.wait(sc, gi, gi // 4)
                if wi == 0:
                    a_qv.inc(sc.activation(evb[gi % 4][:, :], pp[gi % 4][:, :], AF.Copy,
                                           scale=SC1))
                else:
                    a_qv.inc(sc.copy(evb[gi % 4][:, :], pp[gi % 4][:, :]))
            for i, s_ in enumerate(sched):
                h, it = s_["h"], s_["it"]
                prev = sched[i - 1] if i > 0 else None
                d_at.wait(sc, s_["za_end"])
                if prev is not None:
                    d_at.wait(sc, prev["attn_end"])
                a_at.inc(sc.activation(ub[:, :], za[:, :], AF.Exp))
                if prev is not None:
                    p_at.wait(sc, prev["cs_end"])
                a_at.wait(sc, s_["u_end"])
                a_at.inc(sc.activation(spb[:, :], ub[:, :], AF.Ln, bias=1.0))
                p_at.wait(sc, s_["s_end"])
                for half in range(2):
                    a_at.inc(sc.activation(esb[:, half * 512:(half + 1) * 512],
                                           pp[2 + half][:, :], AF.Exp))
                if s_["last"]:
                    p_at.wait(sc, s_["o_end"])
                    fc, off = h // 2, (h % 2) * 64
                    for half in range(2):
                        a_at.inc(sc.copy(
                            ofm[fc][off:off + 64,
                                    it * 1024 + half * 512: it * 1024 + (half + 1) * 512],
                            pp[4 + half][0:64, :]))
            for gi in range(64):
                p_q.wait(sc, NG3 + gi + 1)
                st_qv.wait(sc, NG3 + gi, (NG3 + gi) // 4)
                a_qv.inc(sc.copy(evb[(NG3 + gi) % 4][:, :], pp[gi % 4][:, :]))



def _emit_p5a(nc, T, ph):
    sw1, sw2, w12, combT = T["sw1"], T["sw2"], T["w12"], T["combT"]
    xn0_tm, hsh_fm, hrt_fm, C = T["xn0_tm"], T["hsh_fm"], T["hrt_fm"], T
    sw1_bf = nc.dram_tensor("sw1_bf", [HS, DM], BF16, kind="Internal").ap()
    sw2_bf = nc.dram_tensor("sw2_bf", [HS, DM], BF16, kind="Internal").ap()
    w12_bf = nc.dram_tensor("w12_bf", [E0 * 2 * HR, DM], BF16, kind="Internal").ap()
    NQ = 32

    if True:
        e = ph.st.enter_context
        xn = [e(_sbt(nc, ph, f"xn{i}", [128, S], BF16)) for i in range(16)]
        wb1 = [e(_sbt(nc, ph, f"wb1{i}", [128, 16 * 512], BF16, is_transpose_dest=True)) for i in range(2)]
        wb2 = [e(_sbt(nc, ph, f"wb2{i}", [128, 16 * 512], BF16, is_transpose_dest=True)) for i in range(2)]
        sg = [e(_sbt(nc, ph, f"sg{i}", [128, 512], F32)) for i in range(4)]
        tm = [e(_sbt(nc, ph, f"tm{i}", [128, 512], F32)) for i in range(4)]
        hout = [e(_sbt(nc, ph, f"hout{i}", [128, 512], BF16)) for i in range(4)]
        cb = e(_sbt(nc, ph, "cb", [128, 32 * 512], BF16))
        ctb = [e(_sbt(nc, ph, f"ctb{i}", [1, S], BF16)) for i in range(2)]
        orow = e(_sbt(nc, ph, "orow", [1, 128], BF16))
        psG = [e(_pst(nc, ph, f"psG{i}", [128, 512], F32)) for i in range(4)]
        psU = [e(_pst(nc, ph, f"psU{i}", [128, 512], F32)) for i in range(4)]

        c1 = ph.pipe("c1", 16)
        c2 = ph.pipe("c2", 16)
        c12 = ph.pipe("c12", 16)
        gld = ph.pipe("gld", 16)
        ctp5 = ph.spipe("ctp5", 2)
        sxn = ph.pipe("sxn", 16)
        sw = ph.spipe("swl", 2)
        p_cb = ph.pipe("p_cb")
        d_cb = ph.pipe("d_cb")
        p_mm = ph.pipe("p_mm")
        a_ev = ph.pipe("a_ev")
        d_h = ph.pipe("d_h")
        stp = ph.spipe("stp", 4)
        cda = ph.chain("cda5")
        blk = ph.wrap(e(nc.Block()))

        def q_rows(q):
            if q < 16:
                return (sw1_bf, q * 512, sw2_bf, q * 512, hsh_fm, q * 512)
            r = q - 16
            ee, hb1 = r // 2, r % 2
            return (w12_bf, ee * 2048 + hb1 * 512, w12_bf, ee * 2048 + 1024 + hb1 * 512,
                    hrt_fm, ee * 1024 + hb1 * 512)

        @blk.gpsimd
        def _(g):
            for c in range(4):
                c1.inc(g.dma_start(sw1_bf[c * 2048:(c + 1) * 2048, :],
                                   sw1[c * 2048:(c + 1) * 2048, :]))
            for c in range(4):
                c2.inc(g.dma_start(sw2_bf[c * 2048:(c + 1) * 2048, :],
                                   sw2[c * 2048:(c + 1) * 2048, :]))
            for c in range(8):
                c12.inc(g.dma_start(w12_bf[c * 2048:(c + 1) * 2048, :],
                                    w12[c * 2048:(c + 1) * 2048, :]))
            gld.inc(g.dma_start(orow[:, :], C["ones_row_bf"]))
            for ee in range(E0):
                if ee >= 2:
                    p_cb.wait(g, 4 * (ee - 1))
                ctp5.inc(g.dma_start(ctb[ee % 2][:, :], combT[ee:ee + 1, :]), ee)
            for q in range(NQ):
                for tt in range(4):
                    for oc4 in range(4):
                        s_ = q * 16 + tt * 4 + oc4
                        d_h.wait(g, s_ + 1)
                        h0a, b0, h1a, b1, outh, ob = q_rows(q)
                        stp.inc(g.dma_start(
                            outh[ob + oc4 * 128: ob + (oc4 + 1) * 128,
                                 tt * 512:(tt + 1) * 512],
                            hout[s_ % 4][:, :]), s_)
            stp.wait_all(g, NQ * 16 // 4)

        @blk.sync
        def _(s):
            for ic in range(16):
                sxn.inc(s.dma_start_transpose(xn[ic][:, :], xn0_tm[:, ic * 128:(ic + 1) * 128]))
            c1.wait(s, 4)
            c2.wait(s, 4)
            for q in range(NQ):
                if q == 16:
                    c12.wait(s, 8)
                if q >= 2:
                    p_mm.wait(s, 32 * (q - 1))
                h0a, b0, h1a, b1, _, _ = q_rows(q)
                for ic in range(16):
                    sw.inc(s.dma_start_transpose(
                        wb1[q % 2][:, ic * 512:(ic + 1) * 512],
                        h0a[b0:b0 + 512, ic * 128:(ic + 1) * 128]), q)
                for ic in range(16):
                    sw.inc(s.dma_start_transpose(
                        wb2[q % 2][:, ic * 512:(ic + 1) * 512],
                        h1a[b1:b1 + 512, ic * 128:(ic + 1) * 128]), q)

        @blk.tensor
        def _(t):
            gld.wait(t, 1)
            for k in range(32):
                ee, tt = k // 4, k % 4
                ctp5.wait(t, ee, ee // 2 + 1)
                if k >= 2:
                    d_cb.wait(t, k - 1)
                p_cb.inc(t.matmul(psG[k % 2][:, :], orow[:, :],
                                  ctb[ee % 2][:, tt * 512:(tt + 1) * 512],
                                  start=True, stop=True))
            sxn.wait(t, 16)
            d_cb.wait(t, 32)
            for q in range(NQ):
                sw.wait(t, q, 32 * (q // 2 + 1))
                for tt in range(4):
                    for half in range(2):
                        wb = wb1 if half == 0 else wb2
                        ps = psG if half == 0 else psU
                        for oc4 in range(4):
                            u = q * 32 + tt * 8 + half * 4 + oc4
                            s_ = q * 16 + tt * 4 + oc4
                            if half == 0:
                                a_ev.wait(t, s_ - 3)
                            else:
                                d_h.wait(t, s_ - 3)
                            for ic in range(16):
                                p_mm.inc(t.matmul(
                                    ps[oc4][:, :],
                                    wb[q % 2][:, ic * 512 + oc4 * 128: ic * 512 + (oc4 + 1) * 128],
                                    xn[ic][:, tt * 512:(tt + 1) * 512],
                                    start=(ic == 0), stop=(ic == 15)),
                                    n=(1 if ic == 15 else 0))

        @blk.scalar
        def _(sc):
            for q in range(NQ):
                for tt in range(4):
                    for oc4 in range(4):
                        s_ = q * 16 + tt * 4 + oc4
                        p_mm.wait(sc, q * 32 + tt * 8 + oc4 + 1)
                        if s_ >= 4:
                            d_h.wait(sc, s_ - 3)
                        a_ev.inc(sc.activation(sg[s_ % 4][:, :], psG[oc4][:, :], AF.Silu))

        @blk.vector
        def _(v):
            for k in range(32):
                p_cb.wait(v, k + 1)
                d_cb.inc(v.tensor_copy(cb[:, k * 512:(k + 1) * 512], psG[k % 2][:, :]))
            d_cb.wait(v, 32)
            for q in range(NQ):
                for tt in range(4):
                    for oc4 in range(4):
                        s_ = q * 16 + tt * 4 + oc4
                        u_u = q * 32 + tt * 8 + 4 + oc4
                        p_mm.wait(v, u_u + 1)
                        a_ev.wait(v, s_ + 1)
                        if s_ >= 4:
                            stp.wait(v, s_, s_ // 4)
                        if q < 16:
                            d_h.inc(v.tensor_tensor(hout[s_ % 4][:, :], sg[s_ % 4][:, :],
                                                    psU[oc4][:, :], ALU.mult))
                        else:
                            ee = (q - 16) // 2
                            cda.seal(v.tensor_tensor(tm[s_ % 4][:, :], sg[s_ % 4][:, :],
                                                     psU[oc4][:, :], ALU.mult))
                            cda.gate(v)
                            d_h.inc(v.tensor_tensor(
                                hout[s_ % 4][:, :], tm[s_ % 4][:, :],
                                cb[:, (ee * 4 + tt) * 512:(ee * 4 + tt + 1) * 512],
                                ALU.mult))



def _emit_p5b(nc, T, ph):
    sw3, w3, s0_fm, x0_fm = T["sw3"], T["w3"], T["s0_fm"], T["x0_fm"]
    hsh_fm, hrt_fm, x1_fm, xn1_fm, C = (T["hsh_fm"], T["hrt_fm"], T["x1_fm"],
                                        T["xn1_fm"], T)
    sw3_bf = nc.dram_tensor("sw3_bf", [DM, HS], BF16, kind="Internal").ap()
    w3_bf = nc.dram_tensor("w3_bf", [E0 * DM, HR], BF16, kind="Internal").ap()

    if True:
        e = ph.st.enter_context
        hsl = [e(_sbt(nc, ph, f"hsl{i}", [128, 512], BF16)) for i in range(64)]
        rsl = [e(_sbt(nc, ph, f"rsl{i}", [128, 512], BF16)) for i in range(64)]
        s0t = [e(_sbt(nc, ph, f"s0t{i}", [128, 512], BF16)) for i in range(4)]
        x0t = [e(_sbt(nc, ph, f"x0t{i}", [128, 512], F32)) for i in range(4)]
        x1sl = [e(_sbt(nc, ph, f"x1sl{i}", [128, 512], F32)) for i in range(4)]
        x1r = [e(_sbt(nc, ph, f"x1r{i}", [128, 512], F32)) for i in range(4)]
        wt = [e(_sbt(nc, ph, f"wt{i}", [128, 512], BF16, is_transpose_dest=True)) for i in range(4)]
        gbuf = [e(_sbt(nc, ph, f"gbuf{i}", [128, 512], F32)) for i in range(2)]
        sgb = [e(_sbt(nc, ph, f"sgb{i}", [128, 512], F32)) for i in range(2)]
        sqb = [e(_sbt(nc, ph, f"sqb{i}", [128, 512], F32)) for i in range(2)]
        xnout = [e(_sbt(nc, ph, f"xnout{i}", [128, 512], BF16)) for i in range(4)]
        aptmp = [e(_sbt(nc, ph, f"aptmp{i}", [128, 512], F32)) for i in range(2)]
        onesf = e(_sbt(nc, ph, "onesf", [1, 128], F32))
        onesc = e(_sbt(nc, ph, "onesc", [128, 1], F32))
        murow = e(_sbt(nc, ph, "murow", [1, 512], F32))
        varow = e(_sbt(nc, ph, "varow", [1, 512], F32))
        va2 = e(_sbt(nc, ph, "va2", [1, 512], F32))
        var3 = e(_sbt(nc, ph, "var3", [1, 512], F32))
        lnro = e(_sbt(nc, ph, "lnro", [1, 512], F32))
        rstdr = e(_sbt(nc, ph, "rstdr", [1, 512], F32))
        biasr = e(_sbt(nc, ph, "biasr", [1, 512], F32))
        epsr = e(_sbt(nc, ph, "epsr", [1, 1], F32))
        ps = [e(_pst(nc, ph, f"ps{i}", [128, 512], F32)) for i in range(4)]
        psS = e(_pst(nc, ph, "psS", [1, 512], F32))
        psQ = e(_pst(nc, ph, "psQ", [1, 512], F32))
        psB = [e(_pst(nc, ph, f"psB{i}", [128, 512], F32)) for i in range(2)]

        gcl = ph.pipe("gcl", 16)
        c3 = ph.pipe("c3", 16)
        cw3 = ph.pipe("cw3", 16)
        hld = ph.pipe("hld", 16)
        as0 = ph.spipe("as0", 4)
        ax0 = ph.spipe("ax0", 4)
        ax1 = ph.spipe("ax1", 4)
        swl = ph.spipe("swl", 4)
        p_mm = ph.pipe("p_mm")
        a_g = ph.pipe("a_g")
        d_x = ph.pipe("d_x")
        a_q = ph.pipe("a_q")
        p_st = ph.pipe("p_st")
        d_row = ph.pipe("d_row")
        a_row = ph.pipe("a_row")
        p_bc = ph.pipe("p_bc")
        d_ap = ph.pipe("d_ap")
        st1 = ph.spipe("st1", 4)
        stn = ph.spipe("stn", 4)
        cdv = ph.chain("cdvb")
        cac = ph.chain("cacb")
        blk = ph.wrap(e(nc.Block()))

        NIC = 64 + 64

        @blk.gpsimd
        def _(g):
            g.memset(epsr[:, :], EPS)
            gcl.inc(g.dma_start(onesf[:, :], C["onesf_row"]))
            gcl.inc(g.dma_start(onesc[:, :], C["onesf_col"]))
            for c in range(4):
                c3.inc(g.dma_start(sw3_bf[c * 512:(c + 1) * 512, :],
                                   sw3[c * 512:(c + 1) * 512, :]))
            for c in range(8):
                cw3.inc(g.dma_start(w3_bf[c * 2048:(c + 1) * 2048, :],
                                    w3[c * 2048:(c + 1) * 2048, :]))
            for tt in range(4):
                if tt >= 1:
                    p_mm.wait(g, 512 * 4 * tt)
                for ic in range(64):
                    hld.inc(g.dma_start(hsl[ic][:, :],
                                        hsh_fm[ic * 128:(ic + 1) * 128, tt * 512:(tt + 1) * 512]))
                for ic in range(64):
                    hld.inc(g.dma_start(rsl[ic][:, :],
                                        hrt_fm[ic * 128:(ic + 1) * 128, tt * 512:(tt + 1) * 512]))
                for oi in range(20):
                    if oi < 16:
                        k = tt * 16 + oi
                        d_x.wait(g, k - 3)
                        as0.inc(g.dma_start(s0t[k % 4][:, :],
                                            s0_fm[oi * 128:(oi + 1) * 128, tt * 512:(tt + 1) * 512]), k)
                        ax0.inc(g.dma_start(x0t[k % 4][:, :],
                                            x0_fm[oi * 128:(oi + 1) * 128, tt * 512:(tt + 1) * 512]), k)
                    if oi >= 4:
                        ocg = oi - 4
                        k = tt * 16 + ocg
                        d_x.wait(g, k + 1)
                        st1.inc(g.dma_start(
                            x1_fm[ocg * 128:(ocg + 1) * 128, tt * 512:(tt + 1) * 512],
                            x1sl[k % 4][:, :]), k)
                for ocg in range(16):
                    k = tt * 16 + ocg
                    d_ap.wait(g, k - 3)
                    st1.wait(g, k, k // 4 + 1)
                    ax1.inc(g.dma_start(
                        x1r[k % 4][:, :],
                        x1_fm[ocg * 128:(ocg + 1) * 128, tt * 512:(tt + 1) * 512]), k)
                    d_ap.wait(g, k + 1)
                    stn.inc(g.dma_start(
                        xn1_fm[ocg * 128:(ocg + 1) * 128, tt * 512:(tt + 1) * 512],
                        xnout[k % 4][:, :]), k)
            st1.wait_all(g, 16)
            stn.wait_all(g, 16)

        @blk.sync
        def _(s):
            c3.wait(s, 4)
            cw3.wait(s, 8)
            wcount = 0
            for tt in range(4):
                for ob in range(4):
                    for icg in range(NIC):
                        if wcount >= 4:
                            p_mm.wait(s, (wcount - 3) * 4)
                        if icg < 64:
                            src = sw3_bf[ob * 512:(ob + 1) * 512, icg * 128:(icg + 1) * 128]
                        else:
                            icr = icg - 64
                            ee, rr = icr // 8, icr % 8
                            src = w3_bf[ee * 2048 + ob * 512: ee * 2048 + (ob + 1) * 512,
                                        rr * 128:(rr + 1) * 128]
                        swl.inc(s.dma_start_transpose(wt[wcount % 4][:, :], src), wcount)
                        wcount += 1

        @blk.tensor
        def _(t):
            wcount = 0
            for tt in range(4):
                hld.wait(t, 128 * (tt + 1))
                for ob in range(4):
                    for oc4 in range(4):
                        s_ = (tt * 4 + ob) * 4 + oc4
                        if s_ >= 4:
                            a_g.wait(t, s_ - 3)
                    for icg in range(NIC):
                        swl.wait(t, wcount, wcount // 4 + 1)
                        rhs = hsl[icg] if icg < 64 else rsl[icg - 64]
                        for oc4 in range(4):
                            p_mm.inc(t.matmul(
                                ps[oc4][:, :],
                                wt[wcount % 4][:, oc4 * 128:(oc4 + 1) * 128],
                                rhs[:, :],
                                start=(icg == 0), stop=(icg == NIC - 1)))
                        wcount += 1
                    for oc4 in range(4):
                        ocg = ob * 4 + oc4
                        k = tt * 16 + ocg
                        d_x.wait(t, k + 1)
                        if ocg == 0:
                            gcl.wait(t, 2)
                            d_row.wait(t, 2 * tt)
                            d_ap.wait(t, 16 * tt)
                        p_st.inc(t.matmul(psS[:, :], onesc[:, :], x1sl[k % 4][:, :],
                                          start=(ocg == 0), stop=(ocg == 15)))
                        a_q.wait(t, k + 1)
                        p_st.inc(t.matmul(psQ[:, :], onesc[:, :], sqb[ocg % 2][:, :],
                                          start=(ocg == 0), stop=(ocg == 15)))
                a_row.wait(t, tt + 1)
                d_row.wait(t, 2 * (tt + 1))
                p_bc.inc(t.matmul(psB[0][:, :], onesf[:, :], rstdr[:, :], start=True, stop=True))
                p_bc.inc(t.matmul(psB[1][:, :], onesf[:, :], biasr[:, :], start=True, stop=True))

        @blk.scalar
        def _(sc):
            for tt in range(4):
                for si in range(17):
                    if si >= 1:
                        ocg = si - 1
                        k = tt * 16 + ocg
                        d_x.wait(sc, k + 1)
                        if k >= 2:
                            p_st.wait(sc, 2 * (k - 1))
                        a_q.inc(sc.activation(sqb[ocg % 2][:, :], x1sl[k % 4][:, :], AF.Square))
                    if si < 16:
                        ob, oc4 = si // 4, si % 4
                        s_ = tt * 16 + si
                        p_mm.wait(sc, ((tt * 4 + ob) * NIC) * 4 + (NIC - 1) * 4 + oc4 + 1)
                        if s_ >= 2:
                            d_x.wait(sc, s_ - 1)
                        a_g.inc(sc.activation(gbuf[s_ % 2][:, :], ps[oc4][:, :], AF.Silu))
                d_row.wait(sc, 2 * tt + 1)
                cac.seal(sc.activation(lnro[:, :], var3[:, :], AF.Ln, bias=epsr[:, :]))
                cac.gate(sc)
                a_row.inc(sc.activation(rstdr[:, :], lnro[:, :], AF.Exp, scale=-0.5))

        @blk.vector
        def _(v):
            for tt in range(4):
                for ob in range(4):
                    for oc4 in range(4):
                        s_ = (tt * 4 + ob) * 4 + oc4
                        ocg = ob * 4 + oc4
                        k = tt * 16 + ocg
                        as0.wait(v, k, k // 4 + 1)
                        ax0.wait(v, k, k // 4 + 1)
                        a_g.wait(v, s_ + 1)
                        if k >= 4:
                            st1.wait(v, k, k // 4)
                            a_q.wait(v, k - 3)
                            p_st.wait(v, 2 * (k - 3))
                        cdv.seal(v.tensor_tensor(sgb[oc4 % 2][:, :], gbuf[s_ % 2][:, :],
                                                 s0t[k % 4][:, :], ALU.mult))
                        cdv.gate(v)
                        d_x.inc(v.tensor_tensor(x1sl[k % 4][:, :], sgb[oc4 % 2][:, :],
                                                x0t[k % 4][:, :], ALU.add))
                p_st.wait(v, 2 * 16 * (tt + 1))
                cdv.seal(v.tensor_scalar_mul(murow[:, :], psS[:, :], 1.0 / DM))
                cdv.seal(v.tensor_scalar_mul(varow[:, :], psQ[:, :], 1.0 / DM))
                cdv.gate(v)
                cdv.seal(v.scalar_tensor_tensor(va2[:, :], murow[:, :], -1.0, murow[:, :],
                                                ALU.mult, ALU.mult))
                cdv.gate(v)
                d_row.inc(v.tensor_tensor(var3[:, :], varow[:, :], va2[:, :], ALU.add))
                a_row.wait(v, tt + 1)
                d_row.inc(v.scalar_tensor_tensor(biasr[:, :], murow[:, :], -1.0, rstdr[:, :],
                                                 ALU.mult, ALU.mult))
                p_bc.wait(v, 2 * (tt + 1))
                for ocg in range(16):
                    k = tt * 16 + ocg
                    ax1.wait(v, k, k // 4 + 1)
                    if k >= 4:
                        stn.wait(v, k, k // 4)
                    cdv.seal(v.tensor_tensor(aptmp[k % 2][:, :], x1r[k % 4][:, :],
                                             psB[0][:, :], ALU.mult))
                    cdv.gate(v)
                    d_ap.inc(v.tensor_tensor(xnout[k % 4][:, :], aptmp[k % 2][:, :],
                                             psB[1][:, :], ALU.add))




def _emit_p7(nc, T, ph):
    wbase, bbase, wrout, aw, bw = T["wbase"], T["bbase"], T["wrout"], T["aw"], T["bw"]
    xn1_fm, x1_fm, s1_fm, out_tm, C = (T["xn1_fm"], T["x1_fm"], T["s1_fm"],
                                       T["out_tm"], T)
    wbase_bf = nc.dram_tensor("wbase_bf", [DM, DM], BF16, kind="Internal").ap()
    a_bfd = nc.dram_tensor("a_bfd", [E1 * R1, DM], BF16, kind="Internal").ap()
    b_bfd = nc.dram_tensor("b_bfd", [E1 * DM, R1], BF16, kind="Internal").ap()
    rout_bfd = nc.dram_tensor("rout_bfd", [E1, DM], BF16, kind="Internal").ap()
    bt_dram = nc.dram_tensor("bt_dram", [E1 * R1, DM], BF16, kind="Internal").ap()
    NTC = 16

    def gsplit(gi):
        ob, tt, oc4 = gi // 16, (gi // 4) % 4, gi % 4
        ocg = ob * 4 + oc4
        return ob, tt, oc4, ocg, tt * 16 + ocg

    if True:
        e = ph.st.enter_context
        xn = [e(_sbt(nc, ph, f"xn{i}", [128, S], BF16)) for i in range(16)]
        wbt = [e(_sbt(nc, ph, f"wbt{i}", [128, 16 * 512], BF16, is_transpose_dest=True)) for i in range(2)]
        idb = e(_sbt(nc, ph, "idb", [128, 128], BF16))
        idf = e(_sbt(nc, ph, "idf", [128, 128], F32))
        rrep_sb = e(_sbt(nc, ph, "rrep_sb", [E1, 64], BF16))
        a_sb = e(_sbt(nc, ph, "a_sb", [E1 * R1, S], BF16))
        A_T = e(_sbt(nc, ph, "A_T", [128, 16 * 64], BF16))
        btmp = e(_sbt(nc, ph, "btmp", [128, 2 * E1], BF16))
        bstg = [e(_sbt(nc, ph, f"bstg{i}", [E1, 128], BF16)) for i in range(2)]
        B_T = e(_sbt(nc, ph, "B_T", [E1 * R1, DM], BF16))
        rsb = e(_sbt(nc, ph, "rsb", [E1, DM], BF16))
        routerT = e(_sbt(nc, ph, "routerT", [128, 128], BF16))
        lg_tm = [e(_sbt(nc, ph, f"lg_tm{i}", [128, E1], F32)) for i in range(2)]
        m1 = e(_sbt(nc, ph, "m1c", [128, 1], F32))
        m2 = e(_sbt(nc, ph, "m2c", [128, 1], F32))
        is1 = e(_sbt(nc, ph, "is1c", [128, E1], F32))
        sel2 = e(_sbt(nc, ph, "sel2c", [128, E1], F32))
        ind = e(_sbt(nc, ph, "indc", [128, E1], F32))
        dcol = e(_sbt(nc, ph, "dcol", [128, 1], F32))
        sgc = e(_sbt(nc, ph, "sgc", [128, 1], F32))
        difc = e(_sbt(nc, ph, "difc", [128, 1], F32))
        wbc = e(_sbt(nc, ph, "wbc", [128, 1], F32))
        t1c = e(_sbt(nc, ph, "t1c", [128, E1], F32))
        w_tm = [e(_sbt(nc, ph, f"w_tm{i}", [128, E1], BF16)) for i in range(2)]
        wT_sb = e(_sbt(nc, ph, "wT_sb", [E1, S], BF16))
        wb_sb = e(_sbt(nc, ph, "wb_sb", [64, S], BF16))
        h_sb = e(_sbt(nc, ph, "h_sb", [64, S], BF16))
        hs_sb = e(_sbt(nc, ph, "hs_sb", [64, S], BF16))
        bbrow = e(_sbt(nc, ph, "bbrow", [1, DM], F32))
        bb_cols = e(_sbt(nc, ph, "bb_cols", [128, 16], F32))
        gb = [e(_sbt(nc, ph, f"gb{i}", [128, 512], F32)) for i in range(2)]
        sgb = [e(_sbt(nc, ph, f"sgb{i}", [128, 512], F32)) for i in range(2)]
        x2b = [e(_sbt(nc, ph, f"x2b{i}", [128, 512], F32)) for i in range(2)]
        s1t = [e(_sbt(nc, ph, f"s1t{i}", [128, 512], BF16)) for i in range(4)]
        x1t = [e(_sbt(nc, ph, f"x1t{i}", [128, 512], F32)) for i in range(4)]
        otb = [e(_sbt(nc, ph, f"otb{i}", [128, 128], F32)) for i in range(4)]
        psc = [e(_pst(nc, ph, f"psc{i}", [128, 512], F32)) for i in range(4)]
        pst = [e(_pst(nc, ph, f"pst{i}", [128, 128], F32)) for i in range(2)]
        ps_tb = e(_pst(nc, ph, "ps_tb", [128, 128], BF16))

        cbs = ph.pipe("cbs", 16)
        cbo = ph.pipe("cbo", 16)
        gl7 = ph.pipe("gl7", 16)
        xl7 = ph.pipe("xl7", 16)
        swb = ph.spipe("swb", 2)
        sx1 = ph.spipe("sx1", 4)
        ss1 = ph.spipe("ss1", 4)
        sto = ph.spipe("sto", 4)
        p_su = ph.pipe("p_su")
        d_su = ph.pipe("d_su")
        a_su = ph.pipe("a_su")
        g_su = ph.pipe("g_su", 16)
        p_mm = ph.pipe("p_mm")
        a_g = ph.pipe("a_g")
        d_x = ph.pipe("d_x")
        p_tp = ph.pipe("p_tp")
        d_ev = ph.pipe("d_ev")
        cdv = ph.chain("cdv7")
        blk = ph.wrap(e(nc.Block()))

        ss = SSched({"pe": p_su, "dve": d_su, "act": a_su, "gp": g_su})

        def mk(fn):
            return fn

        for ic in range(16):
            ss.add("pe", (lambda ic: lambda t: t.transpose(
                ps_tb[:, 0:64], a_sb[:, ic * 128:(ic + 1) * 128], idb[0:64, 0:64]))(ic),
                reads=("a_sb",), writes=("ps_tb",), extra=[(gl7, 6)])
            ss.add("dve", (lambda ic: lambda v: v.tensor_copy(
                A_T[:, ic * 64:(ic + 1) * 64], ps_tb[:, 0:64]))(ic),
                reads=("ps_tb",), writes=("A_T",))
        for j in range(128):
            ee, oc = j // 16, j % 16
            ss.add("gp", (lambda j, ee, oc: lambda g: g.dma_start(
                btmp[:, (j % 2) * E1:(j % 2 + 1) * E1],
                b_bfd[ee * 2048 + oc * 128: ee * 2048 + (oc + 1) * 128, :]))(j, ee, oc),
                reads=(), writes=(f"btmp{j % 2}",), extra=[(cbo, 3)])
            ss.add("pe", (lambda j: lambda t: t.transpose(
                ps_tb[0:E1, 0:128], btmp[:, (j % 2) * E1:(j % 2 + 1) * E1], idb[:, :]))(j),
                reads=(f"btmp{j % 2}",), writes=("ps_tb",))
            ss.add("dve", (lambda j: lambda v: v.tensor_copy(
                bstg[j % 2][:, :], ps_tb[0:E1, 0:128]))(j),
                reads=("ps_tb",), writes=(f"bstg{j % 2}",))
            ss.add("gp", (lambda j, ee, oc: lambda g: g.dma_start(
                bt_dram[ee * R1:(ee + 1) * R1, oc * 128:(oc + 1) * 128],
                bstg[j % 2][:, :]))(j, ee, oc),
                reads=(f"bstg{j % 2}",), writes=("bt_dram",))
        ss.add("gp", lambda g: g.dma_start(B_T[:, :], bt_dram),
               reads=("bt_dram",), writes=("B_T",))
        for ic in range(16):
            ss.add("pe", (lambda ic: lambda t: t.transpose(
                ps_tb[:, 0:E1], rsb[:, ic * 128:(ic + 1) * 128], idb[0:E1, 0:E1]))(ic),
                reads=("rsb",), writes=("ps_tb",), extra=[(gl7, 6)])
            ss.add("dve", (lambda ic: lambda v: v.tensor_copy(
                routerT[:, ic * 8:(ic + 1) * 8], ps_tb[:, 0:E1]))(ic),
                reads=("ps_tb",), writes=("routerT",))
        for ic in range(16):
            ss.add("pe", (lambda ic: lambda t: t.transpose(
                pst[0][:, 0:1], bbrow[:, ic * 128:(ic + 1) * 128], idf[0:1, 0:1]))(ic),
                reads=("bbrow",), writes=("pst0",), extra=[(gl7, 6)])
            ss.add("dve", (lambda ic: lambda v: v.tensor_copy(
                bb_cols[:, ic:ic + 1], pst[0][:, 0:1]))(ic),
                reads=("pst0",), writes=("bb_cols",))

        def lg_group(tc):
            def fn(t):
                for ic in range(15):
                    t.matmul(pst[tc % 2][:, 0:E1], xn[ic][:, tc * 128:(tc + 1) * 128],
                             routerT[:, ic * 8:(ic + 1) * 8],
                             start=(ic == 0), stop=False)
                return t.matmul(pst[tc % 2][:, 0:E1], xn[15][:, tc * 128:(tc + 1) * 128],
                                routerT[:, 15 * 8:16 * 8], start=False, stop=True)
            return fn

        def chain_a(tc):
            def fn(v):
                cdv.seal(v.reduce_max(m1[:, :], lg_tm[tc % 2][:, :], axis=AX.X))
                cdv.gate(v)
                cdv.seal(v.tensor_single_scalar(is1[:, :], lg_tm[tc % 2][:, :], m1[:, 0:1], ALU.is_ge))
                cdv.gate(v)
                cdv.seal(v.scalar_tensor_tensor(sel2[:, :], is1[:, :], -1e9, lg_tm[tc % 2][:, :],
                                                ALU.mult, ALU.add))
                cdv.gate(v)
                cdv.seal(v.reduce_max(m2[:, :], sel2[:, :], axis=AX.X))
                cdv.gate(v)
                cdv.seal(v.tensor_single_scalar(ind[:, :], lg_tm[tc % 2][:, :], m2[:, 0:1], ALU.is_ge))
                cdv.gate(v)
                return v.tensor_tensor(dcol[:, :], m1[:, :], m2[:, :], ALU.subtract)
            return fn

        def chain_b(tc):
            def fn(v):
                cdv.seal(v.tensor_scalar(difc[:, :], sgc[:, :], 4.0, -2.0, ALU.mult, ALU.add))
                cdv.seal(v.tensor_scalar(wbc[:, :], sgc[:, :], -2.0, 2.0, ALU.mult, ALU.add))
                cdv.gate(v)
                cdv.seal(v.tensor_single_scalar(t1c[:, :], ind[:, :], wbc[:, 0:1], ALU.mult))
                cdv.gate(v)
                return v.scalar_tensor_tensor(w_tm[tc % 2][:, :], is1[:, :], difc[:, 0:1],
                                              t1c[:, :], ALU.mult, ALU.add)
            return fn

        for tc in range(NTC):
            ss.add("pe", lg_group(tc), reads=("routerT",), writes=(f"pst{tc % 2}",),
                   extra=[(xl7, 16)])
            ss.add("act", (lambda tc: lambda sc: sc.copy(
                lg_tm[tc % 2][:, :], pst[tc % 2][:, 0:E1]))(tc),
                reads=(f"pst{tc % 2}",), writes=(f"lg{tc % 2}",))
            ss.add("dve", chain_a(tc), reads=(f"lg{tc % 2}",),
                   writes=("dcol", "m1", "m2", "is1", "ind", "sel2"))
            ss.add("act", (lambda tc: lambda sc: sc.activation(
                sgc[:, :], dcol[:, :], AF.Sigmoid))(tc),
                reads=("dcol",), writes=("sgc",))
            ss.add("dve", chain_b(tc), reads=("sgc", "is1", "ind"),
                   writes=(f"w_tm{tc % 2}", "difc", "wbc", "t1c"))
            ss.add("pe", (lambda tc: lambda t: t.transpose(
                ps_tb[0:E1, 0:128], w_tm[tc % 2][:, :], idb[:, :]))(tc),
                reads=(f"w_tm{tc % 2}",), writes=("ps_tb",))
            ss.add("dve", (lambda tc: lambda v: v.tensor_copy(
                wT_sb[:, tc * 128:(tc + 1) * 128], ps_tb[0:E1, 0:128]))(tc),
                reads=("ps_tb",), writes=("wT_sb",))
        for tt in range(4):
            ss.add("pe", (lambda tt: lambda t: t.matmul(
                psc[0][0:64, :], rrep_sb[:, :], wT_sb[:, tt * 512:(tt + 1) * 512],
                start=True, stop=True))(tt),
                reads=("wT_sb",), writes=("psc0",), extra=[(gl7, 6)])
            ss.add("dve", (lambda tt: lambda v: v.tensor_copy(
                wb_sb[:, tt * 512:(tt + 1) * 512], psc[0][0:64, :]))(tt),
                reads=("psc0",), writes=("wb_sb",))

        def h_group(tt):
            def fn(t):
                for ic in range(15):
                    t.matmul(psc[1][0:64, :], A_T[:, ic * 64:(ic + 1) * 64],
                             xn[ic][:, tt * 512:(tt + 1) * 512],
                             start=(ic == 0), stop=False)
                return t.matmul(psc[1][0:64, :], A_T[:, 15 * 64:16 * 64],
                                xn[15][:, tt * 512:(tt + 1) * 512], start=False, stop=True)
            return fn

        for tt in range(4):
            ss.add("pe", h_group(tt), reads=("A_T",), writes=("psc1",), extra=[(xl7, 16)])
            ss.add("act", (lambda tt: lambda sc: sc.copy(
                h_sb[:, tt * 512:(tt + 1) * 512], psc[1][0:64, :]))(tt),
                reads=("psc1",), writes=("h_sb",))
            ss.add("dve", (lambda tt: lambda v: v.tensor_tensor(
                hs_sb[:, tt * 512:(tt + 1) * 512], h_sb[:, tt * 512:(tt + 1) * 512],
                wb_sb[:, tt * 512:(tt + 1) * 512], ALU.mult))(tt),
                reads=("h_sb", "wb_sb"), writes=("hs_sb",))

        NPE, NDVE, NACT, NGP = (ss.counts["pe"], ss.counts["dve"],
                                ss.counts["act"], ss.counts["gp"])

        @blk.gpsimd
        def _(g):
            for c in range(4):
                cbs.inc(g.dma_start(wbase_bf[c * 512:(c + 1) * 512, :],
                                    wbase[c * 512:(c + 1) * 512, :]))
            cbo.inc(g.dma_start(a_bfd, aw))
            cbo.inc(g.dma_start(b_bfd, bw))
            cbo.inc(g.dma_start(rout_bfd, wrout))
            gl7.inc(g.dma_start(idb[:, :], C["ident_bf"]))
            gl7.inc(g.dma_start(idf[:, :], C["ident_f32"]))
            gl7.inc(g.dma_start(rrep_sb[:, :], C["rrep"]))
            gl7.inc(g.dma_start(bbrow[:, :], bbase.rearrange("(b a) -> b a", b=1)))
            cbo.wait(g, 3)
            gl7.inc(g.dma_start(a_sb[:, :], a_bfd))
            gl7.inc(g.dma_start(rsb[:, :], rout_bfd))
            for ic in range(16):
                xl7.inc(g.dma_start(xn[ic][:, :], xn1_fm[ic * 128:(ic + 1) * 128, :]))
            ss.emit("gp", g)
            for gg in range(66):
                if gg < 64:
                    gi = gg
                    ob, tt, oc4, ocg, k = gsplit(gi)
                    d_x.wait(g, gi - 3)
                    ss1.inc(g.dma_start(s1t[gi % 4][:, :],
                                        s1_fm[ocg * 128:(ocg + 1) * 128, tt * 512:(tt + 1) * 512]), gi)
                    sx1.inc(g.dma_start(x1t[gi % 4][:, :],
                                        x1_fm[ocg * 128:(ocg + 1) * 128, tt * 512:(tt + 1) * 512]), gi)
                if gg >= 2:
                    gi = gg - 2
                    ob, tt, oc4, ocg, k = gsplit(gi)
                    for q in range(4):
                        u = gi * 4 + q
                        d_ev.wait(g, u + 1)
                        sto.inc(g.dma_start(
                            out_tm[tt * 512 + q * 128: tt * 512 + (q + 1) * 128,
                                   ocg * 128:(ocg + 1) * 128],
                            otb[u % 4][:, :]), u)
            sto.wait_all(g, 64)

        @blk.sync
        def _(s):
            cbs.wait(s, 4)
            for ob in range(4):
                if ob >= 2:
                    p_mm.wait(s, 17 * 16 * (ob - 1))
                for ic in range(16):
                    swb.inc(s.dma_start_transpose(
                        wbt[ob % 2][:, ic * 512:(ic + 1) * 512],
                        wbase_bf[ob * 512:(ob + 1) * 512, ic * 128:(ic + 1) * 128]), ob)

        @blk.tensor
        def _(t):
            gl7.wait(t, 6)
            ss.emit("pe", t)
            d_su.wait(t, NDVE)
            a_su.wait(t, NACT)
            g_su.wait(t, NGP)
            for gi in range(64):
                ob, tt, oc4, ocg, k = gsplit(gi)
                if gi % 16 == 0:
                    swb.wait(t, ob, 16 * (ob // 2 + 1))
                if gi >= 2:
                    a_g.wait(t, gi - 1)
                for ic in range(16):
                    p_mm.inc(t.matmul(
                        psc[oc4][:, :],
                        wbt[ob % 2][:, ic * 512 + oc4 * 128: ic * 512 + (oc4 + 1) * 128],
                        xn[ic][:, tt * 512:(tt + 1) * 512],
                        start=(ic == 0), stop=False))
                p_mm.inc(t.matmul(psc[oc4][:, :],
                                  B_T[:, ocg * 128:(ocg + 1) * 128],
                                  hs_sb[:, tt * 512:(tt + 1) * 512],
                                  start=False, stop=True))
                if gi >= 2:
                    gp_ = gi - 2
                    d_x.wait(t, gp_ + 1)
                    for q in range(4):
                        u = gp_ * 4 + q
                        if u >= 2:
                            d_ev.wait(t, u - 1)
                        p_tp.inc(t.transpose(pst[u % 2][:, :],
                                             x2b[gp_ % 2][:, q * 128:(q + 1) * 128],
                                             idf[:, :]))
            for gp_ in (62, 63):
                d_x.wait(t, gp_ + 1)
                for q in range(4):
                    u = gp_ * 4 + q
                    d_ev.wait(t, u - 1)
                    p_tp.inc(t.transpose(pst[u % 2][:, :],
                                         x2b[gp_ % 2][:, q * 128:(q + 1) * 128],
                                         idf[:, :]))

        @blk.vector
        def _(v):
            ss.emit("dve", v)
            def tp_evacs(v, gp2):
                for q in range(4):
                    u = gp2 * 4 + q
                    p_tp.wait(v, u + 1)
                    if u >= 4:
                        sto.wait(v, u, u // 4)
                    d_ev.inc(v.tensor_copy(otb[u % 4][:, :], pst[u % 2][:, :]))

            for gi in range(64):
                ob, tt, oc4, ocg, k = gsplit(gi)
                if gi >= 2:
                    tp_evacs(v, gi - 2)
                a_g.wait(v, gi + 1)
                ss1.wait(v, gi, gi // 4 + 1)
                sx1.wait(v, gi, gi // 4 + 1)
                if gi >= 2:
                    p_tp.wait(v, 4 * (gi - 1))
                cdv.seal(v.tensor_tensor(sgb[gi % 2][:, :], gb[gi % 2][:, :],
                                         s1t[gi % 4][:, :], ALU.mult))
                cdv.gate(v)
                d_x.inc(v.tensor_tensor(x2b[gi % 2][:, :], sgb[gi % 2][:, :],
                                        x1t[gi % 4][:, :], ALU.add))
            for gp2 in (62, 63):
                tp_evacs(v, gp2)

        @blk.scalar
        def _(sc):
            ss.emit("act", sc)
            for gi in range(64):
                ob, tt, oc4, ocg, k = gsplit(gi)
                p_mm.wait(sc, gi * 17 + 17)
                if gi >= 2:
                    d_x.wait(sc, gi - 1)
                a_g.inc(sc.activation(gb[gi % 2][:, :], psc[oc4][:, :], AF.Silu,
                                      bias=bb_cols[:, ocg:ocg + 1]))


_C_P1 = ["ident_f32"]
_C_P2 = ["ident_bf", "onesf_row"]
_C_P3 = ["ident_f32", "ident_bf", "onescol64_bf", "onesf_row"]
_C_P5A = ["ones_row_bf"]
_C_P5B = ["onesf_row", "onesf_col"]
_C_P6 = ["ntri_bf", "ones_row_bf", "negcol_bf"]
_C_P7 = ["ident_f32", "ident_bf", "onesf_row", "rrep"]

_IO = {
    "x_tm": ([S, DM], F32), "x0_fm": ([DM, S], F32), "xn0_tm": ([S, DM], BF16),
    "qkv_w": ([3 * H0 * DH, DM], F32), "router_w": ([E0, DM], F32),
    "ebias": ([E0], F32), "qkv_fm": ([3 * H0 * DH, S], BF16),
    "combT": ([E0, S], BF16), "norm_w": ([DM], F32), "attnn_fm": ([DM, S], BF16),
    "gate_w": ([DM, DM], F32), "out_w": ([DM, DM], F32), "s0_fm": ([DM, S], BF16),
    "sw1": ([HS, DM], F32), "sw2": ([HS, DM], F32), "w12": ([E0 * 2 * HR, DM], F32),
    "hsh_fm": ([HS, S], BF16), "hrt_fm": ([E0 * HR, S], BF16),
    "sw3": ([DM, HS], F32), "w3": ([E0 * DM, HR], F32),
    "x1_fm": ([DM, S], F32), "xn1_fm": ([DM, S], BF16),
    "wq": ([DM, DM], F32), "wk": ([DM, DM], F32), "wv": ([DM, DM], F32),
    "wo": ([DM, DM], F32), "s1_fm": ([DM, S], BF16),
    "wbase": ([DM, DM], F32), "bbase": ([DM], F32), "wrout": ([E1, DM], F32),
    "aw": ([E1 * R1, DM], F32), "bw": ([E1 * DM, R1], F32), "out_tm": ([S, DM], F32),
}

_PHASES = [
    ("p1", _emit_p1, ["x_tm"], ["x0_fm", "xn0_tm"], _C_P1),
    ("p2", _emit_p2, ["qkv_w", "router_w", "ebias", "xn0_tm"], ["qkv_fm", "combT"], _C_P2),
    ("p3", _emit_p3, ["qkv_fm", "norm_w"], ["attnn_fm"], _C_P3),
    ("p4", _emit_p4, ["gate_w", "out_w", "xn0_tm", "attnn_fm"], ["s0_fm"], _C_P4_ := []),
    ("p5a", _emit_p5a, ["sw1", "sw2", "w12", "combT", "xn0_tm"], ["hsh_fm", "hrt_fm"], _C_P5A),
    ("p5b", _emit_p5b, ["sw3", "w3", "s0_fm", "x0_fm", "hsh_fm", "hrt_fm"],
     ["x1_fm", "xn1_fm"], _C_P5B),
    ("p6", _emit_p6, ["wq", "wk", "wv", "wo", "xn1_fm"], ["s1_fm"], _C_P6),
    ("p7", _emit_p7, ["wbase", "bbase", "wrout", "aw", "bw", "xn1_fm", "x1_fm", "s1_fm"],
     ["out_tm"], _C_P7),
]

_MODULE_INS = ["x_tm", "qkv_w", "router_w", "ebias", "norm_w", "gate_w", "out_w",
               "sw1", "sw2", "w12", "sw3", "w3", "wq", "wk", "wv", "wo",
               "wbase", "bbase", "wrout", "aw", "bw"]
_ALL_CONSTS = sorted({c for (_, _, _, _, cs) in _PHASES for c in cs})


def _build_phase(name):
    (pname, emit, ins, outs, consts) = next(p for p in _PHASES if p[0] == name)
    nc = bass.Bass("TRN2", target_bir_lowering=False)
    T = {}
    for n in ins:
        shp, dt = _IO[n]
        T[n] = nc.dram_tensor(n, shp, dt, kind="ExternalInput").ap()
    for n in outs:
        shp, dt = _IO[n]
        T[n] = nc.dram_tensor(n, shp, dt, kind="ExternalOutput").ap()
    T.update(_decl_consts(nc, consts))
    with ExitStack() as pst:
        ph = PhaseCtx(nc, pst, None, 0)
        emit(nc, T, ph)
    return nc


def _build_p1():
    return _build_phase("p1")


def _build_p2():
    return _build_phase("p2")


def _build_p3():
    return _build_phase("p3")


def _build_p4():
    return _build_phase("p4")


def _build_p5a():
    return _build_phase("p5a")


def _build_p5b():
    return _build_phase("p5b")


def _build_p6():
    return _build_phase("p6")


def _build_p7():
    return _build_phase("p7")


def _build_fused():
    nc = bass.Bass("TRN2", target_bir_lowering=False)
    produced = {o for (_, _, _, outs, _) in _PHASES for o in outs}
    T = {}
    for n in _MODULE_INS:
        shp, dt = _IO[n]
        T[n] = nc.dram_tensor(n, shp, dt, kind="ExternalInput").ap()
    for n in produced:
        shp, dt = _IO[n]
        kind = "ExternalOutput" if n == "out_tm" else "Internal"
        T[n] = nc.dram_tensor(n, shp, dt, kind=kind).ap()
    T.update(_decl_consts(nc, _ALL_CONSTS))
    for pidx, (pname, emit, ins, outs, consts) in enumerate(_PHASES):
        with nc.reset_on_exit():
            with ExitStack() as pst:
                ph = PhaseCtx(nc, pst, None, pidx)
                emit(nc, T, ph)
    return nc


def _run(name, builder, in_maps):
    if name not in _prog_cache:
        _prog_cache[name] = builder()
    nc = _prog_cache[name]
    res = run_bass_kernel_spmd(nc, in_maps, core_ids=list(range(len(in_maps))))
    return res.results


def kernel(**inputs):
    x = np.ascontiguousarray(inputs["L0_attn_hidden_states"], dtype=np.float32)
    n = x.shape[0]
    w = dict(
        qkv_w=np.ascontiguousarray(inputs["L0_attn_qkv_w"]),
        out_w=np.ascontiguousarray(inputs["L0_attn_out_w"]),
        gate_w=np.ascontiguousarray(inputs["L0_attn_gate_w"]),
        norm_w=np.ascontiguousarray(inputs["L0_attn_norm_w"]),
        w12=np.ascontiguousarray(inputs["L0_ffn_w12"]).reshape(E0 * 2 * HR, DM),
        w3=np.ascontiguousarray(inputs["L0_ffn_w3"]).reshape(E0 * DM, HR),
        router_w=np.ascontiguousarray(inputs["L0_ffn_router_w"]),
        ebias=np.ascontiguousarray(inputs["L0_ffn_expert_bias"]),
        sw1=np.ascontiguousarray(inputs["L0_ffn_sw1"]),
        sw2=np.ascontiguousarray(inputs["L0_ffn_sw2"]),
        sw3=np.ascontiguousarray(inputs["L0_ffn_sw3"]),
        wq=np.ascontiguousarray(inputs["L1_attn_Wq"]),
        wk=np.ascontiguousarray(inputs["L1_attn_Wk"]),
        wv=np.ascontiguousarray(inputs["L1_attn_Wv"]),
        wo=np.ascontiguousarray(inputs["L1_attn_Wo"]),
        wbase=np.ascontiguousarray(inputs["L1_ffn_W_base"]),
        bbase=np.ascontiguousarray(inputs["L1_ffn_b_base"]),
        wrout=np.ascontiguousarray(inputs["L1_ffn_W_router"]),
        aw=np.ascontiguousarray(inputs["L1_ffn_A_w"]).reshape(E1 * R1, DM),
        bw=np.ascontiguousarray(inputs["L1_ffn_B_w"]).reshape(E1 * DM, R1),
        **_const_inputs(_ALL_CONSTS),
    )
    res = _run("fused", _build_fused,
               [dict(x_tm=x[c], **w) for c in range(n)])
    return np.stack([np.asarray(res[c]["out_tm"], dtype=np.float32) for c in range(n)])


# version 03
# speedup vs baseline: 1.0269x; 1.0269x over previous
import numpy as np
import ml_dtypes
import concourse.bass as bass
import concourse.mybir as mybir
from concourse.bass_utils import run_bass_kernel_spmd
from contextlib import ExitStack

F32 = mybir.dt.float32
BF16 = mybir.dt.bfloat16
AF = mybir.ActivationFunctionType
ALU = mybir.AluOpType
AX = mybir.AxisListType
BFNP = ml_dtypes.bfloat16

NCORE = 8
S = 2048
DM = 2048
EPS = 1e-5

H0, DH, BS0 = 32, 64, 256
NB0 = S // BS0
_l0_base = 1.0 / 2.0 ** (8.0 / H0)
_l0_factor = 1.0 - 0.0 / (32 - 1 + 1e-5) + 1e-5
SLOPES = [float((_l0_base ** (h + 1)) * _l0_factor) for h in range(H0)]

E0, HS, HR = 8, 8192, 1024
H1, SC1 = 32, 0.125
E1, R1, LSCALE = 8, 8, 2.0

_prog_cache = {}


def cdiv(a, b):
    return -(-a // b)


class Pipe:

    def __init__(self, st, nc, name, step=1):
        self.h = st.enter_context(nc.semaphore(name))
        self.step = step
        self.emitted = 0

    def inc(self, instr, n=1):
        instr.then_inc(self.h, self.step * n)
        self.emitted += self.step * n
        return instr

    def wait(self, eng, units):
        if units > 0:
            eng.wait_ge(self.h, self.step * units)


class Chain:

    def __init__(self, st, nc, name):
        self.h = st.enter_context(nc.semaphore(name))
        self.n = 0
        self.emitted = 0

    def seal(self, instr):
        instr.then_inc(self.h, 1)
        self.n += 1
        self.emitted = self.n
        return instr

    def gate(self, eng):
        if self.n > 0:
            eng.wait_ge(self.h, self.n)


class SPipe:

    def __init__(self, st, nc, name, nslots, step=16):
        self.p = [Pipe(st, nc, f"{name}{i}", step) for i in range(nslots)]
        self.n = nslots

    def inc(self, instr, k, n=1):
        self.p[k % self.n].inc(instr, n)
        return instr

    def wait(self, eng, k, uses):
        self.p[k % self.n].wait(eng, uses)

    def wait_all(self, eng, total_units_per_slot):
        for p in self.p:
            p.wait(eng, total_units_per_slot)


class SSched:

    def __init__(self, pipes):
        self.pipes = pipes
        self.counts = {k: 0 for k in pipes}
        self.steps = []
        self.res = {}

    def add(self, eng, fn, reads=(), writes=(), extra=()):
        if eng == "gp":
            writes = tuple(writes) + ("__gp_tok",)
        deps = {}
        for r in reads:
            ent = self.res.get(r)
            if ent and ent["w"]:
                e2, v = ent["w"]
                deps[e2] = max(deps.get(e2, 0), v)
        for w_ in writes:
            ent = self.res.get(w_)
            if ent:
                if ent["w"]:
                    e2, v = ent["w"]
                    deps[e2] = max(deps.get(e2, 0), v)
                for (e2, v) in ent["r"]:
                    deps[e2] = max(deps.get(e2, 0), v)
        self.counts[eng] += 1
        tag = (eng, self.counts[eng])
        for r in reads:
            self.res.setdefault(r, {"w": None, "r": []})["r"].append(tag)
        for w_ in writes:
            self.res[w_] = {"w": tag, "r": []}
        self.steps.append((eng, fn, deps, tag, list(extra)))
        return tag

    def emit(self, eng_name, engine):
        for (eng, fn, deps, tag, extra) in self.steps:
            if eng != eng_name:
                continue
            for e2, v in deps.items():
                if e2 == eng_name and v >= tag[1]:
                    continue
                self.pipes[e2].wait(engine, v)
            for (pipe, units) in extra:
                pipe.wait(engine, units)
            self.pipes[eng].inc(fn(engine))


class Bar:

    def __init__(self, st, nc):
        self.clr = Pipe(st, nc, "bar_clr")
        self.eng = {k: Pipe(st, nc, f"bar_{k}") for k in ("pe", "dve", "act", "gp", "sp")}
        self.counts = {k: 0 for k in self.eng}


class PhaseCtx:
    def __init__(self, nc, st, bar, pidx):
        self.nc, self.st, self.bar, self.pidx = nc, st, bar, pidx
        self.sems = []

    def pipe(self, name, step=1):
        p = Pipe(self.st, self.nc, f"{name}_{self.pidx}", step)
        self.sems.append(p)
        return p

    def spipe(self, name, nslots, step=16):
        sp = SPipe(self.st, self.nc, f"{name}_{self.pidx}", nslots, step)
        self.sems.extend(sp.p)
        return sp

    def chain(self, name):
        c = Chain(self.st, self.nc, f"{name}_{self.pidx}")
        self.sems.append(c)
        return c

    def wrap(self, blk):
        self.raw_blk = blk
        return blk


class PhBlock:
    _MAP = {"gpsimd": "gp", "vector": "dve", "scalar": "act",
            "tensor": "pe", "sync": "sp"}

    def __init__(self, blk, bar, pidx):
        self._blk, self._bar, self._pidx = blk, bar, pidx

    def _deco(self, attr):
        eng_key = self._MAP[attr]
        bar, pidx = self._bar, self._pidx

        def decorator(f):
            def body(eng):
                bar.clr.wait(eng, pidx)
                f(eng)
                bar.eng[eng_key].inc(eng.nop())
            bar.counts[eng_key] += 1
            getattr(self._blk, attr)(body)
        return decorator

    @property
    def gpsimd(self):
        return self._deco("gpsimd")

    @property
    def vector(self):
        return self._deco("vector")

    @property
    def scalar(self):
        return self._deco("scalar")

    @property
    def tensor(self):
        return self._deco("tensor")

    @property
    def sync(self):
        return self._deco("sync")



def _sbt(nc, ph, name, *a, **kw):
    return nc.sbuf_tensor(f"p{ph.pidx}_{name}", *a, **kw)


def _pst(nc, ph, name, *a, **kw):
    return nc.psum_tensor(f"p{ph.pidx}_{name}", *a, **kw)


def _phase_boundary(nc, bar, ph):
    bb = ph.raw_blk

    @bb.gpsimd
    def _(g):
        for k, cnt in bar.counts.items():
            if cnt:
                bar.eng[k].wait(g, cnt)
        for p in ph.sems:
            if p.emitted:
                g.wait_ge(p.h, p.emitted)
        for p in ph.sems:
            g.sem_clear(p.h)
        bar.clr.inc(g.nop())


def _consts():
    ident_f32 = np.eye(128, dtype=np.float32)
    ident_bf = np.eye(128, dtype=BFNP)
    ntri = np.where(np.arange(128)[:, None] >= np.arange(128)[None, :], -1.0, 0.0)
    out = dict(
        ident_f32=ident_f32,
        ident_bf=ident_bf,
        ntri_bf=ntri.astype(BFNP),
        ones_row_bf=np.ones((1, 128), dtype=BFNP),
        negcol_bf=np.full((128, 1), -1.0, dtype=BFNP),
        onescol64_bf=np.ones((64, 1), dtype=BFNP),
        onesf_row=np.ones((1, 128), dtype=np.float32),
        onesf_col=np.ones((128, 1), dtype=np.float32),
    )
    rrep = np.zeros((8, 64), dtype=BFNP)
    for e in range(8):
        rrep[e, e * 8:(e + 1) * 8] = 1.0
    out["rrep"] = rrep
    return out


_CONSTS = _consts()


def _decl_consts(nc, names):
    aps = {}
    for n in names:
        arr = _CONSTS[n]
        dt = BF16 if arr.dtype == BFNP else F32
        aps[n] = nc.dram_tensor(n, list(arr.shape), dt, kind="ExternalInput").ap()
    return aps


def _const_inputs(names):
    return {n: _CONSTS[n] for n in names}


def _emit_p1(nc, T, ph):
    x, x0_fm, xn0_tm, C = T["x_tm"], T["x0_fm"], T["xn0_tm"], T
    NT = S // 128

    if True:
        e = ph.st.enter_context
        xt = [e(_sbt(nc, ph, f"xt{i}", [128, DM], F32)) for i in range(2)]
        xnt = e(_sbt(nc, ph, "xnt", [128, DM], BF16))
        sq = e(_sbt(nc, ph, "sq", [128, DM], F32))
        idf = e(_sbt(nc, ph, "idf", [128, 128], F32))
        ssum = e(_sbt(nc, ph, "ssum", [128, 1], F32))
        ssq = e(_sbt(nc, ph, "ssq", [128, 1], F32))
        mu = e(_sbt(nc, ph, "mu", [128, 1], F32))
        var = e(_sbt(nc, ph, "var", [128, 1], F32))
        var2 = e(_sbt(nc, ph, "var2", [128, 1], F32))
        lnv = e(_sbt(nc, ph, "lnv", [128, 1], F32))
        rstd = e(_sbt(nc, ph, "rstd", [128, 1], F32))
        bcol = e(_sbt(nc, ph, "bcol", [128, 1], F32))
        epscol = e(_sbt(nc, ph, "epscol", [128, 1], F32))
        evf = [e(_sbt(nc, ph, f"evf{i}", [128, 128], F32)) for i in range(2)]
        psf = [e(_pst(nc, ph, f"psf{i}", [128, 128], F32)) for i in range(2)]

        ld = ph.spipe("ld", 2)
        lc = ph.pipe("lc", 16)
        st_n = ph.pipe("st_n", 16)
        st_f = ph.spipe("st_f", 2)
        s_red = ph.pipe("s_red")
        s_sq = ph.pipe("s_sq")
        s_var = ph.pipe("s_var")
        s_rstd = ph.pipe("s_rstd")
        s_app = ph.pipe("s_app")
        s_tp = ph.pipe("s_tp")
        s_ev = ph.pipe("s_ev")
        cv = ph.chain("cv")
        ca = ph.chain("ca")
        blk = ph.wrap(e(nc.Block()))

        @blk.gpsimd
        def _(g):
            g.memset(epscol[:, :], EPS)
            lc.inc(g.dma_start(idf[:, :], C["ident_f32"]))
            for it in range(NT + 2):
                if it >= 2:
                    u = it - 2
                    s_app.wait(g, u + 1)
                    st_n.inc(g.dma_start(xn0_tm[u * 128:(u + 1) * 128, :], xnt[:, :]))
                    for dc in range(16):
                        k = u * 16 + dc
                        s_ev.wait(g, k + 1)
                        st_f.inc(g.dma_start(
                            x0_fm[dc * 128:(dc + 1) * 128, u * 128:(u + 1) * 128],
                            evf[dc % 2][:, :]), k)
                if it < NT:
                    tc = it
                    if tc >= 2:
                        s_red.wait(g, tc - 1)
                        s_app.wait(g, tc - 1)
                        s_tp.wait(g, 16 * (tc - 1))
                    ld.inc(g.dma_start(xt[tc % 2][:, :], x[tc * 128:(tc + 1) * 128, :]), tc)
            st_f.wait_all(g, NT * 16 // 2)
            st_n.wait(g, NT)

        @blk.vector
        def _(v):
            for tc in range(NT):
                ld.wait(v, tc, tc // 2 + 1)
                s_red.inc(v.reduce_sum(ssum[:, :], xt[tc % 2][:, :], axis=AX.X))
                s_red.wait(v, tc + 1)
                s_sq.wait(v, tc + 1)
                s_rstd.wait(v, tc)
                cv.seal(v.tensor_scalar_mul(mu[:, :], ssum[:, :], 1.0 / DM))
                cv.seal(v.tensor_scalar_mul(var2[:, :], ssq[:, :], 1.0 / DM))
                cv.gate(v)
                cv.seal(v.scalar_tensor_tensor(bcol[:, :], mu[:, :], -1.0, mu[:, :], ALU.mult, ALU.mult))
                cv.gate(v)
                s_var.inc(v.tensor_tensor(var[:, :], var2[:, :], bcol[:, :], ALU.add))
                s_rstd.wait(v, tc + 1)
                cv.seal(v.scalar_tensor_tensor(bcol[:, :], mu[:, :], -1.0, rstd[:, :], ALU.mult, ALU.mult))
                cv.gate(v)
                if tc >= 1:
                    st_n.wait(v, tc)
                s_app.inc(v.tensor_scalar(xnt[:, :], xt[tc % 2][:, :], rstd[:, :], bcol[:, :], ALU.mult, ALU.add))
                for dc in range(16):
                    k = tc * 16 + dc
                    s_tp.wait(v, k + 1)
                    if k >= 2:
                        st_f.wait(v, k, k // 2)
                    s_ev.inc(v.tensor_copy(evf[dc % 2][:, :], psf[dc % 2][:, :]))

        @blk.scalar
        def _(sc):
            for tc in range(NT):
                ld.wait(sc, tc, tc // 2 + 1)
                s_var.wait(sc, tc)
                s_sq.inc(sc.activation(sq[:, :], xt[tc % 2][:, :], AF.Square,
                                       accum_out=ssq[:, :]))
                s_var.wait(sc, tc + 1)
                s_app.wait(sc, tc)
                ca.seal(sc.activation(lnv[:, :], var[:, :], AF.Ln, bias=epscol[:, :]))
                ca.gate(sc)
                s_rstd.inc(sc.activation(rstd[:, :], lnv[:, :], AF.Exp, scale=-0.5))

        @blk.tensor
        def _(t):
            lc.wait(t, 1)
            for tc in range(NT):
                ld.wait(t, tc, tc // 2 + 1)
                for dc in range(16):
                    k = tc * 16 + dc
                    if k >= 2:
                        s_ev.wait(t, k - 1)
                    s_tp.inc(t.transpose(psf[dc % 2][:, :],
                                         xt[tc % 2][:, dc * 128:(dc + 1) * 128],
                                         idf[:, :]))



def _emit_p2(nc, T, ph):
    qkv_w, router_w, ebias = T["qkv_w"], T["router_w"], T["ebias"]
    xn0_tm, qkv_fm, combT, C = T["xn0_tm"], T["qkv_fm"], T["combT"], T
    qkv_bfw = nc.dram_tensor("qkv_bfw", [3 * H0 * DH, DM], BF16, kind="Internal").ap()
    router_bfw = nc.dram_tensor("router_bfw", [E0, DM], BF16, kind="Internal").ap()
    NOB = (3 * H0 * DH) // 512
    NTC = S // 128
    NTT = S // 512

    if True:
        e = ph.st.enter_context
        xn = [e(_sbt(nc, ph, f"xn{i}", [128, S], BF16)) for i in range(16)]
        wbuf = [e(_sbt(nc, ph, f"wbuf{i}", [128, 16 * 512], BF16, is_transpose_dest=True)) for i in range(2)]
        evq = [e(_sbt(nc, ph, f"evq{i}", [128, 512], BF16)) for i in range(4)]
        rsb = e(_sbt(nc, ph, "rsb", [E0, DM], BF16))
        routerT = e(_sbt(nc, ph, "routerT", [128, 128], BF16))
        idb = e(_sbt(nc, ph, "idb", [128, 128], BF16))
        onesf = e(_sbt(nc, ph, "onesf", [1, 128], F32))
        eb = e(_sbt(nc, ph, "eb", [1, E0], F32))
        bias_bc = e(_sbt(nc, ph, "bias_bc", [128, E0], F32))
        sc_tm = [e(_sbt(nc, ph, f"sc_tm{i}", [128, E0], F32)) for i in range(2)]
        comb_tm = [e(_sbt(nc, ph, f"comb_tm{i}", [128, E0], BF16)) for i in range(2)]
        combT_sb = e(_sbt(nc, ph, "combT_sb", [E0, S], BF16))
        sel = e(_sbt(nc, ph, "sel", [128, E0], F32))
        m1 = e(_sbt(nc, ph, "m1", [128, 1], F32))
        m2 = e(_sbt(nc, ph, "m2", [128, 1], F32))
        is1 = e(_sbt(nc, ph, "is1", [128, E0], F32))
        sel2 = e(_sbt(nc, ph, "sel2", [128, E0], F32))
        ind = e(_sbt(nc, ph, "ind", [128, E0], F32))
        wsc = e(_sbt(nc, ph, "wsc", [128, E0], F32))
        ssum = e(_sbt(nc, ph, "ssumr", [128, 1], F32))
        rcp = e(_sbt(nc, ph, "rcp", [128, 1], F32))

        ps_rtp = e(_pst(nc, ph, "ps_rtp", [128, E0], BF16))
        ps_ctp = e(_pst(nc, ph, "ps_ctp", [E0, 128], BF16))
        ps_sc = [e(_pst(nc, ph, f"ps_sc{i}", [128, E0], F32)) for i in range(2)]
        ps_q = [e(_pst(nc, ph, f"ps_q{i}", [128, 512], F32)) for i in range(4)]

        rc = ph.pipe("rc", 16)
        qc = ph.spipe("qc", 6)
        gl = ph.pipe("gl", 16)
        sxn = ph.pipe("sxn", 16)
        sw = ph.spipe("sw", 2)
        stc = ph.pipe("stc", 16)
        r_pe = ph.pipe("r_pe")
        r_dv = ph.pipe("r_dv")
        r_ac = ph.pipe("r_ac")
        s_mm = ph.pipe("s_mm")
        evp = ph.pipe("evp")
        stq = ph.spipe("stq", 4)
        cdv = ph.chain("cdv")
        blk = ph.wrap(e(nc.Block()))

        @blk.gpsimd
        def _(g):
            rc.inc(g.dma_start(router_bfw, router_w))
            for c in range(6):
                qc.inc(g.dma_start(qkv_bfw[c * 1024:(c + 1) * 1024, :],
                                   qkv_w[c * 1024:(c + 1) * 1024, :]), c)
            rc.wait(g, 1)
            gl.inc(g.dma_start(rsb[:, :], router_bfw))
            gl.inc(g.dma_start(eb[:, :], ebias.rearrange("(b a) -> b a", b=1)))
            gl.inc(g.dma_start(idb[:, :], C["ident_bf"]))
            gl.inc(g.dma_start(onesf[:, :], C["onesf_row"]))
            for ob in range(NOB):
                for tt in range(NTT):
                    for oc4 in range(4):
                        gidx = ob * 16 + tt * 4 + oc4
                        evp.wait(g, gidx + 1)
                        stq.inc(g.dma_start(
                            qkv_fm[ob * 512 + oc4 * 128: ob * 512 + (oc4 + 1) * 128,
                                   tt * 512:(tt + 1) * 512],
                            evq[gidx % 4][:, :]), gidx)
            r_dv.wait(g, 49)
            stc.inc(g.dma_start(combT, combT_sb[:, :]))
            stq.wait_all(g, NOB * 16 // 4)
            stc.wait(g, 1)

        @blk.sync
        def _(s):
            for ic in range(16):
                sxn.inc(s.dma_start_transpose(xn[ic][:, :], xn0_tm[:, ic * 128:(ic + 1) * 128]))
            for ob in range(NOB):
                for c in range(cdiv((ob + 1) * 512, 1024)):
                    qc.wait(s, c, 1)
                if ob >= 2:
                    s_mm.wait(s, 16 * (ob - 1))
                for ic in range(16):
                    sw.inc(s.dma_start_transpose(
                        wbuf[ob % 2][:, ic * 512:(ic + 1) * 512],
                        qkv_bfw[ob * 512:(ob + 1) * 512, ic * 128:(ic + 1) * 128]), ob)

        @blk.tensor
        def _(t):
            gl.wait(t, 4)
            for ic in range(16):
                if ic >= 1:
                    r_dv.wait(t, ic)
                r_pe.inc(t.transpose(ps_rtp[:, :], rsb[:, ic * 128:(ic + 1) * 128],
                                     idb[0:E0, 0:E0]))
            r_dv.wait(t, 16)
            r_pe.inc(t.matmul(ps_sc[0][:, :], onesf[:, :], eb[:, :], start=True, stop=True))
            sxn.wait(t, 16)

            def score_group(tc):
                r_dv.wait(t, 17)
                if tc >= 2:
                    r_ac.wait(t, tc - 1)
                for ic in range(16):
                    r_pe.inc(t.matmul(ps_sc[tc % 2][:, :],
                                      xn[ic][:, tc * 128:(tc + 1) * 128],
                                      routerT[:, ic * 8:(ic + 1) * 8],
                                      start=(ic == 0), stop=(ic == 15)), n=(1 if ic == 15 else 0))

            def ctp_item(j):
                r_dv.wait(t, 18 + 2 * j)
                r_pe.inc(t.transpose(ps_ctp[:, :], comb_tm[j % 2][:, :], idb[:, :]))

            for ob in range(NOB):
                sw.wait(t, ob, 16 * (ob // 2 + 1))
                for tt in range(NTT):
                    for oc4 in range(4):
                        gidx = ob * 16 + tt * 4 + oc4
                        if gidx % 10 == 4 and gidx // 10 < NTC:
                            score_group(gidx // 10)
                        if gidx % 10 == 9 and gidx // 10 < NTC:
                            ctp_item(gidx // 10)
                        if gidx >= 4:
                            evp.wait(t, gidx - 3)
                        for ic in range(16):
                            s_mm.inc(t.matmul(ps_q[oc4][:, :],
                                              wbuf[ob % 2][:, ic * 512 + oc4 * 128: ic * 512 + (oc4 + 1) * 128],
                                              xn[ic][:, tt * 512:(tt + 1) * 512],
                                              start=(ic == 0), stop=(ic == 15)),
                                     n=(1 if ic == 15 else 0))

        @blk.vector
        def _(v):
            for ic in range(16):
                r_pe.wait(v, ic + 1)
                r_dv.inc(v.tensor_copy(routerT[:, ic * 8:(ic + 1) * 8], ps_rtp[:, :]))
            r_pe.wait(v, 17)
            r_dv.inc(v.tensor_copy(bias_bc[:, :], ps_sc[0][:, :]))
            def pe_ctp_unit(j):
                return 19 + 2 * j
            for tc in range(NTC):
                if tc >= 1:
                    r_pe.wait(v, pe_ctp_unit(tc - 1))
                    r_dv.inc(v.tensor_copy(combT_sb[:, (tc - 1) * 128:tc * 128], ps_ctp[:, :]))
                r_ac.wait(v, tc + 1)
                if tc >= 2:
                    r_pe.wait(v, pe_ctp_unit(tc - 2))
                cdv.seal(v.tensor_tensor(sel[:, :], sc_tm[tc % 2][:, :], bias_bc[:, :], ALU.add))
                cdv.gate(v)
                cdv.seal(v.reduce_max(m1[:, :], sel[:, :], axis=AX.X))
                cdv.gate(v)
                cdv.seal(v.tensor_single_scalar(is1[:, :], sel[:, :], m1[:, 0:1], ALU.is_ge))
                cdv.gate(v)
                cdv.seal(v.scalar_tensor_tensor(sel2[:, :], is1[:, :], -1e9, sel[:, :], ALU.mult, ALU.add))
                cdv.gate(v)
                cdv.seal(v.reduce_max(m2[:, :], sel2[:, :], axis=AX.X))
                cdv.gate(v)
                cdv.seal(v.tensor_single_scalar(ind[:, :], sel[:, :], m2[:, 0:1], ALU.is_ge))
                cdv.gate(v)
                cdv.seal(v.tensor_tensor(wsc[:, :], ind[:, :], sc_tm[tc % 2][:, :], ALU.mult))
                cdv.gate(v)
                cdv.seal(v.reduce_sum(ssum[:, :], wsc[:, :], axis=AX.X))
                cdv.gate(v)
                cdv.seal(v.tensor_single_scalar(ssum[:, :], ssum[:, :], 1e-9, ALU.add))
                cdv.gate(v)
                cdv.seal(v.reciprocal(rcp[:, :], ssum[:, :]))
                cdv.gate(v)
                r_dv.inc(v.tensor_single_scalar(comb_tm[tc % 2][:, :], wsc[:, :], rcp[:, 0:1], ALU.mult))
            r_pe.wait(v, pe_ctp_unit(15))
            r_dv.inc(v.tensor_copy(combT_sb[:, 15 * 128:16 * 128], ps_ctp[:, :]))

        @blk.scalar
        def _(sc):
            def sig_item(tc):
                r_pe.wait(sc, 18 + 2 * tc)
                if tc >= 2:
                    r_dv.wait(sc, 18 + 2 * (tc - 2))
                r_ac.inc(sc.activation(sc_tm[tc % 2][:, :], ps_sc[tc % 2][:, :], AF.Sigmoid))

            for ob in range(NOB):
                for tt in range(NTT):
                    for oc4 in range(4):
                        gidx = ob * 16 + tt * 4 + oc4
                        if gidx % 10 == 5 and gidx // 10 < NTC:
                            sig_item(gidx // 10)
                        s_mm.wait(sc, gidx + 1)
                        if gidx >= 4:
                            stq.wait(sc, gidx, gidx // 4)
                        evp.inc(sc.activation(evq[gidx % 4][:, :], ps_q[oc4][:, :], AF.Silu))



def _emit_p3(nc, T, ph):
    qkv_fm, norm_w, attnn_fm, C = T["qkv_fm"], T["norm_w"], T["attnn_fm"], T
    NM = H0 * NB0

    if True:
        e = ph.st.enter_context
        qh = [e(_sbt(nc, ph, f"qh{i}", [64, S], BF16)) for i in range(2)]
        kh = [e(_sbt(nc, ph, f"kh{i}", [64, S], BF16)) for i in range(2)]
        vh = [e(_sbt(nc, ph, f"vh{i}", [64, S], BF16)) for i in range(2)]
        attn = [e(_sbt(nc, ph, f"attn{i}", [128, S], BF16)) for i in range(16)]
        D = [e(_sbt(nc, ph, f"D{i}", [128, BS0], F32)) for i in range(2)]
        m01 = [e(_sbt(nc, ph, f"m01{i}", [128, BS0], F32)) for i in range(2)]
        Dc = [e(_sbt(nc, ph, f"Dc{i}", [128, BS0], F32)) for i in range(2)]
        eexp = [e(_sbt(nc, ph, f"eexp{i}", [128, BS0], F32)) for i in range(2)]
        ddT = [e(_sbt(nc, ph, f"ddT{i}", [128, BS0], BF16)) for i in range(2)]
        ior = e(_sbt(nc, ph, "ior", [1, BS0], F32))
        ibc = e(_sbt(nc, ph, "ibc", [64, BS0], F32))
        iocol = [e(_sbt(nc, ph, f"iocol{i}", [128, 1], F32)) for i in range(2)]
        qd = [e(_sbt(nc, ph, f"qd{i}", [64, BS0], BF16)) for i in range(2)]
        kdc = [e(_sbt(nc, ph, f"kdc{i}", [128, 2], F32)) for i in range(2)]
        awT = [e(_sbt(nc, ph, f"awT{i}", [128, BS0], BF16)) for i in range(2)]
        kdkT = [e(_sbt(nc, ph, f"kdkT{i}", [128, 64], BF16)) for i in range(2)]
        vbT = [e(_sbt(nc, ph, f"vbT{i}", [128, 64], BF16)) for i in range(2)]
        qdq = e(_sbt(nc, ph, "qdq", [64, BS0], BF16))
        S_f = [e(_sbt(nc, ph, f"S_f{i}", [64, 64], F32)) for i in range(2)]
        S_b = e(_sbt(nc, ph, "S_b", [64, 64], BF16))
        sq_b = e(_sbt(nc, ph, "sq_b", [64, BS0], BF16))
        ssb = [e(_sbt(nc, ph, f"ssb{i}", [1, S], F32)) for i in range(2)]
        lnr = e(_sbt(nc, ph, "lnr", [1, S], F32))
        rstd_r = e(_sbt(nc, ph, "rstd_r", [1, S], F32))
        ebias1 = e(_sbt(nc, ph, "ebias1", [1, 1], F32))
        ebias2 = e(_sbt(nc, ph, "ebias2", [1, 1], F32))
        nwrow = e(_sbt(nc, ph, "nwrow", [1, DM], F32))
        nw_cols = e(_sbt(nc, ph, "nw_cols", [128, 16], F32))
        o64 = e(_sbt(nc, ph, "o64", [64, 1], BF16))
        onesf = e(_sbt(nc, ph, "onesf", [1, 128], F32))
        idf = e(_sbt(nc, ph, "idf", [128, 128], F32))
        idb3 = e(_sbt(nc, ph, "idb3", [128, 128], BF16))
        anout = [e(_sbt(nc, ph, f"anout{i}", [128, 512], BF16)) for i in range(4)]

        ps_sc = [e(_pst(nc, ph, f"ps_sc{i}", [128, 512], F32)) for i in range(2)]
        ps_y = e(_pst(nc, ph, "ps_y", [64, BS0], F32))
        ps_S = e(_pst(nc, ph, "ps_S", [64, 64], F32))
        ps_tp = [e(_pst(nc, ph, f"ps_tp{i}", [128, 64], BF16)) for i in range(2)]
        ps_ss = e(_pst(nc, ph, "ps_ss", [1, BS0], F32))
        ps_nw = e(_pst(nc, ph, "ps_nw", [128, 16], F32))

        g_init = ph.pipe("g_init", 16)
        g_eng = ph.pipe("g_eng")
        qld = ph.spipe("qld", 2)
        d_pre = ph.pipe("d_pre")
        p_pre = ph.pipe("p_pre")
        a_set = ph.pipe("a_set")
        d_set = ph.pipe("d_set")
        p_sc = ph.pipe("p_sc")
        d_aw = ph.pipe("d_aw")
        p_tp = ph.pipe("p_tp")
        d_tp = ph.pipe("d_tp")
        d_qd = ph.pipe("d_qd")
        p_y = ph.pipe("p_y")
        p_S = ph.pipe("p_S")
        d_S = ph.pipe("d_S")
        a_y = ph.pipe("a_y")
        a_sq = ph.pipe("a_sq")
        p_ss = ph.pipe("p_ss")
        d_ss = ph.pipe("d_ss")
        a_r = ph.pipe("a_r")
        n_pe = ph.pipe("n_pe")
        n_dv = ph.pipe("n_dv")
        n_st = ph.spipe("n_st", 4)
        cdv = ph.chain("cdv3")
        cac = ph.chain("cac3")
        blk = ph.wrap(e(nc.Block()))

        @blk.gpsimd
        def _(g):
            g_eng.inc(g.iota(D[0][:, :], [[1, BS0]], base=0, channel_multiplier=-1,
                             allow_small_or_imprecise_dtypes=True))
            g_eng.inc(g.iota(D[1][:, :], [[1, BS0]], base=-128, channel_multiplier=-1,
                             allow_small_or_imprecise_dtypes=True))
            g_eng.inc(g.iota(ior[:, :], [[1, BS0]], base=1, channel_multiplier=0,
                             allow_small_or_imprecise_dtypes=True))
            g_eng.inc(g.iota(iocol[0][:, :], [[0, 1]], base=1 - BS0, channel_multiplier=1,
                             allow_small_or_imprecise_dtypes=True))
            g_eng.inc(g.iota(iocol[1][:, :], [[0, 1]], base=1 - BS0 + 128, channel_multiplier=1,
                             allow_small_or_imprecise_dtypes=True))
            g_eng.inc(g.memset(ebias1[:, :], float(DM) * EPS))
            g_eng.inc(g.memset(ebias2[:, :], 0.5 * float(np.log(DM))))
            g_eng.inc(g.memset(o64[:, :], 1.0))
            g_init.inc(g.dma_start(nwrow[:, :], norm_w.rearrange("(b a) -> b a", b=1)))
            g_init.inc(g.dma_start(idf[:, :], C["ident_f32"]))
            g_init.inc(g.dma_start(onesf[:, :], C["onesf_row"]))
            g_init.inc(g.dma_start(idb3[:, :], C["ident_bf"]))
            for h in range(H0):
                if h >= 2:
                    p_ss.wait(g, NB0 * (h - 1))
                    d_qd.wait(g, NB0 * (h - 1))
                base = h * 3 * DH
                qld.inc(g.dma_start(qh[h % 2][:, :], qkv_fm[base:base + DH, :]), h)
                qld.inc(g.dma_start(kh[h % 2][:, :], qkv_fm[base + DH:base + 2 * DH, :]), h)
                qld.inc(g.dma_start(vh[h % 2][:, :], qkv_fm[base + 2 * DH:base + 3 * DH, :]), h)
            for tt in range(4):
                for fc in range(16):
                    k = tt * 16 + fc
                    n_dv.wait(g, k + 2)
                    n_st.inc(g.dma_start(
                        attnn_fm[fc * 128:(fc + 1) * 128, tt * 512:(tt + 1) * 512],
                        anout[k % 4][:, :]), k)
            n_st.wait_all(g, 16)

        @blk.tensor
        def _(t):
            g_init.wait(t, 4)
            g_eng.wait(t, 8)
            p_pre.inc(t.matmul(ps_sc[0][0:64, 0:BS0], onesf[:, 0:64], ior[:, :],
                               start=True, stop=True))
            for h in range(H0):
                qld.wait(t, h, 3 * (h // 2 + 1))
                for n in range(NB0):
                    m = h * NB0 + n
                    d_aw.wait(t, 2 * m)
                    if m == 0:
                        d_pre.wait(t, 1)
                    for c in range(2):
                        p_sc.inc(t.matmul(
                            ps_sc[c][:, 0:BS0],
                            kh[h % 2][:, n * BS0 + c * 128: n * BS0 + (c + 1) * 128],
                            qh[h % 2][:, n * BS0:(n + 1) * BS0],
                            start=True, stop=True), n=(1 if c == 1 else 0))
                    for j in range(4):
                        tix = 4 * m + j
                        c = j % 2
                        src_t = kh[h % 2] if j < 2 else vh[h % 2]
                        if tix >= 2:
                            d_tp.wait(t, tix - 1)
                        p_tp.inc(t.transpose(
                            ps_tp[j % 2][:, :],
                            src_t[:, n * BS0 + c * 128: n * BS0 + (c + 1) * 128],
                            idb3[0:DH, 0:DH]))
                    d_aw.wait(t, 2 * m + 2)
                    d_tp.wait(t, 4 * m + 4)
                    d_qd.wait(t, m + 1)
                    d_S.wait(t, 9 * h + 1 + n)
                    a_sq.wait(t, m)
                    t.matmul(ps_y[:, :], vbT[0][:, :], awT[0][:, :], start=True, stop=False)
                    t.matmul(ps_y[:, :], vbT[1][:, :], awT[1][:, :], start=False, stop=False)
                    p_y.inc(t.matmul(ps_y[:, :], S_b[:, :], qdq[:, :], start=False, stop=True))
                    t.matmul(ps_S[:, :], kdkT[0][:, :], vbT[0][:, :], start=True, stop=False)
                    p_S.inc(t.matmul(ps_S[:, :], kdkT[1][:, :], vbT[1][:, :], start=False, stop=True))
                    a_sq.wait(t, m + 1)
                    d_ss.wait(t, m)
                    p_ss.inc(t.matmul(ps_ss[:, :], o64[:, :], sq_b[:, :], start=True, stop=True))
            for c in range(16):
                n_pe.inc(t.transpose(ps_nw[:, c:c + 1], nwrow[:, c * 128:(c + 1) * 128],
                                     idf[0:1, 0:1]))
            a_r.wait(t, 1)
            d_aw.wait(t, 2 * NM)
            for tt in range(4):
                if tt >= 2:
                    n_dv.wait(t, 1 + (tt - 2) * 16 + 16)
                n_pe.inc(t.matmul(ps_sc[tt % 2][:, :], onesf[:, :],
                                  rstd_r[:, tt * 512:(tt + 1) * 512], start=True, stop=True))

        @blk.vector
        def _(v):
            g_init.wait(v, 4)
            g_eng.wait(v, 8)
            for c in range(2):
                v.tensor_single_scalar(m01[c][:, :], D[c][:, :], 0.0, ALU.is_ge)
                v.tensor_single_scalar(Dc[c][:, :], D[c][:, :], 0.0, ALU.max)
            cdv.seal(v.memset(ssb[0][:, :], 0.0))
            p_pre.wait(v, 1)
            d_pre.inc(v.tensor_copy(ibc[:, :], ps_sc[0][0:64, 0:BS0]))
            for h in range(H0):
                a_set.wait(v, 5 * (h + 1))
                d_set.inc(v.tensor_tensor(ddT[0][:, :], eexp[0][:, :], m01[0][:, :], ALU.mult))
                d_set.inc(v.tensor_tensor(ddT[1][:, :], eexp[1][:, :], m01[1][:, :], ALU.mult))
                p_S.wait(v, NB0 * h)
                p_y.wait(v, NB0 * h)
                d_S.wait(v, 9 * h)
                v.memset(S_f[0][:, :], 0.0)
                d_S.inc(v.memset(S_b[:, :], 0.0))
                d_set.wait(v, 2 * (h + 1))
                for n in range(NB0):
                    m = h * NB0 + n
                    p_sc.wait(v, m + 1)
                    for c in range(2):
                        d_aw.inc(v.tensor_tensor(awT[c][:, :], ps_sc[c][:, 0:BS0],
                                                 ddT[c][:, :], ALU.mult))
                    for j in range(4):
                        tix = 4 * m + j
                        c = j % 2
                        p_tp.wait(v, tix + 1)
                        if j < 2:
                            d_tp.inc(v.tensor_single_scalar(kdkT[c][:, :], ps_tp[j % 2][:, :],
                                                            kdc[h % 2][:, c:c + 1], ALU.mult))
                        else:
                            d_tp.inc(v.tensor_copy(vbT[c][:, :], ps_tp[j % 2][:, :]))
                    p_y.wait(v, m)
                    d_qd.inc(v.tensor_tensor(qdq[:, :], qh[h % 2][:, n * BS0:(n + 1) * BS0],
                                             qd[h % 2][:, :], ALU.mult))
                    p_ss.wait(v, m + 1)
                    if m == 0:
                        cdv.gate(v)
                    d_ss.inc(v.tensor_tensor(ssb[(h + 1) % 2][:, n * BS0:(n + 1) * BS0],
                                             ssb[h % 2][:, n * BS0:(n + 1) * BS0],
                                             ps_ss[:, :], ALU.add))
                    p_S.wait(v, m + 1)
                    p_y.wait(v, m + 1)
                    bd = float(np.exp(-SLOPES[h] * BS0))
                    cdv.seal(v.scalar_tensor_tensor(S_f[(n + 1) % 2][:, :], S_f[n % 2][:, :],
                                                    bd, ps_S[:, :], ALU.mult, ALU.add))
                    cdv.gate(v)
                    d_S.inc(v.tensor_copy(S_b[:, :], S_f[(n + 1) % 2][:, :]))
            n_pe.wait(v, 16)
            n_dv.inc(v.tensor_copy(nw_cols[:, :], ps_nw[:, :]))
            n_dv.wait(v, 1)
            for tt in range(4):
                n_pe.wait(v, 16 + tt + 1)
                for fc in range(16):
                    k = tt * 16 + fc
                    if k >= 4:
                        n_st.wait(v, k, k // 4)
                    n_dv.inc(v.scalar_tensor_tensor(anout[k % 4][:, :],
                                                    attn[fc][:, tt * 512:(tt + 1) * 512],
                                                    nw_cols[:, fc:fc + 1],
                                                    ps_sc[tt % 2][:, :],
                                                    ALU.mult, ALU.mult))

        @blk.scalar
        def _(sc):
            g_init.wait(sc, 4)
            g_eng.wait(sc, 8)
            d_pre.wait(sc, 1)
            for h in range(H0):
                slope = SLOPES[h]
                if h >= 1:
                    d_set.wait(sc, 2 * h)
                if h >= 2:
                    d_qd.wait(sc, NB0 * (h - 1))
                    d_tp.wait(sc, 4 * NB0 * (h - 1))
                for c in range(2):
                    a_set.inc(sc.activation(eexp[c][:, :], Dc[c][:, :], AF.Exp, scale=-slope))
                a_set.inc(sc.activation(qd[h % 2][:, :], ibc[:, :], AF.Exp, scale=-slope))
                for c in range(2):
                    a_set.inc(sc.activation(kdc[h % 2][:, c:c + 1], iocol[c][:, :], AF.Exp, scale=slope))
                for n in range(NB0):
                    m = h * NB0 + n
                    p_y.wait(sc, m + 1)
                    fc, off = h // 2, (h % 2) * 64
                    a_y.inc(sc.copy(attn[fc][off:off + 64, n * BS0:(n + 1) * BS0], ps_y[:, :]))
                    p_ss.wait(sc, m)
                    a_sq.inc(sc.activation(sq_b[:, :], ps_y[:, :], AF.Square))
            d_ss.wait(sc, NM)
            cac.seal(sc.activation(lnr[:, :], ssb[0][:, :], AF.Ln, bias=ebias1[:, :]))
            cac.gate(sc)
            a_r.inc(sc.activation(rstd_r[:, :], lnr[:, :], AF.Exp, scale=-0.5,
                                  bias=ebias2[:, :]))



def _emit_p4(nc, T, ph):
    gate_w, out_w = T["gate_w"], T["out_w"]
    xn0_tm, attnn_fm, s0_fm = T["xn0_tm"], T["attnn_fm"], T["s0_fm"]
    gate_bfw = nc.dram_tensor("gate_bfw", [DM, DM], BF16, kind="Internal").ap()
    outw_bfw = nc.dram_tensor("outw_bfw", [DM, DM], BF16, kind="Internal").ap()

    if True:
        e = ph.st.enter_context
        xn = [e(_sbt(nc, ph, f"xn{i}", [128, S], BF16)) for i in range(16)]
        asl = [e(_sbt(nc, ph, f"asl{i}", [128, 512], BF16)) for i in range(16)]
        zsl = [e(_sbt(nc, ph, f"zsl{i}", [128, 512], BF16)) for i in range(16)]
        wA = [e(_sbt(nc, ph, f"wA{i}", [128, 16 * 512], BF16, is_transpose_dest=True)) for i in range(2)]
        wB = [e(_sbt(nc, ph, f"wB{i}", [128, 16 * 512], BF16, is_transpose_dest=True)) for i in range(2)]
        gsig = [e(_sbt(nc, ph, f"gsig{i}", [128, 512], F32)) for i in range(4)]
        sout = [e(_sbt(nc, ph, f"sout{i}", [128, 512], BF16)) for i in range(4)]
        psA = [e(_pst(nc, ph, f"psA{i}", [128, 512], F32)) for i in range(4)]
        psB = [e(_pst(nc, ph, f"psB{i}", [128, 512], F32)) for i in range(4)]

        gc = ph.pipe("gc", 16)
        ocst = ph.pipe("ocst", 16)
        sxn = ph.pipe("sxn", 16)
        ald = ph.pipe("ald", 16)
        swA = ph.spipe("swA", 2)
        swB = ph.spipe("swB", 2)
        a_mm = ph.pipe("a_mm")
        a_ev = ph.pipe("a_ev")
        d_z = ph.pipe("d_z")
        b_mm = ph.pipe("b_mm")
        b_ev = ph.pipe("b_ev")
        stp = ph.spipe("stp", 4)
        blk = ph.wrap(e(nc.Block()))

        @blk.gpsimd
        def _(g):
            for c in range(4):
                gc.inc(g.dma_start(gate_bfw[c * 512:(c + 1) * 512, :],
                                   gate_w[c * 512:(c + 1) * 512, :]))
            for c in range(4):
                ocst.inc(g.dma_start(outw_bfw[c * 512:(c + 1) * 512, :],
                                     out_w[c * 512:(c + 1) * 512, :]))
            for tt in range(4):
                d_z.wait(g, 16 * tt)
                for ic in range(16):
                    ald.inc(g.dma_start(asl[ic][:, :],
                                        attnn_fm[ic * 128:(ic + 1) * 128,
                                                 tt * 512:(tt + 1) * 512]))
                for gb in range(16):
                    k = tt * 16 + gb
                    b_ev.wait(g, k + 1)
                    ob, oc4 = gb // 4, gb % 4
                    stp.inc(g.dma_start(
                        s0_fm[ob * 512 + oc4 * 128: ob * 512 + (oc4 + 1) * 128,
                              tt * 512:(tt + 1) * 512],
                        sout[k % 4][:, :]), k)
            stp.wait_all(g, 16)

        @blk.sync
        def _(s):
            for ic in range(16):
                sxn.inc(s.dma_start_transpose(xn[ic][:, :], xn0_tm[:, ic * 128:(ic + 1) * 128]))
            gc.wait(s, 4)
            ocst.wait(s, 4)
            for tt in range(4):
                for ob in range(4):
                    ga = tt * 4 + ob
                    if ga >= 2:
                        a_mm.wait(s, 4 * (ga - 1))
                    for ic in range(16):
                        swA.inc(s.dma_start_transpose(
                            wA[ga % 2][:, ic * 512:(ic + 1) * 512],
                            gate_bfw[ob * 512:(ob + 1) * 512, ic * 128:(ic + 1) * 128]), ga)
                for ob in range(4):
                    gb = tt * 4 + ob
                    if gb >= 2:
                        b_mm.wait(s, 4 * (gb - 1))
                    for ic in range(16):
                        swB.inc(s.dma_start_transpose(
                            wB[gb % 2][:, ic * 512:(ic + 1) * 512],
                            outw_bfw[ob * 512:(ob + 1) * 512, ic * 128:(ic + 1) * 128]), gb)

        @blk.tensor
        def _(t):
            sxn.wait(t, 16)
            for tt in range(4):
                for ob in range(4):
                    ga = tt * 4 + ob
                    swA.wait(t, ga, 16 * (ga // 2 + 1))
                    for oc4 in range(4):
                        gidx = ga * 4 + oc4
                        if gidx >= 4:
                            a_ev.wait(t, gidx - 3)
                        for ic in range(16):
                            a_mm.inc(t.matmul(psA[oc4][:, :],
                                              wA[ga % 2][:, ic * 512 + oc4 * 128: ic * 512 + (oc4 + 1) * 128],
                                              xn[ic][:, tt * 512:(tt + 1) * 512],
                                              start=(ic == 0), stop=(ic == 15)),
                                     n=(1 if ic == 15 else 0))
                d_z.wait(t, 16 * (tt + 1))
                for ob in range(4):
                    gb = tt * 4 + ob
                    swB.wait(t, gb, 16 * (gb // 2 + 1))
                    for oc4 in range(4):
                        gidx = gb * 4 + oc4
                        if gidx >= 4:
                            b_ev.wait(t, gidx - 3)
                        for ic in range(16):
                            b_mm.inc(t.matmul(psB[oc4][:, :],
                                              wB[gb % 2][:, ic * 512 + oc4 * 128: ic * 512 + (oc4 + 1) * 128],
                                              zsl[ic][:, :],
                                              start=(ic == 0), stop=(ic == 15)),
                                     n=(1 if ic == 15 else 0))

        @blk.scalar
        def _(sc):
            for tt in range(4):
                for gidx in range(tt * 16, (tt + 1) * 16):
                    a_mm.wait(sc, gidx + 1)
                    if gidx >= 4:
                        d_z.wait(sc, gidx - 3)
                    a_ev.inc(sc.activation(gsig[gidx % 4][:, :], psA[gidx % 4 % 4][:, :], AF.Sigmoid))
                for gidx in range(tt * 16, (tt + 1) * 16):
                    b_mm.wait(sc, gidx + 1)
                    if gidx >= 4:
                        stp.wait(sc, gidx, gidx // 4)
                    b_ev.inc(sc.copy(sout[gidx % 4][:, :], psB[gidx % 4][:, :]))

        @blk.vector
        def _(v):
            for tt in range(4):
                ald.wait(v, 16 * (tt + 1))
                if tt >= 1:
                    b_mm.wait(v, 16 * tt)
                for gidx in range(tt * 16, (tt + 1) * 16):
                    a_ev.wait(v, gidx + 1)
                    ocg = (gidx % 16) // 4 * 4 + gidx % 4
                    ocg = gidx % 16
                    d_z.inc(v.tensor_tensor(zsl[ocg][:, :], gsig[gidx % 4][:, :],
                                            asl[ocg][:, :], ALU.mult))



def _p6_sched():
    sched = []
    for hp in range(H1 // 2):
        for it in range(4):
            jbs = list(range(4 * (it + 1)))[::-1]
            for j, jb in enumerate(jbs):
                for par in range(2):
                    sched.append(dict(h=2 * hp + par, it=it, jb=jb, par=par,
                                      first=(j == 0), last=(jb == 0)))
    pe = act = dve = pool = 0
    oev_prev = [0, 0]
    lcar = [0, 0]
    for s in sched:
        par = s["par"]
        s["pe0"] = pe
        s["z_end"] = pe + 1
        nsm = 2 if s["first"] else 3
        s["s_end"] = s["z_end"] + nsm
        s["cs_end"] = s["s_end"] + 1
        s["o_end"] = s["cs_end"] + 1
        pe = s["o_end"]
        s["act0"] = act
        s["u_end"] = act + 1
        s["sp_end"] = s["u_end"] + 1
        s["at_end"] = s["sp_end"] + 1
        act = s["at_end"]
        s["oev_prev"] = oev_prev[par]
        if s["last"]:
            act += 1
            oev_prev[par] = act
        s["act_end"] = act
        s["dve0"] = dve
        s["za_end"] = dve + 1
        s["zb_end"] = s["za_end"] + 1
        dve = s["zb_end"]
        if not s["last"]:
            dve += 1
        s["cv_end"] = dve
        s["pool0"] = pool
        if not s["last"]:
            pool += 2
        s["cr_end"] = pool
        if s["first"]:
            lcar[par] = 0
        s["l"] = lcar[par]
        if not s["last"]:
            lcar[par] += 1
    return sched


def _emit_p6(nc, T, ph):
    wq, wk, wv, wo = T["wq"], T["wk"], T["wv"], T["wo"]
    xn1_fm, s1_fm, C = T["xn1_fm"], T["s1_fm"], T
    bfw = [nc.dram_tensor(n, [DM, DM], BF16, kind="Internal").ap()
           for n in ("wq_bf", "wk_bf", "wv_bf", "wo_bf")]
    qkvf = [nc.dram_tensor(n, [DM, S], BF16, kind="Internal").ap()
            for n in ("q_fm", "k_fm", "v_fm")]
    sched = _p6_sched()

    if True:
        e = ph.st.enter_context
        xsl = [[e(_sbt(nc, ph, f"xsl{j}_{i}", [128, 512], BF16)) for i in range(16)]
               for j in range(2)]
        ofm = [e(_sbt(nc, ph, f"ofm{i}", [128, S], BF16)) for i in range(16)]
        wbuf = [e(_sbt(nc, ph, f"wbuf{i}", [128, 16 * 512], BF16)) for i in range(2)]
        evb = [e(_sbt(nc, ph, f"evb{i}", [128, 512], BF16)) for i in range(4)]
        qh = [e(_sbt(nc, ph, f"qh{i}", [64, S], BF16)) for i in range(4)]
        kh = [e(_sbt(nc, ph, f"kh{i}", [64, S], BF16)) for i in range(4)]
        vtb = [e(_sbt(nc, ph, f"vtb{i}", [128, 16 * 64], BF16, is_transpose_dest=True))
               for i in range(4)]
        za = [e(_sbt(nc, ph, f"za{i}", [128, 512], F32)) for i in range(2)]
        zbf = [e(_sbt(nc, ph, f"zbf{i}", [128, 512], BF16)) for i in range(2)]
        ub = [e(_sbt(nc, ph, f"ub{i}", [128, 512], F32)) for i in range(2)]
        spb = [e(_sbt(nc, ph, f"spb{i}", [128, 512], BF16)) for i in range(2)]
        atb = [e(_sbt(nc, ph, f"atb{i}", [128, 512], BF16)) for i in range(2)]
        cbr = [e(_sbt(nc, ph, f"cbr{i}", [1, 512], BF16)) for i in range(2)]
        colev = [e(_sbt(nc, ph, f"colev{i}", [1, 512], F32)) for i in range(2)]
        carf = [[e(_sbt(nc, ph, f"carf{i}_{k}", [1, 512], F32)) for k in range(2)]
                for i in range(2)]
        mskm = [e(_sbt(nc, ph, f"mskm{i}", [128, 512], BF16)) for i in range(4)]
        ntri = e(_sbt(nc, ph, "ntri", [128, 128], BF16))
        orow = e(_sbt(nc, ph, "orow", [1, 128], BF16))
        ncol = e(_sbt(nc, ph, "ncol", [128, 1], BF16))
        idb6 = e(_sbt(nc, ph, "idb6", [128, 128], BF16))
        pp = [e(_pst(nc, ph, f"pp{i}", [128, 512], F32)) for i in range(8)]

        cst = ph.pipe("cst", 16)
        gld = ph.pipe("gld", 16)
        g_eng = ph.pipe("g_eng")
        xlp = ph.spipe("xlp", 2)
        swp = ph.spipe("swp", 2)
        p_q = ph.pipe("p_q")
        a_qv = ph.pipe("a_qv")
        st_qv = ph.spipe("st_qv", 4)
        qldp = ph.spipe("qldp", 4)
        p_at = ph.pipe("p_at")
        a_at = ph.pipe("a_at")
        d_at = ph.pipe("d_at")
        g_cr = ph.pipe("g_cr")
        d_msk = ph.pipe("d_msk")
        blk = ph.wrap(e(nc.Block()))

        NG3 = 3 * 4 * 16
        last_of_head = {}
        for s_ in sched:
            last_of_head[s_["h"]] = s_

        @blk.gpsimd
        def _(g):
            for wi in range(4):
                cst.inc(g.dma_start(bfw[wi], [wq, wk, wv, wo][wi]))
            gld.inc(g.dma_start(ntri[:, :], C["ntri_bf"]))
            gld.inc(g.dma_start(orow[:, :], C["ones_row_bf"]))
            gld.inc(g.dma_start(ncol[:, :], C["negcol_bf"]))
            gld.inc(g.dma_start(idb6[:, :], C["ident_bf"]))
            g_eng.inc(g.iota(za[0][:, :], [[1, 512]], base=0, channel_multiplier=-1,
                             allow_small_or_imprecise_dtypes=True))
            for tt in range(4):
                if tt >= 2:
                    p_q.wait(g, 48 * (tt - 1))
                for ic in range(16):
                    xlp.inc(g.dma_start(
                        xsl[tt % 2][ic][:, :],
                        xn1_fm[ic * 128:(ic + 1) * 128, tt * 512:(tt + 1) * 512]), tt)
                for gi in range(tt * 48, (tt + 1) * 48):
                    rem = gi % 48
                    wi, ob, oc4 = rem // 16, (rem % 16) // 4, rem % 4
                    a_qv.wait(g, gi + 1)
                    st_qv.inc(g.dma_start(
                        qkvf[wi][ob * 512 + oc4 * 128: ob * 512 + (oc4 + 1) * 128,
                                 tt * 512:(tt + 1) * 512],
                        evb[gi % 4][:, :]), gi)
            for i, s_ in enumerate(sched):
                if s_["last"]:
                    continue
                par, l = s_["par"], s_["l"]
                d_at.wait(g, s_["cv_end"])
                g_cr.wait(g, s_["pool0"])
                if l == 0:
                    g_cr.inc(g.tensor_copy(carf[par][0][:, :], colev[par][:, :]))
                else:
                    g_cr.inc(g.tensor_tensor(carf[par][l % 2][:, :],
                                             carf[par][(l + 1) % 2][:, :],
                                             colev[par][:, :], ALU.add))
                g_cr.wait(g, s_["pool0"] + 1)
                g_cr.inc(g.tensor_copy(cbr[par][:, :], carf[par][l % 2][:, :]))
            for gi in range(64):
                ob, tt, oc4 = gi // 16, (gi % 16) // 4, gi % 4
                a_qv.wait(g, NG3 + gi + 1)
                st_qv.inc(g.dma_start(
                    s1_fm[ob * 512 + oc4 * 128: ob * 512 + (oc4 + 1) * 128,
                          tt * 512:(tt + 1) * 512],
                    evb[(NG3 + gi) % 4][:, :]), NG3 + gi)
            st_qv.wait_all(g, (NG3 + 64) // 4)

        @blk.sync
        def _(s):
            cst.wait(s, 4)
            for w in range(48):
                tt, gw = w // 12, w % 12
                wi, ob = gw // 4, gw % 4
                if w >= 2:
                    p_q.wait(s, 4 * (w - 1))
                for ic in range(16):
                    swp.inc(s.dma_start(
                        wbuf[w % 2][:, ic * 512:(ic + 1) * 512],
                        bfw[wi][ic * 128:(ic + 1) * 128, ob * 512:(ob + 1) * 512]), w)
            st_qv.wait_all(s, NG3 // 4)
            for h in range(H1):
                if h >= 4:
                    p_at.wait(s, last_of_head[h - 4]["o_end"])
                base = h * 64
                qldp.inc(s.dma_start(qh[h % 4][:, :], qkvf[0][base:base + 64, :]), h)
                qldp.inc(s.dma_start(kh[h % 4][:, :], qkvf[1][base:base + 64, :]), h)
                for jc in range(16):
                    qldp.inc(s.dma_start_transpose(
                        vtb[h % 4][:, jc * 64:(jc + 1) * 64],
                        qkvf[2][base:base + 64, jc * 128:(jc + 1) * 128]), h)
            for ob in range(4):
                w = 48 + ob
                if ob < 2:
                    p_q.wait(s, 4 * (w - 1))
                else:
                    p_q.wait(s, NG3 + 16 * (ob - 1))
                for ic in range(16):
                    swp.inc(s.dma_start(
                        wbuf[w % 2][:, ic * 512:(ic + 1) * 512],
                        bfw[3][ic * 128:(ic + 1) * 128, ob * 512:(ob + 1) * 512]), w)

        @blk.tensor
        def _(t):
            for gi in range(NG3):
                tt, rem = gi // 48, gi % 48
                wi, ob, oc4 = rem // 16, (rem % 16) // 4, rem % 4
                w = tt * 12 + wi * 4 + ob
                swp.wait(t, w, 16 * (w // 2 + 1))
                xlp.wait(t, tt, 16 * (tt // 2 + 1))
                if gi >= 4:
                    a_qv.wait(t, gi - 3)
                for ic in range(16):
                    p_q.inc(t.matmul(pp[gi % 4][:, :],
                                     wbuf[w % 2][:, ic * 512 + oc4 * 128: ic * 512 + (oc4 + 1) * 128],
                                     xsl[tt % 2][ic][:, :],
                                     start=(ic == 0), stop=(ic == 15)),
                            n=(1 if ic == 15 else 0))
            gld.wait(t, 4)
            d_msk.wait(t, 4)
            a_qv.wait(t, NG3)
            for i, s_ in enumerate(sched):
                h, it, jb, par = s_["h"], s_["it"], s_["jb"], s_["par"]
                prev2 = sched[i - 2] if i >= 2 else None
                it0 = it * 512
                bz, bx, bo, bc = par * 4, par * 4 + 1, par * 4 + 2, par * 4 + 3
                qldp.wait(t, h, 18 * (h // 4 + 1))
                if prev2 is not None:
                    d_at.wait(t, prev2["za_end"])
                p_at.inc(t.matmul(pp[bz][:, :],
                                  kh[h % 4][:, jb * 128:(jb + 1) * 128],
                                  qh[h % 4][:, it0:it0 + 512],
                                  start=True, stop=True))
                a_at.wait(t, s_["sp_end"])
                d_at.wait(t, s_["zb_end"])
                if prev2 is not None:
                    a_at.wait(t, prev2["at_end"])
                    if not s_["first"]:
                        g_cr.wait(t, prev2["cr_end"])
                p_at.inc(t.matmul(pp[bx][:, :], ntri[:, :], spb[par][:, :],
                                  start=True, stop=False))
                p_at.inc(t.matmul(pp[bx][:, :], idb6[:, :], zbf[par][:, :],
                                  start=False, stop=s_["first"]))
                if not s_["first"]:
                    p_at.inc(t.matmul(pp[bx][:, :], orow[:, :], cbr[par][:, :],
                                      start=False, stop=True))
                if prev2 is not None:
                    d_at.wait(t, prev2["cv_end"])
                p_at.inc(t.matmul(pp[bc][0:1, :], ncol[:, :], spb[par][:, :],
                                  start=True, stop=True))
                a_at.wait(t, s_["at_end"])
                if s_["first"]:
                    a_at.wait(t, s_["oev_prev"])
                p_at.inc(t.matmul(pp[bo][0:64, :],
                                  vtb[h % 4][:, jb * 64:(jb + 1) * 64],
                                  atb[par][:, :],
                                  start=s_["first"], stop=s_["last"]))
            d_at.wait(t, sched[-1]["cv_end"])
            a_at.wait(t, sched[-1]["act_end"])
            for gi in range(64):
                ob, tt, oc4 = gi // 16, (gi % 16) // 4, gi % 4
                w = 48 + ob
                swp.wait(t, w, 16 * (w // 2 + 1))
                a_qv.wait(t, NG3 + gi - 3)
                for ic in range(16):
                    p_q.inc(t.matmul(pp[oc4][:, :],
                                     wbuf[w % 2][:, ic * 512 + oc4 * 128: ic * 512 + (oc4 + 1) * 128],
                                     ofm[ic][:, tt * 512:(tt + 1) * 512],
                                     start=(ic == 0), stop=(ic == 15)),
                            n=(1 if ic == 15 else 0))

        @blk.vector
        def _(v):
            g_eng.wait(v, 1)
            for r in range(4):
                d_msk.inc(v.tensor_scalar(mskm[r][:, :], za[0][:, :], 128.0 * r + 0.5,
                                          -10000.0, ALU.is_le, ALU.mult))
            d_msk.wait(v, 4)
            for i, s_ in enumerate(sched):
                par, it, jb = s_["par"], s_["it"], s_["jb"]
                prev2 = sched[i - 2] if i >= 2 else None
                rel = jb * 128 - it * 512
                bz = par * 4
                p_at.wait(v, s_["z_end"])
                if prev2 is not None:
                    a_at.wait(v, prev2["u_end"])
                if rel >= 0:
                    d_at.inc(v.scalar_tensor_tensor(za[par][:, :], pp[bz][:, :], 80.0,
                                                    mskm[rel // 128][:, :],
                                                    ALU.min, ALU.add))
                else:
                    d_at.inc(v.tensor_single_scalar(za[par][:, :], pp[bz][:, :],
                                                    80.0, ALU.min))
                d_at.wait(v, s_["za_end"])
                if prev2 is not None:
                    p_at.wait(v, prev2["s_end"])
                d_at.inc(v.tensor_copy(zbf[par][:, :], za[par][:, :]))
                if not s_["last"]:
                    p_at.wait(v, s_["cs_end"])
                    g_cr.wait(v, s_["pool0"])
                    d_at.inc(v.tensor_copy(colev[par][:, :], pp[par * 4 + 3][0:1, :]))

        @blk.scalar
        def _(sc):
            for gi in range(NG3):
                wi = (gi % 48) // 16
                p_q.wait(sc, gi + 1)
                if gi >= 4:
                    st_qv.wait(sc, gi, gi // 4)
                if wi == 0:
                    a_qv.inc(sc.activation(evb[gi % 4][:, :], pp[gi % 4][:, :], AF.Copy,
                                           scale=SC1))
                else:
                    a_qv.inc(sc.copy(evb[gi % 4][:, :], pp[gi % 4][:, :]))
            for i, s_ in enumerate(sched):
                h, it, par = s_["h"], s_["it"], s_["par"]
                prev2 = sched[i - 2] if i >= 2 else None
                bx, bo = par * 4 + 1, par * 4 + 2
                d_at.wait(sc, s_["za_end"])
                if prev2 is not None:
                    a_at.wait(sc, prev2["sp_end"])
                a_at.inc(sc.activation(ub[par][:, :], za[par][:, :], AF.Exp))
                a_at.wait(sc, s_["u_end"])
                if prev2 is not None:
                    p_at.wait(sc, prev2["cs_end"])
                a_at.inc(sc.activation(spb[par][:, :], ub[par][:, :], AF.Ln, bias=1.0))
                p_at.wait(sc, s_["s_end"])
                a_at.inc(sc.activation(atb[par][:, :], pp[bx][:, :], AF.Exp))
                if s_["last"]:
                    p_at.wait(sc, s_["o_end"])
                    fc, off = h // 2, (h % 2) * 64
                    a_at.inc(sc.copy(
                        ofm[fc][off:off + 64, it * 512:(it + 1) * 512],
                        pp[bo][0:64, :]))
            for gi in range(64):
                p_q.wait(sc, NG3 + gi + 1)
                st_qv.wait(sc, NG3 + gi, (NG3 + gi) // 4)
                a_qv.inc(sc.copy(evb[(NG3 + gi) % 4][:, :], pp[gi % 4][:, :]))


def _emit_p5a(nc, T, ph):
    sw1, sw2, w12, combT = T["sw1"], T["sw2"], T["w12"], T["combT"]
    xn0_tm, hsh_fm, hrt_fm, C = T["xn0_tm"], T["hsh_fm"], T["hrt_fm"], T
    sw1_bf = nc.dram_tensor("sw1_bf", [HS, DM], BF16, kind="Internal").ap()
    sw2_bf = nc.dram_tensor("sw2_bf", [HS, DM], BF16, kind="Internal").ap()
    w12_bf = nc.dram_tensor("w12_bf", [E0 * 2 * HR, DM], BF16, kind="Internal").ap()
    NQ = 32

    if True:
        e = ph.st.enter_context
        xn = [e(_sbt(nc, ph, f"xn{i}", [128, S], BF16)) for i in range(16)]
        wb1 = [e(_sbt(nc, ph, f"wb1{i}", [128, 16 * 512], BF16, is_transpose_dest=True)) for i in range(2)]
        wb2 = [e(_sbt(nc, ph, f"wb2{i}", [128, 16 * 512], BF16, is_transpose_dest=True)) for i in range(2)]
        sg = [e(_sbt(nc, ph, f"sg{i}", [128, 512], F32)) for i in range(4)]
        tm = [e(_sbt(nc, ph, f"tm{i}", [128, 512], F32)) for i in range(4)]
        hout = [e(_sbt(nc, ph, f"hout{i}", [128, 512], BF16)) for i in range(4)]
        cb = e(_sbt(nc, ph, "cb", [128, 32 * 512], BF16))
        ctb = [e(_sbt(nc, ph, f"ctb{i}", [1, S], BF16)) for i in range(2)]
        orow = e(_sbt(nc, ph, "orow", [1, 128], BF16))
        psG = [e(_pst(nc, ph, f"psG{i}", [128, 512], F32)) for i in range(4)]
        psU = [e(_pst(nc, ph, f"psU{i}", [128, 512], F32)) for i in range(4)]

        c1 = ph.pipe("c1", 16)
        c2 = ph.pipe("c2", 16)
        c12 = ph.pipe("c12", 16)
        gld = ph.pipe("gld", 16)
        ctp5 = ph.spipe("ctp5", 2)
        sxn = ph.pipe("sxn", 16)
        sw = ph.spipe("swl", 2)
        p_cb = ph.pipe("p_cb")
        d_cb = ph.pipe("d_cb")
        p_mm = ph.pipe("p_mm")
        a_ev = ph.pipe("a_ev")
        d_h = ph.pipe("d_h")
        stp = ph.spipe("stp", 4)
        cda = ph.chain("cda5")
        blk = ph.wrap(e(nc.Block()))

        def q_rows(q):
            if q < 16:
                return (sw1_bf, q * 512, sw2_bf, q * 512, hsh_fm, q * 512)
            r = q - 16
            ee, hb1 = r // 2, r % 2
            return (w12_bf, ee * 2048 + hb1 * 512, w12_bf, ee * 2048 + 1024 + hb1 * 512,
                    hrt_fm, ee * 1024 + hb1 * 512)

        @blk.gpsimd
        def _(g):
            for c in range(4):
                c1.inc(g.dma_start(sw1_bf[c * 2048:(c + 1) * 2048, :],
                                   sw1[c * 2048:(c + 1) * 2048, :]))
            for c in range(4):
                c2.inc(g.dma_start(sw2_bf[c * 2048:(c + 1) * 2048, :],
                                   sw2[c * 2048:(c + 1) * 2048, :]))
            for c in range(8):
                c12.inc(g.dma_start(w12_bf[c * 2048:(c + 1) * 2048, :],
                                    w12[c * 2048:(c + 1) * 2048, :]))
            gld.inc(g.dma_start(orow[:, :], C["ones_row_bf"]))
            for ee in range(E0):
                if ee >= 2:
                    p_cb.wait(g, 4 * (ee - 1))
                ctp5.inc(g.dma_start(ctb[ee % 2][:, :], combT[ee:ee + 1, :]), ee)
            for q in range(NQ):
                for tt in range(4):
                    for oc4 in range(4):
                        s_ = q * 16 + tt * 4 + oc4
                        d_h.wait(g, s_ + 1)
                        h0a, b0, h1a, b1, outh, ob = q_rows(q)
                        stp.inc(g.dma_start(
                            outh[ob + oc4 * 128: ob + (oc4 + 1) * 128,
                                 tt * 512:(tt + 1) * 512],
                            hout[s_ % 4][:, :]), s_)
            stp.wait_all(g, NQ * 16 // 4)

        @blk.sync
        def _(s):
            for ic in range(16):
                sxn.inc(s.dma_start_transpose(xn[ic][:, :], xn0_tm[:, ic * 128:(ic + 1) * 128]))
            c1.wait(s, 4)
            c2.wait(s, 4)
            for q in range(NQ):
                if q == 16:
                    c12.wait(s, 8)
                if q >= 2:
                    p_mm.wait(s, 32 * (q - 1))
                h0a, b0, h1a, b1, _, _ = q_rows(q)
                for ic in range(16):
                    sw.inc(s.dma_start_transpose(
                        wb1[q % 2][:, ic * 512:(ic + 1) * 512],
                        h0a[b0:b0 + 512, ic * 128:(ic + 1) * 128]), q)
                for ic in range(16):
                    sw.inc(s.dma_start_transpose(
                        wb2[q % 2][:, ic * 512:(ic + 1) * 512],
                        h1a[b1:b1 + 512, ic * 128:(ic + 1) * 128]), q)

        @blk.tensor
        def _(t):
            gld.wait(t, 1)
            for k in range(32):
                ee, tt = k // 4, k % 4
                ctp5.wait(t, ee, ee // 2 + 1)
                if k >= 2:
                    d_cb.wait(t, k - 1)
                p_cb.inc(t.matmul(psG[k % 2][:, :], orow[:, :],
                                  ctb[ee % 2][:, tt * 512:(tt + 1) * 512],
                                  start=True, stop=True))
            sxn.wait(t, 16)
            d_cb.wait(t, 32)
            for q in range(NQ):
                sw.wait(t, q, 32 * (q // 2 + 1))
                for tt in range(4):
                    for half in range(2):
                        wb = wb1 if half == 0 else wb2
                        ps = psG if half == 0 else psU
                        for oc4 in range(4):
                            u = q * 32 + tt * 8 + half * 4 + oc4
                            s_ = q * 16 + tt * 4 + oc4
                            if half == 0:
                                a_ev.wait(t, s_ - 3)
                            else:
                                d_h.wait(t, s_ - 3)
                            for ic in range(16):
                                p_mm.inc(t.matmul(
                                    ps[oc4][:, :],
                                    wb[q % 2][:, ic * 512 + oc4 * 128: ic * 512 + (oc4 + 1) * 128],
                                    xn[ic][:, tt * 512:(tt + 1) * 512],
                                    start=(ic == 0), stop=(ic == 15)),
                                    n=(1 if ic == 15 else 0))

        @blk.scalar
        def _(sc):
            for q in range(NQ):
                for tt in range(4):
                    for oc4 in range(4):
                        s_ = q * 16 + tt * 4 + oc4
                        p_mm.wait(sc, q * 32 + tt * 8 + oc4 + 1)
                        if s_ >= 4:
                            d_h.wait(sc, s_ - 3)
                        a_ev.inc(sc.activation(sg[s_ % 4][:, :], psG[oc4][:, :], AF.Silu))

        @blk.vector
        def _(v):
            for k in range(32):
                p_cb.wait(v, k + 1)
                d_cb.inc(v.tensor_copy(cb[:, k * 512:(k + 1) * 512], psG[k % 2][:, :]))
            d_cb.wait(v, 32)
            for q in range(NQ):
                for tt in range(4):
                    for oc4 in range(4):
                        s_ = q * 16 + tt * 4 + oc4
                        u_u = q * 32 + tt * 8 + 4 + oc4
                        p_mm.wait(v, u_u + 1)
                        a_ev.wait(v, s_ + 1)
                        if s_ >= 4:
                            stp.wait(v, s_, s_ // 4)
                        if q < 16:
                            d_h.inc(v.tensor_tensor(hout[s_ % 4][:, :], sg[s_ % 4][:, :],
                                                    psU[oc4][:, :], ALU.mult))
                        else:
                            ee = (q - 16) // 2
                            cda.seal(v.tensor_tensor(tm[s_ % 4][:, :], sg[s_ % 4][:, :],
                                                     psU[oc4][:, :], ALU.mult))
                            cda.gate(v)
                            d_h.inc(v.tensor_tensor(
                                hout[s_ % 4][:, :], tm[s_ % 4][:, :],
                                cb[:, (ee * 4 + tt) * 512:(ee * 4 + tt + 1) * 512],
                                ALU.mult))



def _emit_p5b(nc, T, ph):
    sw3, w3, s0_fm, x0_fm = T["sw3"], T["w3"], T["s0_fm"], T["x0_fm"]
    hsh_fm, hrt_fm, x1_fm, xn1_fm, C = (T["hsh_fm"], T["hrt_fm"], T["x1_fm"],
                                        T["xn1_fm"], T)
    sw3_bf = nc.dram_tensor("sw3_bf", [DM, HS], BF16, kind="Internal").ap()
    w3_bf = nc.dram_tensor("w3_bf", [E0 * DM, HR], BF16, kind="Internal").ap()

    if True:
        e = ph.st.enter_context
        hsl = [e(_sbt(nc, ph, f"hsl{i}", [128, 512], BF16)) for i in range(64)]
        rsl = [e(_sbt(nc, ph, f"rsl{i}", [128, 512], BF16)) for i in range(64)]
        s0t = [e(_sbt(nc, ph, f"s0t{i}", [128, 512], BF16)) for i in range(4)]
        x0t = [e(_sbt(nc, ph, f"x0t{i}", [128, 512], F32)) for i in range(4)]
        x1sl = [e(_sbt(nc, ph, f"x1sl{i}", [128, 512], F32)) for i in range(4)]
        x1r = [e(_sbt(nc, ph, f"x1r{i}", [128, 512], F32)) for i in range(4)]
        wt = [e(_sbt(nc, ph, f"wt{i}", [128, 512], BF16, is_transpose_dest=True)) for i in range(4)]
        gbuf = [e(_sbt(nc, ph, f"gbuf{i}", [128, 512], F32)) for i in range(2)]
        sgb = [e(_sbt(nc, ph, f"sgb{i}", [128, 512], F32)) for i in range(2)]
        sqb = [e(_sbt(nc, ph, f"sqb{i}", [128, 512], F32)) for i in range(2)]
        xnout = [e(_sbt(nc, ph, f"xnout{i}", [128, 512], BF16)) for i in range(4)]
        aptmp = [e(_sbt(nc, ph, f"aptmp{i}", [128, 512], F32)) for i in range(2)]
        onesf = e(_sbt(nc, ph, "onesf", [1, 128], F32))
        onesc = e(_sbt(nc, ph, "onesc", [128, 1], F32))
        murow = e(_sbt(nc, ph, "murow", [1, 512], F32))
        varow = e(_sbt(nc, ph, "varow", [1, 512], F32))
        va2 = e(_sbt(nc, ph, "va2", [1, 512], F32))
        var3 = e(_sbt(nc, ph, "var3", [1, 512], F32))
        lnro = e(_sbt(nc, ph, "lnro", [1, 512], F32))
        rstdr = e(_sbt(nc, ph, "rstdr", [1, 512], F32))
        biasr = e(_sbt(nc, ph, "biasr", [1, 512], F32))
        epsr = e(_sbt(nc, ph, "epsr", [1, 1], F32))
        ps = [e(_pst(nc, ph, f"ps{i}", [128, 512], F32)) for i in range(4)]
        psS = e(_pst(nc, ph, "psS", [1, 512], F32))
        psQ = e(_pst(nc, ph, "psQ", [1, 512], F32))
        psB = [e(_pst(nc, ph, f"psB{i}", [128, 512], F32)) for i in range(2)]

        gcl = ph.pipe("gcl", 16)
        c3 = ph.pipe("c3", 16)
        cw3 = ph.pipe("cw3", 16)
        hld = ph.spipe("hld", 16)
        as0 = ph.spipe("as0", 4)
        ax0 = ph.spipe("ax0", 4)
        ax1 = ph.spipe("ax1", 4)
        swl = ph.spipe("swl", 4)
        p_mm = ph.pipe("p_mm")
        a_g = ph.pipe("a_g")
        d_x = ph.pipe("d_x")
        a_q = ph.pipe("a_q")
        p_st = ph.pipe("p_st")
        d_row = ph.pipe("d_row")
        a_row = ph.pipe("a_row")
        p_bc = ph.pipe("p_bc")
        d_ap = ph.pipe("d_ap")
        st1 = ph.spipe("st1", 4)
        stn = ph.spipe("stn", 4)
        cdv = ph.chain("cdvb")
        cac = ph.chain("cacb")
        blk = ph.wrap(e(nc.Block()))

        NIC = 64 + 64

        @blk.gpsimd
        def _(g):
            g.memset(epsr[:, :], EPS)
            gcl.inc(g.dma_start(onesf[:, :], C["onesf_row"]))
            gcl.inc(g.dma_start(onesc[:, :], C["onesf_col"]))
            for c in range(4):
                c3.inc(g.dma_start(sw3_bf[c * 512:(c + 1) * 512, :],
                                   sw3[c * 512:(c + 1) * 512, :]))
            for c in range(8):
                cw3.inc(g.dma_start(w3_bf[c * 2048:(c + 1) * 2048, :],
                                    w3[c * 2048:(c + 1) * 2048, :]))
            for tt in range(4):
                for j in range(128):
                    gate = 2048 * tt
                    if j >= 16:
                        gate += (j - 15) * 4
                    p_mm.wait(g, gate)
                    tgt = hsl[j] if j < 64 else rsl[j - 64]
                    srcd = hsh_fm if j < 64 else hrt_fm
                    ic = j if j < 64 else j - 64
                    hld.inc(g.dma_start(tgt[:, :],
                                        srcd[ic * 128:(ic + 1) * 128, tt * 512:(tt + 1) * 512]),
                            tt * 128 + j)
                for oi in range(20):
                    if oi < 16:
                        k = tt * 16 + oi
                        d_x.wait(g, k - 3)
                        as0.inc(g.dma_start(s0t[k % 4][:, :],
                                            s0_fm[oi * 128:(oi + 1) * 128, tt * 512:(tt + 1) * 512]), k)
                        ax0.inc(g.dma_start(x0t[k % 4][:, :],
                                            x0_fm[oi * 128:(oi + 1) * 128, tt * 512:(tt + 1) * 512]), k)
                    if oi >= 4:
                        ocg = oi - 4
                        k = tt * 16 + ocg
                        d_x.wait(g, k + 1)
                        st1.inc(g.dma_start(
                            x1_fm[ocg * 128:(ocg + 1) * 128, tt * 512:(tt + 1) * 512],
                            x1sl[k % 4][:, :]), k)
                for ocg in range(16):
                    k = tt * 16 + ocg
                    d_ap.wait(g, k - 3)
                    st1.wait(g, k, k // 4 + 1)
                    ax1.inc(g.dma_start(
                        x1r[k % 4][:, :],
                        x1_fm[ocg * 128:(ocg + 1) * 128, tt * 512:(tt + 1) * 512]), k)
                    d_ap.wait(g, k + 1)
                    stn.inc(g.dma_start(
                        xn1_fm[ocg * 128:(ocg + 1) * 128, tt * 512:(tt + 1) * 512],
                        xnout[k % 4][:, :]), k)
            st1.wait_all(g, 16)
            stn.wait_all(g, 16)

        @blk.sync
        def _(s):
            c3.wait(s, 4)
            cw3.wait(s, 8)
            wcount = 0
            for tt in range(4):
                for ob in range(4):
                    for icg in range(NIC):
                        if wcount >= 4:
                            p_mm.wait(s, (wcount - 3) * 4)
                        if icg < 64:
                            src = sw3_bf[ob * 512:(ob + 1) * 512, icg * 128:(icg + 1) * 128]
                        else:
                            icr = icg - 64
                            ee, rr = icr // 8, icr % 8
                            src = w3_bf[ee * 2048 + ob * 512: ee * 2048 + (ob + 1) * 512,
                                        rr * 128:(rr + 1) * 128]
                        swl.inc(s.dma_start_transpose(wt[wcount % 4][:, :], src), wcount)
                        wcount += 1

        @blk.tensor
        def _(t):
            wcount = 0
            for tt in range(4):
                for ob in range(4):
                    for oc4 in range(4):
                        s_ = (tt * 4 + ob) * 4 + oc4
                        if s_ >= 4:
                            a_g.wait(t, s_ - 3)
                    for icg in range(NIC):
                        swl.wait(t, wcount, wcount // 4 + 1)
                        jj = tt * 128 + icg
                        hld.wait(t, jj, jj // 16 + 1)
                        rhs = hsl[icg] if icg < 64 else rsl[icg - 64]
                        for oc4 in range(4):
                            p_mm.inc(t.matmul(
                                ps[oc4][:, :],
                                wt[wcount % 4][:, oc4 * 128:(oc4 + 1) * 128],
                                rhs[:, :],
                                start=(icg == 0), stop=(icg == NIC - 1)))
                        wcount += 1
                    for oc4 in range(4):
                        ocg = ob * 4 + oc4
                        k = tt * 16 + ocg
                        d_x.wait(t, k + 1)
                        if ocg == 0:
                            gcl.wait(t, 2)
                            d_row.wait(t, 2 * tt)
                            d_ap.wait(t, 16 * tt)
                        p_st.inc(t.matmul(psS[:, :], onesc[:, :], x1sl[k % 4][:, :],
                                          start=(ocg == 0), stop=(ocg == 15)))
                        a_q.wait(t, k + 1)
                        p_st.inc(t.matmul(psQ[:, :], onesc[:, :], sqb[ocg % 2][:, :],
                                          start=(ocg == 0), stop=(ocg == 15)))
                a_row.wait(t, tt + 1)
                d_row.wait(t, 2 * (tt + 1))
                p_bc.inc(t.matmul(psB[0][:, :], onesf[:, :], rstdr[:, :], start=True, stop=True))
                p_bc.inc(t.matmul(psB[1][:, :], onesf[:, :], biasr[:, :], start=True, stop=True))

        @blk.scalar
        def _(sc):
            for tt in range(4):
                for si in range(17):
                    if si >= 1:
                        ocg = si - 1
                        k = tt * 16 + ocg
                        d_x.wait(sc, k + 1)
                        if k >= 2:
                            p_st.wait(sc, 2 * (k - 1))
                        a_q.inc(sc.activation(sqb[ocg % 2][:, :], x1sl[k % 4][:, :], AF.Square))
                    if si < 16:
                        ob, oc4 = si // 4, si % 4
                        s_ = tt * 16 + si
                        p_mm.wait(sc, ((tt * 4 + ob) * NIC) * 4 + (NIC - 1) * 4 + oc4 + 1)
                        if s_ >= 2:
                            d_x.wait(sc, s_ - 1)
                        a_g.inc(sc.activation(gbuf[s_ % 2][:, :], ps[oc4][:, :], AF.Silu))
                d_row.wait(sc, 2 * tt + 1)
                cac.seal(sc.activation(lnro[:, :], var3[:, :], AF.Ln, bias=epsr[:, :]))
                cac.gate(sc)
                a_row.inc(sc.activation(rstdr[:, :], lnro[:, :], AF.Exp, scale=-0.5))

        @blk.vector
        def _(v):
            for tt in range(4):
                for ob in range(4):
                    for oc4 in range(4):
                        s_ = (tt * 4 + ob) * 4 + oc4
                        ocg = ob * 4 + oc4
                        k = tt * 16 + ocg
                        as0.wait(v, k, k // 4 + 1)
                        ax0.wait(v, k, k // 4 + 1)
                        a_g.wait(v, s_ + 1)
                        if k >= 4:
                            st1.wait(v, k, k // 4)
                            a_q.wait(v, k - 3)
                            p_st.wait(v, 2 * (k - 3))
                        cdv.seal(v.tensor_tensor(sgb[oc4 % 2][:, :], gbuf[s_ % 2][:, :],
                                                 s0t[k % 4][:, :], ALU.mult))
                        cdv.gate(v)
                        d_x.inc(v.tensor_tensor(x1sl[k % 4][:, :], sgb[oc4 % 2][:, :],
                                                x0t[k % 4][:, :], ALU.add))
                p_st.wait(v, 2 * 16 * (tt + 1))
                cdv.seal(v.tensor_scalar_mul(murow[:, :], psS[:, :], 1.0 / DM))
                cdv.seal(v.tensor_scalar_mul(varow[:, :], psQ[:, :], 1.0 / DM))
                cdv.gate(v)
                cdv.seal(v.scalar_tensor_tensor(va2[:, :], murow[:, :], -1.0, murow[:, :],
                                                ALU.mult, ALU.mult))
                cdv.gate(v)
                d_row.inc(v.tensor_tensor(var3[:, :], varow[:, :], va2[:, :], ALU.add))
                a_row.wait(v, tt + 1)
                d_row.inc(v.scalar_tensor_tensor(biasr[:, :], murow[:, :], -1.0, rstdr[:, :],
                                                 ALU.mult, ALU.mult))
                p_bc.wait(v, 2 * (tt + 1))
                for ocg in range(16):
                    k = tt * 16 + ocg
                    ax1.wait(v, k, k // 4 + 1)
                    if k >= 4:
                        stn.wait(v, k, k // 4)
                    cdv.seal(v.tensor_tensor(aptmp[k % 2][:, :], x1r[k % 4][:, :],
                                             psB[0][:, :], ALU.mult))
                    cdv.gate(v)
                    d_ap.inc(v.tensor_tensor(xnout[k % 4][:, :], aptmp[k % 2][:, :],
                                             psB[1][:, :], ALU.add))




def _emit_p7(nc, T, ph):
    wbase, bbase, wrout, aw, bw = T["wbase"], T["bbase"], T["wrout"], T["aw"], T["bw"]
    xn1_fm, x1_fm, s1_fm, out_tm, C = (T["xn1_fm"], T["x1_fm"], T["s1_fm"],
                                       T["out_tm"], T)
    wbase_bf = nc.dram_tensor("wbase_bf", [DM, DM], BF16, kind="Internal").ap()
    a_bfd = nc.dram_tensor("a_bfd", [E1 * R1, DM], BF16, kind="Internal").ap()
    b_bfd = nc.dram_tensor("b_bfd", [E1 * DM, R1], BF16, kind="Internal").ap()
    rout_bfd = nc.dram_tensor("rout_bfd", [E1, DM], BF16, kind="Internal").ap()
    bt_dram = nc.dram_tensor("bt_dram", [E1 * R1, DM], BF16, kind="Internal").ap()
    NTC = 16

    def gsplit(gi):
        ob, tt, oc4 = gi // 16, (gi // 4) % 4, gi % 4
        ocg = ob * 4 + oc4
        return ob, tt, oc4, ocg, tt * 16 + ocg

    if True:
        e = ph.st.enter_context
        xn = [e(_sbt(nc, ph, f"xn{i}", [128, S], BF16)) for i in range(16)]
        wbt = [e(_sbt(nc, ph, f"wbt{i}", [128, 16 * 512], BF16, is_transpose_dest=True)) for i in range(2)]
        idb = e(_sbt(nc, ph, "idb", [128, 128], BF16))
        idf = e(_sbt(nc, ph, "idf", [128, 128], F32))
        rrep_sb = e(_sbt(nc, ph, "rrep_sb", [E1, 64], BF16))
        a_sb = e(_sbt(nc, ph, "a_sb", [E1 * R1, S], BF16))
        A_T = e(_sbt(nc, ph, "A_T", [128, 16 * 64], BF16))
        btmp = e(_sbt(nc, ph, "btmp", [128, 2 * E1], BF16))
        bstg = [e(_sbt(nc, ph, f"bstg{i}", [E1, 128], BF16)) for i in range(2)]
        B_T = e(_sbt(nc, ph, "B_T", [E1 * R1, DM], BF16))
        rsb = e(_sbt(nc, ph, "rsb", [E1, DM], BF16))
        routerT = e(_sbt(nc, ph, "routerT", [128, 128], BF16))
        lg_tm = [e(_sbt(nc, ph, f"lg_tm{i}", [128, E1], F32)) for i in range(2)]
        m1 = e(_sbt(nc, ph, "m1c", [128, 1], F32))
        m2 = e(_sbt(nc, ph, "m2c", [128, 1], F32))
        is1 = e(_sbt(nc, ph, "is1c", [128, E1], F32))
        sel2 = e(_sbt(nc, ph, "sel2c", [128, E1], F32))
        ind = e(_sbt(nc, ph, "indc", [128, E1], F32))
        dcol = e(_sbt(nc, ph, "dcol", [128, 1], F32))
        sgc = e(_sbt(nc, ph, "sgc", [128, 1], F32))
        difc = e(_sbt(nc, ph, "difc", [128, 1], F32))
        wbc = e(_sbt(nc, ph, "wbc", [128, 1], F32))
        t1c = e(_sbt(nc, ph, "t1c", [128, E1], F32))
        w_tm = [e(_sbt(nc, ph, f"w_tm{i}", [128, E1], BF16)) for i in range(2)]
        wT_sb = e(_sbt(nc, ph, "wT_sb", [E1, S], BF16))
        wb_sb = e(_sbt(nc, ph, "wb_sb", [64, S], BF16))
        h_sb = e(_sbt(nc, ph, "h_sb", [64, S], BF16))
        hs_sb = e(_sbt(nc, ph, "hs_sb", [64, S], BF16))
        bbrow = e(_sbt(nc, ph, "bbrow", [1, DM], F32))
        bb_cols = e(_sbt(nc, ph, "bb_cols", [128, 16], F32))
        gb = [e(_sbt(nc, ph, f"gb{i}", [128, 512], F32)) for i in range(2)]
        sgb = [e(_sbt(nc, ph, f"sgb{i}", [128, 512], F32)) for i in range(2)]
        x2b = [e(_sbt(nc, ph, f"x2b{i}", [128, 512], F32)) for i in range(2)]
        s1t = [e(_sbt(nc, ph, f"s1t{i}", [128, 512], BF16)) for i in range(4)]
        x1t = [e(_sbt(nc, ph, f"x1t{i}", [128, 512], F32)) for i in range(4)]
        otb = [e(_sbt(nc, ph, f"otb{i}", [128, 128], F32)) for i in range(4)]
        psc = [e(_pst(nc, ph, f"psc{i}", [128, 512], F32)) for i in range(4)]
        pst = [e(_pst(nc, ph, f"pst{i}", [128, 128], F32)) for i in range(2)]
        ps_tb = e(_pst(nc, ph, "ps_tb", [128, 128], BF16))

        cbs = ph.pipe("cbs", 16)
        cbo = ph.pipe("cbo", 16)
        gl7 = ph.pipe("gl7", 16)
        xl7 = ph.pipe("xl7", 16)
        swb = ph.spipe("swb", 2)
        sx1 = ph.spipe("sx1", 4)
        ss1 = ph.spipe("ss1", 4)
        sto = ph.spipe("sto", 4)
        p_su = ph.pipe("p_su")
        d_su = ph.pipe("d_su")
        a_su = ph.pipe("a_su")
        g_su = ph.pipe("g_su", 16)
        p_mm = ph.pipe("p_mm")
        a_g = ph.pipe("a_g")
        d_x = ph.pipe("d_x")
        p_tp = ph.pipe("p_tp")
        d_ev = ph.pipe("d_ev")
        cdv = ph.chain("cdv7")
        blk = ph.wrap(e(nc.Block()))

        ss = SSched({"pe": p_su, "dve": d_su, "act": a_su, "gp": g_su})

        def mk(fn):
            return fn

        for ic in range(16):
            ss.add("pe", (lambda ic: lambda t: t.transpose(
                ps_tb[:, 0:64], a_sb[:, ic * 128:(ic + 1) * 128], idb[0:64, 0:64]))(ic),
                reads=("a_sb",), writes=("ps_tb",), extra=[(gl7, 6)])
            ss.add("dve", (lambda ic: lambda v: v.tensor_copy(
                A_T[:, ic * 64:(ic + 1) * 64], ps_tb[:, 0:64]))(ic),
                reads=("ps_tb",), writes=("A_T",))
        for j in range(128):
            ee, oc = j // 16, j % 16
            ss.add("gp", (lambda j, ee, oc: lambda g: g.dma_start(
                btmp[:, (j % 2) * E1:(j % 2 + 1) * E1],
                b_bfd[ee * 2048 + oc * 128: ee * 2048 + (oc + 1) * 128, :]))(j, ee, oc),
                reads=(), writes=(f"btmp{j % 2}",), extra=[(cbo, 3)])
            ss.add("pe", (lambda j: lambda t: t.transpose(
                ps_tb[0:E1, 0:128], btmp[:, (j % 2) * E1:(j % 2 + 1) * E1], idb[:, :]))(j),
                reads=(f"btmp{j % 2}",), writes=("ps_tb",))
            ss.add("dve", (lambda j: lambda v: v.tensor_copy(
                bstg[j % 2][:, :], ps_tb[0:E1, 0:128]))(j),
                reads=("ps_tb",), writes=(f"bstg{j % 2}",))
            ss.add("gp", (lambda j, ee, oc: lambda g: g.dma_start(
                bt_dram[ee * R1:(ee + 1) * R1, oc * 128:(oc + 1) * 128],
                bstg[j % 2][:, :]))(j, ee, oc),
                reads=(f"bstg{j % 2}",), writes=("bt_dram",))
        ss.add("gp", lambda g: g.dma_start(B_T[:, :], bt_dram),
               reads=("bt_dram",), writes=("B_T",))
        for ic in range(16):
            ss.add("pe", (lambda ic: lambda t: t.transpose(
                ps_tb[:, 0:E1], rsb[:, ic * 128:(ic + 1) * 128], idb[0:E1, 0:E1]))(ic),
                reads=("rsb",), writes=("ps_tb",), extra=[(gl7, 6)])
            ss.add("dve", (lambda ic: lambda v: v.tensor_copy(
                routerT[:, ic * 8:(ic + 1) * 8], ps_tb[:, 0:E1]))(ic),
                reads=("ps_tb",), writes=("routerT",))
        for ic in range(16):
            ss.add("pe", (lambda ic: lambda t: t.transpose(
                pst[0][:, 0:1], bbrow[:, ic * 128:(ic + 1) * 128], idf[0:1, 0:1]))(ic),
                reads=("bbrow",), writes=("pst0",), extra=[(gl7, 6)])
            ss.add("dve", (lambda ic: lambda v: v.tensor_copy(
                bb_cols[:, ic:ic + 1], pst[0][:, 0:1]))(ic),
                reads=("pst0",), writes=("bb_cols",))

        def lg_group(tc):
            def fn(t):
                for ic in range(15):
                    t.matmul(pst[tc % 2][:, 0:E1], xn[ic][:, tc * 128:(tc + 1) * 128],
                             routerT[:, ic * 8:(ic + 1) * 8],
                             start=(ic == 0), stop=False)
                return t.matmul(pst[tc % 2][:, 0:E1], xn[15][:, tc * 128:(tc + 1) * 128],
                                routerT[:, 15 * 8:16 * 8], start=False, stop=True)
            return fn

        def chain_a(tc):
            def fn(v):
                cdv.seal(v.reduce_max(m1[:, :], lg_tm[tc % 2][:, :], axis=AX.X))
                cdv.gate(v)
                cdv.seal(v.tensor_single_scalar(is1[:, :], lg_tm[tc % 2][:, :], m1[:, 0:1], ALU.is_ge))
                cdv.gate(v)
                cdv.seal(v.scalar_tensor_tensor(sel2[:, :], is1[:, :], -1e9, lg_tm[tc % 2][:, :],
                                                ALU.mult, ALU.add))
                cdv.gate(v)
                cdv.seal(v.reduce_max(m2[:, :], sel2[:, :], axis=AX.X))
                cdv.gate(v)
                cdv.seal(v.tensor_single_scalar(ind[:, :], lg_tm[tc % 2][:, :], m2[:, 0:1], ALU.is_ge))
                cdv.gate(v)
                return v.tensor_tensor(dcol[:, :], m1[:, :], m2[:, :], ALU.subtract)
            return fn

        def chain_b(tc):
            def fn(v):
                cdv.seal(v.tensor_scalar(difc[:, :], sgc[:, :], 4.0, -2.0, ALU.mult, ALU.add))
                cdv.seal(v.tensor_scalar(wbc[:, :], sgc[:, :], -2.0, 2.0, ALU.mult, ALU.add))
                cdv.gate(v)
                cdv.seal(v.tensor_single_scalar(t1c[:, :], ind[:, :], wbc[:, 0:1], ALU.mult))
                cdv.gate(v)
                return v.scalar_tensor_tensor(w_tm[tc % 2][:, :], is1[:, :], difc[:, 0:1],
                                              t1c[:, :], ALU.mult, ALU.add)
            return fn

        for tc in range(NTC):
            ss.add("pe", lg_group(tc), reads=("routerT",), writes=(f"pst{tc % 2}",),
                   extra=[(xl7, 16)])
            ss.add("act", (lambda tc: lambda sc: sc.copy(
                lg_tm[tc % 2][:, :], pst[tc % 2][:, 0:E1]))(tc),
                reads=(f"pst{tc % 2}",), writes=(f"lg{tc % 2}",))
            ss.add("dve", chain_a(tc), reads=(f"lg{tc % 2}",),
                   writes=("dcol", "m1", "m2", "is1", "ind", "sel2"))
            ss.add("act", (lambda tc: lambda sc: sc.activation(
                sgc[:, :], dcol[:, :], AF.Sigmoid))(tc),
                reads=("dcol",), writes=("sgc",))
            ss.add("dve", chain_b(tc), reads=("sgc", "is1", "ind"),
                   writes=(f"w_tm{tc % 2}", "difc", "wbc", "t1c"))
            ss.add("pe", (lambda tc: lambda t: t.transpose(
                ps_tb[0:E1, 0:128], w_tm[tc % 2][:, :], idb[:, :]))(tc),
                reads=(f"w_tm{tc % 2}",), writes=("ps_tb",))
            ss.add("dve", (lambda tc: lambda v: v.tensor_copy(
                wT_sb[:, tc * 128:(tc + 1) * 128], ps_tb[0:E1, 0:128]))(tc),
                reads=("ps_tb",), writes=("wT_sb",))
        for tt in range(4):
            ss.add("pe", (lambda tt: lambda t: t.matmul(
                psc[0][0:64, :], rrep_sb[:, :], wT_sb[:, tt * 512:(tt + 1) * 512],
                start=True, stop=True))(tt),
                reads=("wT_sb",), writes=("psc0",), extra=[(gl7, 6)])
            ss.add("dve", (lambda tt: lambda v: v.tensor_copy(
                wb_sb[:, tt * 512:(tt + 1) * 512], psc[0][0:64, :]))(tt),
                reads=("psc0",), writes=("wb_sb",))

        def h_group(tt):
            def fn(t):
                for ic in range(15):
                    t.matmul(psc[1][0:64, :], A_T[:, ic * 64:(ic + 1) * 64],
                             xn[ic][:, tt * 512:(tt + 1) * 512],
                             start=(ic == 0), stop=False)
                return t.matmul(psc[1][0:64, :], A_T[:, 15 * 64:16 * 64],
                                xn[15][:, tt * 512:(tt + 1) * 512], start=False, stop=True)
            return fn

        for tt in range(4):
            ss.add("pe", h_group(tt), reads=("A_T",), writes=("psc1",), extra=[(xl7, 16)])
            ss.add("act", (lambda tt: lambda sc: sc.copy(
                h_sb[:, tt * 512:(tt + 1) * 512], psc[1][0:64, :]))(tt),
                reads=("psc1",), writes=("h_sb",))
            ss.add("dve", (lambda tt: lambda v: v.tensor_tensor(
                hs_sb[:, tt * 512:(tt + 1) * 512], h_sb[:, tt * 512:(tt + 1) * 512],
                wb_sb[:, tt * 512:(tt + 1) * 512], ALU.mult))(tt),
                reads=("h_sb", "wb_sb"), writes=("hs_sb",))

        NPE, NDVE, NACT, NGP = (ss.counts["pe"], ss.counts["dve"],
                                ss.counts["act"], ss.counts["gp"])

        @blk.gpsimd
        def _(g):
            for c in range(4):
                cbs.inc(g.dma_start(wbase_bf[c * 512:(c + 1) * 512, :],
                                    wbase[c * 512:(c + 1) * 512, :]))
            cbo.inc(g.dma_start(a_bfd, aw))
            cbo.inc(g.dma_start(b_bfd, bw))
            cbo.inc(g.dma_start(rout_bfd, wrout))
            gl7.inc(g.dma_start(idb[:, :], C["ident_bf"]))
            gl7.inc(g.dma_start(idf[:, :], C["ident_f32"]))
            gl7.inc(g.dma_start(rrep_sb[:, :], C["rrep"]))
            gl7.inc(g.dma_start(bbrow[:, :], bbase.rearrange("(b a) -> b a", b=1)))
            cbo.wait(g, 3)
            gl7.inc(g.dma_start(a_sb[:, :], a_bfd))
            gl7.inc(g.dma_start(rsb[:, :], rout_bfd))
            for ic in range(16):
                xl7.inc(g.dma_start(xn[ic][:, :], xn1_fm[ic * 128:(ic + 1) * 128, :]))
            ss.emit("gp", g)
            for gg in range(66):
                if gg < 64:
                    gi = gg
                    ob, tt, oc4, ocg, k = gsplit(gi)
                    d_x.wait(g, gi - 3)
                    ss1.inc(g.dma_start(s1t[gi % 4][:, :],
                                        s1_fm[ocg * 128:(ocg + 1) * 128, tt * 512:(tt + 1) * 512]), gi)
                    sx1.inc(g.dma_start(x1t[gi % 4][:, :],
                                        x1_fm[ocg * 128:(ocg + 1) * 128, tt * 512:(tt + 1) * 512]), gi)
                if gg >= 2:
                    gi = gg - 2
                    ob, tt, oc4, ocg, k = gsplit(gi)
                    for q in range(4):
                        u = gi * 4 + q
                        d_ev.wait(g, u + 1)
                        sto.inc(g.dma_start(
                            out_tm[tt * 512 + q * 128: tt * 512 + (q + 1) * 128,
                                   ocg * 128:(ocg + 1) * 128],
                            otb[u % 4][:, :]), u)
            sto.wait_all(g, 64)

        @blk.sync
        def _(s):
            cbs.wait(s, 4)
            for ob in range(4):
                if ob >= 2:
                    p_mm.wait(s, 17 * 16 * (ob - 1))
                for ic in range(16):
                    swb.inc(s.dma_start_transpose(
                        wbt[ob % 2][:, ic * 512:(ic + 1) * 512],
                        wbase_bf[ob * 512:(ob + 1) * 512, ic * 128:(ic + 1) * 128]), ob)

        @blk.tensor
        def _(t):
            gl7.wait(t, 6)
            ss.emit("pe", t)
            d_su.wait(t, NDVE)
            a_su.wait(t, NACT)
            g_su.wait(t, NGP)
            for gi in range(64):
                ob, tt, oc4, ocg, k = gsplit(gi)
                if gi % 16 == 0:
                    swb.wait(t, ob, 16 * (ob // 2 + 1))
                if gi >= 2:
                    a_g.wait(t, gi - 1)
                for ic in range(16):
                    p_mm.inc(t.matmul(
                        psc[oc4][:, :],
                        wbt[ob % 2][:, ic * 512 + oc4 * 128: ic * 512 + (oc4 + 1) * 128],
                        xn[ic][:, tt * 512:(tt + 1) * 512],
                        start=(ic == 0), stop=False))
                p_mm.inc(t.matmul(psc[oc4][:, :],
                                  B_T[:, ocg * 128:(ocg + 1) * 128],
                                  hs_sb[:, tt * 512:(tt + 1) * 512],
                                  start=False, stop=True))
                if gi >= 2:
                    gp_ = gi - 2
                    d_x.wait(t, gp_ + 1)
                    for q in range(4):
                        u = gp_ * 4 + q
                        if u >= 2:
                            d_ev.wait(t, u - 1)
                        p_tp.inc(t.transpose(pst[u % 2][:, :],
                                             x2b[gp_ % 2][:, q * 128:(q + 1) * 128],
                                             idf[:, :]))
            for gp_ in (62, 63):
                d_x.wait(t, gp_ + 1)
                for q in range(4):
                    u = gp_ * 4 + q
                    d_ev.wait(t, u - 1)
                    p_tp.inc(t.transpose(pst[u % 2][:, :],
                                         x2b[gp_ % 2][:, q * 128:(q + 1) * 128],
                                         idf[:, :]))

        @blk.vector
        def _(v):
            ss.emit("dve", v)
            def tp_evacs(v, gp2):
                for q in range(4):
                    u = gp2 * 4 + q
                    p_tp.wait(v, u + 1)
                    if u >= 4:
                        sto.wait(v, u, u // 4)
                    d_ev.inc(v.tensor_copy(otb[u % 4][:, :], pst[u % 2][:, :]))

            for gi in range(64):
                ob, tt, oc4, ocg, k = gsplit(gi)
                if gi >= 2:
                    tp_evacs(v, gi - 2)
                a_g.wait(v, gi + 1)
                ss1.wait(v, gi, gi // 4 + 1)
                sx1.wait(v, gi, gi // 4 + 1)
                if gi >= 2:
                    p_tp.wait(v, 4 * (gi - 1))
                cdv.seal(v.tensor_tensor(sgb[gi % 2][:, :], gb[gi % 2][:, :],
                                         s1t[gi % 4][:, :], ALU.mult))
                cdv.gate(v)
                d_x.inc(v.tensor_tensor(x2b[gi % 2][:, :], sgb[gi % 2][:, :],
                                        x1t[gi % 4][:, :], ALU.add))
            for gp2 in (62, 63):
                tp_evacs(v, gp2)

        @blk.scalar
        def _(sc):
            ss.emit("act", sc)
            for gi in range(64):
                ob, tt, oc4, ocg, k = gsplit(gi)
                p_mm.wait(sc, gi * 17 + 17)
                if gi >= 2:
                    d_x.wait(sc, gi - 1)
                a_g.inc(sc.activation(gb[gi % 2][:, :], psc[oc4][:, :], AF.Silu,
                                      bias=bb_cols[:, ocg:ocg + 1]))


_C_P1 = ["ident_f32"]
_C_P2 = ["ident_bf", "onesf_row"]
_C_P3 = ["ident_f32", "ident_bf", "onescol64_bf", "onesf_row"]
_C_P5A = ["ones_row_bf"]
_C_P5B = ["onesf_row", "onesf_col"]
_C_P6 = ["ntri_bf", "ones_row_bf", "negcol_bf", "ident_bf"]
_C_P7 = ["ident_f32", "ident_bf", "onesf_row", "rrep"]

_IO = {
    "x_tm": ([S, DM], F32), "x0_fm": ([DM, S], F32), "xn0_tm": ([S, DM], BF16),
    "qkv_w": ([3 * H0 * DH, DM], F32), "router_w": ([E0, DM], F32),
    "ebias": ([E0], F32), "qkv_fm": ([3 * H0 * DH, S], BF16),
    "combT": ([E0, S], BF16), "norm_w": ([DM], F32), "attnn_fm": ([DM, S], BF16),
    "gate_w": ([DM, DM], F32), "out_w": ([DM, DM], F32), "s0_fm": ([DM, S], BF16),
    "sw1": ([HS, DM], F32), "sw2": ([HS, DM], F32), "w12": ([E0 * 2 * HR, DM], F32),
    "hsh_fm": ([HS, S], BF16), "hrt_fm": ([E0 * HR, S], BF16),
    "sw3": ([DM, HS], F32), "w3": ([E0 * DM, HR], F32),
    "x1_fm": ([DM, S], F32), "xn1_fm": ([DM, S], BF16),
    "wq": ([DM, DM], F32), "wk": ([DM, DM], F32), "wv": ([DM, DM], F32),
    "wo": ([DM, DM], F32), "s1_fm": ([DM, S], BF16),
    "wbase": ([DM, DM], F32), "bbase": ([DM], F32), "wrout": ([E1, DM], F32),
    "aw": ([E1 * R1, DM], F32), "bw": ([E1 * DM, R1], F32), "out_tm": ([S, DM], F32),
}

_PHASES = [
    ("p1", _emit_p1, ["x_tm"], ["x0_fm", "xn0_tm"], _C_P1),
    ("p2", _emit_p2, ["qkv_w", "router_w", "ebias", "xn0_tm"], ["qkv_fm", "combT"], _C_P2),
    ("p3", _emit_p3, ["qkv_fm", "norm_w"], ["attnn_fm"], _C_P3),
    ("p4", _emit_p4, ["gate_w", "out_w", "xn0_tm", "attnn_fm"], ["s0_fm"], _C_P4_ := []),
    ("p5a", _emit_p5a, ["sw1", "sw2", "w12", "combT", "xn0_tm"], ["hsh_fm", "hrt_fm"], _C_P5A),
    ("p5b", _emit_p5b, ["sw3", "w3", "s0_fm", "x0_fm", "hsh_fm", "hrt_fm"],
     ["x1_fm", "xn1_fm"], _C_P5B),
    ("p6", _emit_p6, ["wq", "wk", "wv", "wo", "xn1_fm"], ["s1_fm"], _C_P6),
    ("p7", _emit_p7, ["wbase", "bbase", "wrout", "aw", "bw", "xn1_fm", "x1_fm", "s1_fm"],
     ["out_tm"], _C_P7),
]

_MODULE_INS = ["x_tm", "qkv_w", "router_w", "ebias", "norm_w", "gate_w", "out_w",
               "sw1", "sw2", "w12", "sw3", "w3", "wq", "wk", "wv", "wo",
               "wbase", "bbase", "wrout", "aw", "bw"]
_ALL_CONSTS = sorted({c for (_, _, _, _, cs) in _PHASES for c in cs})


def _build_phase(name):
    (pname, emit, ins, outs, consts) = next(p for p in _PHASES if p[0] == name)
    nc = bass.Bass("TRN2", target_bir_lowering=False)
    T = {}
    for n in ins:
        shp, dt = _IO[n]
        T[n] = nc.dram_tensor(n, shp, dt, kind="ExternalInput").ap()
    for n in outs:
        shp, dt = _IO[n]
        T[n] = nc.dram_tensor(n, shp, dt, kind="ExternalOutput").ap()
    T.update(_decl_consts(nc, consts))
    with ExitStack() as pst:
        ph = PhaseCtx(nc, pst, None, 0)
        emit(nc, T, ph)
    return nc


def _build_p1():
    return _build_phase("p1")


def _build_p2():
    return _build_phase("p2")


def _build_p3():
    return _build_phase("p3")


def _build_p4():
    return _build_phase("p4")


def _build_p5a():
    return _build_phase("p5a")


def _build_p5b():
    return _build_phase("p5b")


def _build_p6():
    return _build_phase("p6")


def _build_p7():
    return _build_phase("p7")


def _build_fused():
    nc = bass.Bass("TRN2", target_bir_lowering=False)
    produced = {o for (_, _, _, outs, _) in _PHASES for o in outs}
    T = {}
    for n in _MODULE_INS:
        shp, dt = _IO[n]
        T[n] = nc.dram_tensor(n, shp, dt, kind="ExternalInput").ap()
    for n in produced:
        shp, dt = _IO[n]
        kind = "ExternalOutput" if n == "out_tm" else "Internal"
        T[n] = nc.dram_tensor(n, shp, dt, kind=kind).ap()
    T.update(_decl_consts(nc, _ALL_CONSTS))
    for pidx, (pname, emit, ins, outs, consts) in enumerate(_PHASES):
        with nc.reset_on_exit():
            with ExitStack() as pst:
                ph = PhaseCtx(nc, pst, None, pidx)
                emit(nc, T, ph)
    return nc


def _run(name, builder, in_maps):
    if name not in _prog_cache:
        _prog_cache[name] = builder()
    nc = _prog_cache[name]
    res = run_bass_kernel_spmd(nc, in_maps, core_ids=list(range(len(in_maps))))
    return res.results


def kernel(**inputs):
    x = np.ascontiguousarray(inputs["L0_attn_hidden_states"], dtype=np.float32)
    n = x.shape[0]
    w = dict(
        qkv_w=np.ascontiguousarray(inputs["L0_attn_qkv_w"]),
        out_w=np.ascontiguousarray(inputs["L0_attn_out_w"]),
        gate_w=np.ascontiguousarray(inputs["L0_attn_gate_w"]),
        norm_w=np.ascontiguousarray(inputs["L0_attn_norm_w"]),
        w12=np.ascontiguousarray(inputs["L0_ffn_w12"]).reshape(E0 * 2 * HR, DM),
        w3=np.ascontiguousarray(inputs["L0_ffn_w3"]).reshape(E0 * DM, HR),
        router_w=np.ascontiguousarray(inputs["L0_ffn_router_w"]),
        ebias=np.ascontiguousarray(inputs["L0_ffn_expert_bias"]),
        sw1=np.ascontiguousarray(inputs["L0_ffn_sw1"]),
        sw2=np.ascontiguousarray(inputs["L0_ffn_sw2"]),
        sw3=np.ascontiguousarray(inputs["L0_ffn_sw3"]),
        wq=np.ascontiguousarray(inputs["L1_attn_Wq"]),
        wk=np.ascontiguousarray(inputs["L1_attn_Wk"]),
        wv=np.ascontiguousarray(inputs["L1_attn_Wv"]),
        wo=np.ascontiguousarray(inputs["L1_attn_Wo"]),
        wbase=np.ascontiguousarray(inputs["L1_ffn_W_base"]),
        bbase=np.ascontiguousarray(inputs["L1_ffn_b_base"]),
        wrout=np.ascontiguousarray(inputs["L1_ffn_W_router"]),
        aw=np.ascontiguousarray(inputs["L1_ffn_A_w"]).reshape(E1 * R1, DM),
        bw=np.ascontiguousarray(inputs["L1_ffn_B_w"]).reshape(E1 * DM, R1),
        **_const_inputs(_ALL_CONSTS),
    )
    res = _run("fused", _build_fused,
               [dict(x_tm=x[c], **w) for c in range(n)])
    return np.stack([np.asarray(res[c]["out_tm"], dtype=np.float32) for c in range(n)])


# version 04
# speedup vs baseline: 1.2070x; 1.1754x over previous
import numpy as np
import ml_dtypes
import concourse.bass as bass
import concourse.mybir as mybir
from concourse.bass_utils import run_bass_kernel_spmd
from contextlib import ExitStack

F32 = mybir.dt.float32
BF16 = mybir.dt.bfloat16
AF = mybir.ActivationFunctionType
ALU = mybir.AluOpType
AX = mybir.AxisListType
BFNP = ml_dtypes.bfloat16

NCORE = 8
S = 2048
DM = 2048
EPS = 1e-5

H0, DH, BS0 = 32, 64, 256
NB0 = S // BS0
_l0_base = 1.0 / 2.0 ** (8.0 / H0)
_l0_factor = 1.0 - 0.0 / (32 - 1 + 1e-5) + 1e-5
SLOPES = [float((_l0_base ** (h + 1)) * _l0_factor) for h in range(H0)]

E0, HS, HR = 8, 8192, 1024
H1, SC1 = 32, 0.125
E1, R1, LSCALE = 8, 8, 2.0

_prog_cache = {}


def cdiv(a, b):
    return -(-a // b)


class Pipe:

    def __init__(self, st, nc, name, step=1):
        self.h = st.enter_context(nc.semaphore(name))
        self.step = step
        self.emitted = 0

    def inc(self, instr, n=1):
        instr.then_inc(self.h, self.step * n)
        self.emitted += self.step * n
        return instr

    def wait(self, eng, units):
        if units > 0:
            eng.wait_ge(self.h, self.step * units)


class Chain:

    def __init__(self, st, nc, name):
        self.h = st.enter_context(nc.semaphore(name))
        self.n = 0
        self.emitted = 0

    def seal(self, instr):
        instr.then_inc(self.h, 1)
        self.n += 1
        self.emitted = self.n
        return instr

    def gate(self, eng):
        if self.n > 0:
            eng.wait_ge(self.h, self.n)


class SPipe:

    def __init__(self, st, nc, name, nslots, step=16):
        self.p = [Pipe(st, nc, f"{name}{i}", step) for i in range(nslots)]
        self.n = nslots

    def inc(self, instr, k, n=1):
        self.p[k % self.n].inc(instr, n)
        return instr

    def wait(self, eng, k, uses):
        self.p[k % self.n].wait(eng, uses)

    def wait_all(self, eng, total_units_per_slot):
        for p in self.p:
            p.wait(eng, total_units_per_slot)


class SSched:

    def __init__(self, pipes):
        self.pipes = pipes
        self.counts = {k: 0 for k in pipes}
        self.steps = []
        self.res = {}

    def add(self, eng, fn, reads=(), writes=(), extra=()):
        if eng == "gp":
            writes = tuple(writes) + ("__gp_tok",)
        deps = {}
        for r in reads:
            ent = self.res.get(r)
            if ent and ent["w"]:
                e2, v = ent["w"]
                deps[e2] = max(deps.get(e2, 0), v)
        for w_ in writes:
            ent = self.res.get(w_)
            if ent:
                if ent["w"]:
                    e2, v = ent["w"]
                    deps[e2] = max(deps.get(e2, 0), v)
                for (e2, v) in ent["r"]:
                    deps[e2] = max(deps.get(e2, 0), v)
        self.counts[eng] += 1
        tag = (eng, self.counts[eng])
        for r in reads:
            self.res.setdefault(r, {"w": None, "r": []})["r"].append(tag)
        for w_ in writes:
            self.res[w_] = {"w": tag, "r": []}
        self.steps.append((eng, fn, deps, tag, list(extra)))
        return tag

    def emit(self, eng_name, engine):
        for (eng, fn, deps, tag, extra) in self.steps:
            if eng != eng_name:
                continue
            for e2, v in deps.items():
                if e2 == eng_name and v >= tag[1]:
                    continue
                self.pipes[e2].wait(engine, v)
            for (pipe, units) in extra:
                pipe.wait(engine, units)
            self.pipes[eng].inc(fn(engine))


class Bar:

    def __init__(self, st, nc):
        self.clr = Pipe(st, nc, "bar_clr")
        self.eng = {k: Pipe(st, nc, f"bar_{k}") for k in ("pe", "dve", "act", "gp", "sp")}
        self.counts = {k: 0 for k in self.eng}


class PhaseCtx:
    def __init__(self, nc, st, bar, pidx):
        self.nc, self.st, self.bar, self.pidx = nc, st, bar, pidx
        self.sems = []

    def pipe(self, name, step=1):
        p = Pipe(self.st, self.nc, f"{name}_{self.pidx}", step)
        self.sems.append(p)
        return p

    def spipe(self, name, nslots, step=16):
        sp = SPipe(self.st, self.nc, f"{name}_{self.pidx}", nslots, step)
        self.sems.extend(sp.p)
        return sp

    def chain(self, name):
        c = Chain(self.st, self.nc, f"{name}_{self.pidx}")
        self.sems.append(c)
        return c

    def wrap(self, blk):
        self.raw_blk = blk
        return blk


class PhBlock:
    _MAP = {"gpsimd": "gp", "vector": "dve", "scalar": "act",
            "tensor": "pe", "sync": "sp"}

    def __init__(self, blk, bar, pidx):
        self._blk, self._bar, self._pidx = blk, bar, pidx

    def _deco(self, attr):
        eng_key = self._MAP[attr]
        bar, pidx = self._bar, self._pidx

        def decorator(f):
            def body(eng):
                bar.clr.wait(eng, pidx)
                f(eng)
                bar.eng[eng_key].inc(eng.nop())
            bar.counts[eng_key] += 1
            getattr(self._blk, attr)(body)
        return decorator

    @property
    def gpsimd(self):
        return self._deco("gpsimd")

    @property
    def vector(self):
        return self._deco("vector")

    @property
    def scalar(self):
        return self._deco("scalar")

    @property
    def tensor(self):
        return self._deco("tensor")

    @property
    def sync(self):
        return self._deco("sync")



def _sbt(nc, ph, name, *a, **kw):
    return nc.sbuf_tensor(f"p{ph.pidx}_{name}", *a, **kw)


def _pst(nc, ph, name, *a, **kw):
    return nc.psum_tensor(f"p{ph.pidx}_{name}", *a, **kw)


def _phase_boundary(nc, bar, ph):
    bb = ph.raw_blk

    @bb.gpsimd
    def _(g):
        for k, cnt in bar.counts.items():
            if cnt:
                bar.eng[k].wait(g, cnt)
        for p in ph.sems:
            if p.emitted:
                g.wait_ge(p.h, p.emitted)
        for p in ph.sems:
            g.sem_clear(p.h)
        bar.clr.inc(g.nop())


def _consts():
    ident_f32 = np.eye(128, dtype=np.float32)
    ident_bf = np.eye(128, dtype=BFNP)
    ntri = np.where(np.arange(128)[:, None] >= np.arange(128)[None, :], -1.0, 0.0)
    out = dict(
        ident_f32=ident_f32,
        ident_bf=ident_bf,
        ntri_bf=ntri.astype(BFNP),
        ones_row_bf=np.ones((1, 128), dtype=BFNP),
        negcol_bf=np.full((128, 1), -1.0, dtype=BFNP),
        onescol64_bf=np.ones((64, 1), dtype=BFNP),
        onesf_row=np.ones((1, 128), dtype=np.float32),
        onesf_col=np.ones((128, 1), dtype=np.float32),
    )
    rrep = np.zeros((8, 64), dtype=BFNP)
    for e in range(8):
        rrep[e, e * 8:(e + 1) * 8] = 1.0
    out["rrep"] = rrep
    return out


_CONSTS = _consts()


def _decl_consts(nc, names):
    aps = {}
    for n in names:
        arr = _CONSTS[n]
        dt = BF16 if arr.dtype == BFNP else F32
        aps[n] = nc.dram_tensor(n, list(arr.shape), dt, kind="ExternalInput").ap()
    return aps


def _const_inputs(names):
    return {n: _CONSTS[n] for n in names}


def _emit_p1(nc, T, ph):
    x, x0_fm, xn0_tm, C = T["x_tm"], T["x0_fm"], T["xn0_tm"], T
    NT = S // 128

    if True:
        e = ph.st.enter_context
        xt = [e(_sbt(nc, ph, f"xt{i}", [128, DM], F32)) for i in range(2)]
        xnt = e(_sbt(nc, ph, "xnt", [128, DM], BF16))
        sq = e(_sbt(nc, ph, "sq", [128, DM], F32))
        idf = e(_sbt(nc, ph, "idf", [128, 128], F32))
        ssum = e(_sbt(nc, ph, "ssum", [128, 1], F32))
        ssq = e(_sbt(nc, ph, "ssq", [128, 1], F32))
        mu = e(_sbt(nc, ph, "mu", [128, 1], F32))
        var = e(_sbt(nc, ph, "var", [128, 1], F32))
        var2 = e(_sbt(nc, ph, "var2", [128, 1], F32))
        lnv = e(_sbt(nc, ph, "lnv", [128, 1], F32))
        rstd = e(_sbt(nc, ph, "rstd", [128, 1], F32))
        bcol = e(_sbt(nc, ph, "bcol", [128, 1], F32))
        epscol = e(_sbt(nc, ph, "epscol", [128, 1], F32))
        evf = [e(_sbt(nc, ph, f"evf{i}", [128, 128], F32)) for i in range(2)]
        psf = [e(_pst(nc, ph, f"psf{i}", [128, 128], F32)) for i in range(2)]

        ld = ph.spipe("ld", 2)
        lc = ph.pipe("lc", 16)
        st_n = ph.pipe("st_n", 16)
        st_f = ph.spipe("st_f", 2)
        s_red = ph.pipe("s_red")
        s_sq = ph.pipe("s_sq")
        s_var = ph.pipe("s_var")
        s_rstd = ph.pipe("s_rstd")
        s_app = ph.pipe("s_app")
        s_tp = ph.pipe("s_tp")
        s_ev = ph.pipe("s_ev")
        cv = ph.chain("cv")
        ca = ph.chain("ca")
        blk = ph.wrap(e(nc.Block()))

        @blk.gpsimd
        def _(g):
            g.memset(epscol[:, :], EPS)
            lc.inc(g.dma_start(idf[:, :], C["ident_f32"]))
            for it in range(NT + 2):
                if it >= 2:
                    u = it - 2
                    s_app.wait(g, u + 1)
                    st_n.inc(g.dma_start(xn0_tm[u * 128:(u + 1) * 128, :], xnt[:, :]))
                    for dc in range(16):
                        k = u * 16 + dc
                        s_ev.wait(g, k + 1)
                        st_f.inc(g.dma_start(
                            x0_fm[dc * 128:(dc + 1) * 128, u * 128:(u + 1) * 128],
                            evf[dc % 2][:, :]), k)
                if it < NT:
                    tc = it
                    if tc >= 2:
                        s_red.wait(g, tc - 1)
                        s_app.wait(g, tc - 1)
                        s_tp.wait(g, 16 * (tc - 1))
                    ld.inc(g.dma_start(xt[tc % 2][:, :], x[tc * 128:(tc + 1) * 128, :]), tc)
            st_f.wait_all(g, NT * 16 // 2)
            st_n.wait(g, NT)

        @blk.vector
        def _(v):
            for tc in range(NT):
                ld.wait(v, tc, tc // 2 + 1)
                s_red.inc(v.reduce_sum(ssum[:, :], xt[tc % 2][:, :], axis=AX.X))
                s_red.wait(v, tc + 1)
                s_sq.wait(v, tc + 1)
                s_rstd.wait(v, tc)
                cv.seal(v.tensor_scalar_mul(mu[:, :], ssum[:, :], 1.0 / DM))
                cv.seal(v.tensor_scalar_mul(var2[:, :], ssq[:, :], 1.0 / DM))
                cv.gate(v)
                cv.seal(v.scalar_tensor_tensor(bcol[:, :], mu[:, :], -1.0, mu[:, :], ALU.mult, ALU.mult))
                cv.gate(v)
                s_var.inc(v.tensor_tensor(var[:, :], var2[:, :], bcol[:, :], ALU.add))
                s_rstd.wait(v, tc + 1)
                cv.seal(v.scalar_tensor_tensor(bcol[:, :], mu[:, :], -1.0, rstd[:, :], ALU.mult, ALU.mult))
                cv.gate(v)
                if tc >= 1:
                    st_n.wait(v, tc)
                s_app.inc(v.tensor_scalar(xnt[:, :], xt[tc % 2][:, :], rstd[:, :], bcol[:, :], ALU.mult, ALU.add))
                for dc in range(16):
                    k = tc * 16 + dc
                    s_tp.wait(v, k + 1)
                    if k >= 2:
                        st_f.wait(v, k, k // 2)
                    s_ev.inc(v.tensor_copy(evf[dc % 2][:, :], psf[dc % 2][:, :]))

        @blk.scalar
        def _(sc):
            for tc in range(NT):
                ld.wait(sc, tc, tc // 2 + 1)
                s_var.wait(sc, tc)
                s_sq.inc(sc.activation(sq[:, :], xt[tc % 2][:, :], AF.Square,
                                       accum_out=ssq[:, :]))
                s_var.wait(sc, tc + 1)
                s_app.wait(sc, tc)
                ca.seal(sc.activation(lnv[:, :], var[:, :], AF.Ln, bias=epscol[:, :]))
                ca.gate(sc)
                s_rstd.inc(sc.activation(rstd[:, :], lnv[:, :], AF.Exp, scale=-0.5))

        @blk.tensor
        def _(t):
            lc.wait(t, 1)
            for tc in range(NT):
                ld.wait(t, tc, tc // 2 + 1)
                for dc in range(16):
                    k = tc * 16 + dc
                    if k >= 2:
                        s_ev.wait(t, k - 1)
                    s_tp.inc(t.transpose(psf[dc % 2][:, :],
                                         xt[tc % 2][:, dc * 128:(dc + 1) * 128],
                                         idf[:, :]))



def _emit_p2(nc, T, ph):
    qkv_w, router_w, ebias = T["qkv_w"], T["router_w"], T["ebias"]
    xn0_tm, qkv_fm, combT, C = T["xn0_tm"], T["qkv_fm"], T["combT"], T
    qkv_bfw = nc.dram_tensor("qkv_bfw", [3 * H0 * DH, DM], BF16, kind="Internal").ap()
    router_bfw = nc.dram_tensor("router_bfw", [E0, DM], BF16, kind="Internal").ap()
    NOB = (3 * H0 * DH) // 512
    NTC = S // 128
    NTT = S // 512

    if True:
        e = ph.st.enter_context
        xn = [e(_sbt(nc, ph, f"xn{i}", [128, S], BF16)) for i in range(16)]
        wbuf = [e(_sbt(nc, ph, f"wbuf{i}", [128, 16 * 512], BF16, is_transpose_dest=True)) for i in range(2)]
        evq = [e(_sbt(nc, ph, f"evq{i}", [128, 512], BF16)) for i in range(4)]
        rsb = e(_sbt(nc, ph, "rsb", [E0, DM], BF16))
        routerT = e(_sbt(nc, ph, "routerT", [128, 128], BF16))
        idb = e(_sbt(nc, ph, "idb", [128, 128], BF16))
        onesf = e(_sbt(nc, ph, "onesf", [1, 128], F32))
        eb = e(_sbt(nc, ph, "eb", [1, E0], F32))
        bias_bc = e(_sbt(nc, ph, "bias_bc", [128, E0], F32))
        sc_tm = [e(_sbt(nc, ph, f"sc_tm{i}", [128, E0], F32)) for i in range(2)]
        comb_tm = [e(_sbt(nc, ph, f"comb_tm{i}", [128, E0], BF16)) for i in range(2)]
        combT_sb = e(_sbt(nc, ph, "combT_sb", [E0, S], BF16))
        sel = e(_sbt(nc, ph, "sel", [128, E0], F32))
        m1 = e(_sbt(nc, ph, "m1", [128, 1], F32))
        m2 = e(_sbt(nc, ph, "m2", [128, 1], F32))
        is1 = e(_sbt(nc, ph, "is1", [128, E0], F32))
        sel2 = e(_sbt(nc, ph, "sel2", [128, E0], F32))
        ind = e(_sbt(nc, ph, "ind", [128, E0], F32))
        wsc = e(_sbt(nc, ph, "wsc", [128, E0], F32))
        ssum = e(_sbt(nc, ph, "ssumr", [128, 1], F32))
        rcp = e(_sbt(nc, ph, "rcp", [128, 1], F32))

        ps_rtp = e(_pst(nc, ph, "ps_rtp", [128, E0], BF16))
        ps_ctp = e(_pst(nc, ph, "ps_ctp", [E0, 128], BF16))
        ps_sc = [e(_pst(nc, ph, f"ps_sc{i}", [128, E0], F32)) for i in range(2)]
        ps_q = [e(_pst(nc, ph, f"ps_q{i}", [128, 512], F32)) for i in range(4)]

        rc = ph.pipe("rc", 16)
        qc = ph.spipe("qc", 6)
        gl = ph.pipe("gl", 16)
        sxn = ph.pipe("sxn", 16)
        sw = ph.spipe("sw", 2)
        stc = ph.pipe("stc", 16)
        r_pe = ph.pipe("r_pe")
        r_dv = ph.pipe("r_dv")
        r_ac = ph.pipe("r_ac")
        s_mm = ph.pipe("s_mm")
        evp = ph.pipe("evp")
        stq = ph.spipe("stq", 4)
        cdv = ph.chain("cdv")
        blk = ph.wrap(e(nc.Block()))

        @blk.gpsimd
        def _(g):
            rc.inc(g.dma_start(router_bfw, router_w))
            for c in range(6):
                qc.inc(g.dma_start(qkv_bfw[c * 1024:(c + 1) * 1024, :],
                                   qkv_w[c * 1024:(c + 1) * 1024, :]), c)
            rc.wait(g, 1)
            gl.inc(g.dma_start(rsb[:, :], router_bfw))
            gl.inc(g.dma_start(eb[:, :], ebias.rearrange("(b a) -> b a", b=1)))
            gl.inc(g.dma_start(idb[:, :], C["ident_bf"]))
            gl.inc(g.dma_start(onesf[:, :], C["onesf_row"]))
            for ob in range(NOB):
                for tt in range(NTT):
                    for oc4 in range(4):
                        gidx = ob * 16 + tt * 4 + oc4
                        evp.wait(g, gidx + 1)
                        stq.inc(g.dma_start(
                            qkv_fm[ob * 512 + oc4 * 128: ob * 512 + (oc4 + 1) * 128,
                                   tt * 512:(tt + 1) * 512],
                            evq[gidx % 4][:, :]), gidx)
            r_dv.wait(g, 49)
            stc.inc(g.dma_start(combT, combT_sb[:, :]))
            stq.wait_all(g, NOB * 16 // 4)
            stc.wait(g, 1)

        @blk.sync
        def _(s):
            for ic in range(16):
                sxn.inc(s.dma_start_transpose(xn[ic][:, :], xn0_tm[:, ic * 128:(ic + 1) * 128]))
            for ob in range(NOB):
                for c in range(cdiv((ob + 1) * 512, 1024)):
                    qc.wait(s, c, 1)
                if ob >= 2:
                    s_mm.wait(s, 16 * (ob - 1))
                for ic in range(16):
                    sw.inc(s.dma_start_transpose(
                        wbuf[ob % 2][:, ic * 512:(ic + 1) * 512],
                        qkv_bfw[ob * 512:(ob + 1) * 512, ic * 128:(ic + 1) * 128]), ob)

        @blk.tensor
        def _(t):
            gl.wait(t, 4)
            for ic in range(16):
                if ic >= 1:
                    r_dv.wait(t, ic)
                r_pe.inc(t.transpose(ps_rtp[:, :], rsb[:, ic * 128:(ic + 1) * 128],
                                     idb[0:E0, 0:E0]))
            r_dv.wait(t, 16)
            r_pe.inc(t.matmul(ps_sc[0][:, :], onesf[:, :], eb[:, :], start=True, stop=True))
            sxn.wait(t, 16)

            def score_group(tc):
                r_dv.wait(t, 17)
                if tc >= 2:
                    r_ac.wait(t, tc - 1)
                for ic in range(16):
                    r_pe.inc(t.matmul(ps_sc[tc % 2][:, :],
                                      xn[ic][:, tc * 128:(tc + 1) * 128],
                                      routerT[:, ic * 8:(ic + 1) * 8],
                                      start=(ic == 0), stop=(ic == 15)), n=(1 if ic == 15 else 0))

            def ctp_item(j):
                r_dv.wait(t, 18 + 2 * j)
                r_pe.inc(t.transpose(ps_ctp[:, :], comb_tm[j % 2][:, :], idb[:, :]))

            for ob in range(NOB):
                sw.wait(t, ob, 16 * (ob // 2 + 1))
                for tt in range(NTT):
                    for oc4 in range(4):
                        gidx = ob * 16 + tt * 4 + oc4
                        if gidx % 10 == 4 and gidx // 10 < NTC:
                            score_group(gidx // 10)
                        if gidx % 10 == 9 and gidx // 10 < NTC:
                            ctp_item(gidx // 10)
                        if gidx >= 4:
                            evp.wait(t, gidx - 3)
                        for ic in range(16):
                            s_mm.inc(t.matmul(ps_q[oc4][:, :],
                                              wbuf[ob % 2][:, ic * 512 + oc4 * 128: ic * 512 + (oc4 + 1) * 128],
                                              xn[ic][:, tt * 512:(tt + 1) * 512],
                                              start=(ic == 0), stop=(ic == 15)),
                                     n=(1 if ic == 15 else 0))

        @blk.vector
        def _(v):
            for ic in range(16):
                r_pe.wait(v, ic + 1)
                r_dv.inc(v.tensor_copy(routerT[:, ic * 8:(ic + 1) * 8], ps_rtp[:, :]))
            r_pe.wait(v, 17)
            r_dv.inc(v.tensor_copy(bias_bc[:, :], ps_sc[0][:, :]))
            def pe_ctp_unit(j):
                return 19 + 2 * j
            for tc in range(NTC):
                if tc >= 1:
                    r_pe.wait(v, pe_ctp_unit(tc - 1))
                    r_dv.inc(v.tensor_copy(combT_sb[:, (tc - 1) * 128:tc * 128], ps_ctp[:, :]))
                r_ac.wait(v, tc + 1)
                if tc >= 2:
                    r_pe.wait(v, pe_ctp_unit(tc - 2))
                cdv.seal(v.tensor_tensor(sel[:, :], sc_tm[tc % 2][:, :], bias_bc[:, :], ALU.add))
                cdv.gate(v)
                cdv.seal(v.reduce_max(m1[:, :], sel[:, :], axis=AX.X))
                cdv.gate(v)
                cdv.seal(v.tensor_single_scalar(is1[:, :], sel[:, :], m1[:, 0:1], ALU.is_ge))
                cdv.gate(v)
                cdv.seal(v.scalar_tensor_tensor(sel2[:, :], is1[:, :], -1e9, sel[:, :], ALU.mult, ALU.add))
                cdv.gate(v)
                cdv.seal(v.reduce_max(m2[:, :], sel2[:, :], axis=AX.X))
                cdv.gate(v)
                cdv.seal(v.tensor_single_scalar(ind[:, :], sel[:, :], m2[:, 0:1], ALU.is_ge))
                cdv.gate(v)
                cdv.seal(v.tensor_tensor(wsc[:, :], ind[:, :], sc_tm[tc % 2][:, :], ALU.mult))
                cdv.gate(v)
                cdv.seal(v.reduce_sum(ssum[:, :], wsc[:, :], axis=AX.X))
                cdv.gate(v)
                cdv.seal(v.tensor_single_scalar(ssum[:, :], ssum[:, :], 1e-9, ALU.add))
                cdv.gate(v)
                cdv.seal(v.reciprocal(rcp[:, :], ssum[:, :]))
                cdv.gate(v)
                r_dv.inc(v.tensor_single_scalar(comb_tm[tc % 2][:, :], wsc[:, :], rcp[:, 0:1], ALU.mult))
            r_pe.wait(v, pe_ctp_unit(15))
            r_dv.inc(v.tensor_copy(combT_sb[:, 15 * 128:16 * 128], ps_ctp[:, :]))

        @blk.scalar
        def _(sc):
            def sig_item(tc):
                r_pe.wait(sc, 18 + 2 * tc)
                if tc >= 2:
                    r_dv.wait(sc, 18 + 2 * (tc - 2))
                r_ac.inc(sc.activation(sc_tm[tc % 2][:, :], ps_sc[tc % 2][:, :], AF.Sigmoid))

            for ob in range(NOB):
                for tt in range(NTT):
                    for oc4 in range(4):
                        gidx = ob * 16 + tt * 4 + oc4
                        if gidx % 10 == 5 and gidx // 10 < NTC:
                            sig_item(gidx // 10)
                        s_mm.wait(sc, gidx + 1)
                        if gidx >= 4:
                            stq.wait(sc, gidx, gidx // 4)
                        evp.inc(sc.activation(evq[gidx % 4][:, :], ps_q[oc4][:, :], AF.Silu))



def _emit_p3(nc, T, ph):
    qkv_fm, norm_w, attnn_fm, C = T["qkv_fm"], T["norm_w"], T["attnn_fm"], T
    NM = H0 * NB0

    if True:
        e = ph.st.enter_context
        qh = [e(_sbt(nc, ph, f"qh{i}", [64, S], BF16)) for i in range(2)]
        kh = [e(_sbt(nc, ph, f"kh{i}", [64, S], BF16)) for i in range(2)]
        vh = [e(_sbt(nc, ph, f"vh{i}", [64, S], BF16)) for i in range(2)]
        attn = [e(_sbt(nc, ph, f"attn{i}", [128, S], BF16)) for i in range(16)]
        D = [e(_sbt(nc, ph, f"D{i}", [128, BS0], F32)) for i in range(2)]
        m01 = [e(_sbt(nc, ph, f"m01{i}", [128, BS0], F32)) for i in range(2)]
        Dc = [e(_sbt(nc, ph, f"Dc{i}", [128, BS0], F32)) for i in range(2)]
        eexp = [e(_sbt(nc, ph, f"eexp{i}", [128, BS0], F32)) for i in range(2)]
        ddT = [e(_sbt(nc, ph, f"ddT{i}", [128, BS0], BF16)) for i in range(2)]
        ior = e(_sbt(nc, ph, "ior", [1, BS0], F32))
        ibc = e(_sbt(nc, ph, "ibc", [64, BS0], F32))
        iocol = [e(_sbt(nc, ph, f"iocol{i}", [128, 1], F32)) for i in range(2)]
        qd = [e(_sbt(nc, ph, f"qd{i}", [64, BS0], BF16)) for i in range(2)]
        kdc = [e(_sbt(nc, ph, f"kdc{i}", [128, 2], F32)) for i in range(2)]
        awT = [e(_sbt(nc, ph, f"awT{i}", [128, BS0], BF16)) for i in range(2)]
        kdkT = [e(_sbt(nc, ph, f"kdkT{i}", [128, 64], BF16)) for i in range(2)]
        vbT = [e(_sbt(nc, ph, f"vbT{i}", [128, 64], BF16)) for i in range(2)]
        qdq = e(_sbt(nc, ph, "qdq", [64, BS0], BF16))
        S_f = [e(_sbt(nc, ph, f"S_f{i}", [64, 64], F32)) for i in range(2)]
        S_b = e(_sbt(nc, ph, "S_b", [64, 64], BF16))
        sq_b = e(_sbt(nc, ph, "sq_b", [64, BS0], BF16))
        ssb = [e(_sbt(nc, ph, f"ssb{i}", [1, S], F32)) for i in range(2)]
        lnr = e(_sbt(nc, ph, "lnr", [1, S], F32))
        rstd_r = e(_sbt(nc, ph, "rstd_r", [1, S], F32))
        ebias1 = e(_sbt(nc, ph, "ebias1", [1, 1], F32))
        ebias2 = e(_sbt(nc, ph, "ebias2", [1, 1], F32))
        nwrow = e(_sbt(nc, ph, "nwrow", [1, DM], F32))
        nw_cols = e(_sbt(nc, ph, "nw_cols", [128, 16], F32))
        o64 = e(_sbt(nc, ph, "o64", [64, 1], BF16))
        onesf = e(_sbt(nc, ph, "onesf", [1, 128], F32))
        idf = e(_sbt(nc, ph, "idf", [128, 128], F32))
        idb3 = e(_sbt(nc, ph, "idb3", [128, 128], BF16))
        anout = [e(_sbt(nc, ph, f"anout{i}", [128, 512], BF16)) for i in range(4)]

        ps_sc = [e(_pst(nc, ph, f"ps_sc{i}", [128, 512], F32)) for i in range(2)]
        ps_y = e(_pst(nc, ph, "ps_y", [64, BS0], F32))
        ps_S = e(_pst(nc, ph, "ps_S", [64, 64], F32))
        ps_tp = [e(_pst(nc, ph, f"ps_tp{i}", [128, 64], BF16)) for i in range(2)]
        ps_ss = e(_pst(nc, ph, "ps_ss", [1, BS0], F32))
        ps_nw = e(_pst(nc, ph, "ps_nw", [128, 16], F32))

        g_init = ph.pipe("g_init", 16)
        g_eng = ph.pipe("g_eng")
        qld = ph.spipe("qld", 2)
        d_pre = ph.pipe("d_pre")
        p_pre = ph.pipe("p_pre")
        a_set = ph.pipe("a_set")
        d_set = ph.pipe("d_set")
        p_sc = ph.pipe("p_sc")
        d_aw = ph.pipe("d_aw")
        p_tp = ph.pipe("p_tp")
        d_tp = ph.pipe("d_tp")
        d_qd = ph.pipe("d_qd")
        p_y = ph.pipe("p_y")
        p_S = ph.pipe("p_S")
        d_S = ph.pipe("d_S")
        a_y = ph.pipe("a_y")
        a_sq = ph.pipe("a_sq")
        p_ss = ph.pipe("p_ss")
        d_ss = ph.pipe("d_ss")
        a_r = ph.pipe("a_r")
        n_pe = ph.pipe("n_pe")
        n_dv = ph.pipe("n_dv")
        n_st = ph.spipe("n_st", 4)
        cdv = ph.chain("cdv3")
        cac = ph.chain("cac3")
        blk = ph.wrap(e(nc.Block()))

        @blk.gpsimd
        def _(g):
            g_eng.inc(g.iota(D[0][:, :], [[1, BS0]], base=0, channel_multiplier=-1,
                             allow_small_or_imprecise_dtypes=True))
            g_eng.inc(g.iota(D[1][:, :], [[1, BS0]], base=-128, channel_multiplier=-1,
                             allow_small_or_imprecise_dtypes=True))
            g_eng.inc(g.iota(ior[:, :], [[1, BS0]], base=1, channel_multiplier=0,
                             allow_small_or_imprecise_dtypes=True))
            g_eng.inc(g.iota(iocol[0][:, :], [[0, 1]], base=1 - BS0, channel_multiplier=1,
                             allow_small_or_imprecise_dtypes=True))
            g_eng.inc(g.iota(iocol[1][:, :], [[0, 1]], base=1 - BS0 + 128, channel_multiplier=1,
                             allow_small_or_imprecise_dtypes=True))
            g_eng.inc(g.memset(ebias1[:, :], float(DM) * EPS))
            g_eng.inc(g.memset(ebias2[:, :], 0.5 * float(np.log(DM))))
            g_eng.inc(g.memset(o64[:, :], 1.0))
            g_init.inc(g.dma_start(nwrow[:, :], norm_w.rearrange("(b a) -> b a", b=1)))
            g_init.inc(g.dma_start(idf[:, :], C["ident_f32"]))
            g_init.inc(g.dma_start(onesf[:, :], C["onesf_row"]))
            g_init.inc(g.dma_start(idb3[:, :], C["ident_bf"]))
            for h in range(H0):
                if h >= 2:
                    p_ss.wait(g, NB0 * (h - 1))
                    d_qd.wait(g, NB0 * (h - 1))
                base = h * 3 * DH
                qld.inc(g.dma_start(qh[h % 2][:, :], qkv_fm[base:base + DH, :]), h)
                qld.inc(g.dma_start(kh[h % 2][:, :], qkv_fm[base + DH:base + 2 * DH, :]), h)
                qld.inc(g.dma_start(vh[h % 2][:, :], qkv_fm[base + 2 * DH:base + 3 * DH, :]), h)
            for tt in range(4):
                for fc in range(16):
                    k = tt * 16 + fc
                    n_dv.wait(g, k + 2)
                    n_st.inc(g.dma_start(
                        attnn_fm[fc * 128:(fc + 1) * 128, tt * 512:(tt + 1) * 512],
                        anout[k % 4][:, :]), k)
            n_st.wait_all(g, 16)

        @blk.tensor
        def _(t):
            g_init.wait(t, 4)
            g_eng.wait(t, 8)
            p_pre.inc(t.matmul(ps_sc[0][0:64, 0:BS0], onesf[:, 0:64], ior[:, :],
                               start=True, stop=True))
            for h in range(H0):
                qld.wait(t, h, 3 * (h // 2 + 1))
                for n in range(NB0):
                    m = h * NB0 + n
                    d_aw.wait(t, 2 * m)
                    if m == 0:
                        d_pre.wait(t, 1)
                    for c in range(2):
                        p_sc.inc(t.matmul(
                            ps_sc[c][:, 0:BS0],
                            kh[h % 2][:, n * BS0 + c * 128: n * BS0 + (c + 1) * 128],
                            qh[h % 2][:, n * BS0:(n + 1) * BS0],
                            start=True, stop=True), n=(1 if c == 1 else 0))
                    for j in range(4):
                        tix = 4 * m + j
                        c = j % 2
                        src_t = kh[h % 2] if j < 2 else vh[h % 2]
                        if tix >= 2:
                            d_tp.wait(t, tix - 1)
                        p_tp.inc(t.transpose(
                            ps_tp[j % 2][:, :],
                            src_t[:, n * BS0 + c * 128: n * BS0 + (c + 1) * 128],
                            idb3[0:DH, 0:DH]))
                    d_aw.wait(t, 2 * m + 2)
                    d_tp.wait(t, 4 * m + 4)
                    d_qd.wait(t, m + 1)
                    d_S.wait(t, 9 * h + 1 + n)
                    a_sq.wait(t, m)
                    t.matmul(ps_y[:, :], vbT[0][:, :], awT[0][:, :], start=True, stop=False)
                    t.matmul(ps_y[:, :], vbT[1][:, :], awT[1][:, :], start=False, stop=False)
                    p_y.inc(t.matmul(ps_y[:, :], S_b[:, :], qdq[:, :], start=False, stop=True))
                    t.matmul(ps_S[:, :], kdkT[0][:, :], vbT[0][:, :], start=True, stop=False)
                    p_S.inc(t.matmul(ps_S[:, :], kdkT[1][:, :], vbT[1][:, :], start=False, stop=True))
                    a_sq.wait(t, m + 1)
                    d_ss.wait(t, m)
                    p_ss.inc(t.matmul(ps_ss[:, :], o64[:, :], sq_b[:, :], start=True, stop=True))
            for c in range(16):
                n_pe.inc(t.transpose(ps_nw[:, c:c + 1], nwrow[:, c * 128:(c + 1) * 128],
                                     idf[0:1, 0:1]))
            a_r.wait(t, 1)
            d_aw.wait(t, 2 * NM)
            for tt in range(4):
                if tt >= 2:
                    n_dv.wait(t, 1 + (tt - 2) * 16 + 16)
                n_pe.inc(t.matmul(ps_sc[tt % 2][:, :], onesf[:, :],
                                  rstd_r[:, tt * 512:(tt + 1) * 512], start=True, stop=True))

        @blk.vector
        def _(v):
            g_init.wait(v, 4)
            g_eng.wait(v, 8)
            for c in range(2):
                v.tensor_single_scalar(m01[c][:, :], D[c][:, :], 0.0, ALU.is_ge)
                v.tensor_single_scalar(Dc[c][:, :], D[c][:, :], 0.0, ALU.max)
            cdv.seal(v.memset(ssb[0][:, :], 0.0))
            p_pre.wait(v, 1)
            d_pre.inc(v.tensor_copy(ibc[:, :], ps_sc[0][0:64, 0:BS0]))
            for h in range(H0):
                a_set.wait(v, 5 * (h + 1))
                d_set.inc(v.tensor_tensor(ddT[0][:, :], eexp[0][:, :], m01[0][:, :], ALU.mult))
                d_set.inc(v.tensor_tensor(ddT[1][:, :], eexp[1][:, :], m01[1][:, :], ALU.mult))
                p_S.wait(v, NB0 * h)
                p_y.wait(v, NB0 * h)
                d_S.wait(v, 9 * h)
                v.memset(S_f[0][:, :], 0.0)
                d_S.inc(v.memset(S_b[:, :], 0.0))
                d_set.wait(v, 2 * (h + 1))
                for n in range(NB0):
                    m = h * NB0 + n
                    p_sc.wait(v, m + 1)
                    for c in range(2):
                        d_aw.inc(v.tensor_tensor(awT[c][:, :], ps_sc[c][:, 0:BS0],
                                                 ddT[c][:, :], ALU.mult))
                    for j in range(4):
                        tix = 4 * m + j
                        c = j % 2
                        p_tp.wait(v, tix + 1)
                        if j < 2:
                            d_tp.inc(v.tensor_single_scalar(kdkT[c][:, :], ps_tp[j % 2][:, :],
                                                            kdc[h % 2][:, c:c + 1], ALU.mult))
                        else:
                            d_tp.inc(v.tensor_copy(vbT[c][:, :], ps_tp[j % 2][:, :]))
                    p_y.wait(v, m)
                    d_qd.inc(v.tensor_tensor(qdq[:, :], qh[h % 2][:, n * BS0:(n + 1) * BS0],
                                             qd[h % 2][:, :], ALU.mult))
                    p_ss.wait(v, m + 1)
                    if m == 0:
                        cdv.gate(v)
                    d_ss.inc(v.tensor_tensor(ssb[(h + 1) % 2][:, n * BS0:(n + 1) * BS0],
                                             ssb[h % 2][:, n * BS0:(n + 1) * BS0],
                                             ps_ss[:, :], ALU.add))
                    p_S.wait(v, m + 1)
                    p_y.wait(v, m + 1)
                    bd = float(np.exp(-SLOPES[h] * BS0))
                    cdv.seal(v.scalar_tensor_tensor(S_f[(n + 1) % 2][:, :], S_f[n % 2][:, :],
                                                    bd, ps_S[:, :], ALU.mult, ALU.add))
                    cdv.gate(v)
                    d_S.inc(v.tensor_copy(S_b[:, :], S_f[(n + 1) % 2][:, :]))
            n_pe.wait(v, 16)
            n_dv.inc(v.tensor_copy(nw_cols[:, :], ps_nw[:, :]))
            n_dv.wait(v, 1)
            for tt in range(4):
                n_pe.wait(v, 16 + tt + 1)
                for fc in range(16):
                    k = tt * 16 + fc
                    if k >= 4:
                        n_st.wait(v, k, k // 4)
                    n_dv.inc(v.scalar_tensor_tensor(anout[k % 4][:, :],
                                                    attn[fc][:, tt * 512:(tt + 1) * 512],
                                                    nw_cols[:, fc:fc + 1],
                                                    ps_sc[tt % 2][:, :],
                                                    ALU.mult, ALU.mult))

        @blk.scalar
        def _(sc):
            g_init.wait(sc, 4)
            g_eng.wait(sc, 8)
            d_pre.wait(sc, 1)
            for h in range(H0):
                slope = SLOPES[h]
                if h >= 1:
                    d_set.wait(sc, 2 * h)
                if h >= 2:
                    d_qd.wait(sc, NB0 * (h - 1))
                    d_tp.wait(sc, 4 * NB0 * (h - 1))
                for c in range(2):
                    a_set.inc(sc.activation(eexp[c][:, :], Dc[c][:, :], AF.Exp, scale=-slope))
                a_set.inc(sc.activation(qd[h % 2][:, :], ibc[:, :], AF.Exp, scale=-slope))
                for c in range(2):
                    a_set.inc(sc.activation(kdc[h % 2][:, c:c + 1], iocol[c][:, :], AF.Exp, scale=slope))
                for n in range(NB0):
                    m = h * NB0 + n
                    p_y.wait(sc, m + 1)
                    fc, off = h // 2, (h % 2) * 64
                    a_y.inc(sc.copy(attn[fc][off:off + 64, n * BS0:(n + 1) * BS0], ps_y[:, :]))
                    p_ss.wait(sc, m)
                    a_sq.inc(sc.activation(sq_b[:, :], ps_y[:, :], AF.Square))
            d_ss.wait(sc, NM)
            cac.seal(sc.activation(lnr[:, :], ssb[0][:, :], AF.Ln, bias=ebias1[:, :]))
            cac.gate(sc)
            a_r.inc(sc.activation(rstd_r[:, :], lnr[:, :], AF.Exp, scale=-0.5,
                                  bias=ebias2[:, :]))



def _emit_p4(nc, T, ph):
    gate_w, out_w = T["gate_w"], T["out_w"]
    xn0_tm, attnn_fm, s0_fm = T["xn0_tm"], T["attnn_fm"], T["s0_fm"]
    gate_bfw = nc.dram_tensor("gate_bfw", [DM, DM], BF16, kind="Internal").ap()
    outw_bfw = nc.dram_tensor("outw_bfw", [DM, DM], BF16, kind="Internal").ap()

    if True:
        e = ph.st.enter_context
        xn = [e(_sbt(nc, ph, f"xn{i}", [128, S], BF16)) for i in range(16)]
        asl = [e(_sbt(nc, ph, f"asl{i}", [128, 512], BF16)) for i in range(16)]
        zsl = [e(_sbt(nc, ph, f"zsl{i}", [128, 512], BF16)) for i in range(16)]
        wA = [e(_sbt(nc, ph, f"wA{i}", [128, 16 * 512], BF16, is_transpose_dest=True)) for i in range(2)]
        wB = [e(_sbt(nc, ph, f"wB{i}", [128, 16 * 512], BF16, is_transpose_dest=True)) for i in range(2)]
        gsig = [e(_sbt(nc, ph, f"gsig{i}", [128, 512], F32)) for i in range(4)]
        sout = [e(_sbt(nc, ph, f"sout{i}", [128, 512], BF16)) for i in range(4)]
        psA = [e(_pst(nc, ph, f"psA{i}", [128, 512], F32)) for i in range(4)]
        psB = [e(_pst(nc, ph, f"psB{i}", [128, 512], F32)) for i in range(4)]

        gc = ph.pipe("gc", 16)
        ocst = ph.pipe("ocst", 16)
        sxn = ph.pipe("sxn", 16)
        ald = ph.pipe("ald", 16)
        swA = ph.spipe("swA", 2)
        swB = ph.spipe("swB", 2)
        a_mm = ph.pipe("a_mm")
        a_ev = ph.pipe("a_ev")
        d_z = ph.pipe("d_z")
        b_mm = ph.pipe("b_mm")
        b_ev = ph.pipe("b_ev")
        stp = ph.spipe("stp", 4)
        blk = ph.wrap(e(nc.Block()))

        @blk.gpsimd
        def _(g):
            for c in range(4):
                gc.inc(g.dma_start(gate_bfw[c * 512:(c + 1) * 512, :],
                                   gate_w[c * 512:(c + 1) * 512, :]))
            for c in range(4):
                ocst.inc(g.dma_start(outw_bfw[c * 512:(c + 1) * 512, :],
                                     out_w[c * 512:(c + 1) * 512, :]))
            for tt in range(4):
                d_z.wait(g, 16 * tt)
                for ic in range(16):
                    ald.inc(g.dma_start(asl[ic][:, :],
                                        attnn_fm[ic * 128:(ic + 1) * 128,
                                                 tt * 512:(tt + 1) * 512]))
                for gb in range(16):
                    k = tt * 16 + gb
                    b_ev.wait(g, k + 1)
                    ob, oc4 = gb // 4, gb % 4
                    stp.inc(g.dma_start(
                        s0_fm[ob * 512 + oc4 * 128: ob * 512 + (oc4 + 1) * 128,
                              tt * 512:(tt + 1) * 512],
                        sout[k % 4][:, :]), k)
            stp.wait_all(g, 16)

        @blk.sync
        def _(s):
            for ic in range(16):
                sxn.inc(s.dma_start_transpose(xn[ic][:, :], xn0_tm[:, ic * 128:(ic + 1) * 128]))
            gc.wait(s, 4)
            ocst.wait(s, 4)
            for tt in range(4):
                for ob in range(4):
                    ga = tt * 4 + ob
                    if ga >= 2:
                        a_mm.wait(s, 4 * (ga - 1))
                    for ic in range(16):
                        swA.inc(s.dma_start_transpose(
                            wA[ga % 2][:, ic * 512:(ic + 1) * 512],
                            gate_bfw[ob * 512:(ob + 1) * 512, ic * 128:(ic + 1) * 128]), ga)
                for ob in range(4):
                    gb = tt * 4 + ob
                    if gb >= 2:
                        b_mm.wait(s, 4 * (gb - 1))
                    for ic in range(16):
                        swB.inc(s.dma_start_transpose(
                            wB[gb % 2][:, ic * 512:(ic + 1) * 512],
                            outw_bfw[ob * 512:(ob + 1) * 512, ic * 128:(ic + 1) * 128]), gb)

        @blk.tensor
        def _(t):
            sxn.wait(t, 16)
            for tt in range(4):
                for ob in range(4):
                    ga = tt * 4 + ob
                    swA.wait(t, ga, 16 * (ga // 2 + 1))
                    for oc4 in range(4):
                        gidx = ga * 4 + oc4
                        if gidx >= 4:
                            a_ev.wait(t, gidx - 3)
                        for ic in range(16):
                            a_mm.inc(t.matmul(psA[oc4][:, :],
                                              wA[ga % 2][:, ic * 512 + oc4 * 128: ic * 512 + (oc4 + 1) * 128],
                                              xn[ic][:, tt * 512:(tt + 1) * 512],
                                              start=(ic == 0), stop=(ic == 15)),
                                     n=(1 if ic == 15 else 0))
                d_z.wait(t, 16 * (tt + 1))
                for ob in range(4):
                    gb = tt * 4 + ob
                    swB.wait(t, gb, 16 * (gb // 2 + 1))
                    for oc4 in range(4):
                        gidx = gb * 4 + oc4
                        if gidx >= 4:
                            b_ev.wait(t, gidx - 3)
                        for ic in range(16):
                            b_mm.inc(t.matmul(psB[oc4][:, :],
                                              wB[gb % 2][:, ic * 512 + oc4 * 128: ic * 512 + (oc4 + 1) * 128],
                                              zsl[ic][:, :],
                                              start=(ic == 0), stop=(ic == 15)),
                                     n=(1 if ic == 15 else 0))

        @blk.scalar
        def _(sc):
            for tt in range(4):
                for gidx in range(tt * 16, (tt + 1) * 16):
                    a_mm.wait(sc, gidx + 1)
                    if gidx >= 4:
                        d_z.wait(sc, gidx - 3)
                    a_ev.inc(sc.activation(gsig[gidx % 4][:, :], psA[gidx % 4 % 4][:, :], AF.Sigmoid))
                for gidx in range(tt * 16, (tt + 1) * 16):
                    b_mm.wait(sc, gidx + 1)
                    if gidx >= 4:
                        stp.wait(sc, gidx, gidx // 4)
                    b_ev.inc(sc.copy(sout[gidx % 4][:, :], psB[gidx % 4][:, :]))

        @blk.vector
        def _(v):
            for tt in range(4):
                ald.wait(v, 16 * (tt + 1))
                if tt >= 1:
                    b_mm.wait(v, 16 * tt)
                for gidx in range(tt * 16, (tt + 1) * 16):
                    a_ev.wait(v, gidx + 1)
                    ocg = (gidx % 16) // 4 * 4 + gidx % 4
                    ocg = gidx % 16
                    d_z.inc(v.tensor_tensor(zsl[ocg][:, :], gsig[gidx % 4][:, :],
                                            asl[ocg][:, :], ALU.mult))



def _p6_sched():
    sched = []
    for hp in range(H1 // 2):
        for it in range(4):
            jbs = list(range(4 * (it + 1)))[::-1]
            for j, jb in enumerate(jbs):
                for par in range(2):
                    sched.append(dict(h=2 * hp + par, it=it, jb=jb, par=par,
                                      first=(j == 0), last=(jb == 0)))
    N = len(sched)
    lcar = [0, 0]
    pool = 0
    for s in sched:
        par = s["par"]
        if s["first"]:
            lcar[par] = 0
        s["l"] = lcar[par]
        s["pool0"] = pool
        if not s["last"]:
            pool += 2
            lcar[par] += 1
        s["cr_end"] = pool
    pe = 0
    for st in range(N + 3):
        if st < N:
            pe += 1
            sched[st]["z_end"] = pe
        if st >= 2 and st - 2 < N:
            s = sched[st - 2]
            pe += 2 if s["first"] else 3
            s["s_end"] = pe
            pe += 1
            s["cs_end"] = pe
        if st >= 3 and st - 3 < N:
            pe += 1
            sched[st - 3]["o_end"] = pe
    act = 0
    for st in range(N + 4):
        if 1 <= st <= N:
            s = sched[st - 1]
            act += 1
            s["u_end"] = act
            act += 1
            s["sp_end"] = act
        if 2 <= st <= N + 1:
            act += 1
            sched[st - 2]["at_end"] = act
        if 4 <= st <= N + 3:
            s = sched[st - 4]
            if s["last"]:
                act += 1
                s["oev_end"] = act
    prev = [0, 0]
    for s in sched:
        s["oev_prev"] = prev[s["par"]]
        if s["last"]:
            prev[s["par"]] = s["oev_end"]
    dve = 0
    for st in range(N + 2):
        if 1 <= st <= N:
            s = sched[st - 1]
            dve += 1
            s["za_end"] = dve
            dve += 1
            s["zb_end"] = dve
        if 2 <= st <= N + 1:
            s = sched[st - 2]
            if not s["last"]:
                dve += 1
            s["cv_end"] = dve
    return sched


def _emit_p6(nc, T, ph):
    wq, wk, wv, wo = T["wq"], T["wk"], T["wv"], T["wo"]
    xn1_fm, s1_fm, C = T["xn1_fm"], T["s1_fm"], T
    bfw = [nc.dram_tensor(n, [DM, DM], BF16, kind="Internal").ap()
           for n in ("wq_bf", "wk_bf", "wv_bf", "wo_bf")]
    qkvf = [nc.dram_tensor(n, [DM, S], BF16, kind="Internal").ap()
            for n in ("q_fm", "k_fm", "v_fm")]
    sched = _p6_sched()

    if True:
        e = ph.st.enter_context
        xsl = [[e(_sbt(nc, ph, f"xsl{j}_{i}", [128, 512], BF16)) for i in range(16)]
               for j in range(2)]
        ofm = [e(_sbt(nc, ph, f"ofm{i}", [128, S], BF16)) for i in range(16)]
        wbuf = [e(_sbt(nc, ph, f"wbuf{i}", [128, 16 * 512], BF16)) for i in range(2)]
        evb = [e(_sbt(nc, ph, f"evb{i}", [128, 512], BF16)) for i in range(4)]
        qh = [e(_sbt(nc, ph, f"qh{i}", [64, S], BF16)) for i in range(4)]
        kh = [e(_sbt(nc, ph, f"kh{i}", [64, S], BF16)) for i in range(4)]
        vtb = [e(_sbt(nc, ph, f"vtb{i}", [128, 16 * 64], BF16, is_transpose_dest=True))
               for i in range(4)]
        za = [e(_sbt(nc, ph, f"za{i}", [128, 512], F32)) for i in range(2)]
        zbf = [e(_sbt(nc, ph, f"zbf{i}", [128, 512], BF16)) for i in range(2)]
        ub = [e(_sbt(nc, ph, f"ub{i}", [128, 512], F32)) for i in range(2)]
        spb = [e(_sbt(nc, ph, f"spb{i}", [128, 512], BF16)) for i in range(2)]
        atb = [e(_sbt(nc, ph, f"atb{i}", [128, 512], BF16)) for i in range(2)]
        cbr = [e(_sbt(nc, ph, f"cbr{i}", [1, 512], BF16)) for i in range(2)]
        colev = [e(_sbt(nc, ph, f"colev{i}", [1, 512], F32)) for i in range(2)]
        carf = [[e(_sbt(nc, ph, f"carf{i}_{k}", [1, 512], F32)) for k in range(2)]
                for i in range(2)]
        mskm = [e(_sbt(nc, ph, f"mskm{i}", [128, 512], BF16)) for i in range(4)]
        ntri = e(_sbt(nc, ph, "ntri", [128, 128], BF16))
        orow = e(_sbt(nc, ph, "orow", [1, 128], BF16))
        ncol = e(_sbt(nc, ph, "ncol", [128, 1], BF16))
        idb6 = e(_sbt(nc, ph, "idb6", [128, 128], BF16))
        pp = [e(_pst(nc, ph, f"pp{i}", [128, 512], F32)) for i in range(8)]

        cst = ph.pipe("cst", 16)
        gld = ph.pipe("gld", 16)
        g_eng = ph.pipe("g_eng")
        xlp = ph.spipe("xlp", 2)
        swp = ph.spipe("swp", 2)
        p_q = ph.pipe("p_q")
        a_qv = ph.pipe("a_qv")
        st_qv = ph.spipe("st_qv", 4)
        qldp = ph.spipe("qldp", 4)
        p_at = ph.pipe("p_at")
        a_at = ph.pipe("a_at")
        d_at = ph.pipe("d_at")
        g_cr = ph.pipe("g_cr")
        d_msk = ph.pipe("d_msk")
        blk = ph.wrap(e(nc.Block()))

        NG3 = 3 * 4 * 16
        last_of_head = {}
        for s_ in sched:
            last_of_head[s_["h"]] = s_

        @blk.gpsimd
        def _(g):
            for wi in range(4):
                cst.inc(g.dma_start(bfw[wi], [wq, wk, wv, wo][wi]))
            gld.inc(g.dma_start(ntri[:, :], C["ntri_bf"]))
            gld.inc(g.dma_start(orow[:, :], C["ones_row_bf"]))
            gld.inc(g.dma_start(ncol[:, :], C["negcol_bf"]))
            gld.inc(g.dma_start(idb6[:, :], C["ident_bf"]))
            g_eng.inc(g.iota(za[0][:, :], [[1, 512]], base=0, channel_multiplier=-1,
                             allow_small_or_imprecise_dtypes=True))
            for tt in range(4):
                if tt >= 2:
                    p_q.wait(g, 48 * (tt - 1))
                for ic in range(16):
                    xlp.inc(g.dma_start(
                        xsl[tt % 2][ic][:, :],
                        xn1_fm[ic * 128:(ic + 1) * 128, tt * 512:(tt + 1) * 512]), tt)
                for gi in range(tt * 48, (tt + 1) * 48):
                    rem = gi % 48
                    wi, ob, oc4 = rem // 16, (rem % 16) // 4, rem % 4
                    a_qv.wait(g, gi + 1)
                    st_qv.inc(g.dma_start(
                        qkvf[wi][ob * 512 + oc4 * 128: ob * 512 + (oc4 + 1) * 128,
                                 tt * 512:(tt + 1) * 512],
                        evb[gi % 4][:, :]), gi)
            for i, s_ in enumerate(sched):
                if s_["last"]:
                    continue
                par, l = s_["par"], s_["l"]
                d_at.wait(g, s_["cv_end"])
                g_cr.wait(g, s_["pool0"])
                if l == 0:
                    g_cr.inc(g.tensor_copy(carf[par][0][:, :], colev[par][:, :]))
                else:
                    g_cr.inc(g.tensor_tensor(carf[par][l % 2][:, :],
                                             carf[par][(l + 1) % 2][:, :],
                                             colev[par][:, :], ALU.add))
                g_cr.wait(g, s_["pool0"] + 1)
                g_cr.inc(g.tensor_copy(cbr[par][:, :], carf[par][l % 2][:, :]))
            for gi in range(64):
                ob, tt, oc4 = gi // 16, (gi % 16) // 4, gi % 4
                a_qv.wait(g, NG3 + gi + 1)
                st_qv.inc(g.dma_start(
                    s1_fm[ob * 512 + oc4 * 128: ob * 512 + (oc4 + 1) * 128,
                          tt * 512:(tt + 1) * 512],
                    evb[(NG3 + gi) % 4][:, :]), NG3 + gi)
            st_qv.wait_all(g, (NG3 + 64) // 4)

        @blk.sync
        def _(s):
            cst.wait(s, 4)
            for w in range(48):
                tt, gw = w // 12, w % 12
                wi, ob = gw // 4, gw % 4
                if w >= 2:
                    p_q.wait(s, 4 * (w - 1))
                for ic in range(16):
                    swp.inc(s.dma_start(
                        wbuf[w % 2][:, ic * 512:(ic + 1) * 512],
                        bfw[wi][ic * 128:(ic + 1) * 128, ob * 512:(ob + 1) * 512]), w)
            st_qv.wait_all(s, NG3 // 4)
            for h in range(H1):
                if h >= 4:
                    p_at.wait(s, last_of_head[h - 4]["o_end"])
                base = h * 64
                qldp.inc(s.dma_start(qh[h % 4][:, :], qkvf[0][base:base + 64, :]), h)
                qldp.inc(s.dma_start(kh[h % 4][:, :], qkvf[1][base:base + 64, :]), h)
                for jc in range(16):
                    qldp.inc(s.dma_start_transpose(
                        vtb[h % 4][:, jc * 64:(jc + 1) * 64],
                        qkvf[2][base:base + 64, jc * 128:(jc + 1) * 128]), h)
            for ob in range(4):
                w = 48 + ob
                if ob < 2:
                    p_q.wait(s, 4 * (w - 1))
                else:
                    p_q.wait(s, NG3 + 16 * (ob - 1))
                for ic in range(16):
                    swp.inc(s.dma_start(
                        wbuf[w % 2][:, ic * 512:(ic + 1) * 512],
                        bfw[3][ic * 128:(ic + 1) * 128, ob * 512:(ob + 1) * 512]), w)

        @blk.tensor
        def _(t):
            for gi in range(NG3):
                tt, rem = gi // 48, gi % 48
                wi, ob, oc4 = rem // 16, (rem % 16) // 4, rem % 4
                w = tt * 12 + wi * 4 + ob
                swp.wait(t, w, 16 * (w // 2 + 1))
                xlp.wait(t, tt, 16 * (tt // 2 + 1))
                if gi >= 4:
                    a_qv.wait(t, gi - 3)
                for ic in range(16):
                    p_q.inc(t.matmul(pp[gi % 4][:, :],
                                     wbuf[w % 2][:, ic * 512 + oc4 * 128: ic * 512 + (oc4 + 1) * 128],
                                     xsl[tt % 2][ic][:, :],
                                     start=(ic == 0), stop=(ic == 15)),
                            n=(1 if ic == 15 else 0))
            gld.wait(t, 4)
            d_msk.wait(t, 4)
            a_qv.wait(t, NG3)
            NSC = len(sched)
            for st in range(NSC + 3):
                if st < NSC:
                    s_ = sched[st]
                    h, it, jb, par = s_["h"], s_["it"], s_["jb"], s_["par"]
                    prev2 = sched[st - 2] if st >= 2 else None
                    qldp.wait(t, h, 18 * (h // 4 + 1))
                    if prev2 is not None:
                        d_at.wait(t, prev2["za_end"])
                    p_at.inc(t.matmul(pp[par * 4][:, :],
                                      kh[h % 4][:, jb * 128:(jb + 1) * 128],
                                      qh[h % 4][:, it * 512:it * 512 + 512],
                                      start=True, stop=True))
                if st >= 2 and st - 2 < NSC:
                    s_ = sched[st - 2]
                    par = s_["par"]
                    prev2 = sched[st - 4] if st >= 4 else None
                    bx, bc = par * 4 + 1, par * 4 + 3
                    a_at.wait(t, s_["sp_end"])
                    d_at.wait(t, s_["zb_end"])
                    if prev2 is not None:
                        a_at.wait(t, prev2["at_end"])
                        if not s_["first"]:
                            g_cr.wait(t, prev2["cr_end"])
                    p_at.inc(t.matmul(pp[bx][:, :], ntri[:, :], spb[par][:, :],
                                      start=True, stop=False))
                    p_at.inc(t.matmul(pp[bx][:, :], idb6[:, :], zbf[par][:, :],
                                      start=False, stop=s_["first"]))
                    if not s_["first"]:
                        p_at.inc(t.matmul(pp[bx][:, :], orow[:, :], cbr[par][:, :],
                                          start=False, stop=True))
                    if prev2 is not None:
                        d_at.wait(t, prev2["cv_end"])
                    p_at.inc(t.matmul(pp[bc][0:1, :], ncol[:, :], spb[par][:, :],
                                      start=True, stop=True))
                if st >= 3 and st - 3 < NSC:
                    s_ = sched[st - 3]
                    h, it, jb, par = s_["h"], s_["it"], s_["jb"], s_["par"]
                    a_at.wait(t, s_["at_end"])
                    if s_["first"]:
                        a_at.wait(t, s_["oev_prev"])
                    p_at.inc(t.matmul(pp[par * 4 + 2][0:64, :],
                                      vtb[h % 4][:, jb * 64:(jb + 1) * 64],
                                      atb[par][:, :],
                                      start=s_["first"], stop=s_["last"]))
            d_at.wait(t, sched[-1]["cv_end"])
            a_at.wait(t, max(x.get("oev_end", 0) for x in sched))
            for gi in range(64):
                ob, tt, oc4 = gi // 16, (gi % 16) // 4, gi % 4
                w = 48 + ob
                swp.wait(t, w, 16 * (w // 2 + 1))
                a_qv.wait(t, NG3 + gi - 3)
                for ic in range(16):
                    p_q.inc(t.matmul(pp[oc4][:, :],
                                     wbuf[w % 2][:, ic * 512 + oc4 * 128: ic * 512 + (oc4 + 1) * 128],
                                     ofm[ic][:, tt * 512:(tt + 1) * 512],
                                     start=(ic == 0), stop=(ic == 15)),
                            n=(1 if ic == 15 else 0))

        @blk.vector
        def _(v):
            g_eng.wait(v, 1)
            for r in range(4):
                d_msk.inc(v.tensor_scalar(mskm[r][:, :], za[0][:, :], 128.0 * r + 0.5,
                                          -10000.0, ALU.is_le, ALU.mult))
            d_msk.wait(v, 4)
            NSC = len(sched)
            for st in range(NSC + 2):
                if 1 <= st <= NSC:
                    s_ = sched[st - 1]
                    par, it, jb = s_["par"], s_["it"], s_["jb"]
                    prev2 = sched[st - 3] if st >= 3 else None
                    rel = jb * 128 - it * 512
                    bz = par * 4
                    p_at.wait(v, s_["z_end"])
                    if prev2 is not None:
                        a_at.wait(v, prev2["u_end"])
                    if rel >= 0:
                        d_at.inc(v.scalar_tensor_tensor(za[par][:, :], pp[bz][:, :], 80.0,
                                                        mskm[rel // 128][:, :],
                                                        ALU.min, ALU.add))
                    else:
                        d_at.inc(v.tensor_single_scalar(za[par][:, :], pp[bz][:, :],
                                                        80.0, ALU.min))
                    d_at.wait(v, s_["za_end"])
                    if prev2 is not None:
                        p_at.wait(v, prev2["s_end"])
                    d_at.inc(v.tensor_copy(zbf[par][:, :], za[par][:, :]))
                if 2 <= st <= NSC + 1:
                    s_ = sched[st - 2]
                    par = s_["par"]
                    if not s_["last"]:
                        p_at.wait(v, s_["cs_end"])
                        g_cr.wait(v, s_["pool0"])
                        d_at.inc(v.tensor_copy(colev[par][:, :], pp[par * 4 + 3][0:1, :]))

        @blk.scalar
        def _(sc):
            for gi in range(NG3):
                wi = (gi % 48) // 16
                p_q.wait(sc, gi + 1)
                if gi >= 4:
                    st_qv.wait(sc, gi, gi // 4)
                if wi == 0:
                    a_qv.inc(sc.activation(evb[gi % 4][:, :], pp[gi % 4][:, :], AF.Copy,
                                           scale=SC1))
                else:
                    a_qv.inc(sc.copy(evb[gi % 4][:, :], pp[gi % 4][:, :]))
            NSC = len(sched)
            for st in range(NSC + 4):
                if 1 <= st <= NSC:
                    s_ = sched[st - 1]
                    par = s_["par"]
                    prev2 = sched[st - 3] if st >= 3 else None
                    d_at.wait(sc, s_["za_end"])
                    if prev2 is not None:
                        a_at.wait(sc, prev2["sp_end"])
                    a_at.inc(sc.activation(ub[par][:, :], za[par][:, :], AF.Exp))
                    a_at.wait(sc, s_["u_end"])
                    if prev2 is not None:
                        p_at.wait(sc, prev2["cs_end"])
                    a_at.inc(sc.activation(spb[par][:, :], ub[par][:, :], AF.Ln, bias=1.0))
                if 2 <= st <= NSC + 1:
                    s_ = sched[st - 2]
                    par = s_["par"]
                    p_at.wait(sc, s_["s_end"])
                    a_at.inc(sc.activation(atb[par][:, :], pp[par * 4 + 1][:, :], AF.Exp))
                if 4 <= st <= NSC + 3:
                    s_ = sched[st - 4]
                    if s_["last"]:
                        h, it, par = s_["h"], s_["it"], s_["par"]
                        p_at.wait(sc, s_["o_end"])
                        fc, off = h // 2, (h % 2) * 64
                        a_at.inc(sc.copy(
                            ofm[fc][off:off + 64, it * 512:(it + 1) * 512],
                            pp[par * 4 + 2][0:64, :]))
            for gi in range(64):
                p_q.wait(sc, NG3 + gi + 1)
                st_qv.wait(sc, NG3 + gi, (NG3 + gi) // 4)
                a_qv.inc(sc.copy(evb[(NG3 + gi) % 4][:, :], pp[gi % 4][:, :]))


def _emit_p5a(nc, T, ph):
    sw1, sw2, w12, combT = T["sw1"], T["sw2"], T["w12"], T["combT"]
    xn0_tm, hsh_fm, hrt_fm, C = T["xn0_tm"], T["hsh_fm"], T["hrt_fm"], T
    sw1_bf = nc.dram_tensor("sw1_bf", [HS, DM], BF16, kind="Internal").ap()
    sw2_bf = nc.dram_tensor("sw2_bf", [HS, DM], BF16, kind="Internal").ap()
    w12_bf = nc.dram_tensor("w12_bf", [E0 * 2 * HR, DM], BF16, kind="Internal").ap()
    NQ = 32

    if True:
        e = ph.st.enter_context
        xn = [e(_sbt(nc, ph, f"xn{i}", [128, S], BF16)) for i in range(16)]
        wb1 = [e(_sbt(nc, ph, f"wb1{i}", [128, 16 * 512], BF16, is_transpose_dest=True)) for i in range(2)]
        wb2 = [e(_sbt(nc, ph, f"wb2{i}", [128, 16 * 512], BF16, is_transpose_dest=True)) for i in range(2)]
        sg = [e(_sbt(nc, ph, f"sg{i}", [128, 512], F32)) for i in range(4)]
        tm = [e(_sbt(nc, ph, f"tm{i}", [128, 512], F32)) for i in range(4)]
        hout = [e(_sbt(nc, ph, f"hout{i}", [128, 512], BF16)) for i in range(4)]
        cb = e(_sbt(nc, ph, "cb", [128, 32 * 512], BF16))
        ctb = [e(_sbt(nc, ph, f"ctb{i}", [1, S], BF16)) for i in range(2)]
        orow = e(_sbt(nc, ph, "orow", [1, 128], BF16))
        psG = [e(_pst(nc, ph, f"psG{i}", [128, 512], F32)) for i in range(4)]
        psU = [e(_pst(nc, ph, f"psU{i}", [128, 512], F32)) for i in range(4)]

        c1 = ph.pipe("c1", 16)
        c2 = ph.pipe("c2", 16)
        c12 = ph.pipe("c12", 16)
        gld = ph.pipe("gld", 16)
        ctp5 = ph.spipe("ctp5", 2)
        sxn = ph.pipe("sxn", 16)
        sw = ph.spipe("swl", 2)
        p_cb = ph.pipe("p_cb")
        d_cb = ph.pipe("d_cb")
        p_mm = ph.pipe("p_mm")
        a_ev = ph.pipe("a_ev")
        d_h = ph.pipe("d_h")
        stp = ph.spipe("stp", 4)
        cda = ph.chain("cda5")
        blk = ph.wrap(e(nc.Block()))

        def q_rows(q):
            if q < 16:
                return (sw1_bf, q * 512, sw2_bf, q * 512, hsh_fm, q * 512)
            r = q - 16
            ee, hb1 = r // 2, r % 2
            return (w12_bf, ee * 2048 + hb1 * 512, w12_bf, ee * 2048 + 1024 + hb1 * 512,
                    hrt_fm, ee * 1024 + hb1 * 512)

        @blk.gpsimd
        def _(g):
            for c in range(4):
                c1.inc(g.dma_start(sw1_bf[c * 2048:(c + 1) * 2048, :],
                                   sw1[c * 2048:(c + 1) * 2048, :]))
            for c in range(4):
                c2.inc(g.dma_start(sw2_bf[c * 2048:(c + 1) * 2048, :],
                                   sw2[c * 2048:(c + 1) * 2048, :]))
            for c in range(8):
                c12.inc(g.dma_start(w12_bf[c * 2048:(c + 1) * 2048, :],
                                    w12[c * 2048:(c + 1) * 2048, :]))
            gld.inc(g.dma_start(orow[:, :], C["ones_row_bf"]))
            for ee in range(E0):
                if ee >= 2:
                    p_cb.wait(g, 4 * (ee - 1))
                ctp5.inc(g.dma_start(ctb[ee % 2][:, :], combT[ee:ee + 1, :]), ee)
            for q in range(NQ):
                for tt in range(4):
                    for oc4 in range(4):
                        s_ = q * 16 + tt * 4 + oc4
                        d_h.wait(g, s_ + 1)
                        h0a, b0, h1a, b1, outh, ob = q_rows(q)
                        stp.inc(g.dma_start(
                            outh[ob + oc4 * 128: ob + (oc4 + 1) * 128,
                                 tt * 512:(tt + 1) * 512],
                            hout[s_ % 4][:, :]), s_)
            stp.wait_all(g, NQ * 16 // 4)

        @blk.sync
        def _(s):
            for ic in range(16):
                sxn.inc(s.dma_start_transpose(xn[ic][:, :], xn0_tm[:, ic * 128:(ic + 1) * 128]))
            c1.wait(s, 4)
            c2.wait(s, 4)
            for q in range(NQ):
                if q == 16:
                    c12.wait(s, 8)
                if q >= 2:
                    p_mm.wait(s, 32 * (q - 1))
                h0a, b0, h1a, b1, _, _ = q_rows(q)
                for ic in range(16):
                    sw.inc(s.dma_start_transpose(
                        wb1[q % 2][:, ic * 512:(ic + 1) * 512],
                        h0a[b0:b0 + 512, ic * 128:(ic + 1) * 128]), q)
                for ic in range(16):
                    sw.inc(s.dma_start_transpose(
                        wb2[q % 2][:, ic * 512:(ic + 1) * 512],
                        h1a[b1:b1 + 512, ic * 128:(ic + 1) * 128]), q)

        @blk.tensor
        def _(t):
            gld.wait(t, 1)
            for k in range(32):
                ee, tt = k // 4, k % 4
                ctp5.wait(t, ee, ee // 2 + 1)
                if k >= 2:
                    d_cb.wait(t, k - 1)
                p_cb.inc(t.matmul(psG[k % 2][:, :], orow[:, :],
                                  ctb[ee % 2][:, tt * 512:(tt + 1) * 512],
                                  start=True, stop=True))
            sxn.wait(t, 16)
            d_cb.wait(t, 32)
            for q in range(NQ):
                sw.wait(t, q, 32 * (q // 2 + 1))
                for tt in range(4):
                    for half in range(2):
                        wb = wb1 if half == 0 else wb2
                        ps = psG if half == 0 else psU
                        for oc4 in range(4):
                            u = q * 32 + tt * 8 + half * 4 + oc4
                            s_ = q * 16 + tt * 4 + oc4
                            if half == 0:
                                a_ev.wait(t, s_ - 3)
                            else:
                                d_h.wait(t, s_ - 3)
                            for ic in range(16):
                                p_mm.inc(t.matmul(
                                    ps[oc4][:, :],
                                    wb[q % 2][:, ic * 512 + oc4 * 128: ic * 512 + (oc4 + 1) * 128],
                                    xn[ic][:, tt * 512:(tt + 1) * 512],
                                    start=(ic == 0), stop=(ic == 15)),
                                    n=(1 if ic == 15 else 0))

        @blk.scalar
        def _(sc):
            for q in range(NQ):
                for tt in range(4):
                    for oc4 in range(4):
                        s_ = q * 16 + tt * 4 + oc4
                        p_mm.wait(sc, q * 32 + tt * 8 + oc4 + 1)
                        if s_ >= 4:
                            d_h.wait(sc, s_ - 3)
                        a_ev.inc(sc.activation(sg[s_ % 4][:, :], psG[oc4][:, :], AF.Silu))

        @blk.vector
        def _(v):
            for k in range(32):
                p_cb.wait(v, k + 1)
                d_cb.inc(v.tensor_copy(cb[:, k * 512:(k + 1) * 512], psG[k % 2][:, :]))
            d_cb.wait(v, 32)
            for q in range(NQ):
                for tt in range(4):
                    for oc4 in range(4):
                        s_ = q * 16 + tt * 4 + oc4
                        u_u = q * 32 + tt * 8 + 4 + oc4
                        p_mm.wait(v, u_u + 1)
                        a_ev.wait(v, s_ + 1)
                        if s_ >= 4:
                            stp.wait(v, s_, s_ // 4)
                        if q < 16:
                            d_h.inc(v.tensor_tensor(hout[s_ % 4][:, :], sg[s_ % 4][:, :],
                                                    psU[oc4][:, :], ALU.mult))
                        else:
                            ee = (q - 16) // 2
                            cda.seal(v.tensor_tensor(tm[s_ % 4][:, :], sg[s_ % 4][:, :],
                                                     psU[oc4][:, :], ALU.mult))
                            cda.gate(v)
                            d_h.inc(v.tensor_tensor(
                                hout[s_ % 4][:, :], tm[s_ % 4][:, :],
                                cb[:, (ee * 4 + tt) * 512:(ee * 4 + tt + 1) * 512],
                                ALU.mult))



def _emit_p5b(nc, T, ph):
    sw3, w3, s0_fm, x0_fm = T["sw3"], T["w3"], T["s0_fm"], T["x0_fm"]
    hsh_fm, hrt_fm, x1_fm, xn1_fm, C = (T["hsh_fm"], T["hrt_fm"], T["x1_fm"],
                                        T["xn1_fm"], T)
    sw3_bf = nc.dram_tensor("sw3_bf", [DM, HS], BF16, kind="Internal").ap()
    w3_bf = nc.dram_tensor("w3_bf", [E0 * DM, HR], BF16, kind="Internal").ap()

    if True:
        e = ph.st.enter_context
        hsl = [e(_sbt(nc, ph, f"hsl{i}", [128, 512], BF16)) for i in range(64)]
        rsl = [e(_sbt(nc, ph, f"rsl{i}", [128, 512], BF16)) for i in range(64)]
        s0t = [e(_sbt(nc, ph, f"s0t{i}", [128, 512], BF16)) for i in range(4)]
        x0t = [e(_sbt(nc, ph, f"x0t{i}", [128, 512], F32)) for i in range(4)]
        x1sl = [e(_sbt(nc, ph, f"x1sl{i}", [128, 512], F32)) for i in range(4)]
        x1r = [e(_sbt(nc, ph, f"x1r{i}", [128, 512], F32)) for i in range(4)]
        wt = [e(_sbt(nc, ph, f"wt{i}", [128, 512], BF16, is_transpose_dest=True)) for i in range(4)]
        gbuf = [e(_sbt(nc, ph, f"gbuf{i}", [128, 512], F32)) for i in range(2)]
        sgb = [e(_sbt(nc, ph, f"sgb{i}", [128, 512], F32)) for i in range(2)]
        sqb = [e(_sbt(nc, ph, f"sqb{i}", [128, 512], F32)) for i in range(2)]
        xnout = [e(_sbt(nc, ph, f"xnout{i}", [128, 512], BF16)) for i in range(4)]
        aptmp = [e(_sbt(nc, ph, f"aptmp{i}", [128, 512], F32)) for i in range(2)]
        onesf = e(_sbt(nc, ph, "onesf", [1, 128], F32))
        onesc = e(_sbt(nc, ph, "onesc", [128, 1], F32))
        murow = e(_sbt(nc, ph, "murow", [1, 512], F32))
        varow = e(_sbt(nc, ph, "varow", [1, 512], F32))
        va2 = e(_sbt(nc, ph, "va2", [1, 512], F32))
        var3 = e(_sbt(nc, ph, "var3", [1, 512], F32))
        lnro = e(_sbt(nc, ph, "lnro", [1, 512], F32))
        rstdr = e(_sbt(nc, ph, "rstdr", [1, 512], F32))
        biasr = e(_sbt(nc, ph, "biasr", [1, 512], F32))
        epsr = e(_sbt(nc, ph, "epsr", [1, 1], F32))
        ps = [e(_pst(nc, ph, f"ps{i}", [128, 512], F32)) for i in range(4)]
        psS = e(_pst(nc, ph, "psS", [1, 512], F32))
        psQ = e(_pst(nc, ph, "psQ", [1, 512], F32))
        psB = [e(_pst(nc, ph, f"psB{i}", [128, 512], F32)) for i in range(2)]

        gcl = ph.pipe("gcl", 16)
        c3 = ph.pipe("c3", 16)
        cw3 = ph.pipe("cw3", 16)
        hld = ph.spipe("hld", 16)
        as0 = ph.spipe("as0", 4)
        ax0 = ph.spipe("ax0", 4)
        ax1 = ph.spipe("ax1", 4)
        swl = ph.spipe("swl", 4)
        p_mm = ph.pipe("p_mm")
        a_g = ph.pipe("a_g")
        d_x = ph.pipe("d_x")
        a_q = ph.pipe("a_q")
        p_st = ph.pipe("p_st")
        d_row = ph.pipe("d_row")
        a_row = ph.pipe("a_row")
        p_bc = ph.pipe("p_bc")
        d_ap = ph.pipe("d_ap")
        st1 = ph.spipe("st1", 4)
        stn = ph.spipe("stn", 4)
        cdv = ph.chain("cdvb")
        cac = ph.chain("cacb")
        blk = ph.wrap(e(nc.Block()))

        NIC = 64 + 64

        @blk.gpsimd
        def _(g):
            g.memset(epsr[:, :], EPS)
            gcl.inc(g.dma_start(onesf[:, :], C["onesf_row"]))
            gcl.inc(g.dma_start(onesc[:, :], C["onesf_col"]))
            for c in range(4):
                c3.inc(g.dma_start(sw3_bf[c * 512:(c + 1) * 512, :],
                                   sw3[c * 512:(c + 1) * 512, :]))
            for c in range(8):
                cw3.inc(g.dma_start(w3_bf[c * 2048:(c + 1) * 2048, :],
                                    w3[c * 2048:(c + 1) * 2048, :]))
            for tt in range(4):
                for j in range(128):
                    gate = 2048 * tt
                    if j >= 16:
                        gate += (j - 15) * 4
                    p_mm.wait(g, gate)
                    tgt = hsl[j] if j < 64 else rsl[j - 64]
                    srcd = hsh_fm if j < 64 else hrt_fm
                    ic = j if j < 64 else j - 64
                    hld.inc(g.dma_start(tgt[:, :],
                                        srcd[ic * 128:(ic + 1) * 128, tt * 512:(tt + 1) * 512]),
                            tt * 128 + j)
                for oi in range(20):
                    if oi < 16:
                        k = tt * 16 + oi
                        d_x.wait(g, k - 3)
                        as0.inc(g.dma_start(s0t[k % 4][:, :],
                                            s0_fm[oi * 128:(oi + 1) * 128, tt * 512:(tt + 1) * 512]), k)
                        ax0.inc(g.dma_start(x0t[k % 4][:, :],
                                            x0_fm[oi * 128:(oi + 1) * 128, tt * 512:(tt + 1) * 512]), k)
                    if oi >= 4:
                        ocg = oi - 4
                        k = tt * 16 + ocg
                        d_x.wait(g, k + 1)
                        st1.inc(g.dma_start(
                            x1_fm[ocg * 128:(ocg + 1) * 128, tt * 512:(tt + 1) * 512],
                            x1sl[k % 4][:, :]), k)
                for ocg in range(16):
                    k = tt * 16 + ocg
                    d_ap.wait(g, k - 3)
                    st1.wait(g, k, k // 4 + 1)
                    ax1.inc(g.dma_start(
                        x1r[k % 4][:, :],
                        x1_fm[ocg * 128:(ocg + 1) * 128, tt * 512:(tt + 1) * 512]), k)
                    d_ap.wait(g, k + 1)
                    stn.inc(g.dma_start(
                        xn1_fm[ocg * 128:(ocg + 1) * 128, tt * 512:(tt + 1) * 512],
                        xnout[k % 4][:, :]), k)
            st1.wait_all(g, 16)
            stn.wait_all(g, 16)

        @blk.sync
        def _(s):
            c3.wait(s, 4)
            cw3.wait(s, 8)
            wcount = 0
            for tt in range(4):
                for ob in range(4):
                    for icg in range(NIC):
                        if wcount >= 4:
                            p_mm.wait(s, (wcount - 3) * 4)
                        if icg < 64:
                            src = sw3_bf[ob * 512:(ob + 1) * 512, icg * 128:(icg + 1) * 128]
                        else:
                            icr = icg - 64
                            ee, rr = icr // 8, icr % 8
                            src = w3_bf[ee * 2048 + ob * 512: ee * 2048 + (ob + 1) * 512,
                                        rr * 128:(rr + 1) * 128]
                        swl.inc(s.dma_start_transpose(wt[wcount % 4][:, :], src), wcount)
                        wcount += 1

        @blk.tensor
        def _(t):
            wcount = 0
            for tt in range(4):
                for ob in range(4):
                    for oc4 in range(4):
                        s_ = (tt * 4 + ob) * 4 + oc4
                        if s_ >= 4:
                            a_g.wait(t, s_ - 3)
                    for icg in range(NIC):
                        swl.wait(t, wcount, wcount // 4 + 1)
                        jj = tt * 128 + icg
                        hld.wait(t, jj, jj // 16 + 1)
                        rhs = hsl[icg] if icg < 64 else rsl[icg - 64]
                        for oc4 in range(4):
                            p_mm.inc(t.matmul(
                                ps[oc4][:, :],
                                wt[wcount % 4][:, oc4 * 128:(oc4 + 1) * 128],
                                rhs[:, :],
                                start=(icg == 0), stop=(icg == NIC - 1)))
                        wcount += 1
                    for oc4 in range(4):
                        ocg = ob * 4 + oc4
                        k = tt * 16 + ocg
                        d_x.wait(t, k + 1)
                        if ocg == 0:
                            gcl.wait(t, 2)
                            d_row.wait(t, 2 * tt)
                            d_ap.wait(t, 16 * tt)
                        p_st.inc(t.matmul(psS[:, :], onesc[:, :], x1sl[k % 4][:, :],
                                          start=(ocg == 0), stop=(ocg == 15)))
                        a_q.wait(t, k + 1)
                        p_st.inc(t.matmul(psQ[:, :], onesc[:, :], sqb[ocg % 2][:, :],
                                          start=(ocg == 0), stop=(ocg == 15)))
                a_row.wait(t, tt + 1)
                d_row.wait(t, 2 * (tt + 1))
                p_bc.inc(t.matmul(psB[0][:, :], onesf[:, :], rstdr[:, :], start=True, stop=True))
                p_bc.inc(t.matmul(psB[1][:, :], onesf[:, :], biasr[:, :], start=True, stop=True))

        @blk.scalar
        def _(sc):
            for tt in range(4):
                for si in range(17):
                    if si >= 1:
                        ocg = si - 1
                        k = tt * 16 + ocg
                        d_x.wait(sc, k + 1)
                        if k >= 2:
                            p_st.wait(sc, 2 * (k - 1))
                        a_q.inc(sc.activation(sqb[ocg % 2][:, :], x1sl[k % 4][:, :], AF.Square))
                    if si < 16:
                        ob, oc4 = si // 4, si % 4
                        s_ = tt * 16 + si
                        p_mm.wait(sc, ((tt * 4 + ob) * NIC) * 4 + (NIC - 1) * 4 + oc4 + 1)
                        if s_ >= 2:
                            d_x.wait(sc, s_ - 1)
                        a_g.inc(sc.activation(gbuf[s_ % 2][:, :], ps[oc4][:, :], AF.Silu))
                d_row.wait(sc, 2 * tt + 1)
                cac.seal(sc.activation(lnro[:, :], var3[:, :], AF.Ln, bias=epsr[:, :]))
                cac.gate(sc)
                a_row.inc(sc.activation(rstdr[:, :], lnro[:, :], AF.Exp, scale=-0.5))

        @blk.vector
        def _(v):
            for tt in range(4):
                for ob in range(4):
                    for oc4 in range(4):
                        s_ = (tt * 4 + ob) * 4 + oc4
                        ocg = ob * 4 + oc4
                        k = tt * 16 + ocg
                        as0.wait(v, k, k // 4 + 1)
                        ax0.wait(v, k, k // 4 + 1)
                        a_g.wait(v, s_ + 1)
                        if k >= 4:
                            st1.wait(v, k, k // 4)
                            a_q.wait(v, k - 3)
                            p_st.wait(v, 2 * (k - 3))
                        cdv.seal(v.tensor_tensor(sgb[oc4 % 2][:, :], gbuf[s_ % 2][:, :],
                                                 s0t[k % 4][:, :], ALU.mult))
                        cdv.gate(v)
                        d_x.inc(v.tensor_tensor(x1sl[k % 4][:, :], sgb[oc4 % 2][:, :],
                                                x0t[k % 4][:, :], ALU.add))
                p_st.wait(v, 2 * 16 * (tt + 1))
                cdv.seal(v.tensor_scalar_mul(murow[:, :], psS[:, :], 1.0 / DM))
                cdv.seal(v.tensor_scalar_mul(varow[:, :], psQ[:, :], 1.0 / DM))
                cdv.gate(v)
                cdv.seal(v.scalar_tensor_tensor(va2[:, :], murow[:, :], -1.0, murow[:, :],
                                                ALU.mult, ALU.mult))
                cdv.gate(v)
                d_row.inc(v.tensor_tensor(var3[:, :], varow[:, :], va2[:, :], ALU.add))
                a_row.wait(v, tt + 1)
                d_row.inc(v.scalar_tensor_tensor(biasr[:, :], murow[:, :], -1.0, rstdr[:, :],
                                                 ALU.mult, ALU.mult))
                p_bc.wait(v, 2 * (tt + 1))
                for ocg in range(16):
                    k = tt * 16 + ocg
                    ax1.wait(v, k, k // 4 + 1)
                    if k >= 4:
                        stn.wait(v, k, k // 4)
                    cdv.seal(v.tensor_tensor(aptmp[k % 2][:, :], x1r[k % 4][:, :],
                                             psB[0][:, :], ALU.mult))
                    cdv.gate(v)
                    d_ap.inc(v.tensor_tensor(xnout[k % 4][:, :], aptmp[k % 2][:, :],
                                             psB[1][:, :], ALU.add))




def _emit_p7(nc, T, ph):
    wbase, bbase, wrout, aw, bw = T["wbase"], T["bbase"], T["wrout"], T["aw"], T["bw"]
    xn1_fm, x1_fm, s1_fm, out_tm, C = (T["xn1_fm"], T["x1_fm"], T["s1_fm"],
                                       T["out_tm"], T)
    wbase_bf = nc.dram_tensor("wbase_bf", [DM, DM], BF16, kind="Internal").ap()
    a_bfd = nc.dram_tensor("a_bfd", [E1 * R1, DM], BF16, kind="Internal").ap()
    b_bfd = nc.dram_tensor("b_bfd", [E1 * DM, R1], BF16, kind="Internal").ap()
    rout_bfd = nc.dram_tensor("rout_bfd", [E1, DM], BF16, kind="Internal").ap()
    bt_dram = nc.dram_tensor("bt_dram", [E1 * R1, DM], BF16, kind="Internal").ap()
    NTC = 16

    def gsplit(gi):
        ob, tt, oc4 = gi // 16, (gi // 4) % 4, gi % 4
        ocg = ob * 4 + oc4
        return ob, tt, oc4, ocg, tt * 16 + ocg

    if True:
        e = ph.st.enter_context
        xn = [e(_sbt(nc, ph, f"xn{i}", [128, S], BF16)) for i in range(16)]
        wbt = [e(_sbt(nc, ph, f"wbt{i}", [128, 16 * 512], BF16, is_transpose_dest=True)) for i in range(2)]
        idb = e(_sbt(nc, ph, "idb", [128, 128], BF16))
        idf = e(_sbt(nc, ph, "idf", [128, 128], F32))
        rrep_sb = e(_sbt(nc, ph, "rrep_sb", [E1, 64], BF16))
        a_sb = e(_sbt(nc, ph, "a_sb", [E1 * R1, S], BF16))
        A_T = e(_sbt(nc, ph, "A_T", [128, 16 * 64], BF16))
        btmp = e(_sbt(nc, ph, "btmp", [128, 2 * E1], BF16))
        bstg = [e(_sbt(nc, ph, f"bstg{i}", [E1, 128], BF16)) for i in range(2)]
        B_T = e(_sbt(nc, ph, "B_T", [E1 * R1, DM], BF16))
        rsb = e(_sbt(nc, ph, "rsb", [E1, DM], BF16))
        routerT = e(_sbt(nc, ph, "routerT", [128, 128], BF16))
        lg_tm = [e(_sbt(nc, ph, f"lg_tm{i}", [128, E1], F32)) for i in range(2)]
        m1 = e(_sbt(nc, ph, "m1c", [128, 1], F32))
        m2 = e(_sbt(nc, ph, "m2c", [128, 1], F32))
        is1 = e(_sbt(nc, ph, "is1c", [128, E1], F32))
        sel2 = e(_sbt(nc, ph, "sel2c", [128, E1], F32))
        ind = e(_sbt(nc, ph, "indc", [128, E1], F32))
        dcol = e(_sbt(nc, ph, "dcol", [128, 1], F32))
        sgc = e(_sbt(nc, ph, "sgc", [128, 1], F32))
        difc = e(_sbt(nc, ph, "difc", [128, 1], F32))
        wbc = e(_sbt(nc, ph, "wbc", [128, 1], F32))
        t1c = e(_sbt(nc, ph, "t1c", [128, E1], F32))
        w_tm = [e(_sbt(nc, ph, f"w_tm{i}", [128, E1], BF16)) for i in range(2)]
        wT_sb = e(_sbt(nc, ph, "wT_sb", [E1, S], BF16))
        wb_sb = e(_sbt(nc, ph, "wb_sb", [64, S], BF16))
        h_sb = e(_sbt(nc, ph, "h_sb", [64, S], BF16))
        hs_sb = e(_sbt(nc, ph, "hs_sb", [64, S], BF16))
        bbrow = e(_sbt(nc, ph, "bbrow", [1, DM], F32))
        bb_cols = e(_sbt(nc, ph, "bb_cols", [128, 16], F32))
        gb = [e(_sbt(nc, ph, f"gb{i}", [128, 512], F32)) for i in range(2)]
        sgb = [e(_sbt(nc, ph, f"sgb{i}", [128, 512], F32)) for i in range(2)]
        x2b = [e(_sbt(nc, ph, f"x2b{i}", [128, 512], F32)) for i in range(2)]
        s1t = [e(_sbt(nc, ph, f"s1t{i}", [128, 512], BF16)) for i in range(4)]
        x1t = [e(_sbt(nc, ph, f"x1t{i}", [128, 512], F32)) for i in range(4)]
        otb = [e(_sbt(nc, ph, f"otb{i}", [128, 128], F32)) for i in range(4)]
        psc = [e(_pst(nc, ph, f"psc{i}", [128, 512], F32)) for i in range(4)]
        pst = [e(_pst(nc, ph, f"pst{i}", [128, 128], F32)) for i in range(2)]
        ps_tb = e(_pst(nc, ph, "ps_tb", [128, 128], BF16))

        cbs = ph.pipe("cbs", 16)
        cbo = ph.pipe("cbo", 16)
        gl7 = ph.pipe("gl7", 16)
        xl7 = ph.pipe("xl7", 16)
        swb = ph.spipe("swb", 2)
        sx1 = ph.spipe("sx1", 4)
        ss1 = ph.spipe("ss1", 4)
        sto = ph.spipe("sto", 4)
        p_su = ph.pipe("p_su")
        d_su = ph.pipe("d_su")
        a_su = ph.pipe("a_su")
        g_su = ph.pipe("g_su", 16)
        p_mm = ph.pipe("p_mm")
        a_g = ph.pipe("a_g")
        d_x = ph.pipe("d_x")
        p_tp = ph.pipe("p_tp")
        d_ev = ph.pipe("d_ev")
        cdv = ph.chain("cdv7")
        blk = ph.wrap(e(nc.Block()))

        ss = SSched({"pe": p_su, "dve": d_su, "act": a_su, "gp": g_su})

        def mk(fn):
            return fn

        for ic in range(16):
            ss.add("pe", (lambda ic: lambda t: t.transpose(
                ps_tb[:, 0:64], a_sb[:, ic * 128:(ic + 1) * 128], idb[0:64, 0:64]))(ic),
                reads=("a_sb",), writes=("ps_tb",), extra=[(gl7, 6)])
            ss.add("dve", (lambda ic: lambda v: v.tensor_copy(
                A_T[:, ic * 64:(ic + 1) * 64], ps_tb[:, 0:64]))(ic),
                reads=("ps_tb",), writes=("A_T",))
        for j in range(128):
            ee, oc = j // 16, j % 16
            ss.add("gp", (lambda j, ee, oc: lambda g: g.dma_start(
                btmp[:, (j % 2) * E1:(j % 2 + 1) * E1],
                b_bfd[ee * 2048 + oc * 128: ee * 2048 + (oc + 1) * 128, :]))(j, ee, oc),
                reads=(), writes=(f"btmp{j % 2}",), extra=[(cbo, 3)])
            ss.add("pe", (lambda j: lambda t: t.transpose(
                ps_tb[0:E1, 0:128], btmp[:, (j % 2) * E1:(j % 2 + 1) * E1], idb[:, :]))(j),
                reads=(f"btmp{j % 2}",), writes=("ps_tb",))
            ss.add("dve", (lambda j: lambda v: v.tensor_copy(
                bstg[j % 2][:, :], ps_tb[0:E1, 0:128]))(j),
                reads=("ps_tb",), writes=(f"bstg{j % 2}",))
            ss.add("gp", (lambda j, ee, oc: lambda g: g.dma_start(
                bt_dram[ee * R1:(ee + 1) * R1, oc * 128:(oc + 1) * 128],
                bstg[j % 2][:, :]))(j, ee, oc),
                reads=(f"bstg{j % 2}",), writes=("bt_dram",))
        ss.add("gp", lambda g: g.dma_start(B_T[:, :], bt_dram),
               reads=("bt_dram",), writes=("B_T",))
        for ic in range(16):
            ss.add("pe", (lambda ic: lambda t: t.transpose(
                ps_tb[:, 0:E1], rsb[:, ic * 128:(ic + 1) * 128], idb[0:E1, 0:E1]))(ic),
                reads=("rsb",), writes=("ps_tb",), extra=[(gl7, 6)])
            ss.add("dve", (lambda ic: lambda v: v.tensor_copy(
                routerT[:, ic * 8:(ic + 1) * 8], ps_tb[:, 0:E1]))(ic),
                reads=("ps_tb",), writes=("routerT",))
        for ic in range(16):
            ss.add("pe", (lambda ic: lambda t: t.transpose(
                pst[0][:, 0:1], bbrow[:, ic * 128:(ic + 1) * 128], idf[0:1, 0:1]))(ic),
                reads=("bbrow",), writes=("pst0",), extra=[(gl7, 6)])
            ss.add("dve", (lambda ic: lambda v: v.tensor_copy(
                bb_cols[:, ic:ic + 1], pst[0][:, 0:1]))(ic),
                reads=("pst0",), writes=("bb_cols",))

        def lg_group(tc):
            def fn(t):
                for ic in range(15):
                    t.matmul(pst[tc % 2][:, 0:E1], xn[ic][:, tc * 128:(tc + 1) * 128],
                             routerT[:, ic * 8:(ic + 1) * 8],
                             start=(ic == 0), stop=False)
                return t.matmul(pst[tc % 2][:, 0:E1], xn[15][:, tc * 128:(tc + 1) * 128],
                                routerT[:, 15 * 8:16 * 8], start=False, stop=True)
            return fn

        def chain_a(tc):
            def fn(v):
                cdv.seal(v.reduce_max(m1[:, :], lg_tm[tc % 2][:, :], axis=AX.X))
                cdv.gate(v)
                cdv.seal(v.tensor_single_scalar(is1[:, :], lg_tm[tc % 2][:, :], m1[:, 0:1], ALU.is_ge))
                cdv.gate(v)
                cdv.seal(v.scalar_tensor_tensor(sel2[:, :], is1[:, :], -1e9, lg_tm[tc % 2][:, :],
                                                ALU.mult, ALU.add))
                cdv.gate(v)
                cdv.seal(v.reduce_max(m2[:, :], sel2[:, :], axis=AX.X))
                cdv.gate(v)
                cdv.seal(v.tensor_single_scalar(ind[:, :], lg_tm[tc % 2][:, :], m2[:, 0:1], ALU.is_ge))
                cdv.gate(v)
                return v.tensor_tensor(dcol[:, :], m1[:, :], m2[:, :], ALU.subtract)
            return fn

        def chain_b(tc):
            def fn(v):
                cdv.seal(v.tensor_scalar(difc[:, :], sgc[:, :], 4.0, -2.0, ALU.mult, ALU.add))
                cdv.seal(v.tensor_scalar(wbc[:, :], sgc[:, :], -2.0, 2.0, ALU.mult, ALU.add))
                cdv.gate(v)
                cdv.seal(v.tensor_single_scalar(t1c[:, :], ind[:, :], wbc[:, 0:1], ALU.mult))
                cdv.gate(v)
                return v.scalar_tensor_tensor(w_tm[tc % 2][:, :], is1[:, :], difc[:, 0:1],
                                              t1c[:, :], ALU.mult, ALU.add)
            return fn

        for tc in range(NTC):
            ss.add("pe", lg_group(tc), reads=("routerT",), writes=(f"pst{tc % 2}",),
                   extra=[(xl7, 16)])
            ss.add("act", (lambda tc: lambda sc: sc.copy(
                lg_tm[tc % 2][:, :], pst[tc % 2][:, 0:E1]))(tc),
                reads=(f"pst{tc % 2}",), writes=(f"lg{tc % 2}",))
            ss.add("dve", chain_a(tc), reads=(f"lg{tc % 2}",),
                   writes=("dcol", "m1", "m2", "is1", "ind", "sel2"))
            ss.add("act", (lambda tc: lambda sc: sc.activation(
                sgc[:, :], dcol[:, :], AF.Sigmoid))(tc),
                reads=("dcol",), writes=("sgc",))
            ss.add("dve", chain_b(tc), reads=("sgc", "is1", "ind"),
                   writes=(f"w_tm{tc % 2}", "difc", "wbc", "t1c"))
            ss.add("pe", (lambda tc: lambda t: t.transpose(
                ps_tb[0:E1, 0:128], w_tm[tc % 2][:, :], idb[:, :]))(tc),
                reads=(f"w_tm{tc % 2}",), writes=("ps_tb",))
            ss.add("dve", (lambda tc: lambda v: v.tensor_copy(
                wT_sb[:, tc * 128:(tc + 1) * 128], ps_tb[0:E1, 0:128]))(tc),
                reads=("ps_tb",), writes=("wT_sb",))
        for tt in range(4):
            ss.add("pe", (lambda tt: lambda t: t.matmul(
                psc[0][0:64, :], rrep_sb[:, :], wT_sb[:, tt * 512:(tt + 1) * 512],
                start=True, stop=True))(tt),
                reads=("wT_sb",), writes=("psc0",), extra=[(gl7, 6)])
            ss.add("dve", (lambda tt: lambda v: v.tensor_copy(
                wb_sb[:, tt * 512:(tt + 1) * 512], psc[0][0:64, :]))(tt),
                reads=("psc0",), writes=("wb_sb",))

        def h_group(tt):
            def fn(t):
                for ic in range(15):
                    t.matmul(psc[1][0:64, :], A_T[:, ic * 64:(ic + 1) * 64],
                             xn[ic][:, tt * 512:(tt + 1) * 512],
                             start=(ic == 0), stop=False)
                return t.matmul(psc[1][0:64, :], A_T[:, 15 * 64:16 * 64],
                                xn[15][:, tt * 512:(tt + 1) * 512], start=False, stop=True)
            return fn

        for tt in range(4):
            ss.add("pe", h_group(tt), reads=("A_T",), writes=("psc1",), extra=[(xl7, 16)])
            ss.add("act", (lambda tt: lambda sc: sc.copy(
                h_sb[:, tt * 512:(tt + 1) * 512], psc[1][0:64, :]))(tt),
                reads=("psc1",), writes=("h_sb",))
            ss.add("dve", (lambda tt: lambda v: v.tensor_tensor(
                hs_sb[:, tt * 512:(tt + 1) * 512], h_sb[:, tt * 512:(tt + 1) * 512],
                wb_sb[:, tt * 512:(tt + 1) * 512], ALU.mult))(tt),
                reads=("h_sb", "wb_sb"), writes=("hs_sb",))

        NPE, NDVE, NACT, NGP = (ss.counts["pe"], ss.counts["dve"],
                                ss.counts["act"], ss.counts["gp"])

        @blk.gpsimd
        def _(g):
            for c in range(4):
                cbs.inc(g.dma_start(wbase_bf[c * 512:(c + 1) * 512, :],
                                    wbase[c * 512:(c + 1) * 512, :]))
            cbo.inc(g.dma_start(a_bfd, aw))
            cbo.inc(g.dma_start(b_bfd, bw))
            cbo.inc(g.dma_start(rout_bfd, wrout))
            gl7.inc(g.dma_start(idb[:, :], C["ident_bf"]))
            gl7.inc(g.dma_start(idf[:, :], C["ident_f32"]))
            gl7.inc(g.dma_start(rrep_sb[:, :], C["rrep"]))
            gl7.inc(g.dma_start(bbrow[:, :], bbase.rearrange("(b a) -> b a", b=1)))
            cbo.wait(g, 3)
            gl7.inc(g.dma_start(a_sb[:, :], a_bfd))
            gl7.inc(g.dma_start(rsb[:, :], rout_bfd))
            for ic in range(16):
                xl7.inc(g.dma_start(xn[ic][:, :], xn1_fm[ic * 128:(ic + 1) * 128, :]))
            ss.emit("gp", g)
            for gg in range(66):
                if gg < 64:
                    gi = gg
                    ob, tt, oc4, ocg, k = gsplit(gi)
                    d_x.wait(g, gi - 3)
                    ss1.inc(g.dma_start(s1t[gi % 4][:, :],
                                        s1_fm[ocg * 128:(ocg + 1) * 128, tt * 512:(tt + 1) * 512]), gi)
                    sx1.inc(g.dma_start(x1t[gi % 4][:, :],
                                        x1_fm[ocg * 128:(ocg + 1) * 128, tt * 512:(tt + 1) * 512]), gi)
                if gg >= 2:
                    gi = gg - 2
                    ob, tt, oc4, ocg, k = gsplit(gi)
                    for q in range(4):
                        u = gi * 4 + q
                        d_ev.wait(g, u + 1)
                        sto.inc(g.dma_start(
                            out_tm[tt * 512 + q * 128: tt * 512 + (q + 1) * 128,
                                   ocg * 128:(ocg + 1) * 128],
                            otb[u % 4][:, :]), u)
            sto.wait_all(g, 64)

        @blk.sync
        def _(s):
            cbs.wait(s, 4)
            for ob in range(4):
                if ob >= 2:
                    p_mm.wait(s, 17 * 16 * (ob - 1))
                for ic in range(16):
                    swb.inc(s.dma_start_transpose(
                        wbt[ob % 2][:, ic * 512:(ic + 1) * 512],
                        wbase_bf[ob * 512:(ob + 1) * 512, ic * 128:(ic + 1) * 128]), ob)

        @blk.tensor
        def _(t):
            gl7.wait(t, 6)
            ss.emit("pe", t)
            d_su.wait(t, NDVE)
            a_su.wait(t, NACT)
            g_su.wait(t, NGP)
            for gi in range(64):
                ob, tt, oc4, ocg, k = gsplit(gi)
                if gi % 16 == 0:
                    swb.wait(t, ob, 16 * (ob // 2 + 1))
                if gi >= 2:
                    a_g.wait(t, gi - 1)
                for ic in range(16):
                    p_mm.inc(t.matmul(
                        psc[oc4][:, :],
                        wbt[ob % 2][:, ic * 512 + oc4 * 128: ic * 512 + (oc4 + 1) * 128],
                        xn[ic][:, tt * 512:(tt + 1) * 512],
                        start=(ic == 0), stop=False))
                p_mm.inc(t.matmul(psc[oc4][:, :],
                                  B_T[:, ocg * 128:(ocg + 1) * 128],
                                  hs_sb[:, tt * 512:(tt + 1) * 512],
                                  start=False, stop=True))
                if gi >= 2:
                    gp_ = gi - 2
                    d_x.wait(t, gp_ + 1)
                    for q in range(4):
                        u = gp_ * 4 + q
                        if u >= 2:
                            d_ev.wait(t, u - 1)
                        p_tp.inc(t.transpose(pst[u % 2][:, :],
                                             x2b[gp_ % 2][:, q * 128:(q + 1) * 128],
                                             idf[:, :]))
            for gp_ in (62, 63):
                d_x.wait(t, gp_ + 1)
                for q in range(4):
                    u = gp_ * 4 + q
                    d_ev.wait(t, u - 1)
                    p_tp.inc(t.transpose(pst[u % 2][:, :],
                                         x2b[gp_ % 2][:, q * 128:(q + 1) * 128],
                                         idf[:, :]))

        @blk.vector
        def _(v):
            ss.emit("dve", v)
            def tp_evacs(v, gp2):
                for q in range(4):
                    u = gp2 * 4 + q
                    p_tp.wait(v, u + 1)
                    if u >= 4:
                        sto.wait(v, u, u // 4)
                    d_ev.inc(v.tensor_copy(otb[u % 4][:, :], pst[u % 2][:, :]))

            for gi in range(64):
                ob, tt, oc4, ocg, k = gsplit(gi)
                if gi >= 2:
                    tp_evacs(v, gi - 2)
                a_g.wait(v, gi + 1)
                ss1.wait(v, gi, gi // 4 + 1)
                sx1.wait(v, gi, gi // 4 + 1)
                if gi >= 2:
                    p_tp.wait(v, 4 * (gi - 1))
                cdv.seal(v.tensor_tensor(sgb[gi % 2][:, :], gb[gi % 2][:, :],
                                         s1t[gi % 4][:, :], ALU.mult))
                cdv.gate(v)
                d_x.inc(v.tensor_tensor(x2b[gi % 2][:, :], sgb[gi % 2][:, :],
                                        x1t[gi % 4][:, :], ALU.add))
            for gp2 in (62, 63):
                tp_evacs(v, gp2)

        @blk.scalar
        def _(sc):
            ss.emit("act", sc)
            for gi in range(64):
                ob, tt, oc4, ocg, k = gsplit(gi)
                p_mm.wait(sc, gi * 17 + 17)
                if gi >= 2:
                    d_x.wait(sc, gi - 1)
                a_g.inc(sc.activation(gb[gi % 2][:, :], psc[oc4][:, :], AF.Silu,
                                      bias=bb_cols[:, ocg:ocg + 1]))


_C_P1 = ["ident_f32"]
_C_P2 = ["ident_bf", "onesf_row"]
_C_P3 = ["ident_f32", "ident_bf", "onescol64_bf", "onesf_row"]
_C_P5A = ["ones_row_bf"]
_C_P5B = ["onesf_row", "onesf_col"]
_C_P6 = ["ntri_bf", "ones_row_bf", "negcol_bf", "ident_bf"]
_C_P7 = ["ident_f32", "ident_bf", "onesf_row", "rrep"]

_IO = {
    "x_tm": ([S, DM], F32), "x0_fm": ([DM, S], F32), "xn0_tm": ([S, DM], BF16),
    "qkv_w": ([3 * H0 * DH, DM], F32), "router_w": ([E0, DM], F32),
    "ebias": ([E0], F32), "qkv_fm": ([3 * H0 * DH, S], BF16),
    "combT": ([E0, S], BF16), "norm_w": ([DM], F32), "attnn_fm": ([DM, S], BF16),
    "gate_w": ([DM, DM], F32), "out_w": ([DM, DM], F32), "s0_fm": ([DM, S], BF16),
    "sw1": ([HS, DM], F32), "sw2": ([HS, DM], F32), "w12": ([E0 * 2 * HR, DM], F32),
    "hsh_fm": ([HS, S], BF16), "hrt_fm": ([E0 * HR, S], BF16),
    "sw3": ([DM, HS], F32), "w3": ([E0 * DM, HR], F32),
    "x1_fm": ([DM, S], F32), "xn1_fm": ([DM, S], BF16),
    "wq": ([DM, DM], F32), "wk": ([DM, DM], F32), "wv": ([DM, DM], F32),
    "wo": ([DM, DM], F32), "s1_fm": ([DM, S], BF16),
    "wbase": ([DM, DM], F32), "bbase": ([DM], F32), "wrout": ([E1, DM], F32),
    "aw": ([E1 * R1, DM], F32), "bw": ([E1 * DM, R1], F32), "out_tm": ([S, DM], F32),
}

_PHASES = [
    ("p1", _emit_p1, ["x_tm"], ["x0_fm", "xn0_tm"], _C_P1),
    ("p2", _emit_p2, ["qkv_w", "router_w", "ebias", "xn0_tm"], ["qkv_fm", "combT"], _C_P2),
    ("p3", _emit_p3, ["qkv_fm", "norm_w"], ["attnn_fm"], _C_P3),
    ("p4", _emit_p4, ["gate_w", "out_w", "xn0_tm", "attnn_fm"], ["s0_fm"], _C_P4_ := []),
    ("p5a", _emit_p5a, ["sw1", "sw2", "w12", "combT", "xn0_tm"], ["hsh_fm", "hrt_fm"], _C_P5A),
    ("p5b", _emit_p5b, ["sw3", "w3", "s0_fm", "x0_fm", "hsh_fm", "hrt_fm"],
     ["x1_fm", "xn1_fm"], _C_P5B),
    ("p6", _emit_p6, ["wq", "wk", "wv", "wo", "xn1_fm"], ["s1_fm"], _C_P6),
    ("p7", _emit_p7, ["wbase", "bbase", "wrout", "aw", "bw", "xn1_fm", "x1_fm", "s1_fm"],
     ["out_tm"], _C_P7),
]

_MODULE_INS = ["x_tm", "qkv_w", "router_w", "ebias", "norm_w", "gate_w", "out_w",
               "sw1", "sw2", "w12", "sw3", "w3", "wq", "wk", "wv", "wo",
               "wbase", "bbase", "wrout", "aw", "bw"]
_ALL_CONSTS = sorted({c for (_, _, _, _, cs) in _PHASES for c in cs})


def _build_phase(name):
    (pname, emit, ins, outs, consts) = next(p for p in _PHASES if p[0] == name)
    nc = bass.Bass("TRN2", target_bir_lowering=False)
    T = {}
    for n in ins:
        shp, dt = _IO[n]
        T[n] = nc.dram_tensor(n, shp, dt, kind="ExternalInput").ap()
    for n in outs:
        shp, dt = _IO[n]
        T[n] = nc.dram_tensor(n, shp, dt, kind="ExternalOutput").ap()
    T.update(_decl_consts(nc, consts))
    with ExitStack() as pst:
        ph = PhaseCtx(nc, pst, None, 0)
        emit(nc, T, ph)
    return nc


def _build_p1():
    return _build_phase("p1")


def _build_p2():
    return _build_phase("p2")


def _build_p3():
    return _build_phase("p3")


def _build_p4():
    return _build_phase("p4")


def _build_p5a():
    return _build_phase("p5a")


def _build_p5b():
    return _build_phase("p5b")


def _build_p6():
    return _build_phase("p6")


def _build_p7():
    return _build_phase("p7")


def _build_fused():
    nc = bass.Bass("TRN2", target_bir_lowering=False)
    produced = {o for (_, _, _, outs, _) in _PHASES for o in outs}
    T = {}
    for n in _MODULE_INS:
        shp, dt = _IO[n]
        T[n] = nc.dram_tensor(n, shp, dt, kind="ExternalInput").ap()
    for n in produced:
        shp, dt = _IO[n]
        kind = "ExternalOutput" if n == "out_tm" else "Internal"
        T[n] = nc.dram_tensor(n, shp, dt, kind=kind).ap()
    T.update(_decl_consts(nc, _ALL_CONSTS))
    for pidx, (pname, emit, ins, outs, consts) in enumerate(_PHASES):
        with nc.reset_on_exit():
            with ExitStack() as pst:
                ph = PhaseCtx(nc, pst, None, pidx)
                emit(nc, T, ph)
    return nc


def _run(name, builder, in_maps):
    if name not in _prog_cache:
        _prog_cache[name] = builder()
    nc = _prog_cache[name]
    res = run_bass_kernel_spmd(nc, in_maps, core_ids=list(range(len(in_maps))))
    return res.results


def kernel(**inputs):
    x = np.ascontiguousarray(inputs["L0_attn_hidden_states"], dtype=np.float32)
    n = x.shape[0]
    w = dict(
        qkv_w=np.ascontiguousarray(inputs["L0_attn_qkv_w"]),
        out_w=np.ascontiguousarray(inputs["L0_attn_out_w"]),
        gate_w=np.ascontiguousarray(inputs["L0_attn_gate_w"]),
        norm_w=np.ascontiguousarray(inputs["L0_attn_norm_w"]),
        w12=np.ascontiguousarray(inputs["L0_ffn_w12"]).reshape(E0 * 2 * HR, DM),
        w3=np.ascontiguousarray(inputs["L0_ffn_w3"]).reshape(E0 * DM, HR),
        router_w=np.ascontiguousarray(inputs["L0_ffn_router_w"]),
        ebias=np.ascontiguousarray(inputs["L0_ffn_expert_bias"]),
        sw1=np.ascontiguousarray(inputs["L0_ffn_sw1"]),
        sw2=np.ascontiguousarray(inputs["L0_ffn_sw2"]),
        sw3=np.ascontiguousarray(inputs["L0_ffn_sw3"]),
        wq=np.ascontiguousarray(inputs["L1_attn_Wq"]),
        wk=np.ascontiguousarray(inputs["L1_attn_Wk"]),
        wv=np.ascontiguousarray(inputs["L1_attn_Wv"]),
        wo=np.ascontiguousarray(inputs["L1_attn_Wo"]),
        wbase=np.ascontiguousarray(inputs["L1_ffn_W_base"]),
        bbase=np.ascontiguousarray(inputs["L1_ffn_b_base"]),
        wrout=np.ascontiguousarray(inputs["L1_ffn_W_router"]),
        aw=np.ascontiguousarray(inputs["L1_ffn_A_w"]).reshape(E1 * R1, DM),
        bw=np.ascontiguousarray(inputs["L1_ffn_B_w"]).reshape(E1 * DM, R1),
        **_const_inputs(_ALL_CONSTS),
    )
    res = _run("fused", _build_fused,
               [dict(x_tm=x[c], **w) for c in range(n)])
    return np.stack([np.asarray(res[c]["out_tm"], dtype=np.float32) for c in range(n)])


# version 05
# speedup vs baseline: 1.2629x; 1.0463x over previous
import numpy as np
import ml_dtypes
import concourse.bass as bass
import concourse.mybir as mybir
from concourse.bass_utils import run_bass_kernel_spmd
from contextlib import ExitStack

F32 = mybir.dt.float32
BF16 = mybir.dt.bfloat16
AF = mybir.ActivationFunctionType
ALU = mybir.AluOpType
AX = mybir.AxisListType
BFNP = ml_dtypes.bfloat16

NCORE = 8
S = 2048
DM = 2048
EPS = 1e-5

H0, DH, BS0 = 32, 64, 256
NB0 = S // BS0
_l0_base = 1.0 / 2.0 ** (8.0 / H0)
_l0_factor = 1.0 - 0.0 / (32 - 1 + 1e-5) + 1e-5
SLOPES = [float((_l0_base ** (h + 1)) * _l0_factor) for h in range(H0)]

E0, HS, HR = 8, 8192, 1024
H1, SC1 = 32, 0.125
E1, R1, LSCALE = 8, 8, 2.0

_prog_cache = {}


def cdiv(a, b):
    return -(-a // b)


class Pipe:

    def __init__(self, st, nc, name, step=1):
        self.h = st.enter_context(nc.semaphore(name))
        self.step = step
        self.emitted = 0

    def inc(self, instr, n=1):
        instr.then_inc(self.h, self.step * n)
        self.emitted += self.step * n
        return instr

    def wait(self, eng, units):
        if units > 0:
            eng.wait_ge(self.h, self.step * units)


class Chain:

    def __init__(self, st, nc, name):
        self.h = st.enter_context(nc.semaphore(name))
        self.n = 0
        self.emitted = 0

    def seal(self, instr):
        instr.then_inc(self.h, 1)
        self.n += 1
        self.emitted = self.n
        return instr

    def gate(self, eng):
        if self.n > 0:
            eng.wait_ge(self.h, self.n)


class SPipe:

    def __init__(self, st, nc, name, nslots, step=16):
        self.p = [Pipe(st, nc, f"{name}{i}", step) for i in range(nslots)]
        self.n = nslots

    def inc(self, instr, k, n=1):
        self.p[k % self.n].inc(instr, n)
        return instr

    def wait(self, eng, k, uses):
        self.p[k % self.n].wait(eng, uses)

    def wait_all(self, eng, total_units_per_slot):
        for p in self.p:
            p.wait(eng, total_units_per_slot)


class SSched:

    def __init__(self, pipes):
        self.pipes = pipes
        self.counts = {k: 0 for k in pipes}
        self.steps = []
        self.res = {}

    def add(self, eng, fn, reads=(), writes=(), extra=()):
        if eng == "gp":
            writes = tuple(writes) + ("__gp_tok",)
        deps = {}
        for r in reads:
            ent = self.res.get(r)
            if ent and ent["w"]:
                e2, v = ent["w"]
                deps[e2] = max(deps.get(e2, 0), v)
        for w_ in writes:
            ent = self.res.get(w_)
            if ent:
                if ent["w"]:
                    e2, v = ent["w"]
                    deps[e2] = max(deps.get(e2, 0), v)
                for (e2, v) in ent["r"]:
                    deps[e2] = max(deps.get(e2, 0), v)
        self.counts[eng] += 1
        tag = (eng, self.counts[eng])
        for r in reads:
            self.res.setdefault(r, {"w": None, "r": []})["r"].append(tag)
        for w_ in writes:
            self.res[w_] = {"w": tag, "r": []}
        self.steps.append((eng, fn, deps, tag, list(extra)))
        return tag

    def emit(self, eng_name, engine):
        for (eng, fn, deps, tag, extra) in self.steps:
            if eng != eng_name:
                continue
            for e2, v in deps.items():
                if e2 == eng_name and v >= tag[1]:
                    continue
                self.pipes[e2].wait(engine, v)
            for (pipe, units) in extra:
                pipe.wait(engine, units)
            self.pipes[eng].inc(fn(engine))


class Bar:

    def __init__(self, st, nc):
        self.clr = Pipe(st, nc, "bar_clr")
        self.eng = {k: Pipe(st, nc, f"bar_{k}") for k in ("pe", "dve", "act", "gp", "sp")}
        self.counts = {k: 0 for k in self.eng}


class PhaseCtx:
    def __init__(self, nc, st, bar, pidx):
        self.nc, self.st, self.bar, self.pidx = nc, st, bar, pidx
        self.sems = []

    def pipe(self, name, step=1):
        p = Pipe(self.st, self.nc, f"{name}_{self.pidx}", step)
        self.sems.append(p)
        return p

    def spipe(self, name, nslots, step=16):
        sp = SPipe(self.st, self.nc, f"{name}_{self.pidx}", nslots, step)
        self.sems.extend(sp.p)
        return sp

    def chain(self, name):
        c = Chain(self.st, self.nc, f"{name}_{self.pidx}")
        self.sems.append(c)
        return c

    def wrap(self, blk):
        self.raw_blk = blk
        return blk


class PhBlock:
    _MAP = {"gpsimd": "gp", "vector": "dve", "scalar": "act",
            "tensor": "pe", "sync": "sp"}

    def __init__(self, blk, bar, pidx):
        self._blk, self._bar, self._pidx = blk, bar, pidx

    def _deco(self, attr):
        eng_key = self._MAP[attr]
        bar, pidx = self._bar, self._pidx

        def decorator(f):
            def body(eng):
                bar.clr.wait(eng, pidx)
                f(eng)
                bar.eng[eng_key].inc(eng.nop())
            bar.counts[eng_key] += 1
            getattr(self._blk, attr)(body)
        return decorator

    @property
    def gpsimd(self):
        return self._deco("gpsimd")

    @property
    def vector(self):
        return self._deco("vector")

    @property
    def scalar(self):
        return self._deco("scalar")

    @property
    def tensor(self):
        return self._deco("tensor")

    @property
    def sync(self):
        return self._deco("sync")



def _sbt(nc, ph, name, *a, **kw):
    return nc.sbuf_tensor(f"p{ph.pidx}_{name}", *a, **kw)


def _pst(nc, ph, name, *a, **kw):
    return nc.psum_tensor(f"p{ph.pidx}_{name}", *a, **kw)


def _phase_boundary(nc, bar, ph):
    bb = ph.raw_blk

    @bb.gpsimd
    def _(g):
        for k, cnt in bar.counts.items():
            if cnt:
                bar.eng[k].wait(g, cnt)
        for p in ph.sems:
            if p.emitted:
                g.wait_ge(p.h, p.emitted)
        for p in ph.sems:
            g.sem_clear(p.h)
        bar.clr.inc(g.nop())


def _consts():
    ident_f32 = np.eye(128, dtype=np.float32)
    ident_bf = np.eye(128, dtype=BFNP)
    ntri = np.where(np.arange(128)[:, None] >= np.arange(128)[None, :], -1.0, 0.0)
    out = dict(
        ident_f32=ident_f32,
        ident_bf=ident_bf,
        ntri_bf=ntri.astype(BFNP),
        ones_row_bf=np.ones((1, 128), dtype=BFNP),
        negcol_bf=np.full((128, 1), -1.0, dtype=BFNP),
        onescol64_bf=np.ones((64, 1), dtype=BFNP),
        onesf_row=np.ones((1, 128), dtype=np.float32),
        onesf_col=np.ones((128, 1), dtype=np.float32),
    )
    rrep = np.zeros((8, 64), dtype=BFNP)
    for e in range(8):
        rrep[e, e * 8:(e + 1) * 8] = 1.0
    out["rrep"] = rrep
    return out


_CONSTS = _consts()


def _decl_consts(nc, names):
    aps = {}
    for n in names:
        arr = _CONSTS[n]
        dt = BF16 if arr.dtype == BFNP else F32
        aps[n] = nc.dram_tensor(n, list(arr.shape), dt, kind="ExternalInput").ap()
    return aps


def _const_inputs(names):
    return {n: _CONSTS[n] for n in names}


def _emit_p1(nc, T, ph):
    x, x0_fm, xn0_tm, C = T["x_tm"], T["x0_fm"], T["xn0_tm"], T
    NT = S // 128

    if True:
        e = ph.st.enter_context
        xt = [e(_sbt(nc, ph, f"xt{i}", [128, DM], F32)) for i in range(2)]
        xnt = e(_sbt(nc, ph, "xnt", [128, DM], BF16))
        sq = e(_sbt(nc, ph, "sq", [128, DM], F32))
        idf = e(_sbt(nc, ph, "idf", [128, 128], F32))
        ssum = e(_sbt(nc, ph, "ssum", [128, 1], F32))
        ssq = e(_sbt(nc, ph, "ssq", [128, 1], F32))
        mu = e(_sbt(nc, ph, "mu", [128, 1], F32))
        var = e(_sbt(nc, ph, "var", [128, 1], F32))
        var2 = e(_sbt(nc, ph, "var2", [128, 1], F32))
        lnv = e(_sbt(nc, ph, "lnv", [128, 1], F32))
        rstd = e(_sbt(nc, ph, "rstd", [128, 1], F32))
        bcol = e(_sbt(nc, ph, "bcol", [128, 1], F32))
        epscol = e(_sbt(nc, ph, "epscol", [128, 1], F32))
        evf = [e(_sbt(nc, ph, f"evf{i}", [128, 128], F32)) for i in range(2)]
        psf = [e(_pst(nc, ph, f"psf{i}", [128, 128], F32)) for i in range(2)]

        ld = ph.spipe("ld", 2)
        lc = ph.pipe("lc", 16)
        st_n = ph.pipe("st_n", 16)
        st_f = ph.spipe("st_f", 2)
        s_red = ph.pipe("s_red")
        s_sq = ph.pipe("s_sq")
        s_var = ph.pipe("s_var")
        s_rstd = ph.pipe("s_rstd")
        s_app = ph.pipe("s_app")
        s_tp = ph.pipe("s_tp")
        s_ev = ph.pipe("s_ev")
        cv = ph.chain("cv")
        ca = ph.chain("ca")
        blk = ph.wrap(e(nc.Block()))

        @blk.gpsimd
        def _(g):
            g.memset(epscol[:, :], EPS)
            lc.inc(g.dma_start(idf[:, :], C["ident_f32"]))
            for it in range(NT + 2):
                if it >= 2:
                    u = it - 2
                    s_app.wait(g, u + 1)
                    st_n.inc(g.dma_start(xn0_tm[u * 128:(u + 1) * 128, :], xnt[:, :]))
                    for dc in range(16):
                        k = u * 16 + dc
                        s_ev.wait(g, k + 1)
                        st_f.inc(g.dma_start(
                            x0_fm[dc * 128:(dc + 1) * 128, u * 128:(u + 1) * 128],
                            evf[dc % 2][:, :]), k)
                if it < NT:
                    tc = it
                    if tc >= 2:
                        s_red.wait(g, tc - 1)
                        s_app.wait(g, tc - 1)
                        s_tp.wait(g, 16 * (tc - 1))
                    ld.inc(g.dma_start(xt[tc % 2][:, :], x[tc * 128:(tc + 1) * 128, :]), tc)
            st_f.wait_all(g, NT * 16 // 2)
            st_n.wait(g, NT)

        @blk.vector
        def _(v):
            for tc in range(NT):
                ld.wait(v, tc, tc // 2 + 1)
                s_red.inc(v.reduce_sum(ssum[:, :], xt[tc % 2][:, :], axis=AX.X))
                s_red.wait(v, tc + 1)
                s_sq.wait(v, tc + 1)
                s_rstd.wait(v, tc)
                cv.seal(v.tensor_scalar_mul(mu[:, :], ssum[:, :], 1.0 / DM))
                cv.seal(v.tensor_scalar_mul(var2[:, :], ssq[:, :], 1.0 / DM))
                cv.gate(v)
                cv.seal(v.scalar_tensor_tensor(bcol[:, :], mu[:, :], -1.0, mu[:, :], ALU.mult, ALU.mult))
                cv.gate(v)
                s_var.inc(v.tensor_tensor(var[:, :], var2[:, :], bcol[:, :], ALU.add))
                s_rstd.wait(v, tc + 1)
                cv.seal(v.scalar_tensor_tensor(bcol[:, :], mu[:, :], -1.0, rstd[:, :], ALU.mult, ALU.mult))
                cv.gate(v)
                if tc >= 1:
                    st_n.wait(v, tc)
                s_app.inc(v.tensor_scalar(xnt[:, :], xt[tc % 2][:, :], rstd[:, :], bcol[:, :], ALU.mult, ALU.add))
                for dc in range(16):
                    k = tc * 16 + dc
                    s_tp.wait(v, k + 1)
                    if k >= 2:
                        st_f.wait(v, k, k // 2)
                    s_ev.inc(v.tensor_copy(evf[dc % 2][:, :], psf[dc % 2][:, :]))

        @blk.scalar
        def _(sc):
            for tc in range(NT):
                ld.wait(sc, tc, tc // 2 + 1)
                s_var.wait(sc, tc)
                s_sq.inc(sc.activation(sq[:, :], xt[tc % 2][:, :], AF.Square,
                                       accum_out=ssq[:, :]))
                s_var.wait(sc, tc + 1)
                s_app.wait(sc, tc)
                ca.seal(sc.activation(lnv[:, :], var[:, :], AF.Ln, bias=epscol[:, :]))
                ca.gate(sc)
                s_rstd.inc(sc.activation(rstd[:, :], lnv[:, :], AF.Exp, scale=-0.5))

        @blk.tensor
        def _(t):
            lc.wait(t, 1)
            for tc in range(NT):
                ld.wait(t, tc, tc // 2 + 1)
                for dc in range(16):
                    k = tc * 16 + dc
                    if k >= 2:
                        s_ev.wait(t, k - 1)
                    s_tp.inc(t.transpose(psf[dc % 2][:, :],
                                         xt[tc % 2][:, dc * 128:(dc + 1) * 128],
                                         idf[:, :]))



def _emit_p2(nc, T, ph):
    qkv_w, router_w, ebias = T["qkv_w"], T["router_w"], T["ebias"]
    xn0_tm, qkv_fm, combT, C = T["xn0_tm"], T["qkv_fm"], T["combT"], T
    qkv_bfw = nc.dram_tensor("qkv_bfw", [3 * H0 * DH, DM], BF16, kind="Internal").ap()
    router_bfw = nc.dram_tensor("router_bfw", [E0, DM], BF16, kind="Internal").ap()
    NOB = (3 * H0 * DH) // 512
    NTC = S // 128
    NTT = S // 512

    if True:
        e = ph.st.enter_context
        xn = [e(_sbt(nc, ph, f"xn{i}", [128, S], BF16)) for i in range(16)]
        wbuf = [e(_sbt(nc, ph, f"wbuf{i}", [128, 16 * 512], BF16, is_transpose_dest=True)) for i in range(2)]
        evq = [e(_sbt(nc, ph, f"evq{i}", [128, 512], BF16)) for i in range(4)]
        rsb = e(_sbt(nc, ph, "rsb", [E0, DM], BF16))
        routerT = e(_sbt(nc, ph, "routerT", [128, 128], BF16))
        idb = e(_sbt(nc, ph, "idb", [128, 128], BF16))
        onesf = e(_sbt(nc, ph, "onesf", [1, 128], F32))
        eb = e(_sbt(nc, ph, "eb", [1, E0], F32))
        bias_bc = e(_sbt(nc, ph, "bias_bc", [128, E0], F32))
        sc_tm = [e(_sbt(nc, ph, f"sc_tm{i}", [128, E0], F32)) for i in range(2)]
        comb_tm = [e(_sbt(nc, ph, f"comb_tm{i}", [128, E0], BF16)) for i in range(2)]
        combT_sb = e(_sbt(nc, ph, "combT_sb", [E0, S], BF16))
        sel = e(_sbt(nc, ph, "sel", [128, E0], F32))
        m1 = e(_sbt(nc, ph, "m1", [128, 1], F32))
        m2 = e(_sbt(nc, ph, "m2", [128, 1], F32))
        is1 = e(_sbt(nc, ph, "is1", [128, E0], F32))
        sel2 = e(_sbt(nc, ph, "sel2", [128, E0], F32))
        ind = e(_sbt(nc, ph, "ind", [128, E0], F32))
        wsc = e(_sbt(nc, ph, "wsc", [128, E0], F32))
        ssum = e(_sbt(nc, ph, "ssumr", [128, 1], F32))
        rcp = e(_sbt(nc, ph, "rcp", [128, 1], F32))

        ps_rtp = e(_pst(nc, ph, "ps_rtp", [128, E0], BF16))
        ps_ctp = e(_pst(nc, ph, "ps_ctp", [E0, 128], BF16))
        ps_sc = [e(_pst(nc, ph, f"ps_sc{i}", [128, E0], F32)) for i in range(2)]
        ps_q = [e(_pst(nc, ph, f"ps_q{i}", [128, 512], F32)) for i in range(4)]

        rc = ph.pipe("rc", 16)
        qc = ph.spipe("qc", 6)
        gl = ph.pipe("gl", 16)
        sxn = ph.pipe("sxn", 16)
        sw = ph.spipe("sw", 2)
        stc = ph.pipe("stc", 16)
        r_pe = ph.pipe("r_pe")
        r_dv = ph.pipe("r_dv")
        r_ac = ph.pipe("r_ac")
        s_mm = ph.pipe("s_mm")
        evp = ph.pipe("evp")
        stq = ph.spipe("stq", 4)
        cdv = ph.chain("cdv")
        blk = ph.wrap(e(nc.Block()))

        @blk.gpsimd
        def _(g):
            rc.inc(g.dma_start(router_bfw, router_w))
            for c in range(6):
                qc.inc(g.dma_start(qkv_bfw[c * 1024:(c + 1) * 1024, :],
                                   qkv_w[c * 1024:(c + 1) * 1024, :]), c)
            rc.wait(g, 1)
            gl.inc(g.dma_start(rsb[:, :], router_bfw))
            gl.inc(g.dma_start(eb[:, :], ebias.rearrange("(b a) -> b a", b=1)))
            gl.inc(g.dma_start(idb[:, :], C["ident_bf"]))
            gl.inc(g.dma_start(onesf[:, :], C["onesf_row"]))
            for ob in range(NOB):
                for tt in range(NTT):
                    for oc4 in range(4):
                        gidx = ob * 16 + tt * 4 + oc4
                        evp.wait(g, gidx + 1)
                        stq.inc(g.dma_start(
                            qkv_fm[ob * 512 + oc4 * 128: ob * 512 + (oc4 + 1) * 128,
                                   tt * 512:(tt + 1) * 512],
                            evq[gidx % 4][:, :]), gidx)
            r_dv.wait(g, 49)
            stc.inc(g.dma_start(combT, combT_sb[:, :]))
            stq.wait_all(g, NOB * 16 // 4)
            stc.wait(g, 1)

        @blk.sync
        def _(s):
            for ic in range(16):
                sxn.inc(s.dma_start_transpose(xn[ic][:, :], xn0_tm[:, ic * 128:(ic + 1) * 128]))
            for ob in range(NOB):
                for c in range(cdiv((ob + 1) * 512, 1024)):
                    qc.wait(s, c, 1)
                if ob >= 2:
                    s_mm.wait(s, 16 * (ob - 1))
                for ic in range(16):
                    sw.inc(s.dma_start_transpose(
                        wbuf[ob % 2][:, ic * 512:(ic + 1) * 512],
                        qkv_bfw[ob * 512:(ob + 1) * 512, ic * 128:(ic + 1) * 128]), ob)

        @blk.tensor
        def _(t):
            gl.wait(t, 4)
            for ic in range(16):
                if ic >= 1:
                    r_dv.wait(t, ic)
                r_pe.inc(t.transpose(ps_rtp[:, :], rsb[:, ic * 128:(ic + 1) * 128],
                                     idb[0:E0, 0:E0]))
            r_dv.wait(t, 16)
            r_pe.inc(t.matmul(ps_sc[0][:, :], onesf[:, :], eb[:, :], start=True, stop=True))
            sxn.wait(t, 16)

            def score_group(tc):
                r_dv.wait(t, 17)
                if tc >= 2:
                    r_ac.wait(t, tc - 1)
                for ic in range(16):
                    r_pe.inc(t.matmul(ps_sc[tc % 2][:, :],
                                      xn[ic][:, tc * 128:(tc + 1) * 128],
                                      routerT[:, ic * 8:(ic + 1) * 8],
                                      start=(ic == 0), stop=(ic == 15)), n=(1 if ic == 15 else 0))

            def ctp_item(j):
                r_dv.wait(t, 18 + 2 * j)
                r_pe.inc(t.transpose(ps_ctp[:, :], comb_tm[j % 2][:, :], idb[:, :]))

            for ob in range(NOB):
                sw.wait(t, ob, 16 * (ob // 2 + 1))
                for tt in range(NTT):
                    for oc4 in range(4):
                        gidx = ob * 16 + tt * 4 + oc4
                        if gidx % 10 == 4 and gidx // 10 < NTC:
                            score_group(gidx // 10)
                        if gidx % 10 == 9 and gidx // 10 < NTC:
                            ctp_item(gidx // 10)
                        if gidx >= 4:
                            evp.wait(t, gidx - 3)
                        for ic in range(16):
                            s_mm.inc(t.matmul(ps_q[oc4][:, :],
                                              wbuf[ob % 2][:, ic * 512 + oc4 * 128: ic * 512 + (oc4 + 1) * 128],
                                              xn[ic][:, tt * 512:(tt + 1) * 512],
                                              start=(ic == 0), stop=(ic == 15)),
                                     n=(1 if ic == 15 else 0))

        @blk.vector
        def _(v):
            for ic in range(16):
                r_pe.wait(v, ic + 1)
                r_dv.inc(v.tensor_copy(routerT[:, ic * 8:(ic + 1) * 8], ps_rtp[:, :]))
            r_pe.wait(v, 17)
            r_dv.inc(v.tensor_copy(bias_bc[:, :], ps_sc[0][:, :]))
            def pe_ctp_unit(j):
                return 19 + 2 * j
            for tc in range(NTC):
                if tc >= 1:
                    r_pe.wait(v, pe_ctp_unit(tc - 1))
                    r_dv.inc(v.tensor_copy(combT_sb[:, (tc - 1) * 128:tc * 128], ps_ctp[:, :]))
                r_ac.wait(v, tc + 1)
                if tc >= 2:
                    r_pe.wait(v, pe_ctp_unit(tc - 2))
                cdv.seal(v.tensor_tensor(sel[:, :], sc_tm[tc % 2][:, :], bias_bc[:, :], ALU.add))
                cdv.gate(v)
                cdv.seal(v.reduce_max(m1[:, :], sel[:, :], axis=AX.X))
                cdv.gate(v)
                cdv.seal(v.tensor_single_scalar(is1[:, :], sel[:, :], m1[:, 0:1], ALU.is_ge))
                cdv.gate(v)
                cdv.seal(v.scalar_tensor_tensor(sel2[:, :], is1[:, :], -1e9, sel[:, :], ALU.mult, ALU.add))
                cdv.gate(v)
                cdv.seal(v.reduce_max(m2[:, :], sel2[:, :], axis=AX.X))
                cdv.gate(v)
                cdv.seal(v.tensor_single_scalar(ind[:, :], sel[:, :], m2[:, 0:1], ALU.is_ge))
                cdv.gate(v)
                cdv.seal(v.tensor_tensor(wsc[:, :], ind[:, :], sc_tm[tc % 2][:, :], ALU.mult))
                cdv.gate(v)
                cdv.seal(v.reduce_sum(ssum[:, :], wsc[:, :], axis=AX.X))
                cdv.gate(v)
                cdv.seal(v.tensor_single_scalar(ssum[:, :], ssum[:, :], 1e-9, ALU.add))
                cdv.gate(v)
                cdv.seal(v.reciprocal(rcp[:, :], ssum[:, :]))
                cdv.gate(v)
                r_dv.inc(v.tensor_single_scalar(comb_tm[tc % 2][:, :], wsc[:, :], rcp[:, 0:1], ALU.mult))
            r_pe.wait(v, pe_ctp_unit(15))
            r_dv.inc(v.tensor_copy(combT_sb[:, 15 * 128:16 * 128], ps_ctp[:, :]))

        @blk.scalar
        def _(sc):
            def sig_item(tc):
                r_pe.wait(sc, 18 + 2 * tc)
                if tc >= 2:
                    r_dv.wait(sc, 18 + 2 * (tc - 2))
                r_ac.inc(sc.activation(sc_tm[tc % 2][:, :], ps_sc[tc % 2][:, :], AF.Sigmoid))

            for ob in range(NOB):
                for tt in range(NTT):
                    for oc4 in range(4):
                        gidx = ob * 16 + tt * 4 + oc4
                        if gidx % 10 == 5 and gidx // 10 < NTC:
                            sig_item(gidx // 10)
                        s_mm.wait(sc, gidx + 1)
                        if gidx >= 4:
                            stq.wait(sc, gidx, gidx // 4)
                        evp.inc(sc.activation(evq[gidx % 4][:, :], ps_q[oc4][:, :], AF.Silu))



def _p3_sched():
    M = H0 * NB0
    sched = [dict(h=m // NB0, n=m % NB0, first=(m % NB0 == 0)) for m in range(M)]
    pe = 0
    for st in range(M + 2):
        if st < M:
            s = sched[st]
            pe += 2
            s["sc_end"] = pe
        if 2 <= st <= M + 1:
            s = sched[st - 2]
            pe += 1
            s["ssmm_end"] = pe
        if 1 <= st <= M:
            s = sched[st - 1]
            pe += 3
            s["y_end"] = pe
            if s["n"] < NB0 - 1:
                pe += 2
                s["smm_end"] = pe
            s["c_end"] = pe
    dve = 0
    for st in range(M + 2):
        if 1 <= st <= M:
            s = sched[st - 1]
            if s["first"]:
                dve += 4
            s["set_end"] = dve
            dve += 2
            s["aw_end"] = dve
            dve += 2
            s["ev_end"] = dve
            dve += 1
            s["qdq_end"] = dve
        if 2 <= st <= M + 1:
            s = sched[st - 2]
            if s["n"] < NB0 - 1:
                dve += 1
                s["supd_end"] = dve
                dve += 1
                s["scast_end"] = dve
            dve += 1
            s["ssacc_end"] = dve
    act = 0
    for st in range(M + 1):
        if 1 <= st <= M:
            s = sched[st - 1]
            if s["first"]:
                act += 5
            s["aset_end"] = act
            act += 1
            s["yev_end"] = act
            act += 1
            s["sq_end"] = act
    return sched


def _emit_p3(nc, T, ph):
    qkv_fm, norm_w, attnn_fm, C = T["qkv_fm"], T["norm_w"], T["attnn_fm"], T
    NM = H0 * NB0
    sched = _p3_sched()

    if True:
        e = ph.st.enter_context
        qh = [e(_sbt(nc, ph, f"qh{i}", [64, S], BF16)) for i in range(2)]
        kh = [e(_sbt(nc, ph, f"kh{i}", [64, S], BF16)) for i in range(2)]
        attn = [e(_sbt(nc, ph, f"attn{i}", [128, S], BF16)) for i in range(16)]
        D = [e(_sbt(nc, ph, f"D{i}", [128, BS0], F32)) for i in range(2)]
        m01 = [e(_sbt(nc, ph, f"m01{i}", [128, BS0], F32)) for i in range(2)]
        Dc = [e(_sbt(nc, ph, f"Dc{i}", [128, BS0], F32)) for i in range(2)]
        eexp = [e(_sbt(nc, ph, f"eexp{i}", [128, BS0], F32)) for i in range(2)]
        ddT = [e(_sbt(nc, ph, f"ddT{i}", [128, 2 * BS0], BF16)) for i in range(2)]
        ior = e(_sbt(nc, ph, "ior", [1, BS0], F32))
        ibc = e(_sbt(nc, ph, "ibc", [64, BS0], F32))
        iocol = [e(_sbt(nc, ph, f"iocol{i}", [128, 1], F32)) for i in range(2)]
        qd = [e(_sbt(nc, ph, f"qd{i}", [64, BS0], BF16)) for i in range(2)]
        kdc = [e(_sbt(nc, ph, f"kdc{i}", [128, 2], F32)) for i in range(2)]
        awT = [e(_sbt(nc, ph, f"awT{i}", [128, 2 * BS0], BF16)) for i in range(2)]
        kraw = [e(_sbt(nc, ph, f"kraw{i}", [128, 128], BF16, is_transpose_dest=True))
                for i in range(2)]
        kdkT = [e(_sbt(nc, ph, f"kdkT{i}", [128, 128], BF16)) for i in range(2)]
        vbT = [e(_sbt(nc, ph, f"vbT{i}", [128, 128], BF16, is_transpose_dest=True))
               for i in range(2)]
        qdq = [e(_sbt(nc, ph, f"qdq{i}", [64, BS0], BF16)) for i in range(2)]
        S_f = [e(_sbt(nc, ph, f"S_f{i}", [64, 64], F32)) for i in range(2)]
        S_b = e(_sbt(nc, ph, "S_b", [64, 64], BF16))
        sq_b = [e(_sbt(nc, ph, f"sq_b{i}", [64, BS0], BF16)) for i in range(2)]
        ssb = [e(_sbt(nc, ph, f"ssb{i}", [1, S], F32)) for i in range(2)]
        lnr = e(_sbt(nc, ph, "lnr", [1, S], F32))
        rstd_r = e(_sbt(nc, ph, "rstd_r", [1, S], F32))
        ebias1 = e(_sbt(nc, ph, "ebias1", [1, 1], F32))
        ebias2 = e(_sbt(nc, ph, "ebias2", [1, 1], F32))
        nwrow = e(_sbt(nc, ph, "nwrow", [1, DM], F32))
        nw_cols = e(_sbt(nc, ph, "nw_cols", [128, 16], F32))
        o64 = e(_sbt(nc, ph, "o64", [64, 1], BF16))
        onesf = e(_sbt(nc, ph, "onesf", [1, 128], F32))
        idf = e(_sbt(nc, ph, "idf", [128, 128], F32))
        anout = [e(_sbt(nc, ph, f"anout{i}", [128, 512], BF16)) for i in range(4)]

        ps_sc = [e(_pst(nc, ph, f"ps_sc{i}", [128, 512], F32)) for i in range(2)]
        ps_y = [e(_pst(nc, ph, f"ps_y{i}", [64, BS0], F32)) for i in range(2)]
        ps_S = e(_pst(nc, ph, "ps_S", [64, 64], F32))
        ps_ss = e(_pst(nc, ph, "ps_ss", [1, BS0], F32))
        ps_nw = e(_pst(nc, ph, "ps_nw", [128, 16], F32))

        g_init = ph.pipe("g_init", 16)
        g_eng = ph.pipe("g_eng")
        qld = ph.spipe("qld", 2)
        vld = ph.spipe("vld", 2)
        kld = ph.spipe("kld", 2)
        d_pre = ph.pipe("d_pre")
        p_pre = ph.pipe("p_pre")
        p3 = ph.pipe("p3")
        d3 = ph.pipe("d3")
        a3 = ph.pipe("a3")
        a_r = ph.pipe("a_r")
        n_pe = ph.pipe("n_pe")
        n_dv = ph.pipe("n_dv")
        n_st = ph.spipe("n_st", 4)
        blk = ph.wrap(e(nc.Block()))

        M = NM

        @blk.sync
        def _(sy):
            for m in range(M):
                h, n = m // NB0, m % NB0
                base = h * 3 * DH
                if m >= 2:
                    p3.wait(sy, sched[m - 2]["c_end"])
                    d3.wait(sy, sched[m - 2]["ev_end"])
                for c in range(2):
                    vld.inc(sy.dma_start_transpose(
                        vbT[m % 2][:, c * 64:(c + 1) * 64],
                        qkv_fm[base + 2 * DH:base + 3 * DH,
                               n * BS0 + c * 128:n * BS0 + (c + 1) * 128]), m)
                    kld.inc(sy.dma_start_transpose(
                        kraw[m % 2][:, c * 64:(c + 1) * 64],
                        qkv_fm[base + DH:base + 2 * DH,
                               n * BS0 + c * 128:n * BS0 + (c + 1) * 128]), m)

        @blk.gpsimd
        def _(g):
            g_eng.inc(g.iota(D[0][:, :], [[1, BS0]], base=0, channel_multiplier=-1,
                             allow_small_or_imprecise_dtypes=True))
            g_eng.inc(g.iota(D[1][:, :], [[1, BS0]], base=-128, channel_multiplier=-1,
                             allow_small_or_imprecise_dtypes=True))
            g_eng.inc(g.iota(ior[:, :], [[1, BS0]], base=1, channel_multiplier=0,
                             allow_small_or_imprecise_dtypes=True))
            g_eng.inc(g.iota(iocol[0][:, :], [[0, 1]], base=1 - BS0, channel_multiplier=1,
                             allow_small_or_imprecise_dtypes=True))
            g_eng.inc(g.iota(iocol[1][:, :], [[0, 1]], base=1 - BS0 + 128, channel_multiplier=1,
                             allow_small_or_imprecise_dtypes=True))
            g_eng.inc(g.memset(ebias1[:, :], float(DM) * EPS))
            g_eng.inc(g.memset(ebias2[:, :], 0.5 * float(np.log(DM))))
            g_eng.inc(g.memset(o64[:, :], 1.0))
            g_init.inc(g.dma_start(nwrow[:, :], norm_w.rearrange("(b a) -> b a", b=1)))
            g_init.inc(g.dma_start(idf[:, :], C["ident_f32"]))
            g_init.inc(g.dma_start(onesf[:, :], C["onesf_row"]))
            for h in range(H0):
                if h >= 2:
                    last = sched[(h - 1) * NB0 - 1]
                    p3.wait(g, last["sc_end"])
                    d3.wait(g, last["qdq_end"])
                base = h * 3 * DH
                qld.inc(g.dma_start(qh[h % 2][:, :], qkv_fm[base:base + DH, :]), h)
                qld.inc(g.dma_start(kh[h % 2][:, :], qkv_fm[base + DH:base + 2 * DH, :]), h)
            for tt in range(4):
                for fc in range(16):
                    k = tt * 16 + fc
                    n_dv.wait(g, k + 2)
                    n_st.inc(g.dma_start(
                        attnn_fm[fc * 128:(fc + 1) * 128, tt * 512:(tt + 1) * 512],
                        anout[k % 4][:, :]), k)
            n_st.wait_all(g, 16)

        @blk.tensor
        def _(t):
            g_init.wait(t, 3)
            g_eng.wait(t, 8)
            p_pre.inc(t.matmul(ps_sc[0][0:64, 0:BS0], onesf[:, 0:64], ior[:, :],
                               start=True, stop=True))
            for st in range(M + 2):
                if st < M:
                    s_ = sched[st]
                    h, n = s_["h"], s_["n"]
                    m = st
                    if s_["first"]:
                        qld.wait(t, h, 2 * (h // 2 + 1))
                    if m >= 2:
                        d3.wait(t, sched[m - 2]["aw_end"])
                    if m == 0:
                        d_pre.wait(t, 1)
                    for c in range(2):
                        p3.inc(t.matmul(
                            ps_sc[m % 2][:, c * BS0:(c + 1) * BS0],
                            kh[h % 2][:, n * BS0 + c * 128:n * BS0 + (c + 1) * 128],
                            qh[h % 2][:, n * BS0:(n + 1) * BS0],
                            start=True, stop=True))
                if 2 <= st <= M + 1:
                    s_ = sched[st - 2]
                    m = st - 2
                    a3.wait(t, s_["sq_end"])
                    if m >= 1:
                        d3.wait(t, sched[m - 1]["ssacc_end"])
                    p3.inc(t.matmul(ps_ss[:, :], o64[:, :], sq_b[m % 2][:, :],
                                    start=True, stop=True))
                if 1 <= st <= M:
                    s_ = sched[st - 1]
                    m = st - 1
                    h, n = s_["h"], s_["n"]
                    d3.wait(t, s_["qdq_end"])
                    vld.wait(t, m, 2 * (m // 2 + 1))
                    if not s_["first"]:
                        d3.wait(t, sched[m - 1]["scast_end"])
                    if m >= 2:
                        a3.wait(t, sched[m - 2]["sq_end"])
                    t.matmul(ps_y[m % 2][:, :], vbT[m % 2][:, 0:64],
                             awT[m % 2][:, 0:BS0], start=True, stop=False)
                    t.matmul(ps_y[m % 2][:, :], vbT[m % 2][:, 64:128],
                             awT[m % 2][:, BS0:2 * BS0], start=False, stop=False)
                    p3.inc(t.matmul(ps_y[m % 2][:, :], S_b[:, :], qdq[m % 2][:, :],
                                    start=False, stop=True), n=3)
                    if n < NB0 - 1:
                        prev = sched[m - 1] if not s_["first"] else (
                            sched[m - 2] if m >= 2 else None)
                        if prev is not None:
                            d3.wait(t, prev["supd_end"])
                        t.matmul(ps_S[:, :], kdkT[m % 2][:, 0:64], vbT[m % 2][:, 0:64],
                                 start=True, stop=False)
                        p3.inc(t.matmul(ps_S[:, :], kdkT[m % 2][:, 64:128],
                                        vbT[m % 2][:, 64:128], start=False, stop=True), n=2)
            for c in range(16):
                n_pe.inc(t.transpose(ps_nw[:, c:c + 1], nwrow[:, c * 128:(c + 1) * 128],
                                     idf[0:1, 0:1]))
            a_r.wait(t, 2)
            d3.wait(t, sched[-1]["aw_end"])
            for tt in range(4):
                if tt >= 2:
                    n_dv.wait(t, 1 + (tt - 2) * 16 + 16)
                n_pe.inc(t.matmul(ps_sc[tt % 2][:, :], onesf[:, :],
                                  rstd_r[:, tt * 512:(tt + 1) * 512], start=True, stop=True))

        @blk.vector
        def _(v):
            g_eng.wait(v, 8)
            for c in range(2):
                v.tensor_single_scalar(m01[c][:, :], D[c][:, :], 0.0, ALU.is_ge)
                v.tensor_single_scalar(Dc[c][:, :], D[c][:, :], 0.0, ALU.max)
            v.memset(ssb[0][:, :], 0.0)
            p_pre.wait(v, 1)
            d_pre.inc(v.tensor_copy(ibc[:, :], ps_sc[0][0:64, 0:BS0]))
            d_pre.wait(v, 1)
            for st in range(M + 2):
                if 1 <= st <= M:
                    s_ = sched[st - 1]
                    m = st - 1
                    h, n = s_["h"], s_["n"]
                    if s_["first"]:
                        a3.wait(v, s_["aset_end"])
                        qld.wait(v, h, 2 * (h // 2 + 1))
                        d3.inc(v.tensor_tensor(ddT[h % 2][:, 0:BS0], eexp[0][:, :],
                                               m01[0][:, :], ALU.mult))
                        d3.inc(v.tensor_tensor(ddT[h % 2][:, BS0:2 * BS0], eexp[1][:, :],
                                               m01[1][:, :], ALU.mult))
                        if m >= 1:
                            p3.wait(v, sched[m - 1]["y_end"])
                        d3.inc(v.memset(S_f[0][:, :], 0.0))
                        d3.inc(v.memset(S_b[:, :], 0.0))
                        d3.wait(v, s_["set_end"])
                    p3.wait(v, s_["sc_end"])
                    if m >= 1:
                        p3.wait(v, sched[m - 1]["c_end"])
                    for c in range(2):
                        d3.inc(v.tensor_tensor(awT[m % 2][:, c * BS0:(c + 1) * BS0],
                                               ps_sc[m % 2][:, c * BS0:(c + 1) * BS0],
                                               ddT[h % 2][:, c * BS0:(c + 1) * BS0], ALU.mult))
                    kld.wait(v, m, 2 * (m // 2 + 1))
                    for c in range(2):
                        d3.inc(v.tensor_single_scalar(kdkT[m % 2][:, c * 64:(c + 1) * 64],
                                                      kraw[m % 2][:, c * 64:(c + 1) * 64],
                                                      kdc[h % 2][:, c:c + 1], ALU.mult))
                    d3.inc(v.tensor_tensor(qdq[m % 2][:, :], qh[h % 2][:, n * BS0:(n + 1) * BS0],
                                           qd[h % 2][:, :], ALU.mult))
                if 2 <= st <= M + 1:
                    s_ = sched[st - 2]
                    m = st - 2
                    h, n = s_["h"], s_["n"]
                    if n < NB0 - 1:
                        p3.wait(v, s_["smm_end"])
                        bd = float(np.exp(-SLOPES[h] * BS0))
                        d3.inc(v.scalar_tensor_tensor(S_f[(n + 1) % 2][:, :],
                                                      S_f[n % 2][:, :], bd,
                                                      ps_S[:, :], ALU.mult, ALU.add))
                        d3.wait(v, s_["supd_end"])
                        d3.inc(v.tensor_copy(S_b[:, :], S_f[(n + 1) % 2][:, :]))
                    p3.wait(v, s_["ssmm_end"])
                    d3.inc(v.tensor_tensor(ssb[(h + 1) % 2][:, n * BS0:(n + 1) * BS0],
                                           ssb[h % 2][:, n * BS0:(n + 1) * BS0],
                                           ps_ss[:, :], ALU.add))
            n_pe.wait(v, 16)
            n_dv.inc(v.tensor_copy(nw_cols[:, :], ps_nw[:, :]))
            n_dv.wait(v, 1)
            a3.wait(v, sched[-1]["sq_end"])
            for tt in range(4):
                n_pe.wait(v, 16 + tt + 1)
                for fc in range(16):
                    k = tt * 16 + fc
                    if k >= 4:
                        n_st.wait(v, k, k // 4)
                    n_dv.inc(v.scalar_tensor_tensor(anout[k % 4][:, :],
                                                    attn[fc][:, tt * 512:(tt + 1) * 512],
                                                    nw_cols[:, fc:fc + 1],
                                                    ps_sc[tt % 2][:, :],
                                                    ALU.mult, ALU.mult))

        @blk.scalar
        def _(sc):
            g_eng.wait(sc, 8)
            d_pre.wait(sc, 1)
            for st in range(M + 1):
                if 1 <= st <= M:
                    s_ = sched[st - 1]
                    m = st - 1
                    h, n = s_["h"], s_["n"]
                    if s_["first"]:
                        slope = SLOPES[h]
                        if h >= 1:
                            d3.wait(sc, sched[(h - 1) * NB0]["set_end"])
                        if h >= 2:
                            d3.wait(sc, sched[(h - 1) * NB0 - 1]["qdq_end"])
                        for c in range(2):
                            a3.inc(sc.activation(eexp[c][:, :], Dc[c][:, :], AF.Exp,
                                                 scale=-slope))
                        a3.inc(sc.activation(qd[h % 2][:, :], ibc[:, :], AF.Exp,
                                             scale=-slope))
                        for c in range(2):
                            a3.inc(sc.activation(kdc[h % 2][:, c:c + 1], iocol[c][:, :],
                                                 AF.Exp, scale=slope))
                    p3.wait(sc, s_["y_end"])
                    fc, off = h // 2, (h % 2) * 64
                    a3.inc(sc.copy(attn[fc][off:off + 64, n * BS0:(n + 1) * BS0],
                                   ps_y[m % 2][:, :]))
                    if m >= 2:
                        p3.wait(sc, sched[m - 2]["ssmm_end"])
                    a3.inc(sc.activation(sq_b[m % 2][:, :], ps_y[m % 2][:, :], AF.Square))
            d3.wait(sc, sched[-1]["ssacc_end"])
            sc.activation(lnr[:, :], ssb[0][:, :], AF.Ln, bias=ebias1[:, :]).then_inc(a_r.h, 1)
            a_r.wait(sc, 1)
            a_r.inc(sc.activation(rstd_r[:, :], lnr[:, :], AF.Exp, scale=-0.5,
                                  bias=ebias2[:, :]))

def _emit_p4(nc, T, ph):
    gate_w, out_w = T["gate_w"], T["out_w"]
    xn0_tm, attnn_fm, s0_fm = T["xn0_tm"], T["attnn_fm"], T["s0_fm"]
    gate_bfw = nc.dram_tensor("gate_bfw", [DM, DM], BF16, kind="Internal").ap()
    outw_bfw = nc.dram_tensor("outw_bfw", [DM, DM], BF16, kind="Internal").ap()

    if True:
        e = ph.st.enter_context
        xn = [e(_sbt(nc, ph, f"xn{i}", [128, S], BF16)) for i in range(16)]
        asl = [e(_sbt(nc, ph, f"asl{i}", [128, 512], BF16)) for i in range(16)]
        zsl = [e(_sbt(nc, ph, f"zsl{i}", [128, 512], BF16)) for i in range(16)]
        wA = [e(_sbt(nc, ph, f"wA{i}", [128, 16 * 512], BF16, is_transpose_dest=True)) for i in range(2)]
        wB = [e(_sbt(nc, ph, f"wB{i}", [128, 16 * 512], BF16, is_transpose_dest=True)) for i in range(2)]
        gsig = [e(_sbt(nc, ph, f"gsig{i}", [128, 512], F32)) for i in range(4)]
        sout = [e(_sbt(nc, ph, f"sout{i}", [128, 512], BF16)) for i in range(4)]
        psA = [e(_pst(nc, ph, f"psA{i}", [128, 512], F32)) for i in range(4)]
        psB = [e(_pst(nc, ph, f"psB{i}", [128, 512], F32)) for i in range(4)]

        gc = ph.pipe("gc", 16)
        ocst = ph.pipe("ocst", 16)
        sxn = ph.pipe("sxn", 16)
        ald = ph.pipe("ald", 16)
        swA = ph.spipe("swA", 2)
        swB = ph.spipe("swB", 2)
        a_mm = ph.pipe("a_mm")
        a_ev = ph.pipe("a_ev")
        d_z = ph.pipe("d_z")
        b_mm = ph.pipe("b_mm")
        b_ev = ph.pipe("b_ev")
        stp = ph.spipe("stp", 4)
        blk = ph.wrap(e(nc.Block()))

        @blk.gpsimd
        def _(g):
            for c in range(4):
                gc.inc(g.dma_start(gate_bfw[c * 512:(c + 1) * 512, :],
                                   gate_w[c * 512:(c + 1) * 512, :]))
            for c in range(4):
                ocst.inc(g.dma_start(outw_bfw[c * 512:(c + 1) * 512, :],
                                     out_w[c * 512:(c + 1) * 512, :]))
            for tt in range(4):
                d_z.wait(g, 16 * tt)
                for ic in range(16):
                    ald.inc(g.dma_start(asl[ic][:, :],
                                        attnn_fm[ic * 128:(ic + 1) * 128,
                                                 tt * 512:(tt + 1) * 512]))
                for gb in range(16):
                    k = tt * 16 + gb
                    b_ev.wait(g, k + 1)
                    ob, oc4 = gb // 4, gb % 4
                    stp.inc(g.dma_start(
                        s0_fm[ob * 512 + oc4 * 128: ob * 512 + (oc4 + 1) * 128,
                              tt * 512:(tt + 1) * 512],
                        sout[k % 4][:, :]), k)
            stp.wait_all(g, 16)

        @blk.sync
        def _(s):
            for ic in range(16):
                sxn.inc(s.dma_start_transpose(xn[ic][:, :], xn0_tm[:, ic * 128:(ic + 1) * 128]))
            gc.wait(s, 4)
            ocst.wait(s, 4)
            for tt in range(4):
                for ob in range(4):
                    ga = tt * 4 + ob
                    if ga >= 2:
                        a_mm.wait(s, 4 * (ga - 1))
                    for ic in range(16):
                        swA.inc(s.dma_start_transpose(
                            wA[ga % 2][:, ic * 512:(ic + 1) * 512],
                            gate_bfw[ob * 512:(ob + 1) * 512, ic * 128:(ic + 1) * 128]), ga)
                for ob in range(4):
                    gb = tt * 4 + ob
                    if gb >= 2:
                        b_mm.wait(s, 4 * (gb - 1))
                    for ic in range(16):
                        swB.inc(s.dma_start_transpose(
                            wB[gb % 2][:, ic * 512:(ic + 1) * 512],
                            outw_bfw[ob * 512:(ob + 1) * 512, ic * 128:(ic + 1) * 128]), gb)

        @blk.tensor
        def _(t):
            sxn.wait(t, 16)
            for tt in range(4):
                for ob in range(4):
                    ga = tt * 4 + ob
                    swA.wait(t, ga, 16 * (ga // 2 + 1))
                    for oc4 in range(4):
                        gidx = ga * 4 + oc4
                        if gidx >= 4:
                            a_ev.wait(t, gidx - 3)
                        for ic in range(16):
                            a_mm.inc(t.matmul(psA[oc4][:, :],
                                              wA[ga % 2][:, ic * 512 + oc4 * 128: ic * 512 + (oc4 + 1) * 128],
                                              xn[ic][:, tt * 512:(tt + 1) * 512],
                                              start=(ic == 0), stop=(ic == 15)),
                                     n=(1 if ic == 15 else 0))
                d_z.wait(t, 16 * (tt + 1))
                for ob in range(4):
                    gb = tt * 4 + ob
                    swB.wait(t, gb, 16 * (gb // 2 + 1))
                    for oc4 in range(4):
                        gidx = gb * 4 + oc4
                        if gidx >= 4:
                            b_ev.wait(t, gidx - 3)
                        for ic in range(16):
                            b_mm.inc(t.matmul(psB[oc4][:, :],
                                              wB[gb % 2][:, ic * 512 + oc4 * 128: ic * 512 + (oc4 + 1) * 128],
                                              zsl[ic][:, :],
                                              start=(ic == 0), stop=(ic == 15)),
                                     n=(1 if ic == 15 else 0))

        @blk.scalar
        def _(sc):
            for tt in range(4):
                for gidx in range(tt * 16, (tt + 1) * 16):
                    a_mm.wait(sc, gidx + 1)
                    if gidx >= 4:
                        d_z.wait(sc, gidx - 3)
                    a_ev.inc(sc.activation(gsig[gidx % 4][:, :], psA[gidx % 4 % 4][:, :], AF.Sigmoid))
                for gidx in range(tt * 16, (tt + 1) * 16):
                    b_mm.wait(sc, gidx + 1)
                    if gidx >= 4:
                        stp.wait(sc, gidx, gidx // 4)
                    b_ev.inc(sc.copy(sout[gidx % 4][:, :], psB[gidx % 4][:, :]))

        @blk.vector
        def _(v):
            for tt in range(4):
                ald.wait(v, 16 * (tt + 1))
                if tt >= 1:
                    b_mm.wait(v, 16 * tt)
                for gidx in range(tt * 16, (tt + 1) * 16):
                    a_ev.wait(v, gidx + 1)
                    ocg = (gidx % 16) // 4 * 4 + gidx % 4
                    ocg = gidx % 16
                    d_z.inc(v.tensor_tensor(zsl[ocg][:, :], gsig[gidx % 4][:, :],
                                            asl[ocg][:, :], ALU.mult))



def _p6_sched():
    sched = []
    for hp in range(H1 // 2):
        for it in range(4):
            jbs = list(range(4 * (it + 1)))[::-1]
            for j, jb in enumerate(jbs):
                for par in range(2):
                    sched.append(dict(h=2 * hp + par, it=it, jb=jb, par=par,
                                      first=(j == 0), last=(jb == 0)))
    N = len(sched)
    lcar = [0, 0]
    pool = 0
    for s in sched:
        par = s["par"]
        if s["first"]:
            lcar[par] = 0
        s["l"] = lcar[par]
        s["pool0"] = pool
        if not s["last"]:
            pool += 2
            lcar[par] += 1
        s["cr_end"] = pool
    pe = 0
    for st in range(N + 3):
        if st < N:
            pe += 1
            sched[st]["z_end"] = pe
        if st >= 2 and st - 2 < N:
            s = sched[st - 2]
            pe += 2 if s["first"] else 3
            s["s_end"] = pe
            pe += 1
            s["cs_end"] = pe
        if st >= 3 and st - 3 < N:
            pe += 1
            sched[st - 3]["o_end"] = pe
    act = 0
    for st in range(N + 4):
        if 1 <= st <= N:
            s = sched[st - 1]
            act += 1
            s["u_end"] = act
            act += 1
            s["sp_end"] = act
        if 2 <= st <= N + 1:
            act += 1
            sched[st - 2]["at_end"] = act
        if 4 <= st <= N + 3:
            s = sched[st - 4]
            if s["last"]:
                act += 1
                s["oev_end"] = act
    prev = [0, 0]
    for s in sched:
        s["oev_prev"] = prev[s["par"]]
        if s["last"]:
            prev[s["par"]] = s["oev_end"]
    dve = 0
    for st in range(N + 2):
        if 1 <= st <= N:
            s = sched[st - 1]
            dve += 1
            s["za_end"] = dve
            dve += 1
            s["zb_end"] = dve
        if 2 <= st <= N + 1:
            s = sched[st - 2]
            if not s["last"]:
                dve += 1
            s["cv_end"] = dve
    return sched


def _emit_p6(nc, T, ph):
    wq, wk, wv, wo = T["wq"], T["wk"], T["wv"], T["wo"]
    xn1_fm, s1_fm, C = T["xn1_fm"], T["s1_fm"], T
    bfw = [nc.dram_tensor(n, [DM, DM], BF16, kind="Internal").ap()
           for n in ("wq_bf", "wk_bf", "wv_bf", "wo_bf")]
    qkvf = [nc.dram_tensor(n, [DM, S], BF16, kind="Internal").ap()
            for n in ("q_fm", "k_fm", "v_fm")]
    sched = _p6_sched()

    if True:
        e = ph.st.enter_context
        xsl = [[e(_sbt(nc, ph, f"xsl{j}_{i}", [128, 512], BF16)) for i in range(16)]
               for j in range(2)]
        ofm = [e(_sbt(nc, ph, f"ofm{i}", [128, S], BF16)) for i in range(16)]
        wbuf = [e(_sbt(nc, ph, f"wbuf{i}", [128, 16 * 512], BF16)) for i in range(2)]
        evb = [e(_sbt(nc, ph, f"evb{i}", [128, 512], BF16)) for i in range(4)]
        qh = [e(_sbt(nc, ph, f"qh{i}", [64, S], BF16)) for i in range(4)]
        kh = [e(_sbt(nc, ph, f"kh{i}", [64, S], BF16)) for i in range(4)]
        vtb = [e(_sbt(nc, ph, f"vtb{i}", [128, 16 * 64], BF16, is_transpose_dest=True))
               for i in range(4)]
        za = [e(_sbt(nc, ph, f"za{i}", [128, 512], F32)) for i in range(2)]
        zbf = [e(_sbt(nc, ph, f"zbf{i}", [128, 512], BF16)) for i in range(2)]
        ub = [e(_sbt(nc, ph, f"ub{i}", [128, 512], F32)) for i in range(2)]
        spb = [e(_sbt(nc, ph, f"spb{i}", [128, 512], BF16)) for i in range(2)]
        atb = [e(_sbt(nc, ph, f"atb{i}", [128, 512], BF16)) for i in range(2)]
        cbr = [e(_sbt(nc, ph, f"cbr{i}", [1, 512], BF16)) for i in range(2)]
        colev = [e(_sbt(nc, ph, f"colev{i}", [1, 512], F32)) for i in range(2)]
        carf = [[e(_sbt(nc, ph, f"carf{i}_{k}", [1, 512], F32)) for k in range(2)]
                for i in range(2)]
        mskm = [e(_sbt(nc, ph, f"mskm{i}", [128, 512], BF16)) for i in range(4)]
        ntri = e(_sbt(nc, ph, "ntri", [128, 128], BF16))
        orow = e(_sbt(nc, ph, "orow", [1, 128], BF16))
        ncol = e(_sbt(nc, ph, "ncol", [128, 1], BF16))
        idb6 = e(_sbt(nc, ph, "idb6", [128, 128], BF16))
        pp = [e(_pst(nc, ph, f"pp{i}", [128, 512], F32)) for i in range(8)]

        cst = ph.pipe("cst", 16)
        gld = ph.pipe("gld", 16)
        g_eng = ph.pipe("g_eng")
        xlp = ph.spipe("xlp", 2)
        swp = ph.spipe("swp", 2)
        p_q = ph.pipe("p_q")
        a_qv = ph.pipe("a_qv")
        st_qv = ph.spipe("st_qv", 4)
        qldp = ph.spipe("qldp", 4)
        p_at = ph.pipe("p_at")
        a_at = ph.pipe("a_at")
        d_at = ph.pipe("d_at")
        g_cr = ph.pipe("g_cr")
        d_msk = ph.pipe("d_msk")
        blk = ph.wrap(e(nc.Block()))

        NG3 = 3 * 4 * 16
        last_of_head = {}
        for s_ in sched:
            last_of_head[s_["h"]] = s_

        @blk.gpsimd
        def _(g):
            for wi in range(4):
                cst.inc(g.dma_start(bfw[wi], [wq, wk, wv, wo][wi]))
            gld.inc(g.dma_start(ntri[:, :], C["ntri_bf"]))
            gld.inc(g.dma_start(orow[:, :], C["ones_row_bf"]))
            gld.inc(g.dma_start(ncol[:, :], C["negcol_bf"]))
            gld.inc(g.dma_start(idb6[:, :], C["ident_bf"]))
            g_eng.inc(g.iota(za[0][:, :], [[1, 512]], base=0, channel_multiplier=-1,
                             allow_small_or_imprecise_dtypes=True))
            for tt in range(4):
                if tt >= 2:
                    p_q.wait(g, 48 * (tt - 1))
                for ic in range(16):
                    xlp.inc(g.dma_start(
                        xsl[tt % 2][ic][:, :],
                        xn1_fm[ic * 128:(ic + 1) * 128, tt * 512:(tt + 1) * 512]), tt)
                for gi in range(tt * 48, (tt + 1) * 48):
                    rem = gi % 48
                    wi, ob, oc4 = rem // 16, (rem % 16) // 4, rem % 4
                    a_qv.wait(g, gi + 1)
                    st_qv.inc(g.dma_start(
                        qkvf[wi][ob * 512 + oc4 * 128: ob * 512 + (oc4 + 1) * 128,
                                 tt * 512:(tt + 1) * 512],
                        evb[gi % 4][:, :]), gi)
            for i, s_ in enumerate(sched):
                if s_["last"]:
                    continue
                par, l = s_["par"], s_["l"]
                d_at.wait(g, s_["cv_end"])
                g_cr.wait(g, s_["pool0"])
                if l == 0:
                    g_cr.inc(g.tensor_copy(carf[par][0][:, :], colev[par][:, :]))
                else:
                    g_cr.inc(g.tensor_tensor(carf[par][l % 2][:, :],
                                             carf[par][(l + 1) % 2][:, :],
                                             colev[par][:, :], ALU.add))
                g_cr.wait(g, s_["pool0"] + 1)
                g_cr.inc(g.tensor_copy(cbr[par][:, :], carf[par][l % 2][:, :]))
            for gi in range(64):
                ob, tt, oc4 = gi // 16, (gi % 16) // 4, gi % 4
                a_qv.wait(g, NG3 + gi + 1)
                st_qv.inc(g.dma_start(
                    s1_fm[ob * 512 + oc4 * 128: ob * 512 + (oc4 + 1) * 128,
                          tt * 512:(tt + 1) * 512],
                    evb[(NG3 + gi) % 4][:, :]), NG3 + gi)
            st_qv.wait_all(g, (NG3 + 64) // 4)

        @blk.sync
        def _(s):
            cst.wait(s, 4)
            for w in range(48):
                tt, gw = w // 12, w % 12
                wi, ob = gw // 4, gw % 4
                if w >= 2:
                    p_q.wait(s, 4 * (w - 1))
                for ic in range(16):
                    swp.inc(s.dma_start(
                        wbuf[w % 2][:, ic * 512:(ic + 1) * 512],
                        bfw[wi][ic * 128:(ic + 1) * 128, ob * 512:(ob + 1) * 512]), w)
            st_qv.wait_all(s, NG3 // 4)
            for h in range(H1):
                if h >= 4:
                    p_at.wait(s, last_of_head[h - 4]["o_end"])
                base = h * 64
                qldp.inc(s.dma_start(qh[h % 4][:, :], qkvf[0][base:base + 64, :]), h)
                qldp.inc(s.dma_start(kh[h % 4][:, :], qkvf[1][base:base + 64, :]), h)
                for jc in range(16):
                    qldp.inc(s.dma_start_transpose(
                        vtb[h % 4][:, jc * 64:(jc + 1) * 64],
                        qkvf[2][base:base + 64, jc * 128:(jc + 1) * 128]), h)
            for ob in range(4):
                w = 48 + ob
                if ob < 2:
                    p_q.wait(s, 4 * (w - 1))
                else:
                    p_q.wait(s, NG3 + 16 * (ob - 1))
                for ic in range(16):
                    swp.inc(s.dma_start(
                        wbuf[w % 2][:, ic * 512:(ic + 1) * 512],
                        bfw[3][ic * 128:(ic + 1) * 128, ob * 512:(ob + 1) * 512]), w)

        @blk.tensor
        def _(t):
            for gi in range(NG3):
                tt, rem = gi // 48, gi % 48
                wi, ob, oc4 = rem // 16, (rem % 16) // 4, rem % 4
                w = tt * 12 + wi * 4 + ob
                swp.wait(t, w, 16 * (w // 2 + 1))
                xlp.wait(t, tt, 16 * (tt // 2 + 1))
                if gi >= 4:
                    a_qv.wait(t, gi - 3)
                for ic in range(16):
                    p_q.inc(t.matmul(pp[gi % 4][:, :],
                                     wbuf[w % 2][:, ic * 512 + oc4 * 128: ic * 512 + (oc4 + 1) * 128],
                                     xsl[tt % 2][ic][:, :],
                                     start=(ic == 0), stop=(ic == 15)),
                            n=(1 if ic == 15 else 0))
            gld.wait(t, 4)
            d_msk.wait(t, 4)
            a_qv.wait(t, NG3)
            NSC = len(sched)
            for st in range(NSC + 3):
                if st < NSC:
                    s_ = sched[st]
                    h, it, jb, par = s_["h"], s_["it"], s_["jb"], s_["par"]
                    prev2 = sched[st - 2] if st >= 2 else None
                    qldp.wait(t, h, 18 * (h // 4 + 1))
                    if prev2 is not None:
                        d_at.wait(t, prev2["za_end"])
                    p_at.inc(t.matmul(pp[par * 4][:, :],
                                      kh[h % 4][:, jb * 128:(jb + 1) * 128],
                                      qh[h % 4][:, it * 512:it * 512 + 512],
                                      start=True, stop=True))
                if st >= 2 and st - 2 < NSC:
                    s_ = sched[st - 2]
                    par = s_["par"]
                    prev2 = sched[st - 4] if st >= 4 else None
                    bx, bc = par * 4 + 1, par * 4 + 3
                    a_at.wait(t, s_["sp_end"])
                    d_at.wait(t, s_["zb_end"])
                    if prev2 is not None:
                        a_at.wait(t, prev2["at_end"])
                        if not s_["first"]:
                            g_cr.wait(t, prev2["cr_end"])
                    p_at.inc(t.matmul(pp[bx][:, :], ntri[:, :], spb[par][:, :],
                                      start=True, stop=False))
                    p_at.inc(t.matmul(pp[bx][:, :], idb6[:, :], zbf[par][:, :],
                                      start=False, stop=s_["first"]))
                    if not s_["first"]:
                        p_at.inc(t.matmul(pp[bx][:, :], orow[:, :], cbr[par][:, :],
                                          start=False, stop=True))
                    if prev2 is not None:
                        d_at.wait(t, prev2["cv_end"])
                    p_at.inc(t.matmul(pp[bc][0:1, :], ncol[:, :], spb[par][:, :],
                                      start=True, stop=True))
                if st >= 3 and st - 3 < NSC:
                    s_ = sched[st - 3]
                    h, it, jb, par = s_["h"], s_["it"], s_["jb"], s_["par"]
                    a_at.wait(t, s_["at_end"])
                    if s_["first"]:
                        a_at.wait(t, s_["oev_prev"])
                    p_at.inc(t.matmul(pp[par * 4 + 2][0:64, :],
                                      vtb[h % 4][:, jb * 64:(jb + 1) * 64],
                                      atb[par][:, :],
                                      start=s_["first"], stop=s_["last"]))
            d_at.wait(t, sched[-1]["cv_end"])
            a_at.wait(t, max(x.get("oev_end", 0) for x in sched))
            for gi in range(64):
                ob, tt, oc4 = gi // 16, (gi % 16) // 4, gi % 4
                w = 48 + ob
                swp.wait(t, w, 16 * (w // 2 + 1))
                a_qv.wait(t, NG3 + gi - 3)
                for ic in range(16):
                    p_q.inc(t.matmul(pp[oc4][:, :],
                                     wbuf[w % 2][:, ic * 512 + oc4 * 128: ic * 512 + (oc4 + 1) * 128],
                                     ofm[ic][:, tt * 512:(tt + 1) * 512],
                                     start=(ic == 0), stop=(ic == 15)),
                            n=(1 if ic == 15 else 0))

        @blk.vector
        def _(v):
            g_eng.wait(v, 1)
            for r in range(4):
                d_msk.inc(v.tensor_scalar(mskm[r][:, :], za[0][:, :], 128.0 * r + 0.5,
                                          -10000.0, ALU.is_le, ALU.mult))
            d_msk.wait(v, 4)
            NSC = len(sched)
            for st in range(NSC + 2):
                if 1 <= st <= NSC:
                    s_ = sched[st - 1]
                    par, it, jb = s_["par"], s_["it"], s_["jb"]
                    prev2 = sched[st - 3] if st >= 3 else None
                    rel = jb * 128 - it * 512
                    bz = par * 4
                    p_at.wait(v, s_["z_end"])
                    if prev2 is not None:
                        a_at.wait(v, prev2["u_end"])
                    if rel >= 0:
                        d_at.inc(v.scalar_tensor_tensor(za[par][:, :], pp[bz][:, :], 80.0,
                                                        mskm[rel // 128][:, :],
                                                        ALU.min, ALU.add))
                    else:
                        d_at.inc(v.tensor_single_scalar(za[par][:, :], pp[bz][:, :],
                                                        80.0, ALU.min))
                    d_at.wait(v, s_["za_end"])
                    if prev2 is not None:
                        p_at.wait(v, prev2["s_end"])
                    d_at.inc(v.tensor_copy(zbf[par][:, :], za[par][:, :]))
                if 2 <= st <= NSC + 1:
                    s_ = sched[st - 2]
                    par = s_["par"]
                    if not s_["last"]:
                        p_at.wait(v, s_["cs_end"])
                        g_cr.wait(v, s_["pool0"])
                        d_at.inc(v.tensor_copy(colev[par][:, :], pp[par * 4 + 3][0:1, :]))

        @blk.scalar
        def _(sc):
            for gi in range(NG3):
                wi = (gi % 48) // 16
                p_q.wait(sc, gi + 1)
                if gi >= 4:
                    st_qv.wait(sc, gi, gi // 4)
                if wi == 0:
                    a_qv.inc(sc.activation(evb[gi % 4][:, :], pp[gi % 4][:, :], AF.Copy,
                                           scale=SC1))
                else:
                    a_qv.inc(sc.copy(evb[gi % 4][:, :], pp[gi % 4][:, :]))
            NSC = len(sched)
            for st in range(NSC + 4):
                if 1 <= st <= NSC:
                    s_ = sched[st - 1]
                    par = s_["par"]
                    prev2 = sched[st - 3] if st >= 3 else None
                    d_at.wait(sc, s_["za_end"])
                    if prev2 is not None:
                        a_at.wait(sc, prev2["sp_end"])
                    a_at.inc(sc.activation(ub[par][:, :], za[par][:, :], AF.Exp))
                    a_at.wait(sc, s_["u_end"])
                    if prev2 is not None:
                        p_at.wait(sc, prev2["cs_end"])
                    a_at.inc(sc.activation(spb[par][:, :], ub[par][:, :], AF.Ln, bias=1.0))
                if 2 <= st <= NSC + 1:
                    s_ = sched[st - 2]
                    par = s_["par"]
                    p_at.wait(sc, s_["s_end"])
                    a_at.inc(sc.activation(atb[par][:, :], pp[par * 4 + 1][:, :], AF.Exp))
                if 4 <= st <= NSC + 3:
                    s_ = sched[st - 4]
                    if s_["last"]:
                        h, it, par = s_["h"], s_["it"], s_["par"]
                        p_at.wait(sc, s_["o_end"])
                        fc, off = h // 2, (h % 2) * 64
                        a_at.inc(sc.copy(
                            ofm[fc][off:off + 64, it * 512:(it + 1) * 512],
                            pp[par * 4 + 2][0:64, :]))
            for gi in range(64):
                p_q.wait(sc, NG3 + gi + 1)
                st_qv.wait(sc, NG3 + gi, (NG3 + gi) // 4)
                a_qv.inc(sc.copy(evb[(NG3 + gi) % 4][:, :], pp[gi % 4][:, :]))


def _emit_p5a(nc, T, ph):
    sw1, sw2, w12, combT = T["sw1"], T["sw2"], T["w12"], T["combT"]
    xn0_tm, hsh_fm, hrt_fm, C = T["xn0_tm"], T["hsh_fm"], T["hrt_fm"], T
    sw1_bf = nc.dram_tensor("sw1_bf", [HS, DM], BF16, kind="Internal").ap()
    sw2_bf = nc.dram_tensor("sw2_bf", [HS, DM], BF16, kind="Internal").ap()
    w12_bf = nc.dram_tensor("w12_bf", [E0 * 2 * HR, DM], BF16, kind="Internal").ap()
    NQ = 32

    if True:
        e = ph.st.enter_context
        xn = [e(_sbt(nc, ph, f"xn{i}", [128, S], BF16)) for i in range(16)]
        wb1 = [e(_sbt(nc, ph, f"wb1{i}", [128, 16 * 512], BF16, is_transpose_dest=True)) for i in range(2)]
        wb2 = [e(_sbt(nc, ph, f"wb2{i}", [128, 16 * 512], BF16, is_transpose_dest=True)) for i in range(2)]
        sg = [e(_sbt(nc, ph, f"sg{i}", [128, 512], F32)) for i in range(4)]
        tm = [e(_sbt(nc, ph, f"tm{i}", [128, 512], F32)) for i in range(4)]
        hout = [e(_sbt(nc, ph, f"hout{i}", [128, 512], BF16)) for i in range(4)]
        cb = e(_sbt(nc, ph, "cb", [128, 32 * 512], BF16))
        ctb = [e(_sbt(nc, ph, f"ctb{i}", [1, S], BF16)) for i in range(2)]
        orow = e(_sbt(nc, ph, "orow", [1, 128], BF16))
        psG = [e(_pst(nc, ph, f"psG{i}", [128, 512], F32)) for i in range(4)]
        psU = [e(_pst(nc, ph, f"psU{i}", [128, 512], F32)) for i in range(4)]

        c1 = ph.pipe("c1", 16)
        c2 = ph.pipe("c2", 16)
        c12 = ph.pipe("c12", 16)
        gld = ph.pipe("gld", 16)
        ctp5 = ph.spipe("ctp5", 2)
        sxn = ph.pipe("sxn", 16)
        sw = ph.spipe("swl", 2)
        p_cb = ph.pipe("p_cb")
        d_cb = ph.pipe("d_cb")
        p_mm = ph.pipe("p_mm")
        a_ev = ph.pipe("a_ev")
        d_h = ph.pipe("d_h")
        stp = ph.spipe("stp", 4)
        cda = ph.chain("cda5")
        blk = ph.wrap(e(nc.Block()))

        def q_rows(q):
            if q < 16:
                return (sw1_bf, q * 512, sw2_bf, q * 512, hsh_fm, q * 512)
            r = q - 16
            ee, hb1 = r // 2, r % 2
            return (w12_bf, ee * 2048 + hb1 * 512, w12_bf, ee * 2048 + 1024 + hb1 * 512,
                    hrt_fm, ee * 1024 + hb1 * 512)

        @blk.gpsimd
        def _(g):
            for c in range(4):
                c1.inc(g.dma_start(sw1_bf[c * 2048:(c + 1) * 2048, :],
                                   sw1[c * 2048:(c + 1) * 2048, :]))
            for c in range(4):
                c2.inc(g.dma_start(sw2_bf[c * 2048:(c + 1) * 2048, :],
                                   sw2[c * 2048:(c + 1) * 2048, :]))
            for c in range(8):
                c12.inc(g.dma_start(w12_bf[c * 2048:(c + 1) * 2048, :],
                                    w12[c * 2048:(c + 1) * 2048, :]))
            gld.inc(g.dma_start(orow[:, :], C["ones_row_bf"]))
            for ee in range(E0):
                if ee >= 2:
                    p_cb.wait(g, 4 * (ee - 1))
                ctp5.inc(g.dma_start(ctb[ee % 2][:, :], combT[ee:ee + 1, :]), ee)
            for q in range(NQ):
                for tt in range(4):
                    for oc4 in range(4):
                        s_ = q * 16 + tt * 4 + oc4
                        d_h.wait(g, s_ + 1)
                        h0a, b0, h1a, b1, outh, ob = q_rows(q)
                        stp.inc(g.dma_start(
                            outh[ob + oc4 * 128: ob + (oc4 + 1) * 128,
                                 tt * 512:(tt + 1) * 512],
                            hout[s_ % 4][:, :]), s_)
            stp.wait_all(g, NQ * 16 // 4)

        @blk.sync
        def _(s):
            for ic in range(16):
                sxn.inc(s.dma_start_transpose(xn[ic][:, :], xn0_tm[:, ic * 128:(ic + 1) * 128]))
            c1.wait(s, 4)
            c2.wait(s, 4)
            for q in range(NQ):
                if q == 16:
                    c12.wait(s, 8)
                if q >= 2:
                    p_mm.wait(s, 32 * (q - 1))
                h0a, b0, h1a, b1, _, _ = q_rows(q)
                for ic in range(16):
                    sw.inc(s.dma_start_transpose(
                        wb1[q % 2][:, ic * 512:(ic + 1) * 512],
                        h0a[b0:b0 + 512, ic * 128:(ic + 1) * 128]), q)
                for ic in range(16):
                    sw.inc(s.dma_start_transpose(
                        wb2[q % 2][:, ic * 512:(ic + 1) * 512],
                        h1a[b1:b1 + 512, ic * 128:(ic + 1) * 128]), q)

        @blk.tensor
        def _(t):
            gld.wait(t, 1)
            for k in range(32):
                ee, tt = k // 4, k % 4
                ctp5.wait(t, ee, ee // 2 + 1)
                if k >= 2:
                    d_cb.wait(t, k - 1)
                p_cb.inc(t.matmul(psG[k % 2][:, :], orow[:, :],
                                  ctb[ee % 2][:, tt * 512:(tt + 1) * 512],
                                  start=True, stop=True))
            sxn.wait(t, 16)
            d_cb.wait(t, 32)
            for q in range(NQ):
                sw.wait(t, q, 32 * (q // 2 + 1))
                for tt in range(4):
                    for half in range(2):
                        wb = wb1 if half == 0 else wb2
                        ps = psG if half == 0 else psU
                        for oc4 in range(4):
                            u = q * 32 + tt * 8 + half * 4 + oc4
                            s_ = q * 16 + tt * 4 + oc4
                            if half == 0:
                                a_ev.wait(t, s_ - 3)
                            else:
                                d_h.wait(t, s_ - 3)
                            for ic in range(16):
                                p_mm.inc(t.matmul(
                                    ps[oc4][:, :],
                                    wb[q % 2][:, ic * 512 + oc4 * 128: ic * 512 + (oc4 + 1) * 128],
                                    xn[ic][:, tt * 512:(tt + 1) * 512],
                                    start=(ic == 0), stop=(ic == 15)),
                                    n=(1 if ic == 15 else 0))

        @blk.scalar
        def _(sc):
            for q in range(NQ):
                for tt in range(4):
                    for oc4 in range(4):
                        s_ = q * 16 + tt * 4 + oc4
                        p_mm.wait(sc, q * 32 + tt * 8 + oc4 + 1)
                        if s_ >= 4:
                            d_h.wait(sc, s_ - 3)
                        a_ev.inc(sc.activation(sg[s_ % 4][:, :], psG[oc4][:, :], AF.Silu))

        @blk.vector
        def _(v):
            for k in range(32):
                p_cb.wait(v, k + 1)
                d_cb.inc(v.tensor_copy(cb[:, k * 512:(k + 1) * 512], psG[k % 2][:, :]))
            d_cb.wait(v, 32)
            for q in range(NQ):
                for tt in range(4):
                    for oc4 in range(4):
                        s_ = q * 16 + tt * 4 + oc4
                        u_u = q * 32 + tt * 8 + 4 + oc4
                        p_mm.wait(v, u_u + 1)
                        a_ev.wait(v, s_ + 1)
                        if s_ >= 4:
                            stp.wait(v, s_, s_ // 4)
                        if q < 16:
                            d_h.inc(v.tensor_tensor(hout[s_ % 4][:, :], sg[s_ % 4][:, :],
                                                    psU[oc4][:, :], ALU.mult))
                        else:
                            ee = (q - 16) // 2
                            cda.seal(v.tensor_tensor(tm[s_ % 4][:, :], sg[s_ % 4][:, :],
                                                     psU[oc4][:, :], ALU.mult))
                            cda.gate(v)
                            d_h.inc(v.tensor_tensor(
                                hout[s_ % 4][:, :], tm[s_ % 4][:, :],
                                cb[:, (ee * 4 + tt) * 512:(ee * 4 + tt + 1) * 512],
                                ALU.mult))



def _emit_p5b(nc, T, ph):
    sw3, w3, s0_fm, x0_fm = T["sw3"], T["w3"], T["s0_fm"], T["x0_fm"]
    hsh_fm, hrt_fm, x1_fm, xn1_fm, C = (T["hsh_fm"], T["hrt_fm"], T["x1_fm"],
                                        T["xn1_fm"], T)
    sw3_bf = nc.dram_tensor("sw3_bf", [DM, HS], BF16, kind="Internal").ap()
    w3_bf = nc.dram_tensor("w3_bf", [E0 * DM, HR], BF16, kind="Internal").ap()

    if True:
        e = ph.st.enter_context
        hsl4 = [e(_sbt(nc, ph, f"hsl4_{i}", [128, 4 * 512], BF16)) for i in range(16)]
        rsl4 = [e(_sbt(nc, ph, f"rsl4_{i}", [128, 4 * 512], BF16)) for i in range(16)]
        s0t = [e(_sbt(nc, ph, f"s0t{i}", [128, 512], BF16)) for i in range(4)]
        x0t = [e(_sbt(nc, ph, f"x0t{i}", [128, 512], F32)) for i in range(4)]
        x1sl = [e(_sbt(nc, ph, f"x1sl{i}", [128, 512], F32)) for i in range(4)]
        x1r = [e(_sbt(nc, ph, f"x1r{i}", [128, 512], F32)) for i in range(4)]
        wt = [e(_sbt(nc, ph, f"wt{i}", [128, 512], BF16, is_transpose_dest=True)) for i in range(8)]
        gbuf = [e(_sbt(nc, ph, f"gbuf{i}", [128, 512], F32)) for i in range(2)]
        sgb = [e(_sbt(nc, ph, f"sgb{i}", [128, 512], F32)) for i in range(2)]
        sqb = [e(_sbt(nc, ph, f"sqb{i}", [128, 512], F32)) for i in range(2)]
        xnout = [e(_sbt(nc, ph, f"xnout{i}", [128, 512], BF16)) for i in range(4)]
        aptmp = [e(_sbt(nc, ph, f"aptmp{i}", [128, 512], F32)) for i in range(2)]
        onesf = e(_sbt(nc, ph, "onesf", [1, 128], F32))
        onesc = e(_sbt(nc, ph, "onesc", [128, 1], F32))
        murow = e(_sbt(nc, ph, "murow", [1, 512], F32))
        varow = e(_sbt(nc, ph, "varow", [1, 512], F32))
        va2 = e(_sbt(nc, ph, "va2", [1, 512], F32))
        var3 = e(_sbt(nc, ph, "var3", [1, 512], F32))
        lnro = e(_sbt(nc, ph, "lnro", [1, 512], F32))
        rstdr = e(_sbt(nc, ph, "rstdr", [1, 512], F32))
        biasr = e(_sbt(nc, ph, "biasr", [1, 512], F32))
        epsr = e(_sbt(nc, ph, "epsr", [1, 1], F32))
        ps = [e(_pst(nc, ph, f"ps{i}", [128, 512], F32)) for i in range(4)]
        psS = e(_pst(nc, ph, "psS", [1, 512], F32))
        psQ = e(_pst(nc, ph, "psQ", [1, 512], F32))
        psB = [e(_pst(nc, ph, f"psB{i}", [128, 512], F32)) for i in range(2)]

        gcl = ph.pipe("gcl", 16)
        c3 = ph.pipe("c3", 16)
        cw3 = ph.pipe("cw3", 16)
        hld = ph.spipe("hld", 8)
        as0 = ph.spipe("as0", 4)
        ax0 = ph.spipe("ax0", 4)
        ax1 = ph.spipe("ax1", 4)
        swl = ph.spipe("swl", 8)
        p_mm = ph.pipe("p_mm")
        a_g = ph.pipe("a_g")
        d_x = ph.pipe("d_x")
        a_q = ph.pipe("a_q")
        p_st = ph.pipe("p_st")
        d_row = ph.pipe("d_row")
        a_row = ph.pipe("a_row")
        p_bc = ph.pipe("p_bc")
        d_ap = ph.pipe("d_ap")
        st1 = ph.spipe("st1", 4)
        stn = ph.spipe("stn", 4)
        cdv = ph.chain("cdvb")
        cac = ph.chain("cacb")
        blk = ph.wrap(e(nc.Block()))

        NIC = 64 + 64

        @blk.gpsimd
        def _(g):
            g.memset(epsr[:, :], EPS)
            gcl.inc(g.dma_start(onesf[:, :], C["onesf_row"]))
            gcl.inc(g.dma_start(onesc[:, :], C["onesf_col"]))
            for c in range(4):
                c3.inc(g.dma_start(sw3_bf[c * 512:(c + 1) * 512, :],
                                   sw3[c * 512:(c + 1) * 512, :]))
            for c in range(8):
                cw3.inc(g.dma_start(w3_bf[c * 2048:(c + 1) * 2048, :],
                                    w3[c * 2048:(c + 1) * 2048, :]))
            for tt in range(4):
                for jb4 in range(32):
                    j = jb4 * 4
                    gate = 2048 * tt
                    if j >= 16:
                        gate += (j - 15) * 4
                    p_mm.wait(g, gate)
                    srcd = hsh_fm if j < 64 else hrt_fm
                    ic = j if j < 64 else j - 64
                    tgt = hquad[jb4 % 2][jb4 // 2 % 2] if False else None
                    hld.inc(g.dma_start(
                        (hsl4[j // 4] if j < 64 else rsl4[(j - 64) // 4])[:, :].rearrange(
                            "p (a n) -> p a n", a=4),
                        srcd[ic * 128:(ic + 4) * 128,
                             tt * 512:(tt + 1) * 512].rearrange("(a p) n -> p a n", a=4)),
                        tt * 32 + jb4)
                for oi in range(20):
                    if oi < 16:
                        k = tt * 16 + oi
                        d_x.wait(g, k - 3)
                        as0.inc(g.dma_start(s0t[k % 4][:, :],
                                            s0_fm[oi * 128:(oi + 1) * 128, tt * 512:(tt + 1) * 512]), k)
                        ax0.inc(g.dma_start(x0t[k % 4][:, :],
                                            x0_fm[oi * 128:(oi + 1) * 128, tt * 512:(tt + 1) * 512]), k)
                    if oi >= 4:
                        ocg = oi - 4
                        k = tt * 16 + ocg
                        d_x.wait(g, k + 1)
                        st1.inc(g.dma_start(
                            x1_fm[ocg * 128:(ocg + 1) * 128, tt * 512:(tt + 1) * 512],
                            x1sl[k % 4][:, :]), k)
                for ocg in range(16):
                    k = tt * 16 + ocg
                    d_ap.wait(g, k - 3)
                    st1.wait(g, k, k // 4 + 1)
                    ax1.inc(g.dma_start(
                        x1r[k % 4][:, :],
                        x1_fm[ocg * 128:(ocg + 1) * 128, tt * 512:(tt + 1) * 512]), k)
                    d_ap.wait(g, k + 1)
                    stn.inc(g.dma_start(
                        xn1_fm[ocg * 128:(ocg + 1) * 128, tt * 512:(tt + 1) * 512],
                        xnout[k % 4][:, :]), k)
            st1.wait_all(g, 16)
            stn.wait_all(g, 16)

        @blk.sync
        def _(s):
            c3.wait(s, 4)
            cw3.wait(s, 8)
            wcount = 0
            for tt in range(4):
                for ob in range(4):
                    for icg in range(NIC):
                        if wcount >= 8:
                            p_mm.wait(s, (wcount - 7) * 4)
                        if icg < 64:
                            src = sw3_bf[ob * 512:(ob + 1) * 512, icg * 128:(icg + 1) * 128]
                        else:
                            icr = icg - 64
                            ee, rr = icr // 8, icr % 8
                            src = w3_bf[ee * 2048 + ob * 512: ee * 2048 + (ob + 1) * 512,
                                        rr * 128:(rr + 1) * 128]
                        swl.inc(s.dma_start_transpose(wt[wcount % 8][:, :], src), wcount)
                        wcount += 1

        @blk.tensor
        def _(t):
            wcount = 0
            for tt in range(4):
                for ob in range(4):
                    for oc4 in range(4):
                        s_ = (tt * 4 + ob) * 4 + oc4
                        if s_ >= 4:
                            a_g.wait(t, s_ - 3)
                    for icg in range(NIC):
                        swl.wait(t, wcount, wcount // 8 + 1)
                        jq = tt * 32 + icg // 4
                        hld.wait(t, jq, jq // 8 + 1)
                        fat = hsl4[icg // 4] if icg < 64 else rsl4[(icg - 64) // 4]
                        rhs = fat[:, (icg % 4) * 512:(icg % 4 + 1) * 512]
                        for oc4 in range(4):
                            p_mm.inc(t.matmul(
                                ps[oc4][:, :],
                                wt[wcount % 8][:, oc4 * 128:(oc4 + 1) * 128],
                                rhs,
                                start=(icg == 0), stop=(icg == NIC - 1)))
                        wcount += 1
                    for oc4 in range(4):
                        ocg = ob * 4 + oc4
                        k = tt * 16 + ocg
                        d_x.wait(t, k + 1)
                        if ocg == 0:
                            gcl.wait(t, 2)
                            d_row.wait(t, 2 * tt)
                            d_ap.wait(t, 16 * tt)
                        p_st.inc(t.matmul(psS[:, :], onesc[:, :], x1sl[k % 4][:, :],
                                          start=(ocg == 0), stop=(ocg == 15)))
                        a_q.wait(t, k + 1)
                        p_st.inc(t.matmul(psQ[:, :], onesc[:, :], sqb[ocg % 2][:, :],
                                          start=(ocg == 0), stop=(ocg == 15)))
                a_row.wait(t, tt + 1)
                d_row.wait(t, 2 * (tt + 1))
                p_bc.inc(t.matmul(psB[0][:, :], onesf[:, :], rstdr[:, :], start=True, stop=True))
                p_bc.inc(t.matmul(psB[1][:, :], onesf[:, :], biasr[:, :], start=True, stop=True))

        @blk.scalar
        def _(sc):
            for tt in range(4):
                for si in range(17):
                    if si >= 1:
                        ocg = si - 1
                        k = tt * 16 + ocg
                        d_x.wait(sc, k + 1)
                        if k >= 2:
                            p_st.wait(sc, 2 * (k - 1))
                        a_q.inc(sc.activation(sqb[ocg % 2][:, :], x1sl[k % 4][:, :], AF.Square))
                    if si < 16:
                        ob, oc4 = si // 4, si % 4
                        s_ = tt * 16 + si
                        p_mm.wait(sc, ((tt * 4 + ob) * NIC) * 4 + (NIC - 1) * 4 + oc4 + 1)
                        if s_ >= 2:
                            d_x.wait(sc, s_ - 1)
                        a_g.inc(sc.activation(gbuf[s_ % 2][:, :], ps[oc4][:, :], AF.Silu))
                d_row.wait(sc, 2 * tt + 1)
                cac.seal(sc.activation(lnro[:, :], var3[:, :], AF.Ln, bias=epsr[:, :]))
                cac.gate(sc)
                a_row.inc(sc.activation(rstdr[:, :], lnro[:, :], AF.Exp, scale=-0.5))

        @blk.vector
        def _(v):
            for tt in range(4):
                for ob in range(4):
                    for oc4 in range(4):
                        s_ = (tt * 4 + ob) * 4 + oc4
                        ocg = ob * 4 + oc4
                        k = tt * 16 + ocg
                        as0.wait(v, k, k // 4 + 1)
                        ax0.wait(v, k, k // 4 + 1)
                        a_g.wait(v, s_ + 1)
                        if k >= 4:
                            st1.wait(v, k, k // 4)
                            a_q.wait(v, k - 3)
                            p_st.wait(v, 2 * (k - 3))
                        cdv.seal(v.tensor_tensor(sgb[oc4 % 2][:, :], gbuf[s_ % 2][:, :],
                                                 s0t[k % 4][:, :], ALU.mult))
                        cdv.gate(v)
                        d_x.inc(v.tensor_tensor(x1sl[k % 4][:, :], sgb[oc4 % 2][:, :],
                                                x0t[k % 4][:, :], ALU.add))
                p_st.wait(v, 2 * 16 * (tt + 1))
                cdv.seal(v.tensor_scalar_mul(murow[:, :], psS[:, :], 1.0 / DM))
                cdv.seal(v.tensor_scalar_mul(varow[:, :], psQ[:, :], 1.0 / DM))
                cdv.gate(v)
                cdv.seal(v.scalar_tensor_tensor(va2[:, :], murow[:, :], -1.0, murow[:, :],
                                                ALU.mult, ALU.mult))
                cdv.gate(v)
                d_row.inc(v.tensor_tensor(var3[:, :], varow[:, :], va2[:, :], ALU.add))
                a_row.wait(v, tt + 1)
                d_row.inc(v.scalar_tensor_tensor(biasr[:, :], murow[:, :], -1.0, rstdr[:, :],
                                                 ALU.mult, ALU.mult))
                p_bc.wait(v, 2 * (tt + 1))
                for ocg in range(16):
                    k = tt * 16 + ocg
                    ax1.wait(v, k, k // 4 + 1)
                    if k >= 4:
                        stn.wait(v, k, k // 4)
                    cdv.seal(v.tensor_tensor(aptmp[k % 2][:, :], x1r[k % 4][:, :],
                                             psB[0][:, :], ALU.mult))
                    cdv.gate(v)
                    d_ap.inc(v.tensor_tensor(xnout[k % 4][:, :], aptmp[k % 2][:, :],
                                             psB[1][:, :], ALU.add))




def _emit_p7(nc, T, ph):
    wbase, bbase, wrout, aw, bw = T["wbase"], T["bbase"], T["wrout"], T["aw"], T["bw"]
    xn1_fm, x1_fm, s1_fm, out_tm, C = (T["xn1_fm"], T["x1_fm"], T["s1_fm"],
                                       T["out_tm"], T)
    wbase_bf = nc.dram_tensor("wbase_bf", [DM, DM], BF16, kind="Internal").ap()
    a_bfd = nc.dram_tensor("a_bfd", [E1 * R1, DM], BF16, kind="Internal").ap()
    b_bfd = nc.dram_tensor("b_bfd", [E1 * DM, R1], BF16, kind="Internal").ap()
    rout_bfd = nc.dram_tensor("rout_bfd", [E1, DM], BF16, kind="Internal").ap()
    bt_dram = nc.dram_tensor("bt_dram", [E1 * R1, DM], BF16, kind="Internal").ap()
    NTC = 16

    def gsplit(gi):
        ob, tt, oc4 = gi // 16, (gi // 4) % 4, gi % 4
        ocg = ob * 4 + oc4
        return ob, tt, oc4, ocg, tt * 16 + ocg

    if True:
        e = ph.st.enter_context
        xn = [e(_sbt(nc, ph, f"xn{i}", [128, S], BF16)) for i in range(16)]
        wbt = [e(_sbt(nc, ph, f"wbt{i}", [128, 16 * 512], BF16, is_transpose_dest=True)) for i in range(2)]
        idb = e(_sbt(nc, ph, "idb", [128, 128], BF16))
        idf = e(_sbt(nc, ph, "idf", [128, 128], F32))
        rrep_sb = e(_sbt(nc, ph, "rrep_sb", [E1, 64], BF16))
        a_sb = e(_sbt(nc, ph, "a_sb", [E1 * R1, S], BF16))
        A_T = e(_sbt(nc, ph, "A_T", [128, 16 * 64], BF16))
        btmp = e(_sbt(nc, ph, "btmp", [128, 2 * E1], BF16))
        bstg = [e(_sbt(nc, ph, f"bstg{i}", [E1, 128], BF16)) for i in range(2)]
        B_T = e(_sbt(nc, ph, "B_T", [E1 * R1, DM], BF16))
        rsb = e(_sbt(nc, ph, "rsb", [E1, DM], BF16))
        routerT = e(_sbt(nc, ph, "routerT", [128, 128], BF16))
        lg_tm = [e(_sbt(nc, ph, f"lg_tm{i}", [128, E1], F32)) for i in range(2)]
        m1 = e(_sbt(nc, ph, "m1c", [128, 1], F32))
        m2 = e(_sbt(nc, ph, "m2c", [128, 1], F32))
        is1 = e(_sbt(nc, ph, "is1c", [128, E1], F32))
        sel2 = e(_sbt(nc, ph, "sel2c", [128, E1], F32))
        ind = e(_sbt(nc, ph, "indc", [128, E1], F32))
        dcol = e(_sbt(nc, ph, "dcol", [128, 1], F32))
        sgc = e(_sbt(nc, ph, "sgc", [128, 1], F32))
        difc = e(_sbt(nc, ph, "difc", [128, 1], F32))
        wbc = e(_sbt(nc, ph, "wbc", [128, 1], F32))
        t1c = e(_sbt(nc, ph, "t1c", [128, E1], F32))
        w_tm = [e(_sbt(nc, ph, f"w_tm{i}", [128, E1], BF16)) for i in range(2)]
        wT_sb = e(_sbt(nc, ph, "wT_sb", [E1, S], BF16))
        wb_sb = e(_sbt(nc, ph, "wb_sb", [64, S], BF16))
        h_sb = e(_sbt(nc, ph, "h_sb", [64, S], BF16))
        hs_sb = e(_sbt(nc, ph, "hs_sb", [64, S], BF16))
        bbrow = e(_sbt(nc, ph, "bbrow", [1, DM], F32))
        bb_cols = e(_sbt(nc, ph, "bb_cols", [128, 16], F32))
        gb = [e(_sbt(nc, ph, f"gb{i}", [128, 512], F32)) for i in range(2)]
        sgb = [e(_sbt(nc, ph, f"sgb{i}", [128, 512], F32)) for i in range(2)]
        x2b = [e(_sbt(nc, ph, f"x2b{i}", [128, 512], F32)) for i in range(2)]
        s1t = [e(_sbt(nc, ph, f"s1t{i}", [128, 512], BF16)) for i in range(4)]
        x1t = [e(_sbt(nc, ph, f"x1t{i}", [128, 512], F32)) for i in range(4)]
        otb = [e(_sbt(nc, ph, f"otb{i}", [128, 128], F32)) for i in range(4)]
        psc = [e(_pst(nc, ph, f"psc{i}", [128, 512], F32)) for i in range(4)]
        pst = [e(_pst(nc, ph, f"pst{i}", [128, 128], F32)) for i in range(2)]
        ps_tb = e(_pst(nc, ph, "ps_tb", [128, 128], BF16))

        cbs = ph.pipe("cbs", 16)
        cbo = ph.pipe("cbo", 16)
        gl7 = ph.pipe("gl7", 16)
        xl7 = ph.pipe("xl7", 16)
        swb = ph.spipe("swb", 2)
        sx1 = ph.spipe("sx1", 4)
        ss1 = ph.spipe("ss1", 4)
        sto = ph.spipe("sto", 4)
        p_su = ph.pipe("p_su")
        d_su = ph.pipe("d_su")
        a_su = ph.pipe("a_su")
        g_su = ph.pipe("g_su", 16)
        p_mm = ph.pipe("p_mm")
        a_g = ph.pipe("a_g")
        d_x = ph.pipe("d_x")
        p_tp = ph.pipe("p_tp")
        d_ev = ph.pipe("d_ev")
        cdv = ph.chain("cdv7")
        blk = ph.wrap(e(nc.Block()))

        ss = SSched({"pe": p_su, "dve": d_su, "act": a_su, "gp": g_su})

        def mk(fn):
            return fn

        for ic in range(16):
            ss.add("pe", (lambda ic: lambda t: t.transpose(
                ps_tb[:, 0:64], a_sb[:, ic * 128:(ic + 1) * 128], idb[0:64, 0:64]))(ic),
                reads=("a_sb",), writes=("ps_tb",), extra=[(gl7, 6)])
            ss.add("dve", (lambda ic: lambda v: v.tensor_copy(
                A_T[:, ic * 64:(ic + 1) * 64], ps_tb[:, 0:64]))(ic),
                reads=("ps_tb",), writes=("A_T",))
        for j in range(128):
            ee, oc = j // 16, j % 16
            ss.add("gp", (lambda j, ee, oc: lambda g: g.dma_start(
                btmp[:, (j % 2) * E1:(j % 2 + 1) * E1],
                b_bfd[ee * 2048 + oc * 128: ee * 2048 + (oc + 1) * 128, :]))(j, ee, oc),
                reads=(), writes=(f"btmp{j % 2}",), extra=[(cbo, 3)])
            ss.add("pe", (lambda j: lambda t: t.transpose(
                ps_tb[0:E1, 0:128], btmp[:, (j % 2) * E1:(j % 2 + 1) * E1], idb[:, :]))(j),
                reads=(f"btmp{j % 2}",), writes=("ps_tb",))
            ss.add("dve", (lambda j: lambda v: v.tensor_copy(
                bstg[j % 2][:, :], ps_tb[0:E1, 0:128]))(j),
                reads=("ps_tb",), writes=(f"bstg{j % 2}",))
            ss.add("gp", (lambda j, ee, oc: lambda g: g.dma_start(
                bt_dram[ee * R1:(ee + 1) * R1, oc * 128:(oc + 1) * 128],
                bstg[j % 2][:, :]))(j, ee, oc),
                reads=(f"bstg{j % 2}",), writes=("bt_dram",))
        ss.add("gp", lambda g: g.dma_start(B_T[:, :], bt_dram),
               reads=("bt_dram",), writes=("B_T",))
        for ic in range(16):
            ss.add("pe", (lambda ic: lambda t: t.transpose(
                ps_tb[:, 0:E1], rsb[:, ic * 128:(ic + 1) * 128], idb[0:E1, 0:E1]))(ic),
                reads=("rsb",), writes=("ps_tb",), extra=[(gl7, 6)])
            ss.add("dve", (lambda ic: lambda v: v.tensor_copy(
                routerT[:, ic * 8:(ic + 1) * 8], ps_tb[:, 0:E1]))(ic),
                reads=("ps_tb",), writes=("routerT",))
        for ic in range(16):
            ss.add("pe", (lambda ic: lambda t: t.transpose(
                pst[0][:, 0:1], bbrow[:, ic * 128:(ic + 1) * 128], idf[0:1, 0:1]))(ic),
                reads=("bbrow",), writes=("pst0",), extra=[(gl7, 6)])
            ss.add("dve", (lambda ic: lambda v: v.tensor_copy(
                bb_cols[:, ic:ic + 1], pst[0][:, 0:1]))(ic),
                reads=("pst0",), writes=("bb_cols",))

        def lg_group(tc):
            def fn(t):
                for ic in range(15):
                    t.matmul(pst[tc % 2][:, 0:E1], xn[ic][:, tc * 128:(tc + 1) * 128],
                             routerT[:, ic * 8:(ic + 1) * 8],
                             start=(ic == 0), stop=False)
                return t.matmul(pst[tc % 2][:, 0:E1], xn[15][:, tc * 128:(tc + 1) * 128],
                                routerT[:, 15 * 8:16 * 8], start=False, stop=True)
            return fn

        def chain_a(tc):
            def fn(v):
                cdv.seal(v.reduce_max(m1[:, :], lg_tm[tc % 2][:, :], axis=AX.X))
                cdv.gate(v)
                cdv.seal(v.tensor_single_scalar(is1[:, :], lg_tm[tc % 2][:, :], m1[:, 0:1], ALU.is_ge))
                cdv.gate(v)
                cdv.seal(v.scalar_tensor_tensor(sel2[:, :], is1[:, :], -1e9, lg_tm[tc % 2][:, :],
                                                ALU.mult, ALU.add))
                cdv.gate(v)
                cdv.seal(v.reduce_max(m2[:, :], sel2[:, :], axis=AX.X))
                cdv.gate(v)
                cdv.seal(v.tensor_single_scalar(ind[:, :], lg_tm[tc % 2][:, :], m2[:, 0:1], ALU.is_ge))
                cdv.gate(v)
                return v.tensor_tensor(dcol[:, :], m1[:, :], m2[:, :], ALU.subtract)
            return fn

        def chain_b(tc):
            def fn(v):
                cdv.seal(v.tensor_scalar(difc[:, :], sgc[:, :], 4.0, -2.0, ALU.mult, ALU.add))
                cdv.seal(v.tensor_scalar(wbc[:, :], sgc[:, :], -2.0, 2.0, ALU.mult, ALU.add))
                cdv.gate(v)
                cdv.seal(v.tensor_single_scalar(t1c[:, :], ind[:, :], wbc[:, 0:1], ALU.mult))
                cdv.gate(v)
                return v.scalar_tensor_tensor(w_tm[tc % 2][:, :], is1[:, :], difc[:, 0:1],
                                              t1c[:, :], ALU.mult, ALU.add)
            return fn

        for tc in range(NTC):
            ss.add("pe", lg_group(tc), reads=("routerT",), writes=(f"pst{tc % 2}",),
                   extra=[(xl7, 16)])
            ss.add("act", (lambda tc: lambda sc: sc.copy(
                lg_tm[tc % 2][:, :], pst[tc % 2][:, 0:E1]))(tc),
                reads=(f"pst{tc % 2}",), writes=(f"lg{tc % 2}",))
            ss.add("dve", chain_a(tc), reads=(f"lg{tc % 2}",),
                   writes=("dcol", "m1", "m2", "is1", "ind", "sel2"))
            ss.add("act", (lambda tc: lambda sc: sc.activation(
                sgc[:, :], dcol[:, :], AF.Sigmoid))(tc),
                reads=("dcol",), writes=("sgc",))
            ss.add("dve", chain_b(tc), reads=("sgc", "is1", "ind"),
                   writes=(f"w_tm{tc % 2}", "difc", "wbc", "t1c"))
            ss.add("pe", (lambda tc: lambda t: t.transpose(
                ps_tb[0:E1, 0:128], w_tm[tc % 2][:, :], idb[:, :]))(tc),
                reads=(f"w_tm{tc % 2}",), writes=("ps_tb",))
            ss.add("dve", (lambda tc: lambda v: v.tensor_copy(
                wT_sb[:, tc * 128:(tc + 1) * 128], ps_tb[0:E1, 0:128]))(tc),
                reads=("ps_tb",), writes=("wT_sb",))
        for tt in range(4):
            ss.add("pe", (lambda tt: lambda t: t.matmul(
                psc[0][0:64, :], rrep_sb[:, :], wT_sb[:, tt * 512:(tt + 1) * 512],
                start=True, stop=True))(tt),
                reads=("wT_sb",), writes=("psc0",), extra=[(gl7, 6)])
            ss.add("dve", (lambda tt: lambda v: v.tensor_copy(
                wb_sb[:, tt * 512:(tt + 1) * 512], psc[0][0:64, :]))(tt),
                reads=("psc0",), writes=("wb_sb",))

        def h_group(tt):
            def fn(t):
                for ic in range(15):
                    t.matmul(psc[1][0:64, :], A_T[:, ic * 64:(ic + 1) * 64],
                             xn[ic][:, tt * 512:(tt + 1) * 512],
                             start=(ic == 0), stop=False)
                return t.matmul(psc[1][0:64, :], A_T[:, 15 * 64:16 * 64],
                                xn[15][:, tt * 512:(tt + 1) * 512], start=False, stop=True)
            return fn

        for tt in range(4):
            ss.add("pe", h_group(tt), reads=("A_T",), writes=("psc1",), extra=[(xl7, 16)])
            ss.add("act", (lambda tt: lambda sc: sc.copy(
                h_sb[:, tt * 512:(tt + 1) * 512], psc[1][0:64, :]))(tt),
                reads=("psc1",), writes=("h_sb",))
            ss.add("dve", (lambda tt: lambda v: v.tensor_tensor(
                hs_sb[:, tt * 512:(tt + 1) * 512], h_sb[:, tt * 512:(tt + 1) * 512],
                wb_sb[:, tt * 512:(tt + 1) * 512], ALU.mult))(tt),
                reads=("h_sb", "wb_sb"), writes=("hs_sb",))

        NPE, NDVE, NACT, NGP = (ss.counts["pe"], ss.counts["dve"],
                                ss.counts["act"], ss.counts["gp"])

        @blk.gpsimd
        def _(g):
            for c in range(4):
                cbs.inc(g.dma_start(wbase_bf[c * 512:(c + 1) * 512, :],
                                    wbase[c * 512:(c + 1) * 512, :]))
            cbo.inc(g.dma_start(a_bfd, aw))
            cbo.inc(g.dma_start(b_bfd, bw))
            cbo.inc(g.dma_start(rout_bfd, wrout))
            gl7.inc(g.dma_start(idb[:, :], C["ident_bf"]))
            gl7.inc(g.dma_start(idf[:, :], C["ident_f32"]))
            gl7.inc(g.dma_start(rrep_sb[:, :], C["rrep"]))
            gl7.inc(g.dma_start(bbrow[:, :], bbase.rearrange("(b a) -> b a", b=1)))
            cbo.wait(g, 3)
            gl7.inc(g.dma_start(a_sb[:, :], a_bfd))
            gl7.inc(g.dma_start(rsb[:, :], rout_bfd))
            for ic in range(16):
                xl7.inc(g.dma_start(xn[ic][:, :], xn1_fm[ic * 128:(ic + 1) * 128, :]))
            ss.emit("gp", g)
            for gg in range(66):
                if gg < 64:
                    gi = gg
                    ob, tt, oc4, ocg, k = gsplit(gi)
                    d_x.wait(g, gi - 3)
                    ss1.inc(g.dma_start(s1t[gi % 4][:, :],
                                        s1_fm[ocg * 128:(ocg + 1) * 128, tt * 512:(tt + 1) * 512]), gi)
                    sx1.inc(g.dma_start(x1t[gi % 4][:, :],
                                        x1_fm[ocg * 128:(ocg + 1) * 128, tt * 512:(tt + 1) * 512]), gi)
                if gg >= 2:
                    gi = gg - 2
                    ob, tt, oc4, ocg, k = gsplit(gi)
                    for q in range(4):
                        u = gi * 4 + q
                        d_ev.wait(g, u + 1)
                        sto.inc(g.dma_start(
                            out_tm[tt * 512 + q * 128: tt * 512 + (q + 1) * 128,
                                   ocg * 128:(ocg + 1) * 128],
                            otb[u % 4][:, :]), u)
            sto.wait_all(g, 64)

        @blk.sync
        def _(s):
            cbs.wait(s, 4)
            for ob in range(4):
                if ob >= 2:
                    p_mm.wait(s, 17 * 16 * (ob - 1))
                for ic in range(16):
                    swb.inc(s.dma_start_transpose(
                        wbt[ob % 2][:, ic * 512:(ic + 1) * 512],
                        wbase_bf[ob * 512:(ob + 1) * 512, ic * 128:(ic + 1) * 128]), ob)

        @blk.tensor
        def _(t):
            gl7.wait(t, 6)
            ss.emit("pe", t)
            d_su.wait(t, NDVE)
            a_su.wait(t, NACT)
            g_su.wait(t, NGP)
            for gi in range(64):
                ob, tt, oc4, ocg, k = gsplit(gi)
                if gi % 16 == 0:
                    swb.wait(t, ob, 16 * (ob // 2 + 1))
                if gi >= 2:
                    a_g.wait(t, gi - 1)
                for ic in range(16):
                    p_mm.inc(t.matmul(
                        psc[oc4][:, :],
                        wbt[ob % 2][:, ic * 512 + oc4 * 128: ic * 512 + (oc4 + 1) * 128],
                        xn[ic][:, tt * 512:(tt + 1) * 512],
                        start=(ic == 0), stop=False))
                p_mm.inc(t.matmul(psc[oc4][:, :],
                                  B_T[:, ocg * 128:(ocg + 1) * 128],
                                  hs_sb[:, tt * 512:(tt + 1) * 512],
                                  start=False, stop=True))
                if gi >= 2:
                    gp_ = gi - 2
                    d_x.wait(t, gp_ + 1)
                    for q in range(4):
                        u = gp_ * 4 + q
                        if u >= 2:
                            d_ev.wait(t, u - 1)
                        p_tp.inc(t.transpose(pst[u % 2][:, :],
                                             x2b[gp_ % 2][:, q * 128:(q + 1) * 128],
                                             idf[:, :]))
            for gp_ in (62, 63):
                d_x.wait(t, gp_ + 1)
                for q in range(4):
                    u = gp_ * 4 + q
                    d_ev.wait(t, u - 1)
                    p_tp.inc(t.transpose(pst[u % 2][:, :],
                                         x2b[gp_ % 2][:, q * 128:(q + 1) * 128],
                                         idf[:, :]))

        @blk.vector
        def _(v):
            ss.emit("dve", v)
            def tp_evacs(v, gp2):
                for q in range(4):
                    u = gp2 * 4 + q
                    p_tp.wait(v, u + 1)
                    if u >= 4:
                        sto.wait(v, u, u // 4)
                    d_ev.inc(v.tensor_copy(otb[u % 4][:, :], pst[u % 2][:, :]))

            for gi in range(64):
                ob, tt, oc4, ocg, k = gsplit(gi)
                if gi >= 2:
                    tp_evacs(v, gi - 2)
                a_g.wait(v, gi + 1)
                ss1.wait(v, gi, gi // 4 + 1)
                sx1.wait(v, gi, gi // 4 + 1)
                if gi >= 2:
                    p_tp.wait(v, 4 * (gi - 1))
                cdv.seal(v.tensor_tensor(sgb[gi % 2][:, :], gb[gi % 2][:, :],
                                         s1t[gi % 4][:, :], ALU.mult))
                cdv.gate(v)
                d_x.inc(v.tensor_tensor(x2b[gi % 2][:, :], sgb[gi % 2][:, :],
                                        x1t[gi % 4][:, :], ALU.add))
            for gp2 in (62, 63):
                tp_evacs(v, gp2)

        @blk.scalar
        def _(sc):
            ss.emit("act", sc)
            for gi in range(64):
                ob, tt, oc4, ocg, k = gsplit(gi)
                p_mm.wait(sc, gi * 17 + 17)
                if gi >= 2:
                    d_x.wait(sc, gi - 1)
                a_g.inc(sc.activation(gb[gi % 2][:, :], psc[oc4][:, :], AF.Silu,
                                      bias=bb_cols[:, ocg:ocg + 1]))


_C_P1 = ["ident_f32"]
_C_P2 = ["ident_bf", "onesf_row"]
_C_P3 = ["ident_f32", "onesf_row"]
_C_P5A = ["ones_row_bf"]
_C_P5B = ["onesf_row", "onesf_col"]
_C_P6 = ["ntri_bf", "ones_row_bf", "negcol_bf", "ident_bf"]
_C_P7 = ["ident_f32", "ident_bf", "onesf_row", "rrep"]

_IO = {
    "x_tm": ([S, DM], F32), "x0_fm": ([DM, S], F32), "xn0_tm": ([S, DM], BF16),
    "qkv_w": ([3 * H0 * DH, DM], F32), "router_w": ([E0, DM], F32),
    "ebias": ([E0], F32), "qkv_fm": ([3 * H0 * DH, S], BF16),
    "combT": ([E0, S], BF16), "norm_w": ([DM], F32), "attnn_fm": ([DM, S], BF16),
    "gate_w": ([DM, DM], F32), "out_w": ([DM, DM], F32), "s0_fm": ([DM, S], BF16),
    "sw1": ([HS, DM], F32), "sw2": ([HS, DM], F32), "w12": ([E0 * 2 * HR, DM], F32),
    "hsh_fm": ([HS, S], BF16), "hrt_fm": ([E0 * HR, S], BF16),
    "sw3": ([DM, HS], F32), "w3": ([E0 * DM, HR], F32),
    "x1_fm": ([DM, S], F32), "xn1_fm": ([DM, S], BF16),
    "wq": ([DM, DM], F32), "wk": ([DM, DM], F32), "wv": ([DM, DM], F32),
    "wo": ([DM, DM], F32), "s1_fm": ([DM, S], BF16),
    "wbase": ([DM, DM], F32), "bbase": ([DM], F32), "wrout": ([E1, DM], F32),
    "aw": ([E1 * R1, DM], F32), "bw": ([E1 * DM, R1], F32), "out_tm": ([S, DM], F32),
}

_PHASES = [
    ("p1", _emit_p1, ["x_tm"], ["x0_fm", "xn0_tm"], _C_P1),
    ("p2", _emit_p2, ["qkv_w", "router_w", "ebias", "xn0_tm"], ["qkv_fm", "combT"], _C_P2),
    ("p3", _emit_p3, ["qkv_fm", "norm_w"], ["attnn_fm"], _C_P3),
    ("p4", _emit_p4, ["gate_w", "out_w", "xn0_tm", "attnn_fm"], ["s0_fm"], _C_P4_ := []),
    ("p5a", _emit_p5a, ["sw1", "sw2", "w12", "combT", "xn0_tm"], ["hsh_fm", "hrt_fm"], _C_P5A),
    ("p5b", _emit_p5b, ["sw3", "w3", "s0_fm", "x0_fm", "hsh_fm", "hrt_fm"],
     ["x1_fm", "xn1_fm"], _C_P5B),
    ("p6", _emit_p6, ["wq", "wk", "wv", "wo", "xn1_fm"], ["s1_fm"], _C_P6),
    ("p7", _emit_p7, ["wbase", "bbase", "wrout", "aw", "bw", "xn1_fm", "x1_fm", "s1_fm"],
     ["out_tm"], _C_P7),
]

_MODULE_INS = ["x_tm", "qkv_w", "router_w", "ebias", "norm_w", "gate_w", "out_w",
               "sw1", "sw2", "w12", "sw3", "w3", "wq", "wk", "wv", "wo",
               "wbase", "bbase", "wrout", "aw", "bw"]
_ALL_CONSTS = sorted({c for (_, _, _, _, cs) in _PHASES for c in cs})


def _build_phase(name):
    (pname, emit, ins, outs, consts) = next(p for p in _PHASES if p[0] == name)
    nc = bass.Bass("TRN2", target_bir_lowering=False)
    T = {}
    for n in ins:
        shp, dt = _IO[n]
        T[n] = nc.dram_tensor(n, shp, dt, kind="ExternalInput").ap()
    for n in outs:
        shp, dt = _IO[n]
        T[n] = nc.dram_tensor(n, shp, dt, kind="ExternalOutput").ap()
    T.update(_decl_consts(nc, consts))
    with ExitStack() as pst:
        ph = PhaseCtx(nc, pst, None, 0)
        emit(nc, T, ph)
    return nc


def _build_p1():
    return _build_phase("p1")


def _build_p2():
    return _build_phase("p2")


def _build_p3():
    return _build_phase("p3")


def _build_p4():
    return _build_phase("p4")


def _build_p5a():
    return _build_phase("p5a")


def _build_p5b():
    return _build_phase("p5b")


def _build_p6():
    return _build_phase("p6")


def _build_p7():
    return _build_phase("p7")


def _build_fused():
    nc = bass.Bass("TRN2", target_bir_lowering=False)
    produced = {o for (_, _, _, outs, _) in _PHASES for o in outs}
    T = {}
    for n in _MODULE_INS:
        shp, dt = _IO[n]
        T[n] = nc.dram_tensor(n, shp, dt, kind="ExternalInput").ap()
    for n in produced:
        shp, dt = _IO[n]
        kind = "ExternalOutput" if n == "out_tm" else "Internal"
        T[n] = nc.dram_tensor(n, shp, dt, kind=kind).ap()
    T.update(_decl_consts(nc, _ALL_CONSTS))
    for pidx, (pname, emit, ins, outs, consts) in enumerate(_PHASES):
        with nc.reset_on_exit():
            with ExitStack() as pst:
                ph = PhaseCtx(nc, pst, None, pidx)
                emit(nc, T, ph)
    return nc


def _run(name, builder, in_maps):
    if name not in _prog_cache:
        _prog_cache[name] = builder()
    nc = _prog_cache[name]
    res = run_bass_kernel_spmd(nc, in_maps, core_ids=list(range(len(in_maps))))
    return res.results


def kernel(**inputs):
    x = np.ascontiguousarray(inputs["L0_attn_hidden_states"], dtype=np.float32)
    n = x.shape[0]
    w = dict(
        qkv_w=np.ascontiguousarray(inputs["L0_attn_qkv_w"]),
        out_w=np.ascontiguousarray(inputs["L0_attn_out_w"]),
        gate_w=np.ascontiguousarray(inputs["L0_attn_gate_w"]),
        norm_w=np.ascontiguousarray(inputs["L0_attn_norm_w"]),
        w12=np.ascontiguousarray(inputs["L0_ffn_w12"]).reshape(E0 * 2 * HR, DM),
        w3=np.ascontiguousarray(inputs["L0_ffn_w3"]).reshape(E0 * DM, HR),
        router_w=np.ascontiguousarray(inputs["L0_ffn_router_w"]),
        ebias=np.ascontiguousarray(inputs["L0_ffn_expert_bias"]),
        sw1=np.ascontiguousarray(inputs["L0_ffn_sw1"]),
        sw2=np.ascontiguousarray(inputs["L0_ffn_sw2"]),
        sw3=np.ascontiguousarray(inputs["L0_ffn_sw3"]),
        wq=np.ascontiguousarray(inputs["L1_attn_Wq"]),
        wk=np.ascontiguousarray(inputs["L1_attn_Wk"]),
        wv=np.ascontiguousarray(inputs["L1_attn_Wv"]),
        wo=np.ascontiguousarray(inputs["L1_attn_Wo"]),
        wbase=np.ascontiguousarray(inputs["L1_ffn_W_base"]),
        bbase=np.ascontiguousarray(inputs["L1_ffn_b_base"]),
        wrout=np.ascontiguousarray(inputs["L1_ffn_W_router"]),
        aw=np.ascontiguousarray(inputs["L1_ffn_A_w"]).reshape(E1 * R1, DM),
        bw=np.ascontiguousarray(inputs["L1_ffn_B_w"]).reshape(E1 * DM, R1),
        **_const_inputs(_ALL_CONSTS),
    )
    res = _run("fused", _build_fused,
               [dict(x_tm=x[c], **w) for c in range(n)])
    return np.stack([np.asarray(res[c]["out_tm"], dtype=np.float32) for c in range(n)])
